# version 15
# baseline (speedup 1.0000x reference)
import contextlib
import numpy as np
import concourse.bass as bass
import concourse.mybir as mybir
from concourse.bass_utils import run_bass_kernel_spmd

from concourse import library_config

F32 = mybir.dt.float32
BF16 = mybir.dt.bfloat16
I16 = mybir.dt.int16
U32 = mybir.dt.uint32
AF = mybir.ActivationFunctionType
ALU = mybir.AluOpType
AX = mybir.AxisListType

FULL = dict(D=1024, T=2048, HID=4096, E=8, FT=256, CAP=768)
MINI = dict(D=256, T=256, HID=512, E=8, FT=256, CAP=256)

LN_EPS = 1e-5
ATTN_EPS = 1e-8


class Ctr:
    def __init__(self, nc, name):
        self.h = nc.alloc_semaphore(name=name)
        self.n = 0

    def tick(self, instr, n=1):
        instr.then_inc(self.h, n)
        self.n += n
        return self.n


class Syncer:
    def __init__(self):
        self.wm = {}

    def wait(self, eng, ctr, val):
        if not val or val <= 0:
            return
        k = (id(eng), id(ctr))
        if self.wm.get(k, 0) >= val:
            return
        eng.wait_ge(ctr.h, val)
        self.wm[k] = val


def build_program(cfg):
    D, T, HID, E, FT = (cfg[k] for k in ("D", "T", "HID", "E", "FT"))
    CAP = cfg["CAP"]
    CSUB = FT
    NCS = CAP // CSUB
    CAPS = CAP // 128
    KC = D // 128
    NH = D // 64
    HP = NH // 2
    TC = T // 128
    HC = HID // 128
    FTC = FT // 128
    NFT = T // FT
    PT = min(512, T)
    W2W = min(512, D)
    NFH = D // W2W
    NBY = FTC * NFH
    assert NBY <= 4

    nc = bass.Bass("TRN2", target_bir_lowering=False, debug=False)
    S = Syncer()

    x_own = nc.dram_tensor("x_own", [128, TC, D], F32, kind="ExternalInput").ap()
    x_pre = nc.dram_tensor("x_pre", [128, TC, D], F32, kind="ExternalInput").ap()
    gate = nc.dram_tensor("gate", [128, 1], F32, kind="ExternalInput").ap()
    wq = nc.dram_tensor("wq", [HP, 128, KC * 128], F32, kind="ExternalInput").ap()
    wk = nc.dram_tensor("wk", [HP, 128, KC * 128], F32, kind="ExternalInput").ap()
    wv = nc.dram_tensor("wv", [HP, 128, KC * 128], F32, kind="ExternalInput").ap()
    wo = nc.dram_tensor("wo", [HP, 128, D], F32, kind="ExternalInput").ap()
    w1 = nc.dram_tensor("w1", [E, HC, 128, KC * 128], F32, kind="ExternalInput").ap()
    w2 = nc.dram_tensor("w2", [E, HC, 128, D], F32, kind="ExternalInput").ap()
    b1 = nc.dram_tensor("b1", [128, E, HC], F32, kind="ExternalInput").ap()
    b2 = nc.dram_tensor("b2", [8, D], F32, kind="ExternalInput").ap()
    wr = nc.dram_tensor("wr", [128, KC, 8], F32, kind="ExternalInput").ap()
    um = nc.dram_tensor("um", [128, 256], F32, kind="ExternalInput").ap()
    ident = nc.dram_tensor("ident", [128, 128], F32, kind="ExternalInput").ap()
    slt_in = nc.dram_tensor("slt_in", [128, 128], BF16, kind="ExternalInput").ap()
    onesb_in = nc.dram_tensor("onesb_in", [128, 128], BF16, kind="ExternalInput").ap()
    offs_in = nc.dram_tensor("offs_in", [128, 8], F32, kind="ExternalInput").ap()
    capd_in = nc.dram_tensor("capd_in", [128, 1], F32, kind="ExternalInput").ap()
    iota0_in = nc.dram_tensor("iota0_in", [128, TC], F32, kind="ExternalInput").ap()
    iop_in = nc.dram_tensor("iop_in", [128, 1], F32, kind="ExternalInput").ap()
    y = nc.dram_tensor("y", [T, D], F32, kind="ExternalOutput").ap()
    y_pc = y.rearrange("(c p) d -> p c d", p=128)
    r2h = nc.dram_tensor("r2h", [128, TC, D], F32, kind="Internal").ap()
    XCR = E * (CAP + 128)
    XCW = D + 64
    YAR = T + XCR + 256
    xc = nc.dram_tensor("xc", [XCR, XCW], F32, kind="Internal").ap()
    y_acc = nc.dram_tensor("y_acc", [YAR, D], F32, kind="Internal").ap()

    es = contextlib.ExitStack()
    sbuf = lambda name, shape, dt=F32: es.enter_context(nc.sbuf_tensor(name, shape, dt))

    cP = Ctr(nc, "cP")
    cV = Ctr(nc, "cV")
    cA = Ctr(nc, "cA")
    dX = Ctr(nc, "dX")
    dW = [Ctr(nc, "dW0"), Ctr(nc, "dW1")]
    dG = [Ctr(nc, "dG0"), Ctr(nc, "dG1")]
    dO = Ctr(nc, "dO")

    pe, ve, ae, sy, gp = nc.tensor, nc.vector, nc.scalar, nc.sync, nc.gpsimd
    ALL_ENG = (pe, ve, ae, sy, gp)

    def vop(instr):
        t = cV.tick(instr)
        S.wait(ve, cV, t)
        return t

    def aop(instr):
        t = cA.tick(instr)
        S.wait(ae, cA, t)
        return t

    def sdma(out, in_):
        S.wait(sy, dX, dX.n)
        sy.dma_start(out=out, in_=in_).then_inc(dX.h, 16)
        dX.n += 16
        return dX.n

    def barrier():
        for ctr in (cP, cV, cA):
            for eng in ALL_ENG:
                S.wait(eng, ctr, ctr.n)

    um_s = sbuf("um_s", [128, 256])
    id_s = sbuf("id_s", [128, 128])
    gate_s = sbuf("gate_s", [128, 1])
    wr_f = sbuf("wr_f", [128, KC, 8])
    wr_h = sbuf("wr_h", [128, KC, 8], BF16)
    wr_l = sbuf("wr_l", [128, KC, 8], BF16)
    b1_s = sbuf("b1_s", [128, E, HC])
    b2_h = sbuf("b2_h", [8, D], BF16)
    b2_l = sbuf("b2_l", [8, D], BF16)
    lnx = sbuf("lnx", [128, max(D + 64, max(D, 2 * min(512, T)) + min(512, T))])
    lntm = lnx[:, 0:D]
    lnsc = sbuf("lnsc", [128, 16])
    lnh = sbuf("lnh", [128, KC, T], BF16)
    lnl = sbuf("lnl", [128, KC, T], BF16)
    wstg = [sbuf(f"wstg{i}", [128, KC * 128]) for i in range(2)]
    wsh = [sbuf(f"wsh{i}", [128, KC * 128], BF16) for i in range(2)]
    wsl = [sbuf(f"wsl{i}", [128, KC * 128], BF16) for i in range(2)]

    pb = [es.enter_context(nc.psum_tensor(f"pb{i}", [128, 512], F32)) for i in range(8)]
    bank_free = {i: [] for i in range(8)}

    def bank_wait(b):
        for ctr, tok in bank_free[b]:
            S.wait(pe, ctr, tok)
        bank_free[b] = []

    def bank_used(b, ctr, tok):
        bank_free[b].append((ctr, tok))

    sdma(um_s[:, :], um)
    sdma(id_s[:, :], ident)
    sdma(gate_s[:, :], gate)
    sdma(wr_f[:, :, :], wr)
    sdma(b1_s[:, :, :], b1)
    tok_consts = sdma(lnx[0:8, 0:D], b2)
    S.wait(ae, dX, tok_consts)
    t_wrh = aop(ae.activation(wr_h[:, :, :], wr_f[:, :, :], AF.Copy))
    t_b2h = aop(ae.activation(b2_h[:, :], lnx[0:8, 0:D], AF.Copy))
    S.wait(ve, dX, tok_consts)
    S.wait(ve, cA, t_b2h)
    t_wrl = vop(ve.tensor_sub(wr_l[:, :, :], wr_f[:, :, :], wr_h[:, :, :]))
    t_b2l = vop(ve.tensor_sub(b2_l[:, :], lnx[0:8, 0:D], b2_h[:, :]))

    attn_es = contextlib.ExitStack()
    asb = lambda name, shape, dt=F32: attn_es.enter_context(nc.sbuf_tensor(name, shape, dt))
    xbuf = asb("xbuf", [128, TC, D])
    qb = asb("qb", [128, T])
    kb = asb("kb", [128, T])
    vb = asb("vb", [128, T])
    s_run = asb("s_run", [128, NH, 128])
    ktm = [asb(f"ktm{i}", [128, 128]) for i in range(2)]
    vaug = [asb(f"vaug{i}", [128, 256]) for i in range(2)]
    qpad = [[asb(f"qpad{i}_{j}", [128, 128]) for i in range(2)] for j in range(2)]
    kpad = [[asb(f"kpad{i}_{j}", [128, 128]) for i in range(2)] for j in range(2)]
    am2_sb = [asb(f"am2_{j}", [128, 256]) for j in range(2)]
    attn_tm = asb("attn_tm", [128, TC, 128])
    ah_sb = asb("ah_sb", [128, 128], BF16)
    al_sb = asb("al_sb", [128, 128], BF16)
    esc1 = lnx[:, 0:PT]
    esc2 = lnx[:, PT:2 * PT]
    esc3 = lnx[:, 2 * PT:3 * PT] if 3 * PT <= lnx.shape[1] else lnx[:, lnx.shape[1] - PT:]
    numsb = asb("numsb", [128, 256])
    densb = asb("densb", [128, 4])

    vop(ve.memset(s_run[:, :, :], 0.0))
    for h in range(2):
        for j in range(2):
            vop(ve.memset(qpad[h][j][:, :], 0.0))
            vop(ve.memset(kpad[h][j][:, :], 0.0))
    for i in range(2):
        vop(ve.memset(vaug[i][:, :], 0.0))
        vop(ve.memset(vaug[i][:, 64:65], 1.0))
        vop(ve.memset(vaug[i][:, 192:193], 1.0))

    def probe_exit(src=None, dst=None, close_attn=True):
        S.wait(gp, cV, cV.n)
        S.wait(gp, cA, cA.n)
        for ctr in (cP,):
            S.wait(gp, ctr, ctr.n)
        S.wait(gp, dO, dO.n)
        gp.dma_start(out=(y_pc if dst is None else dst),
                     in_=(xbuf[:, :, :] if src is None else src)).then_inc(dO.h, 16)
        dO.n += 16
        S.wait(gp, dO, dO.n)
        if close_attn:
            attn_es.close()
        es.close()
        return nc

    war = {
        "lntm_pe": 0,
        "lntm_dma": None,
        "combpad_dma": None,
        "mbf_pe": 0,
        "stage": [[(None, 0)], [(None, 0)]],
        "split": [0, 0],
        "vaug": [0, 0],
        "ktm": [0, 0],
        "pad": [0, 0],
        "am2": [0, 0],
        "sadd": 0,
    }

    def emit_ln_chunk(src_ap, col0, tr_banks):
        S.wait(ae, cP, war["lntm_pe"])
        if war["lntm_dma"]:
            S.wait(ae, *war["lntm_dma"])
        t1 = aop(ae.activation(lntm, src_ap, AF.Square, accum_out=lnsc[:, 0:1]))
        t2 = aop(ae.activation(lntm, src_ap, AF.Identity, accum_out=lnsc[:, 1:2]))
        S.wait(ve, cA, t2)
        vop(ve.tensor_scalar_mul(lnsc[:, 2:3], lnsc[:, 1:2], 1.0 / D))
        vop(ve.tensor_mul(lnsc[:, 3:4], lnsc[:, 2:3], lnsc[:, 2:3]))
        vop(ve.scalar_tensor_tensor(lnsc[:, 4:5], lnsc[:, 0:1], 1.0 / D, lnsc[:, 3:4],
                                    ALU.mult, ALU.subtract))
        tv = vop(ve.tensor_scalar_add(lnsc[:, 5:6], lnsc[:, 4:5], LN_EPS))
        S.wait(ae, cV, tv)
        ts = aop(ae.activation(lnsc[:, 6:7], lnsc[:, 5:6], AF.Sqrt))
        S.wait(ve, cA, ts)
        vop(ve.reciprocal(lnsc[:, 7:8], lnsc[:, 6:7]))
        ycol = 7
        for it in range(2):
            a, b_, c_, yn = 8, 9, 10, 11 + it
            vop(ve.tensor_mul(lnsc[:, a:a + 1], lnsc[:, ycol:ycol + 1], lnsc[:, ycol:ycol + 1]))
            vop(ve.tensor_mul(lnsc[:, b_:b_ + 1], lnsc[:, 5:6], lnsc[:, a:a + 1]))
            vop(ve.tensor_scalar(lnsc[:, c_:c_ + 1], lnsc[:, b_:b_ + 1], -0.5, 1.5,
                                 ALU.mult, ALU.add))
            vop(ve.tensor_mul(lnsc[:, yn:yn + 1], lnsc[:, ycol:ycol + 1], lnsc[:, c_:c_ + 1]))
            ycol = yn
        S.wait(ve, cA, t2)
        S.wait(ve, cP, war["lntm_pe"])
        if war["lntm_dma"]:
            S.wait(ve, *war["lntm_dma"])
        t_ap = vop(ve.tensor_scalar(lntm, src_ap, lnsc[:, 2:3], lnsc[:, ycol:ycol + 1],
                                    ALU.subtract, ALU.mult))
        for k in range(KC):
            b = tr_banks[k % 2]
            S.wait(pe, cV, t_ap)
            bank_wait(b)
            t_tr = cP.tick(pe.transpose(pb[b][:, 0:128], lntm[:, k * 128:(k + 1) * 128], id_s[:, :]))
            S.wait(ae, cP, t_tr)
            t_hi = aop(ae.activation(lnh[:, k, col0:col0 + 128], pb[b][:, 0:128], AF.Copy))
            S.wait(ve, cP, t_tr)
            S.wait(ve, cA, t_hi)
            t_lo = vop(ve.tensor_sub(lnl[:, k, col0:col0 + 128], pb[b][:, 0:128],
                                     lnh[:, k, col0:col0 + 128]))
            bank_used(b, cV, t_lo)
            bank_used(b, cA, t_hi)
        war["lntm_pe"] = cP.n
        return t_ap

    def w_dma(par, src_ap):
        for ctr, tok in war["stage"][par]:
            if ctr is not None:
                S.wait(sy, ctr, tok)
        S.wait(sy, dW[par], dW[par].n)
        sy.dma_start(out=wstg[par][:, :], in_=src_ap).then_inc(dW[par].h, 16)
        dW[par].n += 16
        return dW[par].n

    def w_split(par, tok_dma):
        S.wait(ae, dW[par], tok_dma)
        S.wait(ae, cP, war["split"][par])
        t_hi = aop(ae.activation(wsh[par][:, :], wstg[par][:, :], AF.Copy))
        S.wait(ve, dW[par], tok_dma)
        S.wait(ve, cP, war["split"][par])
        S.wait(ve, cA, t_hi)
        t_lo = vop(ve.tensor_sub(wsl[par][:, :], wstg[par][:, :], wsh[par][:, :]))
        war["stage"][par] = [(cA, t_hi), (cV, t_lo)]
        return t_hi, t_lo

    def mm3(bank, wpar, n, col, src_h=None, src_l=None):
        if src_h is None:
            src_h, src_l = lnh, lnl
        tok = None
        first = True
        for k in range(KC):
            for p in range(3):
                lhsT = (wsh if p != 2 else wsl)[wpar][:, k * 128:(k + 1) * 128]
                rhs = (src_h if p != 1 else src_l)[:, k, col:col + n]
                last = (k == KC - 1 and p == 2)
                i = pe.matmul(pb[bank][:, 0:n], lhsT, rhs, start=first, stop=last)
                first = False
                if last:
                    tok = cP.tick(i)
        return tok

    def emit_elu_evac(ps_ap, dst_ap, n):
        t_r = vop(ve.tensor_relu(esc1[:, 0:n], ps_ap))
        t_m = vop(ve.tensor_single_scalar(esc2[:, 0:n], ps_ap, 0.0, ALU.min))
        S.wait(ae, cV, t_m)
        t_e = aop(ae.activation(esc3[:, 0:n], esc2[:, 0:n], AF.Exp))
        S.wait(ve, cA, t_e)
        return vop(ve.tensor_add(dst_ap, esc1[:, 0:n], esc3[:, 0:n]))

    def emit_proj(dst, elu, wpar, tok_hi, tok_lo, banks):
        t_ev = 0
        for i0, t0 in enumerate(range(0, T, PT)):
            b = banks[i0 % 2]
            S.wait(pe, cA, tok_hi)
            S.wait(pe, cV, tok_lo)
            bank_wait(b)
            tok_mm = mm3(b, wpar, PT, t0)
            S.wait(ve, cP, tok_mm)
            if elu:
                t_ev = emit_elu_evac(pb[b][:, 0:PT], dst[:, t0:t0 + PT], PT)
            else:
                t_ev = vop(ve.tensor_copy(dst[:, t0:t0 + PT], pb[b][:, 0:PT]))
            bank_used(b, cV, t_ev)
        war["split"][wpar] = cP.n
        return t_ev

    def mix_state_chunk(c, hp, par, tok_k, tok_v, update_state=True):
        bt, bs = 6 + (c % 2), 5
        S.wait(pe, cV, max(tok_k, tok_v))
        bank_wait(bt)
        t_trk = cP.tick(pe.transpose(pb[bt][:, 0:128], kb[:, c * 128:(c + 1) * 128], id_s[:, :]))
        t_trv = cP.tick(pe.transpose(pb[bt][:, 128:256], vb[:, c * 128:(c + 1) * 128], id_s[:, :]))
        S.wait(ae, cP, max(t_trv, war["ktm"][par], war["vaug"][par]))
        t_ktm = aop(ae.activation(ktm[par][:, :], pb[bt][:, 0:128], AF.Copy))
        aop(ae.activation(vaug[par][:, 0:64], pb[bt][:, 128:192], AF.Copy))
        t_v1 = aop(ae.activation(vaug[par][:, 128:192], pb[bt][:, 192:256], AF.Copy))
        bank_used(bt, cA, t_v1)
        if update_state:
            S.wait(pe, cA, t_v1)
            bank_wait(bs)
            cP.tick(pe.matmul(pb[bs][:, 0:128], ktm[par][:, :], vaug[par][:, 0:128],
                              start=True, stop=True))
            t_s = cP.tick(pe.matmul(pb[bs][:, 128:256], ktm[par][:, :], vaug[par][:, 128:256],
                                    start=True, stop=True))
            S.wait(ve, cP, t_s)
            gh0, gh1 = 2 * hp, 2 * hp + 1
            vop(ve.tensor_add(s_run[0:64, gh0, 0:128], s_run[0:64, gh0, 0:128],
                              pb[bs][0:64, 0:128]))
            t_up = vop(ve.tensor_add(s_run[64:128, gh1, 0:128], s_run[64:128, gh1, 0:128],
                                     pb[bs][64:128, 128:256]))
            bank_used(bs, cV, t_up)
        war["ktm"][par] = cP.n
        war["vaug"][par] = cP.n
        return t_ktm, t_v1

    if cfg.get("stop") == "ph0":
        return probe_exit()
    tok_xpre = sdma(xbuf[:, :, :], x_pre)
    S.wait(ae, dX, tok_xpre)
    S.wait(ve, dX, tok_xpre)
    S.wait(ae, cV, t_b2l)
    for j in range(TC):
        emit_ln_chunk(xbuf[:, j, :], j * 128, (0, 1))

    pend_k = w_dma(0, wk[0])
    pend_v = w_dma(1, wv[0])
    for hp in range(HP):
        tok_hi, tok_lo = w_split(0, pend_k)
        tok_k = emit_proj(kb, True, 0, tok_hi, tok_lo, (2, 3))
        tok_hi, tok_lo = w_split(1, pend_v)
        tok_v = emit_proj(vb, False, 1, tok_hi, tok_lo, (2, 3))
        if hp + 1 < HP:
            pend_k = w_dma(0, wk[hp + 1])
            pend_v = w_dma(1, wv[hp + 1])
        for c in range(TC):
            mix_state_chunk(c, hp, c % 2, tok_k, tok_v)

    vop(ve.tensor_scalar_mul(s_run[:, :, :], s_run[:, :, :], gate_s[:, 0:1]))
    if cfg.get("stop") == "prefix":
        return probe_exit()

    barrier()
    tok_xown = sdma(xbuf[:, :, :], x_own)
    S.wait(ae, dX, tok_xown)
    S.wait(ve, dX, tok_xown)
    for j in range(TC):
        emit_ln_chunk(xbuf[:, j, :], j * 128, (0, 1))

    pend_q = w_dma(0, wq[0])
    pend_k = w_dma(1, wk[0])
    for hp in range(HP):
        tok_hi, tok_lo = w_split(0, pend_q)
        tok_q = emit_proj(qb, True, 0, tok_hi, tok_lo, (2, 3))
        tok_hi, tok_lo = w_split(1, pend_k)
        tok_k = emit_proj(kb, True, 1, tok_hi, tok_lo, (2, 3))
        pend_v = w_dma(0, wv[hp])
        tok_hi, tok_lo = w_split(0, pend_v)
        tok_v = emit_proj(vb, False, 0, tok_hi, tok_lo, (2, 3))
        pend_wo = w_dma(1, wo[hp])

        pend_mix = None
        for cstep in range(TC + 1):
            if cstep < TC:
                c = cstep
                cpar = c % 2
                par = c % 2
                t_ktm, t_v1 = mix_state_chunk(c, hp, par, tok_k, tok_v, update_state=False)
                S.wait(ae, cV, tok_q)
                S.wait(ae, cP, war["pad"][cpar])
                aop(ae.activation(qpad[0][cpar][0:64, :], qb[0:64, c * 128:(c + 1) * 128],
                                  AF.Copy))
                aop(ae.activation(kpad[0][cpar][0:64, :], kb[0:64, c * 128:(c + 1) * 128],
                                  AF.Copy))
                aop(ae.activation(qpad[1][cpar][64:128, :],
                                  qb[64:128, c * 128:(c + 1) * 128], AF.Copy))
                t_kp = aop(ae.activation(kpad[1][cpar][64:128, :],
                                         kb[64:128, c * 128:(c + 1) * 128], AF.Copy))
                this_mix = (c, cpar, par, t_kp, t_v1)
            else:
                this_mix = None
            if pend_mix is not None:
                c, cpar, par, t_kp, t_v1 = pend_mix
                b_at = 2 + cpar
                b_o = cpar
                b_s = 5
                gh0, gh1 = 2 * hp, 2 * hp + 1
                S.wait(pe, cA, t_kp)
                bank_wait(b_at)
                cP.tick(pe.matmul(pb[b_at][:, 0:128], kpad[0][cpar][:, :],
                                  qpad[0][cpar][:, :], start=True, stop=True))
                t_at = cP.tick(pe.matmul(pb[b_at][:, 128:256], kpad[1][cpar][:, :],
                                         qpad[1][cpar][:, :], start=True, stop=True))
                S.wait(ve, cP, max(t_at, war["am2"][cpar]))
                t_am = vop(ve.tensor_mul(am2_sb[cpar][:, :], pb[b_at][:, 0:256], um_s[:, :]))
                bank_used(b_at, cV, t_am)
                S.wait(pe, cV, t_am)
                S.wait(pe, cA, t_v1)
                bank_wait(b_o)
                cP.tick(pe.matmul(pb[b_o][:, 0:128], am2_sb[cpar][:, 0:128],
                                  vaug[par][:, 0:128], start=True, stop=True))
                t_o1 = cP.tick(pe.matmul(pb[b_o][:, 128:256], am2_sb[cpar][:, 128:256],
                                         vaug[par][:, 128:256], start=True, stop=True))
                S.wait(pe, cV, war["sadd"])
                cP.tick(pe.matmul(pb[b_o][:, 256:384], qpad[0][cpar][:, :],
                                  s_run[:, gh0, :], start=True, stop=True))
                t_o2 = cP.tick(pe.matmul(pb[b_o][:, 384:512], qpad[1][cpar][:, :],
                                         s_run[:, gh1, :], start=True, stop=True))
                war["pad"][cpar] = t_o2
                war["am2"][cpar] = t_o1
                war["vaug"][par] = max(war["vaug"][par], t_o1)
                bank_wait(b_s)
                cP.tick(pe.matmul(pb[b_s][:, 0:128], ktm[par][:, :], vaug[par][:, 0:128],
                                  start=True, stop=True))
                t_s = cP.tick(pe.matmul(pb[b_s][:, 128:256], ktm[par][:, :],
                                        vaug[par][:, 128:256], start=True, stop=True))
                war["ktm"][par] = max(war["ktm"][par], t_s)
                war["vaug"][par] = max(war["vaug"][par], t_s)
                S.wait(ve, cP, t_s)
                vop(ve.tensor_add(s_run[0:64, gh0, 0:128], s_run[0:64, gh0, 0:128],
                                  pb[b_s][0:64, 0:128]))
                t_up = vop(ve.tensor_add(s_run[64:128, gh1, 0:128],
                                         s_run[64:128, gh1, 0:128],
                                         pb[b_s][64:128, 128:256]))
                bank_used(b_s, cV, t_up)
                war["sadd"] = t_up
                t_cp1 = vop(ve.tensor_copy(numsb[:, 0:256], pb[b_o][:, 0:256]))
                vop(ve.scalar_tensor_tensor(densb[:, 0:2], pb[b_o][:, 320:449:128],
                                            ATTN_EPS, numsb[:, 64:193:128],
                                            ALU.add, ALU.add))
                vop(ve.reciprocal(densb[:, 2:4], densb[:, 0:2]))
                t_n = vop(ve.tensor_add(numsb[:, 0:256], numsb[:, 0:256],
                                        pb[b_o][:, 256:512]))
                bank_used(b_o, cV, t_n)
                vop(ve.tensor_scalar(attn_tm[:, c, 0:64], numsb[:, 0:64],
                                     densb[:, 2:3], None, ALU.mult))
                vop(ve.tensor_scalar(attn_tm[:, c, 64:128], numsb[:, 128:192],
                                     densb[:, 3:4], None, ALU.mult))
            pend_mix = this_mix
        tok_hi_o, tok_lo_o = w_split(1, pend_wo)
        t_wo_last = 0
        for j in range(TC):
            b_tr = 6
            S.wait(pe, cV, cV.n)
            bank_wait(b_tr)
            t_tr = cP.tick(pe.transpose(pb[b_tr][:, 0:128], attn_tm[:, j, :], id_s[:, :]))
            S.wait(ae, cP, t_tr)
            t_ah = aop(ae.activation(ah_sb[:, :], pb[b_tr][:, 0:128], AF.Copy))
            S.wait(ve, cP, t_tr)
            S.wait(ve, cA, t_ah)
            t_al = vop(ve.tensor_sub(al_sb[:, :], pb[b_tr][:, 0:128], ah_sb[:, :]))
            bank_used(b_tr, cV, t_al)
            for fh in range(NFH):
                b = (0, 1)[fh % 2]
                S.wait(pe, cA, max(t_ah, tok_hi_o))
                S.wait(pe, cV, max(t_al, tok_lo_o))
                bank_wait(b)
                pe.matmul(pb[b][:, 0:W2W], ah_sb[:, :], wsh[1][:, fh * W2W:(fh + 1) * W2W],
                          start=True, stop=False)
                pe.matmul(pb[b][:, 0:W2W], ah_sb[:, :], wsl[1][:, fh * W2W:(fh + 1) * W2W],
                          start=False, stop=False)
                i3 = pe.matmul(pb[b][:, 0:W2W], al_sb[:, :], wsh[1][:, fh * W2W:(fh + 1) * W2W],
                               start=False, stop=True)
                t_wo = cP.tick(i3)
                t_wo_last = t_wo
                S.wait(ve, cP, t_wo)
                t_acc = vop(ve.scalar_tensor_tensor(xbuf[:, j, fh * W2W:(fh + 1) * W2W],
                                                    pb[b][:, 0:W2W], 1.0,
                                                    xbuf[:, j, fh * W2W:(fh + 1) * W2W],
                                                    ALU.mult, ALU.add))
                bank_used(b, cV, t_acc)
        war["split"][1] = cP.n
        if hp + 1 < HP:
            pend_q = w_dma(0, wq[hp + 1])
            pend_k = w_dma(1, wk[hp + 1])

    S.wait(gp, cV, cV.n)
    S.wait(gp, dO, dO.n)
    gp.dma_start(out=r2h, in_=xbuf[:, :, :]).then_inc(dO.h, 16)
    dO.n += 16
    tok_r2w = dO.n

    if cfg.get("stop") == "own":
        return probe_exit()

    barrier()
    for eng in ALL_ENG:
        S.wait(eng, dO, tok_r2w)
    attn_es.close()

    r2t = sbuf("r2t", [128, FTC, D])
    hst = sbuf("hst", [128, FT])
    hh = [sbuf(f"hh{i}", [128, FT], BF16) for i in range(2)]
    hl = [sbuf(f"hl{i}", [128, FT], BF16) for i in range(2)]
    w2stg = [sbuf(f"w2stg{i}", [128, D]) for i in range(2)]
    w2h = [sbuf(f"w2h{i}", [128, D], BF16) for i in range(2)]
    w2l = [sbuf(f"w2l{i}", [128, D], BF16) for i in range(2)]
    lg = sbuf("lg", [128, 8])
    lgm1 = sbuf("lgm1", [128, 8])
    lgk = sbuf("lgk", [128, 8])
    lgm2 = sbuf("lgm2", [128, 8])
    lgsc = sbuf("lgsc", [128, 8])
    combine = sbuf("combine", [128, TC, 8])
    comb_pad = sbuf("comb_pad", [128, 128])
    cth = sbuf("cth", [8, TC, 128], BF16)
    ctl = sbuf("ctl", [8, TC, 128], BF16)
    ct_f = sbuf("ct_f", [8, 128])
    slt_s = sbuf("slt_s", [128, 128], BF16)
    onesb_s = sbuf("onesb_s", [128, 128], BF16)
    offs_s = sbuf("offs_s", [128, 8])
    capd_s = sbuf("capd_s", [128, 1])
    iota0_s = sbuf("iota0_s", [128, TC])
    iop_s = sbuf("iop_s", [128, 1])
    maskp = sbuf("maskp", [128, 128])
    masks_bf = sbuf("masks_bf", [128, 128], BF16)
    base_sb = sbuf("base_sb", [128, 8])
    posx = sbuf("posx", [128, 8])
    post = sbuf("post", [128, 8])
    posf = sbuf("posf", [128, 2])
    posu = sbuf("posu", [128, 2], U32)
    xg = [sbuf(f"xg{i}", [128, CAPS, XCW]) for i in range(2)]
    xgh = sbuf("xgh", [128, KC, FT], BF16)
    xgl = sbuf("xgl", [128, KC, FT], BF16)
    ysb = sbuf("ysb", [128, FT // 128, D])
    idxu = [sbuf(f"idxu{i}", [128, CAPS], U32) for i in range(2)]
    dSc = Ctr(nc, "dSc")
    sdma(slt_s[:, :], slt_in)
    sdma(onesb_s[:, :], onesb_in)
    sdma(offs_s[:, :], offs_in)
    sdma(capd_s[:, :], capd_in)
    sdma(iota0_s[:, :], iota0_in)
    tok_iot = sdma(iop_s[:, :], iop_in)
    vop(ve.memset(comb_pad[:, :], 0.0))
    vop(ve.memset(maskp[:, :], 0.0))
    vop(ve.memset(base_sb[:, :], 0.0))
    vop(ve.memset(xg[0][:, :, :], 0.0))
    vop(ve.memset(xg[1][:, :, :], 0.0))

    war2 = {"w2stage": [[(None, 0)], [(None, 0)]], "w2split": [0, 0], "h": [0, 0]}

    def w2_dma(par, src_ap):
        for ctr, tok in war2["w2stage"][par]:
            if ctr is not None:
                S.wait(gp, ctr, tok)
        S.wait(gp, dG[par], dG[par].n)
        gp.dma_start(out=w2stg[par][:, :], in_=src_ap).then_inc(dG[par].h, 16)
        dG[par].n += 16
        return dG[par].n

    def w2_split(par, tok_dma):
        S.wait(ae, dG[par], tok_dma)
        S.wait(ae, cP, war2["w2split"][par])
        t_hi = aop(ae.activation(w2h[par][:, :], w2stg[par][:, :], AF.Copy))
        S.wait(ve, dG[par], tok_dma)
        S.wait(ve, cP, war2["w2split"][par])
        S.wait(ve, cA, t_hi)
        t_lo = vop(ve.tensor_sub(w2l[par][:, :], w2stg[par][:, :], w2h[par][:, :]))
        war2["w2stage"][par] = [(cA, t_hi), (cV, t_lo)]
        return t_hi, t_lo

    S.wait(ve, dX, tok_iot)
    for s0 in range(XCR // 128):
        S.wait(ve, dX, dX.n)
        t_id = vop(ve.tensor_scalar_add(xg[0][:, 0, D + 8:D + 9], iop_s[:, :],
                                        float(T + s0 * 128)))
        S.wait(sy, cV, t_id)
        sdma(xc[s0 * 128:(s0 + 1) * 128, :], xg[0][:, 0, :])
    S.wait(sy, cV, cV.n)
    for r0 in range(T, YAR, 128):
        n_r = min(128, YAR - r0)
        sdma(y_acc[r0:r0 + n_r, :], xg[1][0:n_r, 0, 0:D])
    tok_xc_init = dX.n

    NG = E * NCS * HC

    def g_src(g):
        e = g // (NCS * HC)
        m = g % HC
        return e, m

    tok_w1 = {}
    tok_w2 = {}
    for g0 in range(min(2, NG)):
        e0, m0 = g_src(g0)
        tok_w1[g0] = w_dma(g0 % 2, w1[e0, m0])
        tok_w2[g0] = w2_dma(g0 % 2, w2[e0, m0])

    last_out_dma = 0
    tok_r2t_consumed = [(None, 0)]
    for ft in range(NFT):
        for ctr, tok in tok_r2t_consumed:
            if ctr is not None:
                S.wait(sy, ctr, tok)
        S.wait(sy, dO, last_out_dma)
        tok_r2t = sdma(r2t[:, :, :], r2h[:, ft * FTC:(ft + 1) * FTC, :])
        S.wait(ae, dX, tok_r2t)
        S.wait(ve, dX, tok_r2t)
        for jj in range(FTC):
            jabs = ft * FTC + jj
            col0 = jabs * 128
            t_ap = emit_ln_chunk(r2t[:, jj, :], col0, (4, 5))
            first = True
            i = None
            for k in range(KC):
                for p in range(3):
                    lhsT = (lnh if p != 2 else lnl)[:, k, col0:col0 + 128]
                    rhs = (wr_h if p != 1 else wr_l)[:, k, :]
                    if first:
                        S.wait(pe, cV, cV.n)
                        S.wait(pe, cA, cA.n)
                        bank_wait(6)
                    i = pe.matmul(pb[6][:, 0:8], lhsT, rhs, start=first,
                                  stop=(k == KC - 1 and p == 2))
                    first = False
            t_lgmm = cP.tick(i)
            S.wait(ve, cP, t_lgmm)
            t_lg = vop(ve.tensor_copy(lg[:, :], pb[6][:, 0:8]))
            bank_used(6, cV, t_lg)
            vop(ve.reduce_max(lgsc[:, 0:1], lg[:, :], axis=AX.X))
            vop(ve.tensor_scalar(lgm1[:, :], lg[:, :], lgsc[:, 0:1], None, ALU.is_equal))
            vop(ve.scalar_tensor_tensor(lgk[:, :], lgm1[:, :], -1e30, lg[:, :],
                                        ALU.mult, ALU.add))
            vop(ve.reduce_max(lgsc[:, 1:2], lgk[:, :], axis=AX.X))
            vop(ve.tensor_scalar(lgm2[:, :], lgk[:, :], lgsc[:, 1:2], None, ALU.is_equal))
            t_d = vop(ve.tensor_sub(lgsc[:, 2:3], lgsc[:, 1:2], lgsc[:, 0:1]))
            S.wait(ae, cV, t_d)
            t_e = aop(ae.activation(lgsc[:, 3:4], lgsc[:, 2:3], AF.Exp))
            S.wait(ve, cA, t_e)
            vop(ve.tensor_scalar_add(lgsc[:, 4:5], lgsc[:, 3:4], 1.0))
            vop(ve.reciprocal(lgsc[:, 5:6], lgsc[:, 4:5]))
            vop(ve.tensor_mul(lgsc[:, 6:7], lgsc[:, 3:4], lgsc[:, 5:6]))
            vop(ve.tensor_scalar(combine[:, jabs, :], lgm1[:, :], lgsc[:, 5:6], None, ALU.mult))
            t_c2 = vop(ve.scalar_tensor_tensor(combine[:, jabs, :], lgm2[:, :], lgsc[:, 6:7],
                                               combine[:, jabs, :], ALU.mult, ALU.add))
            t_cp = vop(ve.tensor_copy(comb_pad[:, 0:8], combine[:, jabs, :]))
            S.wait(pe, cV, t_cp)
            bank_wait(7)
            t_ct = cP.tick(pe.transpose(pb[7][:, 0:128], comb_pad[:, :], id_s[:, :]))
            S.wait(ve, cP, t_ct)
            t_ctf = vop(ve.tensor_copy(ct_f[:, :], pb[7][0:8, 0:128]))
            bank_used(7, cV, t_ctf)
            t_cth = vop(ve.tensor_copy(cth[:, jabs, :], ct_f[:, :]))
            vop(ve.tensor_sub(ctl[:, jabs, :], ct_f[:, :], cth[:, jabs, :]))
            S.wait(ve, cP, war["mbf_pe"])
            vop(ve.tensor_add(maskp[:, 0:8], lgm1[:, :], lgm2[:, :]))
            t_mb = vop(ve.tensor_copy(masks_bf[:, :], maskp[:, :]))
            S.wait(pe, cV, t_mb)
            bank_wait(3)
            t_pf = cP.tick(pe.matmul(pb[3][:, 0:128], slt_s[:, :], masks_bf[:, :],
                                     start=True, stop=True))
            S.wait(ve, cP, t_pf)
            t_px = vop(ve.tensor_add(posx[:, :], base_sb[:, :], pb[3][:, 0:8]))
            bank_used(3, cV, t_px)
            vop(ve.tensor_scalar(posx[:, :], posx[:, :], capd_s[:, 0:1], None, ALU.min))
            vop(ve.tensor_add(posx[:, :], posx[:, :], offs_s[:, :]))
            vop(ve.tensor_mul(post[:, :], lgm1[:, :], posx[:, :]))
            vop(ve.reduce_sum(posf[:, 0:1], post[:, :], axis=AX.X))
            vop(ve.tensor_mul(post[:, :], lgm2[:, :], posx[:, :]))
            vop(ve.reduce_sum(posf[:, 1:2], post[:, :], axis=AX.X))
            S.wait(ve, dSc, dSc.n)
            t_pu = vop(ve.tensor_copy(posu[:, :], posf[:, :]))
            bank_wait(7)
            t_tt = cP.tick(pe.matmul(pb[7][:, 0:128], onesb_s[:, :], masks_bf[:, :],
                                     start=True, stop=True))
            war["mbf_pe"] = t_tt
            S.wait(ve, cP, t_tt)
            t_bu = vop(ve.tensor_add(base_sb[:, :], base_sb[:, :], pb[7][:, 0:8]))
            bank_used(7, cV, t_bu)
            t_g8 = vop(ve.tensor_copy(lnx[:, D:D + 8], combine[:, jabs, :]))
            t_gi = vop(ve.tensor_copy(lnx[:, D + 8:D + 9], iota0_s[:, jabs:jabs + 1]))
            S.wait(gp, cV, cV.n)
            S.wait(gp, dX, tok_xc_init)
            for sl in range(2):
                S.wait(gp, dSc, dSc.n)
                gp.indirect_dma_start(
                    xc, bass.IndirectOffsetOnAxis(ap=posu[:, sl:sl + 1], axis=0),
                    lnx[:, 0:XCW], None).then_inc(dSc.h, 16)
                dSc.n += 16
            war["lntm_dma"] = (dSc, dSc.n)
        S.wait(ve, dO, last_out_dma)
        for jj in range(FTC):
            jabs = ft * FTC + jj
            for fh in range(NFH):
                b = 6 + ((jj * NFH + fh) % 2)
                S.wait(pe, cV, cV.n)
                bank_wait(b)
                pe.matmul(pb[b][:, 0:W2W], cth[:, jabs, :], b2_h[:, fh * W2W:(fh + 1) * W2W],
                          start=True, stop=False)
                pe.matmul(pb[b][:, 0:W2W], cth[:, jabs, :], b2_l[:, fh * W2W:(fh + 1) * W2W],
                          start=False, stop=False)
                i3 = pe.matmul(pb[b][:, 0:W2W], ctl[:, jabs, :], b2_h[:, fh * W2W:(fh + 1) * W2W],
                               start=False, stop=True)
                t_b = cP.tick(i3)
                S.wait(ve, cP, t_b)
                t_ev = vop(ve.scalar_tensor_tensor(
                    r2t[:, jj, fh * W2W:(fh + 1) * W2W], pb[b][:, 0:W2W], 1.0,
                    r2t[:, jj, fh * W2W:(fh + 1) * W2W], ALU.mult, ALU.add))
                bank_used(b, cV, t_ev)
        S.wait(gp, cV, cV.n)
        S.wait(gp, dO, dO.n)
        gp.dma_start(out=y_acc[ft * FT:(ft + 1) * FT, :].rearrange("(c p) d -> p c d", p=128),
                     in_=r2t[:, :, :]).then_inc(dO.h, 16)
        dO.n += 16
        last_out_dma = dO.n
        tok_r2t_consumed = [(cA, cA.n), (cV, cV.n)]

    if cfg.get("ffn_mode") == "router":
        return probe_exit(src=r2t[:, :, :], dst=y_pc[:, 0:FTC, :], close_attn=False)

    tok_comp_done = dSc.n
    war3 = {"xg_pe": [0, 0], "xg_dve": [0, 0]}
    tok_xr = {}

    def load_region(e):
        pe_ = e % 2
        S.wait(sy, dSc, tok_comp_done)
        S.wait(sy, cP, war3["xg_pe"][pe_])
        S.wait(sy, cV, war3["xg_dve"][pe_])
        tok_xr[e] = sdma(xg[pe_][:, :, :],
                         xc[e * (CAP + 128):e * (CAP + 128) + CAP, :]
                         .rearrange("(q p) c -> p q c", p=128))

    load_region(0)
    g = 0
    for e in range(E):
        par_e = e % 2
        xge = xg[par_e]
        idxe = idxu[par_e]
        if e + 1 < E:
            load_region(e + 1)
        S.wait(ve, dX, tok_xr[e])
        S.wait(ve, dSc, dSc.n)
        t_idx = vop(ve.tensor_copy(idxe[:, :], xge[:, :, D + 8]))
        for csub in range(NCS):
            for sl in range(CSUB // 128):
                slot = csub * (CSUB // 128) + sl
                for k in range(KC):
                    b = 6 + (k % 2)
                    S.wait(pe, dX, tok_xr[e])
                    S.wait(pe, cV, cV.n)
                    bank_wait(b)
                    t_tr = cP.tick(pe.transpose(pb[b][:, 0:128],
                                                xge[:, slot, k * 128:(k + 1) * 128],
                                                id_s[:, :]))
                    S.wait(ae, cP, t_tr)
                    t_hi = aop(ae.activation(xgh[:, k, sl * 128:(sl + 1) * 128],
                                             pb[b][:, 0:128], AF.Copy))
                    S.wait(ve, cP, t_tr)
                    S.wait(ve, cA, t_hi)
                    t_lo = vop(ve.tensor_sub(xgl[:, k, sl * 128:(sl + 1) * 128],
                                             pb[b][:, 0:128],
                                             xgh[:, k, sl * 128:(sl + 1) * 128]))
                    bank_used(b, cV, t_lo)
                    bank_used(b, cA, t_hi)
            war3["xg_pe"][par_e] = cP.n
            t_split = cV.n
            y_stop = [0] * NBY
            for m in range(HC):
                par = g % 2
                tok_hi, tok_lo = w_split(par, tok_w1[g])
                bw1 = 4 + (m % 2)
                S.wait(pe, cA, tok_hi)
                S.wait(pe, cV, max(tok_lo, t_split))
                bank_wait(bw1)
                tok_mm = mm3(bw1, par, CSUB, 0, src_h=xgh, src_l=xgl)
                war["split"][par] = tok_mm
                tok2_hi, tok2_lo = w2_split(par, tok_w2[g])
                if g + 2 < NG:
                    e2, m2 = g_src(g + 2)
                    tok_w1[g + 2] = w_dma(par, w1[e2, m2])
                    tok_w2[g + 2] = w2_dma(par, w2[e2, m2])
                hpar = m % 2
                S.wait(ae, cP, max(tok_mm, war2["h"][hpar]))
                t_g = aop(ae.activation(hst[:, 0:CSUB], pb[bw1][:, 0:CSUB], AF.Gelu,
                                        bias=b1_s[:, e, m:m + 1]))
                bank_used(bw1, cA, t_g)
                t_hh = aop(ae.activation(hh[hpar][:, :], hst[:, 0:CSUB], AF.Copy))
                S.wait(ve, cA, t_hh)
                S.wait(ve, cP, war2["h"][hpar])
                t_hl = vop(ve.tensor_sub(hl[hpar][:, :], hst[:, 0:CSUB], hh[hpar][:, :]))
                S.wait(pe, cA, max(t_hh, tok2_hi))
                S.wait(pe, cV, max(t_hl, tok2_lo))
                i = None
                for j in range(CSUB // 128):
                    for fh in range(NFH):
                        by = j * NFH + fh
                        if m == 0:
                            bank_wait(by)
                        for p in range(3):
                            lhsT = (hh if p != 2 else hl)[hpar][:, j * 128:(j + 1) * 128]
                            rhs = (w2h if p != 1 else w2l)[par][:, fh * W2W:(fh + 1) * W2W]
                            i = pe.matmul(pb[by][:, 0:W2W], lhsT, rhs,
                                          start=(m == 0 and p == 0),
                                          stop=(m == HC - 1 and p == 2))
                        if m == HC - 1:
                            y_stop[by] = cP.tick(i)
                if m != HC - 1:
                    t_end = cP.tick(i)
                    war2["w2split"][par] = t_end
                    war2["h"][hpar] = t_end
                else:
                    war2["w2split"][par] = y_stop[-1]
                    war2["h"][hpar] = y_stop[-1]
                g += 1
            for j in range(CSUB // 128):
                slot = csub * (CSUB // 128) + j
                for fh in range(NFH):
                    by = j * NFH + fh
                    S.wait(ve, cP, y_stop[by])
                    S.wait(ve, dSc, dSc.n)
                    t_ev = vop(ve.tensor_scalar(
                        ysb[:, j, fh * W2W:(fh + 1) * W2W], pb[by][:, 0:W2W],
                        xge[:, slot, D + e:D + e + 1], None, ALU.mult))
                    bank_used(by, cV, t_ev)
            for j in range(CSUB // 128):
                slot = csub * (CSUB // 128) + j
                S.wait(gp, cV, cV.n)
                S.wait(gp, dO, dO.n)
                S.wait(gp, dSc, dSc.n)
                gp.indirect_dma_start(
                    y_acc, bass.IndirectOffsetOnAxis(ap=idxe[:, slot:slot + 1], axis=0),
                    ysb[:, j, :], None,
                    compute_op=ALU.add).then_inc(dSc.h, 16)
                dSc.n += 16
        war3["xg_dve"][par_e] = cV.n

    for ft in range(NFT):
        S.wait(sy, dSc, dSc.n)
        S.wait(sy, dO, dO.n)
        S.wait(sy, cV, cV.n)
        tok_l = sdma(r2t[:, :, :],
                     y_acc[ft * FT:(ft + 1) * FT, :].rearrange("(c p) d -> p c d", p=128))
        S.wait(gp, dX, tok_l)
        S.wait(gp, dO, dO.n)
        gp.dma_start(out=y_pc[:, ft * FTC:(ft + 1) * FTC, :], in_=r2t[:, :, :]).then_inc(dO.h, 16)
        dO.n += 16

    S.wait(gp, dO, dO.n)
    S.wait(gp, dSc, dSc.n)
    es.close()
    return nc


def _stage_inputs(inputs, cfg, core):
    D, T = cfg["D"], cfg["T"]
    TC = T // 128
    x = np.ascontiguousarray(np.asarray(inputs["L0_attn_x"], dtype=np.float32))
    b, half = core // 2, core % 2
    own = x[b, half * T:(half + 1) * T]
    pre = x[b, 0:T] if half == 1 else own
    shuffle = lambda a: np.ascontiguousarray(a.reshape(TC, 128, D).transpose(1, 0, 2))
    return {
        "x_own": shuffle(own),
        "x_pre": shuffle(pre),
        "gate": np.full((128, 1), float(half), np.float32),
    }


def _stage_weights(inputs, cfg):
    D, HID, E = cfg["D"], cfg["HID"], cfg["E"]
    KC, HC = D // 128, HID // 128
    HP = (D // 64) // 2
    f32 = lambda a: np.asarray(a, dtype=np.float32)
    Wq, Wk, Wv, Wo = (f32(inputs[f"L0_attn_W{n}"]) for n in "qkvo")
    W1, W2 = f32(inputs["L0_ffn_W1"]), f32(inputs["L0_ffn_W2"])
    b1, b2, Wr = f32(inputs["L0_ffn_b1"]), f32(inputs["L0_ffn_b2"]), f32(inputs["L0_ffn_Wr"])

    def lhsT_lay(W):
        a = W.reshape(HP, 128, KC, 128).transpose(0, 3, 2, 1)
        return np.ascontiguousarray(a.reshape(HP, 128, KC * 128))

    import ml_dtypes
    T, CAP, E_ = cfg["T"], cfg["CAP"], cfg["E"]
    TC = T // 128
    bfl = np.dtype(ml_dtypes.bfloat16)
    return {
        "slt_in": np.tril(np.ones((128, 128), np.float32), -1).T.astype(bfl),
        "onesb_in": np.ones((128, 128), bfl),
        "offs_in": np.tile((np.arange(8) * (CAP + 128)).astype(np.float32), (128, 1)),
        "capd_in": (CAP + np.arange(128, dtype=np.float32))[:, None].copy(),
        "iota0_in": np.ascontiguousarray(
            (np.arange(TC)[None, :] * 128 + np.arange(128)[:, None]).astype(np.float32)),
        "iop_in": np.arange(128, dtype=np.float32)[:, None].copy(),
        "wq": lhsT_lay(Wq), "wk": lhsT_lay(Wk), "wv": lhsT_lay(Wv),
        "wo": np.ascontiguousarray(Wo.T.reshape(HP, 128, D)),
        "w1": np.ascontiguousarray(
            W1.reshape(E, KC, 128, HC, 128).transpose(0, 3, 2, 1, 4).reshape(E, HC, 128, KC * 128)),
        "w2": np.ascontiguousarray(W2.reshape(E, HC, 128, D)),
        "b1": np.ascontiguousarray(b1.reshape(E, HC, 128).transpose(2, 0, 1)),
        "b2": np.ascontiguousarray(b2),
        "wr": np.ascontiguousarray(Wr.reshape(KC, 128, 8).transpose(1, 0, 2)),
        "um": np.tile(np.triu(np.ones((128, 128), np.float32)), (1, 2)),
        "ident": np.eye(128, dtype=np.float32),
    }


def kernel(**inputs):
    cfg = FULL
    n_cores = 8
    weights = _stage_weights(inputs, cfg)
    in_maps = []
    for core in range(n_cores):
        m = dict(weights)
        m.update(_stage_inputs(inputs, cfg, core))
        in_maps.append(m)
    nc = build_program(cfg)
    res = run_bass_kernel_spmd(nc, in_maps, core_ids=list(range(n_cores)))
    T, D = cfg["T"], cfg["D"]
    out = np.empty((4, 4096, D), np.float32)
    for core in range(n_cores):
        b, half = core // 2, core % 2
        out[b, half * T:(half + 1) * T] = np.asarray(res.results[core]["y"])
    return out


# version 16
# speedup vs baseline: 1.0830x; 1.0830x over previous
import contextlib
import numpy as np
import concourse.bass as bass
import concourse.mybir as mybir
from concourse.bass_utils import run_bass_kernel_spmd

from concourse import library_config

F32 = mybir.dt.float32
BF16 = mybir.dt.bfloat16
I16 = mybir.dt.int16
U32 = mybir.dt.uint32
AF = mybir.ActivationFunctionType
ALU = mybir.AluOpType
AX = mybir.AxisListType

FULL = dict(D=1024, T=2048, HID=4096, E=8, FT=256, CAP=768)
MINI = dict(D=256, T=256, HID=512, E=8, FT=256, CAP=256)

LN_EPS = 1e-5
ATTN_EPS = 1e-8


class Ctr:
    def __init__(self, nc, name):
        self.h = nc.alloc_semaphore(name=name)
        self.n = 0

    def tick(self, instr, n=1):
        instr.then_inc(self.h, n)
        self.n += n
        return self.n


class Syncer:
    def __init__(self):
        self.wm = {}

    def wait(self, eng, ctr, val):
        if not val or val <= 0:
            return
        k = (id(eng), id(ctr))
        if self.wm.get(k, 0) >= val:
            return
        eng.wait_ge(ctr.h, val)
        self.wm[k] = val


def build_program(cfg):
    D, T, HID, E, FT = (cfg[k] for k in ("D", "T", "HID", "E", "FT"))
    CAP = cfg["CAP"]
    CSUB = FT
    NCS = CAP // CSUB
    CAPS = CAP // 128
    KC = D // 128
    NH = D // 64
    HP = NH // 2
    TC = T // 128
    HC = HID // 128
    FTC = FT // 128
    NFT = T // FT
    PT = min(512, T)
    W2W = min(512, D)
    NFH = D // W2W
    NBY = FTC * NFH
    assert NBY <= 4

    nc = bass.Bass("TRN2", target_bir_lowering=False, debug=False)
    S = Syncer()

    x_own = nc.dram_tensor("x_own", [128, TC, D], F32, kind="ExternalInput").ap()
    x_pre = nc.dram_tensor("x_pre", [128, TC, D], F32, kind="ExternalInput").ap()
    gate = nc.dram_tensor("gate", [128, 1], F32, kind="ExternalInput").ap()
    wq = nc.dram_tensor("wq", [HP, 128, KC * 128], F32, kind="ExternalInput").ap()
    wk = nc.dram_tensor("wk", [HP, 128, KC * 128], F32, kind="ExternalInput").ap()
    wv = nc.dram_tensor("wv", [HP, 128, KC * 128], F32, kind="ExternalInput").ap()
    wo = nc.dram_tensor("wo", [HP, 128, D], F32, kind="ExternalInput").ap()
    w1 = nc.dram_tensor("w1", [E, HC, 128, KC * 128], F32, kind="ExternalInput").ap()
    w2 = nc.dram_tensor("w2", [E, HC, 128, D], F32, kind="ExternalInput").ap()
    b1 = nc.dram_tensor("b1", [128, E, HC], F32, kind="ExternalInput").ap()
    b2 = nc.dram_tensor("b2", [8, D], F32, kind="ExternalInput").ap()
    wr = nc.dram_tensor("wr", [128, KC, 8], F32, kind="ExternalInput").ap()
    um = nc.dram_tensor("um", [128, 256], F32, kind="ExternalInput").ap()
    ident = nc.dram_tensor("ident", [128, 128], F32, kind="ExternalInput").ap()
    slt_in = nc.dram_tensor("slt_in", [128, 128], BF16, kind="ExternalInput").ap()
    onesb_in = nc.dram_tensor("onesb_in", [128, 128], BF16, kind="ExternalInput").ap()
    offs_in = nc.dram_tensor("offs_in", [128, 8], F32, kind="ExternalInput").ap()
    capd_in = nc.dram_tensor("capd_in", [128, 1], F32, kind="ExternalInput").ap()
    iota0_in = nc.dram_tensor("iota0_in", [128, TC], F32, kind="ExternalInput").ap()
    iop_in = nc.dram_tensor("iop_in", [128, 1], F32, kind="ExternalInput").ap()
    y = nc.dram_tensor("y", [T, D], F32, kind="ExternalOutput").ap()
    y_pc = y.rearrange("(c p) d -> p c d", p=128)
    r2h = nc.dram_tensor("r2h", [128, TC, D], F32, kind="Internal").ap()
    XCR = E * (CAP + 128)
    XCW = D + 64
    YAR = T + XCR + 256
    xc = nc.dram_tensor("xc", [XCR, XCW], F32, kind="Internal").ap()
    y_acc = nc.dram_tensor("y_acc", [YAR, D], F32, kind="Internal").ap()

    es = contextlib.ExitStack()
    sbuf = lambda name, shape, dt=F32: es.enter_context(nc.sbuf_tensor(name, shape, dt))

    cP = Ctr(nc, "cP")
    cV = Ctr(nc, "cV")
    cA = Ctr(nc, "cA")
    dX = Ctr(nc, "dX")
    dW = [Ctr(nc, "dW0"), Ctr(nc, "dW1")]
    dG = [Ctr(nc, "dG0"), Ctr(nc, "dG1")]
    dO = Ctr(nc, "dO")

    pe, ve, ae, sy, gp = nc.tensor, nc.vector, nc.scalar, nc.sync, nc.gpsimd
    ALL_ENG = (pe, ve, ae, sy, gp)

    def vop(instr):
        t = cV.tick(instr)
        S.wait(ve, cV, t)
        return t

    def aop(instr):
        t = cA.tick(instr)
        S.wait(ae, cA, t)
        return t

    def sdma(out, in_):
        S.wait(sy, dX, dX.n)
        sy.dma_start(out=out, in_=in_).then_inc(dX.h, 16)
        dX.n += 16
        return dX.n

    def barrier():
        for ctr in (cP, cV, cA):
            for eng in ALL_ENG:
                S.wait(eng, ctr, ctr.n)

    um_s = sbuf("um_s", [128, 256])
    id_s = sbuf("id_s", [128, 128])
    gate_s = sbuf("gate_s", [128, 1])
    wr_f = sbuf("wr_f", [128, KC, 8])
    wr_h = sbuf("wr_h", [128, KC, 8], BF16)
    wr_l = sbuf("wr_l", [128, KC, 8], BF16)
    b1_s = sbuf("b1_s", [128, E, HC])
    b2_h = sbuf("b2_h", [8, D], BF16)
    b2_l = sbuf("b2_l", [8, D], BF16)
    lnx = sbuf("lnx", [128, max(D + 64, max(D, 2 * min(512, T)) + min(512, T))])
    lntm = lnx[:, 0:D]
    lnsc = sbuf("lnsc", [128, 16])
    lnh = sbuf("lnh", [128, KC, T], BF16)
    lnl = sbuf("lnl", [128, KC, T], BF16)
    wstg = [sbuf(f"wstg{i}", [128, KC * 128]) for i in range(2)]
    wsh = [sbuf(f"wsh{i}", [128, KC * 128], BF16) for i in range(2)]
    wsl = [sbuf(f"wsl{i}", [128, KC * 128], BF16) for i in range(2)]

    pb = [es.enter_context(nc.psum_tensor(f"pb{i}", [128, 512], F32)) for i in range(8)]
    bank_free = {i: [] for i in range(8)}

    def bank_wait(b):
        for ctr, tok in bank_free[b]:
            S.wait(pe, ctr, tok)
        bank_free[b] = []

    def bank_used(b, ctr, tok):
        bank_free[b].append((ctr, tok))

    sdma(um_s[:, :], um)
    sdma(id_s[:, :], ident)
    sdma(gate_s[:, :], gate)
    sdma(wr_f[:, :, :], wr)
    sdma(b1_s[:, :, :], b1)
    tok_consts = sdma(lnx[0:8, 0:D], b2)
    S.wait(ae, dX, tok_consts)
    t_wrh = aop(ae.activation(wr_h[:, :, :], wr_f[:, :, :], AF.Copy))
    t_b2h = aop(ae.activation(b2_h[:, :], lnx[0:8, 0:D], AF.Copy))
    S.wait(ve, dX, tok_consts)
    S.wait(ve, cA, t_b2h)
    t_wrl = vop(ve.tensor_sub(wr_l[:, :, :], wr_f[:, :, :], wr_h[:, :, :]))
    t_b2l = vop(ve.tensor_sub(b2_l[:, :], lnx[0:8, 0:D], b2_h[:, :]))

    attn_es = contextlib.ExitStack()
    asb = lambda name, shape, dt=F32: attn_es.enter_context(nc.sbuf_tensor(name, shape, dt))
    xbuf = asb("xbuf", [128, TC, D])
    qb = asb("qb", [128, T])
    kb = asb("kb", [128, T])
    vb = asb("vb", [128, T])
    s_run = asb("s_run", [128, NH, 128])
    ktm = [asb(f"ktm{i}", [128, 128]) for i in range(2)]
    vaug = [asb(f"vaug{i}", [128, 256]) for i in range(2)]
    qpad = [[asb(f"qpad{i}_{j}", [128, 128]) for i in range(2)] for j in range(2)]
    kpad = [[asb(f"kpad{i}_{j}", [128, 128]) for i in range(2)] for j in range(2)]
    am2_sb = [asb(f"am2_{j}", [128, 256]) for j in range(2)]
    attn_tm = asb("attn_tm", [128, TC, 128])
    ah_sb = asb("ah_sb", [128, 128], BF16)
    al_sb = asb("al_sb", [128, 128], BF16)
    esc1 = lnx[:, 0:PT]
    esc2 = lnx[:, PT:2 * PT]
    esc3 = lnx[:, 2 * PT:3 * PT] if 3 * PT <= lnx.shape[1] else lnx[:, lnx.shape[1] - PT:]
    numsb = asb("numsb", [128, 256])
    densb = asb("densb", [128, 4])

    vop(ve.memset(s_run[:, :, :], 0.0))
    for h in range(2):
        for j in range(2):
            vop(ve.memset(qpad[h][j][:, :], 0.0))
            vop(ve.memset(kpad[h][j][:, :], 0.0))
    for i in range(2):
        vop(ve.memset(vaug[i][:, :], 0.0))
        vop(ve.memset(vaug[i][:, 64:65], 1.0))
        vop(ve.memset(vaug[i][:, 192:193], 1.0))

    def probe_exit(src=None, dst=None, close_attn=True):
        S.wait(gp, cV, cV.n)
        S.wait(gp, cA, cA.n)
        for ctr in (cP,):
            S.wait(gp, ctr, ctr.n)
        S.wait(gp, dO, dO.n)
        gp.dma_start(out=(y_pc if dst is None else dst),
                     in_=(xbuf[:, :, :] if src is None else src)).then_inc(dO.h, 16)
        dO.n += 16
        S.wait(gp, dO, dO.n)
        if close_attn:
            attn_es.close()
        es.close()
        return nc

    war = {
        "lntm_pe": 0,
        "lntm_dma": None,
        "combpad_dma": None,
        "mbf_pe": 0,
        "stage": [[(None, 0)], [(None, 0)]],
        "split": [0, 0],
        "vaug": [0, 0],
        "ktm": [0, 0],
        "pad": [0, 0],
        "am2": [0, 0],
        "sadd": 0,
    }

    def emit_ln_chunk(src_ap, col0, tr_banks):
        S.wait(ae, cP, war["lntm_pe"])
        if war["lntm_dma"]:
            S.wait(ae, *war["lntm_dma"])
        t1 = aop(ae.activation(lntm, src_ap, AF.Square, accum_out=lnsc[:, 0:1]))
        t2 = aop(ae.activation(lntm, src_ap, AF.Identity, accum_out=lnsc[:, 1:2]))
        S.wait(ve, cA, t2)
        vop(ve.tensor_scalar_mul(lnsc[:, 2:3], lnsc[:, 1:2], 1.0 / D))
        vop(ve.tensor_mul(lnsc[:, 3:4], lnsc[:, 2:3], lnsc[:, 2:3]))
        vop(ve.scalar_tensor_tensor(lnsc[:, 4:5], lnsc[:, 0:1], 1.0 / D, lnsc[:, 3:4],
                                    ALU.mult, ALU.subtract))
        tv = vop(ve.tensor_scalar_add(lnsc[:, 5:6], lnsc[:, 4:5], LN_EPS))
        S.wait(ae, cV, tv)
        ts = aop(ae.activation(lnsc[:, 6:7], lnsc[:, 5:6], AF.Sqrt))
        S.wait(ve, cA, ts)
        vop(ve.reciprocal(lnsc[:, 7:8], lnsc[:, 6:7]))
        ycol = 7
        for it in range(2):
            a, b_, c_, yn = 8, 9, 10, 11 + it
            vop(ve.tensor_mul(lnsc[:, a:a + 1], lnsc[:, ycol:ycol + 1], lnsc[:, ycol:ycol + 1]))
            vop(ve.tensor_mul(lnsc[:, b_:b_ + 1], lnsc[:, 5:6], lnsc[:, a:a + 1]))
            vop(ve.tensor_scalar(lnsc[:, c_:c_ + 1], lnsc[:, b_:b_ + 1], -0.5, 1.5,
                                 ALU.mult, ALU.add))
            vop(ve.tensor_mul(lnsc[:, yn:yn + 1], lnsc[:, ycol:ycol + 1], lnsc[:, c_:c_ + 1]))
            ycol = yn
        S.wait(ve, cA, t2)
        S.wait(ve, cP, war["lntm_pe"])
        if war["lntm_dma"]:
            S.wait(ve, *war["lntm_dma"])
        t_ap = vop(ve.tensor_scalar(lntm, src_ap, lnsc[:, 2:3], lnsc[:, ycol:ycol + 1],
                                    ALU.subtract, ALU.mult))
        for k in range(KC):
            b = tr_banks[k % 2]
            S.wait(pe, cV, t_ap)
            bank_wait(b)
            t_tr = cP.tick(pe.transpose(pb[b][:, 0:128], lntm[:, k * 128:(k + 1) * 128], id_s[:, :]))
            S.wait(ae, cP, t_tr)
            t_hi = aop(ae.activation(lnh[:, k, col0:col0 + 128], pb[b][:, 0:128], AF.Copy))
            S.wait(ve, cP, t_tr)
            S.wait(ve, cA, t_hi)
            t_lo = vop(ve.tensor_sub(lnl[:, k, col0:col0 + 128], pb[b][:, 0:128],
                                     lnh[:, k, col0:col0 + 128]))
            bank_used(b, cV, t_lo)
            bank_used(b, cA, t_hi)
        war["lntm_pe"] = cP.n
        return t_ap

    def w_dma(par, src_ap):
        for ctr, tok in war["stage"][par]:
            if ctr is not None:
                S.wait(sy, ctr, tok)
        S.wait(sy, dW[par], dW[par].n)
        sy.dma_start(out=wstg[par][:, :], in_=src_ap).then_inc(dW[par].h, 16)
        dW[par].n += 16
        return dW[par].n

    def w_split(par, tok_dma):
        S.wait(ae, dW[par], tok_dma)
        S.wait(ae, cP, war["split"][par])
        t_hi = aop(ae.activation(wsh[par][:, :], wstg[par][:, :], AF.Copy))
        S.wait(ve, dW[par], tok_dma)
        S.wait(ve, cP, war["split"][par])
        S.wait(ve, cA, t_hi)
        t_lo = vop(ve.tensor_sub(wsl[par][:, :], wstg[par][:, :], wsh[par][:, :]))
        war["stage"][par] = [(cA, t_hi), (cV, t_lo)]
        return t_hi, t_lo

    def mm3(bank, wpar, n, col, src_h=None, src_l=None):
        if src_h is None:
            src_h, src_l = lnh, lnl
        tok = None
        first = True
        for k in range(KC):
            for p in range(3):
                lhsT = (wsh if p != 2 else wsl)[wpar][:, k * 128:(k + 1) * 128]
                rhs = (src_h if p != 1 else src_l)[:, k, col:col + n]
                last = (k == KC - 1 and p == 2)
                i = pe.matmul(pb[bank][:, 0:n], lhsT, rhs, start=first, stop=last)
                first = False
                if last:
                    tok = cP.tick(i)
        return tok

    def emit_elu_evac(ps_ap, dst_ap, n):
        t_r = vop(ve.tensor_relu(esc1[:, 0:n], ps_ap))
        t_m = vop(ve.tensor_single_scalar(esc2[:, 0:n], ps_ap, 0.0, ALU.min))
        S.wait(ae, cV, t_m)
        t_e = aop(ae.activation(esc3[:, 0:n], esc2[:, 0:n], AF.Exp))
        S.wait(ve, cA, t_e)
        return vop(ve.tensor_add(dst_ap, esc1[:, 0:n], esc3[:, 0:n]))

    def emit_proj(dst, elu, wpar, tok_hi, tok_lo, banks):
        t_ev = 0
        for i0, t0 in enumerate(range(0, T, PT)):
            b = banks[i0 % 2]
            S.wait(pe, cA, tok_hi)
            S.wait(pe, cV, tok_lo)
            bank_wait(b)
            tok_mm = mm3(b, wpar, PT, t0)
            S.wait(ve, cP, tok_mm)
            if elu:
                t_ev = emit_elu_evac(pb[b][:, 0:PT], dst[:, t0:t0 + PT], PT)
            else:
                t_ev = vop(ve.tensor_copy(dst[:, t0:t0 + PT], pb[b][:, 0:PT]))
            bank_used(b, cV, t_ev)
        war["split"][wpar] = cP.n
        return t_ev

    def mix_state_chunk(c, hp, par, tok_k, tok_v, update_state=True):
        bt, bs = 6 + (c % 2), 5
        S.wait(pe, cV, max(tok_k, tok_v))
        bank_wait(bt)
        t_trk = cP.tick(pe.transpose(pb[bt][:, 0:128], kb[:, c * 128:(c + 1) * 128], id_s[:, :]))
        t_trv = cP.tick(pe.transpose(pb[bt][:, 128:256], vb[:, c * 128:(c + 1) * 128], id_s[:, :]))
        S.wait(ae, cP, max(t_trv, war["ktm"][par], war["vaug"][par]))
        t_ktm = aop(ae.activation(ktm[par][:, :], pb[bt][:, 0:128], AF.Copy))
        aop(ae.activation(vaug[par][:, 0:64], pb[bt][:, 128:192], AF.Copy))
        t_v1 = aop(ae.activation(vaug[par][:, 128:192], pb[bt][:, 192:256], AF.Copy))
        bank_used(bt, cA, t_v1)
        if update_state:
            S.wait(pe, cA, t_v1)
            bank_wait(bs)
            cP.tick(pe.matmul(pb[bs][:, 0:128], ktm[par][:, :], vaug[par][:, 0:128],
                              start=True, stop=True))
            t_s = cP.tick(pe.matmul(pb[bs][:, 128:256], ktm[par][:, :], vaug[par][:, 128:256],
                                    start=True, stop=True))
            S.wait(ve, cP, t_s)
            gh0, gh1 = 2 * hp, 2 * hp + 1
            vop(ve.tensor_add(s_run[0:64, gh0, 0:128], s_run[0:64, gh0, 0:128],
                              pb[bs][0:64, 0:128]))
            t_up = vop(ve.tensor_add(s_run[64:128, gh1, 0:128], s_run[64:128, gh1, 0:128],
                                     pb[bs][64:128, 128:256]))
            bank_used(bs, cV, t_up)
        war["ktm"][par] = cP.n
        war["vaug"][par] = cP.n
        return t_ktm, t_v1

    if cfg.get("stop") == "ph0":
        return probe_exit()
    tok_xpre = sdma(xbuf[:, :, :], x_pre)
    S.wait(ae, dX, tok_xpre)
    S.wait(ve, dX, tok_xpre)
    S.wait(ae, cV, t_b2l)
    for j in range(TC):
        emit_ln_chunk(xbuf[:, j, :], j * 128, (0, 1))

    pend_k = w_dma(0, wk[0])
    pend_v = w_dma(1, wv[0])
    for hp in range(HP):
        tok_hi, tok_lo = w_split(0, pend_k)
        tok_k = emit_proj(kb, True, 0, tok_hi, tok_lo, (2, 3))
        tok_hi, tok_lo = w_split(1, pend_v)
        tok_v = emit_proj(vb, False, 1, tok_hi, tok_lo, (2, 3))
        if hp + 1 < HP:
            pend_k = w_dma(0, wk[hp + 1])
            pend_v = w_dma(1, wv[hp + 1])
        for c in range(TC):
            mix_state_chunk(c, hp, c % 2, tok_k, tok_v)

    vop(ve.tensor_scalar_mul(s_run[:, :, :], s_run[:, :, :], gate_s[:, 0:1]))
    if cfg.get("stop") == "prefix":
        return probe_exit()

    barrier()
    tok_xown = sdma(xbuf[:, :, :], x_own)
    S.wait(ae, dX, tok_xown)
    S.wait(ve, dX, tok_xown)
    for j in range(TC):
        emit_ln_chunk(xbuf[:, j, :], j * 128, (0, 1))

    pend_q = w_dma(0, wq[0])
    pend_k = w_dma(1, wk[0])
    for hp in range(HP):
        tok_hi, tok_lo = w_split(0, pend_q)
        tok_q = emit_proj(qb, True, 0, tok_hi, tok_lo, (2, 3))
        tok_hi, tok_lo = w_split(1, pend_k)
        tok_k = emit_proj(kb, True, 1, tok_hi, tok_lo, (2, 3))
        pend_v = w_dma(0, wv[hp])
        tok_hi, tok_lo = w_split(0, pend_v)
        tok_v = emit_proj(vb, False, 0, tok_hi, tok_lo, (2, 3))
        pend_wo = w_dma(1, wo[hp])

        pend_mix = None
        for cstep in range(TC + 1):
            if cstep < TC:
                c = cstep
                cpar = c % 2
                par = c % 2
                t_ktm, t_v1 = mix_state_chunk(c, hp, par, tok_k, tok_v, update_state=False)
                S.wait(ae, cV, tok_q)
                S.wait(ae, cP, war["pad"][cpar])
                aop(ae.activation(qpad[0][cpar][0:64, :], qb[0:64, c * 128:(c + 1) * 128],
                                  AF.Copy))
                aop(ae.activation(kpad[0][cpar][0:64, :], kb[0:64, c * 128:(c + 1) * 128],
                                  AF.Copy))
                aop(ae.activation(qpad[1][cpar][64:128, :],
                                  qb[64:128, c * 128:(c + 1) * 128], AF.Copy))
                t_kp = aop(ae.activation(kpad[1][cpar][64:128, :],
                                         kb[64:128, c * 128:(c + 1) * 128], AF.Copy))
                this_mix = (c, cpar, par, t_kp, t_v1)
            else:
                this_mix = None
            if pend_mix is not None:
                c, cpar, par, t_kp, t_v1 = pend_mix
                b_at = 2 + cpar
                b_o = cpar
                b_s = 5
                gh0, gh1 = 2 * hp, 2 * hp + 1
                S.wait(pe, cA, t_kp)
                bank_wait(b_at)
                cP.tick(pe.matmul(pb[b_at][:, 0:128], kpad[0][cpar][:, :],
                                  qpad[0][cpar][:, :], start=True, stop=True))
                t_at = cP.tick(pe.matmul(pb[b_at][:, 128:256], kpad[1][cpar][:, :],
                                         qpad[1][cpar][:, :], start=True, stop=True))
                S.wait(ve, cP, max(t_at, war["am2"][cpar]))
                t_am = vop(ve.tensor_mul(am2_sb[cpar][:, :], pb[b_at][:, 0:256], um_s[:, :]))
                bank_used(b_at, cV, t_am)
                S.wait(pe, cV, t_am)
                S.wait(pe, cA, t_v1)
                bank_wait(b_o)
                cP.tick(pe.matmul(pb[b_o][:, 0:128], am2_sb[cpar][:, 0:128],
                                  vaug[par][:, 0:128], start=True, stop=True))
                t_o1 = cP.tick(pe.matmul(pb[b_o][:, 128:256], am2_sb[cpar][:, 128:256],
                                         vaug[par][:, 128:256], start=True, stop=True))
                S.wait(pe, cV, war["sadd"])
                cP.tick(pe.matmul(pb[b_o][:, 256:384], qpad[0][cpar][:, :],
                                  s_run[:, gh0, :], start=True, stop=True))
                t_o2 = cP.tick(pe.matmul(pb[b_o][:, 384:512], qpad[1][cpar][:, :],
                                         s_run[:, gh1, :], start=True, stop=True))
                war["pad"][cpar] = t_o2
                war["am2"][cpar] = t_o1
                war["vaug"][par] = max(war["vaug"][par], t_o1)
                bank_wait(b_s)
                cP.tick(pe.matmul(pb[b_s][:, 0:128], ktm[par][:, :], vaug[par][:, 0:128],
                                  start=True, stop=True))
                t_s = cP.tick(pe.matmul(pb[b_s][:, 128:256], ktm[par][:, :],
                                        vaug[par][:, 128:256], start=True, stop=True))
                war["ktm"][par] = max(war["ktm"][par], t_s)
                war["vaug"][par] = max(war["vaug"][par], t_s)
                S.wait(ve, cP, t_s)
                vop(ve.tensor_add(s_run[0:64, gh0, 0:128], s_run[0:64, gh0, 0:128],
                                  pb[b_s][0:64, 0:128]))
                t_up = vop(ve.tensor_add(s_run[64:128, gh1, 0:128],
                                         s_run[64:128, gh1, 0:128],
                                         pb[b_s][64:128, 128:256]))
                bank_used(b_s, cV, t_up)
                war["sadd"] = t_up
                t_cp1 = vop(ve.tensor_copy(numsb[:, 0:256], pb[b_o][:, 0:256]))
                vop(ve.scalar_tensor_tensor(densb[:, 0:2], pb[b_o][:, 320:449:128],
                                            ATTN_EPS, numsb[:, 64:193:128],
                                            ALU.add, ALU.add))
                vop(ve.reciprocal(densb[:, 2:4], densb[:, 0:2]))
                t_n = vop(ve.tensor_add(numsb[:, 0:256], numsb[:, 0:256],
                                        pb[b_o][:, 256:512]))
                bank_used(b_o, cV, t_n)
                vop(ve.tensor_scalar(attn_tm[:, c, 0:64], numsb[:, 0:64],
                                     densb[:, 2:3], None, ALU.mult))
                vop(ve.tensor_scalar(attn_tm[:, c, 64:128], numsb[:, 128:192],
                                     densb[:, 3:4], None, ALU.mult))
            pend_mix = this_mix
        tok_hi_o, tok_lo_o = w_split(1, pend_wo)
        t_wo_last = 0
        for j in range(TC):
            b_tr = 6
            S.wait(pe, cV, cV.n)
            bank_wait(b_tr)
            t_tr = cP.tick(pe.transpose(pb[b_tr][:, 0:128], attn_tm[:, j, :], id_s[:, :]))
            S.wait(ae, cP, t_tr)
            t_ah = aop(ae.activation(ah_sb[:, :], pb[b_tr][:, 0:128], AF.Copy))
            S.wait(ve, cP, t_tr)
            S.wait(ve, cA, t_ah)
            t_al = vop(ve.tensor_sub(al_sb[:, :], pb[b_tr][:, 0:128], ah_sb[:, :]))
            bank_used(b_tr, cV, t_al)
            for fh in range(NFH):
                b = (0, 1)[fh % 2]
                S.wait(pe, cA, max(t_ah, tok_hi_o))
                S.wait(pe, cV, max(t_al, tok_lo_o))
                bank_wait(b)
                pe.matmul(pb[b][:, 0:W2W], ah_sb[:, :], wsh[1][:, fh * W2W:(fh + 1) * W2W],
                          start=True, stop=False)
                pe.matmul(pb[b][:, 0:W2W], ah_sb[:, :], wsl[1][:, fh * W2W:(fh + 1) * W2W],
                          start=False, stop=False)
                i3 = pe.matmul(pb[b][:, 0:W2W], al_sb[:, :], wsh[1][:, fh * W2W:(fh + 1) * W2W],
                               start=False, stop=True)
                t_wo = cP.tick(i3)
                t_wo_last = t_wo
                S.wait(ve, cP, t_wo)
                t_acc = vop(ve.scalar_tensor_tensor(xbuf[:, j, fh * W2W:(fh + 1) * W2W],
                                                    pb[b][:, 0:W2W], 1.0,
                                                    xbuf[:, j, fh * W2W:(fh + 1) * W2W],
                                                    ALU.mult, ALU.add))
                bank_used(b, cV, t_acc)
        war["split"][1] = cP.n
        if hp + 1 < HP:
            pend_q = w_dma(0, wq[hp + 1])
            pend_k = w_dma(1, wk[hp + 1])

    S.wait(gp, cV, cV.n)
    S.wait(gp, dO, dO.n)
    gp.dma_start(out=r2h, in_=xbuf[:, :, :]).then_inc(dO.h, 16)
    dO.n += 16
    tok_r2w = dO.n

    if cfg.get("stop") == "own":
        return probe_exit()

    barrier()
    for eng in ALL_ENG:
        S.wait(eng, dO, tok_r2w)
    attn_es.close()

    r2t = sbuf("r2t", [128, FTC, D])
    hst = sbuf("hst", [128, FT])
    hh = [sbuf(f"hh{i}", [128, FT], BF16) for i in range(2)]
    hl = [sbuf(f"hl{i}", [128, FT], BF16) for i in range(2)]
    w2stg = [sbuf(f"w2stg{i}", [128, D]) for i in range(2)]
    w2h = [sbuf(f"w2h{i}", [128, D], BF16) for i in range(2)]
    w2l = [sbuf(f"w2l{i}", [128, D], BF16) for i in range(2)]
    lg = sbuf("lg", [128, 8])
    lgm1 = sbuf("lgm1", [128, 8])
    lgk = sbuf("lgk", [128, 8])
    lgm2 = sbuf("lgm2", [128, 8])
    lgsc = sbuf("lgsc", [128, 8])
    combine = sbuf("combine", [128, TC, 8])
    comb_pad = sbuf("comb_pad", [128, 128])
    cth = sbuf("cth", [8, TC, 128], BF16)
    ctl = sbuf("ctl", [8, TC, 128], BF16)
    ct_f = sbuf("ct_f", [8, 128])
    slt_s = sbuf("slt_s", [128, 128], BF16)
    onesb_s = sbuf("onesb_s", [128, 128], BF16)
    offs_s = sbuf("offs_s", [128, 8])
    capd_s = sbuf("capd_s", [128, 1])
    iota0_s = sbuf("iota0_s", [128, TC])
    iop_s = sbuf("iop_s", [128, 1])
    maskp = sbuf("maskp", [128, 128])
    masks_bf = sbuf("masks_bf", [128, 128], BF16)
    base_sb = sbuf("base_sb", [128, 8])
    posx = sbuf("posx", [128, 8])
    post = sbuf("post", [128, 8])
    posf = sbuf("posf", [128, 2])
    posu = sbuf("posu", [128, 2], U32)
    xg = [sbuf(f"xg{i}", [128, CAPS, XCW]) for i in range(2)]
    xgh = sbuf("xgh", [128, KC, FT], BF16)
    xgl = sbuf("xgl", [128, KC, FT], BF16)
    ysb = sbuf("ysb", [128, FT // 128, D])
    idxu = [sbuf(f"idxu{i}", [128, CAPS], U32) for i in range(2)]
    dSc = Ctr(nc, "dSc")
    sdma(slt_s[:, :], slt_in)
    sdma(onesb_s[:, :], onesb_in)
    sdma(offs_s[:, :], offs_in)
    sdma(capd_s[:, :], capd_in)
    sdma(iota0_s[:, :], iota0_in)
    tok_iot = sdma(iop_s[:, :], iop_in)
    vop(ve.memset(comb_pad[:, :], 0.0))
    vop(ve.memset(maskp[:, :], 0.0))
    vop(ve.memset(base_sb[:, :], 0.0))
    vop(ve.memset(xg[0][:, :, :], 0.0))
    vop(ve.memset(xg[1][:, :, :], 0.0))

    war2 = {"w2stage": [[(None, 0)], [(None, 0)]], "w2split": [0, 0], "h": [0, 0]}

    def w2_dma(par, src_ap):
        for ctr, tok in war2["w2stage"][par]:
            if ctr is not None:
                S.wait(gp, ctr, tok)
        S.wait(gp, dG[par], dG[par].n)
        gp.dma_start(out=w2stg[par][:, :], in_=src_ap).then_inc(dG[par].h, 16)
        dG[par].n += 16
        return dG[par].n

    def w2_split(par, tok_dma):
        S.wait(ae, dG[par], tok_dma)
        S.wait(ae, cP, war2["w2split"][par])
        t_hi = aop(ae.activation(w2h[par][:, :], w2stg[par][:, :], AF.Copy))
        S.wait(ve, dG[par], tok_dma)
        S.wait(ve, cP, war2["w2split"][par])
        S.wait(ve, cA, t_hi)
        t_lo = vop(ve.tensor_sub(w2l[par][:, :], w2stg[par][:, :], w2h[par][:, :]))
        war2["w2stage"][par] = [(cA, t_hi), (cV, t_lo)]
        return t_hi, t_lo

    S.wait(ve, dX, tok_iot)
    for s0 in range(XCR // 128):
        S.wait(ve, dX, dX.n)
        t_id = vop(ve.tensor_scalar_add(xg[0][:, 0, D + 8:D + 9], iop_s[:, :],
                                        float(T + s0 * 128)))
        S.wait(sy, cV, t_id)
        sdma(xc[s0 * 128:(s0 + 1) * 128, :], xg[0][:, 0, :])
    S.wait(sy, cV, cV.n)
    for r0 in range(T, YAR, 128):
        n_r = min(128, YAR - r0)
        sdma(y_acc[r0:r0 + n_r, :], xg[1][0:n_r, 0, 0:D])
    tok_xc_init = dX.n

    NG = E * NCS * HC

    def g_src(g):
        e = g // (NCS * HC)
        m = g % HC
        return e, m

    tok_w1 = {}
    tok_w2 = {}
    for g0 in range(min(2, NG)):
        e0, m0 = g_src(g0)
        tok_w1[g0] = w_dma(g0 % 2, w1[e0, m0])
        tok_w2[g0] = w2_dma(g0 % 2, w2[e0, m0])

    last_out_dma = 0
    tok_r2t_consumed = [(None, 0)]
    for ft in range(NFT):
        for ctr, tok in tok_r2t_consumed:
            if ctr is not None:
                S.wait(sy, ctr, tok)
        S.wait(sy, dO, last_out_dma)
        tok_r2t = sdma(r2t[:, :, :], r2h[:, ft * FTC:(ft + 1) * FTC, :])
        S.wait(ae, dX, tok_r2t)
        S.wait(ve, dX, tok_r2t)
        for jj in range(FTC):
            jabs = ft * FTC + jj
            col0 = jabs * 128
            t_ap = emit_ln_chunk(r2t[:, jj, :], col0, (4, 5))
            first = True
            i = None
            for k in range(KC):
                for p in range(3):
                    lhsT = (lnh if p != 2 else lnl)[:, k, col0:col0 + 128]
                    rhs = (wr_h if p != 1 else wr_l)[:, k, :]
                    if first:
                        S.wait(pe, cV, cV.n)
                        S.wait(pe, cA, cA.n)
                        bank_wait(6)
                    i = pe.matmul(pb[6][:, 0:8], lhsT, rhs, start=first,
                                  stop=(k == KC - 1 and p == 2))
                    first = False
            t_lgmm = cP.tick(i)
            S.wait(ve, cP, t_lgmm)
            t_lg = vop(ve.tensor_copy(lg[:, :], pb[6][:, 0:8]))
            bank_used(6, cV, t_lg)
            vop(ve.reduce_max(lgsc[:, 0:1], lg[:, :], axis=AX.X))
            vop(ve.tensor_scalar(lgm1[:, :], lg[:, :], lgsc[:, 0:1], None, ALU.is_equal))
            vop(ve.scalar_tensor_tensor(lgk[:, :], lgm1[:, :], -1e30, lg[:, :],
                                        ALU.mult, ALU.add))
            vop(ve.reduce_max(lgsc[:, 1:2], lgk[:, :], axis=AX.X))
            vop(ve.tensor_scalar(lgm2[:, :], lgk[:, :], lgsc[:, 1:2], None, ALU.is_equal))
            t_d = vop(ve.tensor_sub(lgsc[:, 2:3], lgsc[:, 1:2], lgsc[:, 0:1]))
            S.wait(ae, cV, t_d)
            t_e = aop(ae.activation(lgsc[:, 3:4], lgsc[:, 2:3], AF.Exp))
            S.wait(ve, cA, t_e)
            vop(ve.tensor_scalar_add(lgsc[:, 4:5], lgsc[:, 3:4], 1.0))
            vop(ve.reciprocal(lgsc[:, 5:6], lgsc[:, 4:5]))
            vop(ve.tensor_mul(lgsc[:, 6:7], lgsc[:, 3:4], lgsc[:, 5:6]))
            vop(ve.tensor_scalar(combine[:, jabs, :], lgm1[:, :], lgsc[:, 5:6], None, ALU.mult))
            t_c2 = vop(ve.scalar_tensor_tensor(combine[:, jabs, :], lgm2[:, :], lgsc[:, 6:7],
                                               combine[:, jabs, :], ALU.mult, ALU.add))
            t_cp = vop(ve.tensor_copy(comb_pad[:, 0:8], combine[:, jabs, :]))
            S.wait(pe, cV, t_cp)
            bank_wait(7)
            t_ct = cP.tick(pe.transpose(pb[7][:, 0:128], comb_pad[:, :], id_s[:, :]))
            S.wait(ve, cP, t_ct)
            t_ctf = vop(ve.tensor_copy(ct_f[:, :], pb[7][0:8, 0:128]))
            bank_used(7, cV, t_ctf)
            t_cth = vop(ve.tensor_copy(cth[:, jabs, :], ct_f[:, :]))
            vop(ve.tensor_sub(ctl[:, jabs, :], ct_f[:, :], cth[:, jabs, :]))
            S.wait(ve, cP, war["mbf_pe"])
            vop(ve.tensor_add(maskp[:, 0:8], lgm1[:, :], lgm2[:, :]))
            t_mb = vop(ve.tensor_copy(masks_bf[:, :], maskp[:, :]))
            S.wait(pe, cV, t_mb)
            bank_wait(3)
            t_pf = cP.tick(pe.matmul(pb[3][:, 0:128], slt_s[:, :], masks_bf[:, :],
                                     start=True, stop=True))
            S.wait(ve, cP, t_pf)
            t_px = vop(ve.tensor_add(posx[:, :], base_sb[:, :], pb[3][:, 0:8]))
            bank_used(3, cV, t_px)
            vop(ve.tensor_scalar(posx[:, :], posx[:, :], capd_s[:, 0:1], None, ALU.min))
            vop(ve.tensor_add(posx[:, :], posx[:, :], offs_s[:, :]))
            vop(ve.tensor_mul(post[:, :], lgm1[:, :], posx[:, :]))
            vop(ve.reduce_sum(posf[:, 0:1], post[:, :], axis=AX.X))
            vop(ve.tensor_mul(post[:, :], lgm2[:, :], posx[:, :]))
            vop(ve.reduce_sum(posf[:, 1:2], post[:, :], axis=AX.X))
            S.wait(ve, dSc, dSc.n)
            t_pu = vop(ve.tensor_copy(posu[:, :], posf[:, :]))
            bank_wait(7)
            t_tt = cP.tick(pe.matmul(pb[7][:, 0:128], onesb_s[:, :], masks_bf[:, :],
                                     start=True, stop=True))
            war["mbf_pe"] = t_tt
            S.wait(ve, cP, t_tt)
            t_bu = vop(ve.tensor_add(base_sb[:, :], base_sb[:, :], pb[7][:, 0:8]))
            bank_used(7, cV, t_bu)
            t_g8 = vop(ve.tensor_copy(lnx[:, D:D + 8], combine[:, jabs, :]))
            t_gi = vop(ve.tensor_copy(lnx[:, D + 8:D + 9], iota0_s[:, jabs:jabs + 1]))
            S.wait(gp, cV, cV.n)
            S.wait(gp, dX, tok_xc_init)
            for sl in range(2):
                S.wait(gp, dSc, dSc.n)
                gp.indirect_dma_start(
                    xc, bass.IndirectOffsetOnAxis(ap=posu[:, sl:sl + 1], axis=0),
                    lnx[:, 0:XCW], None).then_inc(dSc.h, 16)
                dSc.n += 16
            war["lntm_dma"] = (dSc, dSc.n)
        S.wait(ve, dO, last_out_dma)
        for jj in range(FTC):
            jabs = ft * FTC + jj
            for fh in range(NFH):
                b = 6 + ((jj * NFH + fh) % 2)
                S.wait(pe, cV, cV.n)
                bank_wait(b)
                pe.matmul(pb[b][:, 0:W2W], cth[:, jabs, :], b2_h[:, fh * W2W:(fh + 1) * W2W],
                          start=True, stop=False)
                pe.matmul(pb[b][:, 0:W2W], cth[:, jabs, :], b2_l[:, fh * W2W:(fh + 1) * W2W],
                          start=False, stop=False)
                i3 = pe.matmul(pb[b][:, 0:W2W], ctl[:, jabs, :], b2_h[:, fh * W2W:(fh + 1) * W2W],
                               start=False, stop=True)
                t_b = cP.tick(i3)
                S.wait(ve, cP, t_b)
                t_ev = vop(ve.scalar_tensor_tensor(
                    r2t[:, jj, fh * W2W:(fh + 1) * W2W], pb[b][:, 0:W2W], 1.0,
                    r2t[:, jj, fh * W2W:(fh + 1) * W2W], ALU.mult, ALU.add))
                bank_used(b, cV, t_ev)
        S.wait(gp, cV, cV.n)
        S.wait(gp, dO, dO.n)
        gp.dma_start(out=y_acc[ft * FT:(ft + 1) * FT, :].rearrange("(c p) d -> p c d", p=128),
                     in_=r2t[:, :, :]).then_inc(dO.h, 16)
        dO.n += 16
        last_out_dma = dO.n
        tok_r2t_consumed = [(cA, cA.n), (cV, cV.n)]

    if cfg.get("ffn_mode") == "router":
        return probe_exit(src=r2t[:, :, :], dst=y_pc[:, 0:FTC, :], close_attn=False)

    tok_comp_done = dSc.n
    war3 = {"xg_pe": [0, 0], "xg_dve": [0, 0]}
    tok_xr = {}

    def load_region(e):
        pe_ = e % 2
        S.wait(sy, dSc, tok_comp_done)
        S.wait(sy, cP, war3["xg_pe"][pe_])
        S.wait(sy, cV, war3["xg_dve"][pe_])
        tok_xr[e] = sdma(xg[pe_][:, :, :],
                         xc[e * (CAP + 128):e * (CAP + 128) + CAP, :]
                         .rearrange("(q p) c -> p q c", p=128))

    load_region(0)
    g = 0
    for e in range(E):
        par_e = e % 2
        xge = xg[par_e]
        idxe = idxu[par_e]
        if e + 1 < E:
            load_region(e + 1)
        S.wait(ve, dX, tok_xr[e])
        S.wait(ve, dSc, dSc.n)
        t_idx = vop(ve.tensor_copy(idxe[:, :], xge[:, :, D + 8]))
        for csub in range(NCS):
            for sl in range(CSUB // 128):
                slot = csub * (CSUB // 128) + sl
                for k in range(KC):
                    b = 6 + (k % 2)
                    S.wait(pe, dX, tok_xr[e])
                    S.wait(pe, cV, cV.n)
                    bank_wait(b)
                    t_tr = cP.tick(pe.transpose(pb[b][:, 0:128],
                                                xge[:, slot, k * 128:(k + 1) * 128],
                                                id_s[:, :]))
                    S.wait(ae, cP, t_tr)
                    t_hi = aop(ae.activation(xgh[:, k, sl * 128:(sl + 1) * 128],
                                             pb[b][:, 0:128], AF.Copy))
                    S.wait(ve, cP, t_tr)
                    S.wait(ve, cA, t_hi)
                    t_lo = vop(ve.tensor_sub(xgl[:, k, sl * 128:(sl + 1) * 128],
                                             pb[b][:, 0:128],
                                             xgh[:, k, sl * 128:(sl + 1) * 128]))
                    bank_used(b, cV, t_lo)
                    bank_used(b, cA, t_hi)
            war3["xg_pe"][par_e] = cP.n
            t_split = cV.n
            y_stop = [0] * NBY
            pend_w2 = None
            for mstep in range(HC + 1):
                if mstep < HC:
                    m = mstep
                    par = g % 2
                    tok_hi, tok_lo = w_split(par, tok_w1[g])
                    bw1 = 4 + (m % 2)
                    S.wait(pe, cA, tok_hi)
                    S.wait(pe, cV, max(tok_lo, t_split))
                    bank_wait(bw1)
                    tok_mm = mm3(bw1, par, CSUB, 0, src_h=xgh, src_l=xgl)
                    war["split"][par] = tok_mm
                    tok2_hi, tok2_lo = w2_split(par, tok_w2[g])
                    if g + 2 < NG:
                        e2, m2 = g_src(g + 2)
                        tok_w1[g + 2] = w_dma(par, w1[e2, m2])
                        tok_w2[g + 2] = w2_dma(par, w2[e2, m2])
                    hpar = m % 2
                    S.wait(ae, cP, max(tok_mm, war2["h"][hpar]))
                    t_g = aop(ae.activation(hst[:, 0:CSUB], pb[bw1][:, 0:CSUB], AF.Gelu,
                                            bias=b1_s[:, e, m:m + 1]))
                    bank_used(bw1, cA, t_g)
                    t_hh = aop(ae.activation(hh[hpar][:, :], hst[:, 0:CSUB], AF.Copy))
                    S.wait(ve, cA, t_hh)
                    S.wait(ve, cP, war2["h"][hpar])
                    t_hl = vop(ve.tensor_sub(hl[hpar][:, :], hst[:, 0:CSUB], hh[hpar][:, :]))
                    this_w2 = (m, par, hpar, t_hh, t_hl, tok2_hi, tok2_lo)
                    g += 1
                else:
                    this_w2 = None
                if pend_w2 is not None:
                    m, par, hpar, t_hh, t_hl, tok2_hi, tok2_lo = pend_w2
                    S.wait(pe, cA, max(t_hh, tok2_hi))
                    S.wait(pe, cV, max(t_hl, tok2_lo))
                    i = None
                    for j in range(CSUB // 128):
                        for fh in range(NFH):
                            by = j * NFH + fh
                            if m == 0:
                                bank_wait(by)
                            for p in range(3):
                                lhsT = (hh if p != 2 else hl)[hpar][:, j * 128:(j + 1) * 128]
                                rhs = (w2h if p != 1 else w2l)[par][:, fh * W2W:(fh + 1) * W2W]
                                i = pe.matmul(pb[by][:, 0:W2W], lhsT, rhs,
                                              start=(m == 0 and p == 0),
                                              stop=(m == HC - 1 and p == 2))
                            if m == HC - 1:
                                y_stop[by] = cP.tick(i)
                    if m != HC - 1:
                        t_end = cP.tick(i)
                        war2["w2split"][par] = t_end
                        war2["h"][hpar] = t_end
                    else:
                        war2["w2split"][par] = y_stop[-1]
                        war2["h"][hpar] = y_stop[-1]
                pend_w2 = this_w2
            for j in range(CSUB // 128):
                slot = csub * (CSUB // 128) + j
                for fh in range(NFH):
                    by = j * NFH + fh
                    S.wait(ve, cP, y_stop[by])
                    S.wait(ve, dSc, dSc.n)
                    t_ev = vop(ve.tensor_scalar(
                        ysb[:, j, fh * W2W:(fh + 1) * W2W], pb[by][:, 0:W2W],
                        xge[:, slot, D + e:D + e + 1], None, ALU.mult))
                    bank_used(by, cV, t_ev)
            for j in range(CSUB // 128):
                slot = csub * (CSUB // 128) + j
                S.wait(gp, cV, cV.n)
                S.wait(gp, dO, dO.n)
                S.wait(gp, dSc, dSc.n)
                gp.indirect_dma_start(
                    y_acc, bass.IndirectOffsetOnAxis(ap=idxe[:, slot:slot + 1], axis=0),
                    ysb[:, j, :], None,
                    compute_op=ALU.add).then_inc(dSc.h, 16)
                dSc.n += 16
        war3["xg_dve"][par_e] = cV.n

    for ft in range(NFT):
        S.wait(sy, dSc, dSc.n)
        S.wait(sy, dO, dO.n)
        S.wait(sy, cV, cV.n)
        tok_l = sdma(r2t[:, :, :],
                     y_acc[ft * FT:(ft + 1) * FT, :].rearrange("(c p) d -> p c d", p=128))
        S.wait(gp, dX, tok_l)
        S.wait(gp, dO, dO.n)
        gp.dma_start(out=y_pc[:, ft * FTC:(ft + 1) * FTC, :], in_=r2t[:, :, :]).then_inc(dO.h, 16)
        dO.n += 16

    S.wait(gp, dO, dO.n)
    S.wait(gp, dSc, dSc.n)
    es.close()
    return nc


def _stage_inputs(inputs, cfg, core):
    D, T = cfg["D"], cfg["T"]
    TC = T // 128
    x = np.ascontiguousarray(np.asarray(inputs["L0_attn_x"], dtype=np.float32))
    b, half = core // 2, core % 2
    own = x[b, half * T:(half + 1) * T]
    pre = x[b, 0:T] if half == 1 else own
    shuffle = lambda a: np.ascontiguousarray(a.reshape(TC, 128, D).transpose(1, 0, 2))
    return {
        "x_own": shuffle(own),
        "x_pre": shuffle(pre),
        "gate": np.full((128, 1), float(half), np.float32),
    }


def _stage_weights(inputs, cfg):
    D, HID, E = cfg["D"], cfg["HID"], cfg["E"]
    KC, HC = D // 128, HID // 128
    HP = (D // 64) // 2
    f32 = lambda a: np.asarray(a, dtype=np.float32)
    Wq, Wk, Wv, Wo = (f32(inputs[f"L0_attn_W{n}"]) for n in "qkvo")
    W1, W2 = f32(inputs["L0_ffn_W1"]), f32(inputs["L0_ffn_W2"])
    b1, b2, Wr = f32(inputs["L0_ffn_b1"]), f32(inputs["L0_ffn_b2"]), f32(inputs["L0_ffn_Wr"])

    def lhsT_lay(W):
        a = W.reshape(HP, 128, KC, 128).transpose(0, 3, 2, 1)
        return np.ascontiguousarray(a.reshape(HP, 128, KC * 128))

    import ml_dtypes
    T, CAP, E_ = cfg["T"], cfg["CAP"], cfg["E"]
    TC = T // 128
    bfl = np.dtype(ml_dtypes.bfloat16)
    return {
        "slt_in": np.tril(np.ones((128, 128), np.float32), -1).T.astype(bfl),
        "onesb_in": np.ones((128, 128), bfl),
        "offs_in": np.tile((np.arange(8) * (CAP + 128)).astype(np.float32), (128, 1)),
        "capd_in": (CAP + np.arange(128, dtype=np.float32))[:, None].copy(),
        "iota0_in": np.ascontiguousarray(
            (np.arange(TC)[None, :] * 128 + np.arange(128)[:, None]).astype(np.float32)),
        "iop_in": np.arange(128, dtype=np.float32)[:, None].copy(),
        "wq": lhsT_lay(Wq), "wk": lhsT_lay(Wk), "wv": lhsT_lay(Wv),
        "wo": np.ascontiguousarray(Wo.T.reshape(HP, 128, D)),
        "w1": np.ascontiguousarray(
            W1.reshape(E, KC, 128, HC, 128).transpose(0, 3, 2, 1, 4).reshape(E, HC, 128, KC * 128)),
        "w2": np.ascontiguousarray(W2.reshape(E, HC, 128, D)),
        "b1": np.ascontiguousarray(b1.reshape(E, HC, 128).transpose(2, 0, 1)),
        "b2": np.ascontiguousarray(b2),
        "wr": np.ascontiguousarray(Wr.reshape(KC, 128, 8).transpose(1, 0, 2)),
        "um": np.tile(np.triu(np.ones((128, 128), np.float32)), (1, 2)),
        "ident": np.eye(128, dtype=np.float32),
    }


def kernel(**inputs):
    cfg = FULL
    n_cores = 8
    weights = _stage_weights(inputs, cfg)
    in_maps = []
    for core in range(n_cores):
        m = dict(weights)
        m.update(_stage_inputs(inputs, cfg, core))
        in_maps.append(m)
    nc = build_program(cfg)
    res = run_bass_kernel_spmd(nc, in_maps, core_ids=list(range(n_cores)))
    T, D = cfg["T"], cfg["D"]
    out = np.empty((4, 4096, D), np.float32)
    for core in range(n_cores):
        b, half = core // 2, core % 2
        out[b, half * T:(half + 1) * T] = np.asarray(res.results[core]["y"])
    return out


# version 17
# speedup vs baseline: 1.1308x; 1.0442x over previous
import contextlib
import numpy as np
import concourse.bass as bass
import concourse.mybir as mybir
from concourse.bass_utils import run_bass_kernel_spmd

from concourse import library_config

F32 = mybir.dt.float32
BF16 = mybir.dt.bfloat16
I16 = mybir.dt.int16
U32 = mybir.dt.uint32
AF = mybir.ActivationFunctionType
ALU = mybir.AluOpType
AX = mybir.AxisListType

FULL = dict(D=1024, T=2048, HID=4096, E=8, FT=256, CAP=768)
MINI = dict(D=256, T=256, HID=512, E=8, FT=256, CAP=256)

LN_EPS = 1e-5
ATTN_EPS = 1e-8


class Ctr:
    def __init__(self, nc, name):
        self.h = nc.alloc_semaphore(name=name)
        self.n = 0

    def tick(self, instr, n=1):
        instr.then_inc(self.h, n)
        self.n += n
        return self.n


class Syncer:
    def __init__(self):
        self.wm = {}

    def wait(self, eng, ctr, val):
        if not val or val <= 0:
            return
        k = (id(eng), id(ctr))
        if self.wm.get(k, 0) >= val:
            return
        eng.wait_ge(ctr.h, val)
        self.wm[k] = val


def build_program(cfg):
    D, T, HID, E, FT = (cfg[k] for k in ("D", "T", "HID", "E", "FT"))
    CAP = cfg["CAP"]
    CSUB = FT
    NCS = CAP // CSUB
    CAPS = CAP // 128
    KC = D // 128
    NH = D // 64
    HP = NH // 2
    TC = T // 128
    HC = HID // 128
    FTC = FT // 128
    NFT = T // FT
    PT = min(512, T)
    W2W = min(512, D)
    NFH = D // W2W
    NBY = FTC * NFH
    assert NBY <= 4

    nc = bass.Bass("TRN2", target_bir_lowering=False, debug=False)
    S = Syncer()

    x_own = nc.dram_tensor("x_own", [128, TC, D], F32, kind="ExternalInput").ap()
    x_pre = nc.dram_tensor("x_pre", [128, TC, D], F32, kind="ExternalInput").ap()
    gate = nc.dram_tensor("gate", [128, 1], F32, kind="ExternalInput").ap()
    wq = nc.dram_tensor("wq", [HP, 128, KC * 128], F32, kind="ExternalInput").ap()
    wk = nc.dram_tensor("wk", [HP, 128, KC * 128], F32, kind="ExternalInput").ap()
    wv = nc.dram_tensor("wv", [HP, 128, KC * 128], F32, kind="ExternalInput").ap()
    wo = nc.dram_tensor("wo", [HP, 128, D], F32, kind="ExternalInput").ap()
    w1 = nc.dram_tensor("w1", [E, HC, 128, KC * 128], F32, kind="ExternalInput").ap()
    w2 = nc.dram_tensor("w2", [E, HC, 128, D], F32, kind="ExternalInput").ap()
    b1 = nc.dram_tensor("b1", [128, E, HC], F32, kind="ExternalInput").ap()
    b2 = nc.dram_tensor("b2", [8, D], F32, kind="ExternalInput").ap()
    wr = nc.dram_tensor("wr", [128, KC, 8], F32, kind="ExternalInput").ap()
    um = nc.dram_tensor("um", [128, 256], F32, kind="ExternalInput").ap()
    ident = nc.dram_tensor("ident", [128, 128], F32, kind="ExternalInput").ap()
    slt_in = nc.dram_tensor("slt_in", [128, 128], BF16, kind="ExternalInput").ap()
    onesb_in = nc.dram_tensor("onesb_in", [128, 128], BF16, kind="ExternalInput").ap()
    offs_in = nc.dram_tensor("offs_in", [128, 8], F32, kind="ExternalInput").ap()
    capd_in = nc.dram_tensor("capd_in", [128, 1], F32, kind="ExternalInput").ap()
    iota0_in = nc.dram_tensor("iota0_in", [128, TC], F32, kind="ExternalInput").ap()
    iop_in = nc.dram_tensor("iop_in", [128, 1], F32, kind="ExternalInput").ap()
    y = nc.dram_tensor("y", [T, D], F32, kind="ExternalOutput").ap()
    y_pc = y.rearrange("(c p) d -> p c d", p=128)
    r2h = nc.dram_tensor("r2h", [128, TC, D], F32, kind="Internal").ap()
    XCR = E * (CAP + 128)
    XCW = D + 64
    YAR = T + XCR + 256
    xc = nc.dram_tensor("xc", [XCR, XCW], F32, kind="Internal").ap()
    y_acc = nc.dram_tensor("y_acc", [YAR, D], F32, kind="Internal").ap()

    es = contextlib.ExitStack()
    sbuf = lambda name, shape, dt=F32: es.enter_context(nc.sbuf_tensor(name, shape, dt))

    cP = Ctr(nc, "cP")
    cV = Ctr(nc, "cV")
    cA = Ctr(nc, "cA")
    dX = Ctr(nc, "dX")
    dW = [Ctr(nc, "dW0"), Ctr(nc, "dW1")]
    dG = [Ctr(nc, "dG0"), Ctr(nc, "dG1")]
    dO = Ctr(nc, "dO")

    pe, ve, ae, sy, gp = nc.tensor, nc.vector, nc.scalar, nc.sync, nc.gpsimd
    ALL_ENG = (pe, ve, ae, sy, gp)

    def vop(instr):
        t = cV.tick(instr)
        S.wait(ve, cV, t)
        return t

    def aop(instr):
        t = cA.tick(instr)
        S.wait(ae, cA, t)
        return t

    def sdma(out, in_):
        S.wait(sy, dX, dX.n)
        sy.dma_start(out=out, in_=in_).then_inc(dX.h, 16)
        dX.n += 16
        return dX.n

    def barrier():
        for ctr in (cP, cV, cA):
            for eng in ALL_ENG:
                S.wait(eng, ctr, ctr.n)

    um_s = sbuf("um_s", [128, 256])
    id_s = sbuf("id_s", [128, 128])
    gate_s = sbuf("gate_s", [128, 1])
    wr_f = sbuf("wr_f", [128, KC, 8])
    wr_h = sbuf("wr_h", [128, KC, 8], BF16)
    wr_l = sbuf("wr_l", [128, KC, 8], BF16)
    b1_s = sbuf("b1_s", [128, E, HC])
    b2_h = sbuf("b2_h", [8, D], BF16)
    b2_l = sbuf("b2_l", [8, D], BF16)
    lnx = sbuf("lnx", [128, max(D + 64, max(D, 2 * min(512, T)) + min(512, T))])
    lntm = lnx[:, 0:D]
    lnsc = sbuf("lnsc", [128, 16])
    lnh = sbuf("lnh", [128, KC, T], BF16)
    lnl = sbuf("lnl", [128, KC, T], BF16)
    wstg = [sbuf(f"wstg{i}", [128, KC * 128]) for i in range(2)]
    wsh = [sbuf(f"wsh{i}", [128, KC * 128], BF16) for i in range(2)]
    wsl = [sbuf(f"wsl{i}", [128, KC * 128], BF16) for i in range(2)]

    pb = [es.enter_context(nc.psum_tensor(f"pb{i}", [128, 512], F32)) for i in range(8)]
    bank_free = {i: [] for i in range(8)}

    def bank_wait(b):
        for ctr, tok in bank_free[b]:
            S.wait(pe, ctr, tok)
        bank_free[b] = []

    def bank_used(b, ctr, tok):
        bank_free[b].append((ctr, tok))

    sdma(um_s[:, :], um)
    sdma(id_s[:, :], ident)
    sdma(gate_s[:, :], gate)
    sdma(wr_f[:, :, :], wr)
    sdma(b1_s[:, :, :], b1)
    tok_consts = sdma(lnx[0:8, 0:D], b2)
    S.wait(ae, dX, tok_consts)
    t_wrh = aop(ae.activation(wr_h[:, :, :], wr_f[:, :, :], AF.Copy))
    t_b2h = aop(ae.activation(b2_h[:, :], lnx[0:8, 0:D], AF.Copy))
    S.wait(ve, dX, tok_consts)
    S.wait(ve, cA, t_b2h)
    t_wrl = vop(ve.tensor_sub(wr_l[:, :, :], wr_f[:, :, :], wr_h[:, :, :]))
    t_b2l = vop(ve.tensor_sub(b2_l[:, :], lnx[0:8, 0:D], b2_h[:, :]))

    attn_es = contextlib.ExitStack()
    asb = lambda name, shape, dt=F32: attn_es.enter_context(nc.sbuf_tensor(name, shape, dt))
    xbuf = asb("xbuf", [128, TC, D])
    qb = asb("qb", [128, T])
    kb = asb("kb", [128, T])
    vb = asb("vb", [128, T])
    s_run = asb("s_run", [128, NH, 128])
    ktm = [asb(f"ktm{i}", [128, 128]) for i in range(2)]
    vaug = [asb(f"vaug{i}", [128, 256]) for i in range(2)]
    qpad = [[asb(f"qpad{i}_{j}", [128, 128]) for i in range(2)] for j in range(2)]
    kpad = [[asb(f"kpad{i}_{j}", [128, 128]) for i in range(2)] for j in range(2)]
    am2_sb = [asb(f"am2_{j}", [128, 256]) for j in range(2)]
    attn_tm = asb("attn_tm", [128, TC, 128])
    ah_sb = asb("ah_sb", [128, 128], BF16)
    al_sb = asb("al_sb", [128, 128], BF16)
    esc1 = lnx[:, 0:PT]
    esc2 = lnx[:, PT:2 * PT]
    esc3 = lnx[:, 2 * PT:3 * PT] if 3 * PT <= lnx.shape[1] else lnx[:, lnx.shape[1] - PT:]
    numsb = asb("numsb", [128, 256])
    densb = asb("densb", [128, 4])

    vop(ve.memset(s_run[:, :, :], 0.0))
    for h in range(2):
        for j in range(2):
            vop(ve.memset(qpad[h][j][:, :], 0.0))
            vop(ve.memset(kpad[h][j][:, :], 0.0))
    for i in range(2):
        vop(ve.memset(vaug[i][:, :], 0.0))
        vop(ve.memset(vaug[i][:, 64:65], 1.0))
        vop(ve.memset(vaug[i][:, 192:193], 1.0))

    def probe_exit(src=None, dst=None, close_attn=True):
        S.wait(gp, cV, cV.n)
        S.wait(gp, cA, cA.n)
        for ctr in (cP,):
            S.wait(gp, ctr, ctr.n)
        S.wait(gp, dO, dO.n)
        gp.dma_start(out=(y_pc if dst is None else dst),
                     in_=(xbuf[:, :, :] if src is None else src)).then_inc(dO.h, 16)
        dO.n += 16
        S.wait(gp, dO, dO.n)
        if close_attn:
            attn_es.close()
        es.close()
        return nc

    war = {
        "lntm_pe": 0,
        "lntm_dma": None,
        "combpad_dma": None,
        "mbf_pe": 0,
        "stage": [[(None, 0)], [(None, 0)]],
        "split": [0, 0],
        "vaug": [0, 0],
        "ktm": [0, 0],
        "pad": [0, 0],
        "am2": [0, 0],
        "sadd": 0,
    }

    def emit_ln_chunk(src_ap, col0, tr_banks):
        S.wait(ae, cP, war["lntm_pe"])
        if war["lntm_dma"]:
            S.wait(ae, *war["lntm_dma"])
        t1 = aop(ae.activation(lntm, src_ap, AF.Square, accum_out=lnsc[:, 0:1]))
        t2 = aop(ae.activation(lntm, src_ap, AF.Identity, accum_out=lnsc[:, 1:2]))
        S.wait(ve, cA, t2)
        vop(ve.tensor_scalar_mul(lnsc[:, 2:3], lnsc[:, 1:2], 1.0 / D))
        vop(ve.tensor_mul(lnsc[:, 3:4], lnsc[:, 2:3], lnsc[:, 2:3]))
        vop(ve.scalar_tensor_tensor(lnsc[:, 4:5], lnsc[:, 0:1], 1.0 / D, lnsc[:, 3:4],
                                    ALU.mult, ALU.subtract))
        tv = vop(ve.tensor_scalar_add(lnsc[:, 5:6], lnsc[:, 4:5], LN_EPS))
        S.wait(ae, cV, tv)
        ts = aop(ae.activation(lnsc[:, 6:7], lnsc[:, 5:6], AF.Sqrt))
        S.wait(ve, cA, ts)
        vop(ve.reciprocal(lnsc[:, 7:8], lnsc[:, 6:7]))
        ycol = 7
        for it in range(2):
            a, b_, c_, yn = 8, 9, 10, 11 + it
            vop(ve.tensor_mul(lnsc[:, a:a + 1], lnsc[:, ycol:ycol + 1], lnsc[:, ycol:ycol + 1]))
            vop(ve.tensor_mul(lnsc[:, b_:b_ + 1], lnsc[:, 5:6], lnsc[:, a:a + 1]))
            vop(ve.tensor_scalar(lnsc[:, c_:c_ + 1], lnsc[:, b_:b_ + 1], -0.5, 1.5,
                                 ALU.mult, ALU.add))
            vop(ve.tensor_mul(lnsc[:, yn:yn + 1], lnsc[:, ycol:ycol + 1], lnsc[:, c_:c_ + 1]))
            ycol = yn
        S.wait(ve, cA, t2)
        S.wait(ve, cP, war["lntm_pe"])
        if war["lntm_dma"]:
            S.wait(ve, *war["lntm_dma"])
        t_ap = vop(ve.tensor_scalar(lntm, src_ap, lnsc[:, 2:3], lnsc[:, ycol:ycol + 1],
                                    ALU.subtract, ALU.mult))
        for k in range(KC):
            b = tr_banks[k % 2]
            S.wait(pe, cV, t_ap)
            bank_wait(b)
            t_tr = cP.tick(pe.transpose(pb[b][:, 0:128], lntm[:, k * 128:(k + 1) * 128], id_s[:, :]))
            S.wait(ae, cP, t_tr)
            t_hi = aop(ae.activation(lnh[:, k, col0:col0 + 128], pb[b][:, 0:128], AF.Copy))
            S.wait(ve, cP, t_tr)
            S.wait(ve, cA, t_hi)
            t_lo = vop(ve.tensor_sub(lnl[:, k, col0:col0 + 128], pb[b][:, 0:128],
                                     lnh[:, k, col0:col0 + 128]))
            bank_used(b, cV, t_lo)
            bank_used(b, cA, t_hi)
        war["lntm_pe"] = cP.n
        return t_ap

    def w_dma(par, src_ap):
        for ctr, tok in war["stage"][par]:
            if ctr is not None:
                S.wait(sy, ctr, tok)
        S.wait(sy, dW[par], dW[par].n)
        sy.dma_start(out=wstg[par][:, :], in_=src_ap).then_inc(dW[par].h, 16)
        dW[par].n += 16
        return dW[par].n

    def w_split(par, tok_dma):
        S.wait(ae, dW[par], tok_dma)
        S.wait(ae, cP, war["split"][par])
        t_hi = aop(ae.activation(wsh[par][:, :], wstg[par][:, :], AF.Copy))
        S.wait(ve, dW[par], tok_dma)
        S.wait(ve, cP, war["split"][par])
        S.wait(ve, cA, t_hi)
        t_lo = vop(ve.tensor_sub(wsl[par][:, :], wstg[par][:, :], wsh[par][:, :]))
        war["stage"][par] = [(cA, t_hi), (cV, t_lo)]
        return t_hi, t_lo

    def mm3(bank, wpar, n, col, src_h=None, src_l=None):
        if src_h is None:
            src_h, src_l = lnh, lnl
        tok = None
        first = True
        for k in range(KC):
            for p in range(3):
                lhsT = (wsh if p != 2 else wsl)[wpar][:, k * 128:(k + 1) * 128]
                rhs = (src_h if p != 1 else src_l)[:, k, col:col + n]
                last = (k == KC - 1 and p == 2)
                i = pe.matmul(pb[bank][:, 0:n], lhsT, rhs, start=first, stop=last)
                first = False
                if last:
                    tok = cP.tick(i)
        return tok

    def emit_elu_evac(ps_ap, dst_ap, n):
        t_r = vop(ve.tensor_relu(esc1[:, 0:n], ps_ap))
        t_m = vop(ve.tensor_single_scalar(esc2[:, 0:n], ps_ap, 0.0, ALU.min))
        S.wait(ae, cV, t_m)
        t_e = aop(ae.activation(esc3[:, 0:n], esc2[:, 0:n], AF.Exp))
        S.wait(ve, cA, t_e)
        return vop(ve.tensor_add(dst_ap, esc1[:, 0:n], esc3[:, 0:n]))

    def emit_proj(dst, elu, wpar, tok_hi, tok_lo, banks):
        t_ev = 0
        for i0, t0 in enumerate(range(0, T, PT)):
            b = banks[i0 % len(banks)]
            S.wait(pe, cA, tok_hi)
            S.wait(pe, cV, tok_lo)
            bank_wait(b)
            tok_mm = mm3(b, wpar, PT, t0)
            S.wait(ve, cP, tok_mm)
            if elu:
                t_ev = emit_elu_evac(pb[b][:, 0:PT], dst[:, t0:t0 + PT], PT)
            else:
                t_ev = vop(ve.tensor_copy(dst[:, t0:t0 + PT], pb[b][:, 0:PT]))
            bank_used(b, cV, t_ev)
        war["split"][wpar] = cP.n
        return t_ev

    def mix_state_chunk(c, hp, par, tok_k, tok_v, update_state=True):
        bt, bs = 6 + (c % 2), 5
        S.wait(pe, cV, max(tok_k, tok_v))
        bank_wait(bt)
        t_trk = cP.tick(pe.transpose(pb[bt][:, 0:128], kb[:, c * 128:(c + 1) * 128], id_s[:, :]))
        t_trv = cP.tick(pe.transpose(pb[bt][:, 128:256], vb[:, c * 128:(c + 1) * 128], id_s[:, :]))
        S.wait(ae, cP, max(t_trv, war["ktm"][par], war["vaug"][par]))
        t_ktm = aop(ae.activation(ktm[par][:, :], pb[bt][:, 0:128], AF.Copy))
        aop(ae.activation(vaug[par][:, 0:64], pb[bt][:, 128:192], AF.Copy))
        t_v1 = aop(ae.activation(vaug[par][:, 128:192], pb[bt][:, 192:256], AF.Copy))
        bank_used(bt, cA, t_v1)
        if update_state:
            S.wait(pe, cA, t_v1)
            bank_wait(bs)
            cP.tick(pe.matmul(pb[bs][:, 0:128], ktm[par][:, :], vaug[par][:, 0:128],
                              start=True, stop=True))
            t_s = cP.tick(pe.matmul(pb[bs][:, 128:256], ktm[par][:, :], vaug[par][:, 128:256],
                                    start=True, stop=True))
            S.wait(ve, cP, t_s)
            gh0, gh1 = 2 * hp, 2 * hp + 1
            vop(ve.tensor_add(s_run[0:64, gh0, 0:128], s_run[0:64, gh0, 0:128],
                              pb[bs][0:64, 0:128]))
            t_up = vop(ve.tensor_add(s_run[64:128, gh1, 0:128], s_run[64:128, gh1, 0:128],
                                     pb[bs][64:128, 128:256]))
            bank_used(bs, cV, t_up)
        war["ktm"][par] = cP.n
        war["vaug"][par] = cP.n
        return t_ktm, t_v1

    if cfg.get("stop") == "ph0":
        return probe_exit()
    tok_xpre = sdma(xbuf[:, :, :], x_pre)
    S.wait(ae, dX, tok_xpre)
    S.wait(ve, dX, tok_xpre)
    S.wait(ae, cV, t_b2l)
    for j in range(TC):
        emit_ln_chunk(xbuf[:, j, :], j * 128, (0, 1))

    pend_k = w_dma(0, wk[0])
    pend_v = w_dma(1, wv[0])
    for hp in range(HP):
        tok_hi, tok_lo = w_split(0, pend_k)
        tok_k = emit_proj(kb, True, 0, tok_hi, tok_lo, (2, 3, 0))
        tok_hi, tok_lo = w_split(1, pend_v)
        tok_v = emit_proj(vb, False, 1, tok_hi, tok_lo, (2, 3, 0))
        if hp + 1 < HP:
            pend_k = w_dma(0, wk[hp + 1])
            pend_v = w_dma(1, wv[hp + 1])
        for c in range(TC):
            mix_state_chunk(c, hp, c % 2, tok_k, tok_v)

    vop(ve.tensor_scalar_mul(s_run[:, :, :], s_run[:, :, :], gate_s[:, 0:1]))
    if cfg.get("stop") == "prefix":
        return probe_exit()

    barrier()
    tok_xown = sdma(xbuf[:, :, :], x_own)
    S.wait(ae, dX, tok_xown)
    S.wait(ve, dX, tok_xown)
    for j in range(TC):
        emit_ln_chunk(xbuf[:, j, :], j * 128, (0, 1))

    pend_q = w_dma(0, wq[0])
    pend_k = w_dma(1, wk[0])
    for hp in range(HP):
        tok_hi, tok_lo = w_split(0, pend_q)
        tok_q = emit_proj(qb, True, 0, tok_hi, tok_lo, (2, 3, 0))
        tok_hi, tok_lo = w_split(1, pend_k)
        tok_k = emit_proj(kb, True, 1, tok_hi, tok_lo, (2, 3, 0))
        pend_v = w_dma(0, wv[hp])
        tok_hi, tok_lo = w_split(0, pend_v)
        tok_v = emit_proj(vb, False, 0, tok_hi, tok_lo, (2, 3, 0))
        pend_wo = w_dma(1, wo[hp])

        pend_mix = None
        for cstep in range(TC + 1):
            if cstep < TC:
                c = cstep
                cpar = c % 2
                par = c % 2
                t_ktm, t_v1 = mix_state_chunk(c, hp, par, tok_k, tok_v, update_state=False)
                S.wait(ae, cV, tok_q)
                S.wait(ae, cP, war["pad"][cpar])
                aop(ae.activation(qpad[0][cpar][0:64, :], qb[0:64, c * 128:(c + 1) * 128],
                                  AF.Copy))
                aop(ae.activation(kpad[0][cpar][0:64, :], kb[0:64, c * 128:(c + 1) * 128],
                                  AF.Copy))
                aop(ae.activation(qpad[1][cpar][64:128, :],
                                  qb[64:128, c * 128:(c + 1) * 128], AF.Copy))
                t_kp = aop(ae.activation(kpad[1][cpar][64:128, :],
                                         kb[64:128, c * 128:(c + 1) * 128], AF.Copy))
                this_mix = (c, cpar, par, t_kp, t_v1)
            else:
                this_mix = None
            if pend_mix is not None:
                c, cpar, par, t_kp, t_v1 = pend_mix
                b_at = 2 + cpar
                b_o = cpar
                b_s = 5
                gh0, gh1 = 2 * hp, 2 * hp + 1
                S.wait(pe, cA, t_kp)
                bank_wait(b_at)
                cP.tick(pe.matmul(pb[b_at][:, 0:128], kpad[0][cpar][:, :],
                                  qpad[0][cpar][:, :], start=True, stop=True))
                t_at = cP.tick(pe.matmul(pb[b_at][:, 128:256], kpad[1][cpar][:, :],
                                         qpad[1][cpar][:, :], start=True, stop=True))
                S.wait(ve, cP, max(t_at, war["am2"][cpar]))
                t_am = vop(ve.tensor_mul(am2_sb[cpar][:, :], pb[b_at][:, 0:256], um_s[:, :]))
                bank_used(b_at, cV, t_am)
                S.wait(pe, cV, t_am)
                S.wait(pe, cA, t_v1)
                bank_wait(b_o)
                cP.tick(pe.matmul(pb[b_o][:, 0:128], am2_sb[cpar][:, 0:128],
                                  vaug[par][:, 0:128], start=True, stop=True))
                t_o1 = cP.tick(pe.matmul(pb[b_o][:, 128:256], am2_sb[cpar][:, 128:256],
                                         vaug[par][:, 128:256], start=True, stop=True))
                S.wait(pe, cV, war["sadd"])
                cP.tick(pe.matmul(pb[b_o][:, 256:384], qpad[0][cpar][:, :],
                                  s_run[:, gh0, :], start=True, stop=True))
                t_o2 = cP.tick(pe.matmul(pb[b_o][:, 384:512], qpad[1][cpar][:, :],
                                         s_run[:, gh1, :], start=True, stop=True))
                war["pad"][cpar] = t_o2
                war["am2"][cpar] = t_o1
                war["vaug"][par] = max(war["vaug"][par], t_o1)
                bank_wait(b_s)
                cP.tick(pe.matmul(pb[b_s][:, 0:128], ktm[par][:, :], vaug[par][:, 0:128],
                                  start=True, stop=True))
                t_s = cP.tick(pe.matmul(pb[b_s][:, 128:256], ktm[par][:, :],
                                        vaug[par][:, 128:256], start=True, stop=True))
                war["ktm"][par] = max(war["ktm"][par], t_s)
                war["vaug"][par] = max(war["vaug"][par], t_s)
                S.wait(ve, cP, t_s)
                vop(ve.tensor_add(s_run[0:64, gh0, 0:128], s_run[0:64, gh0, 0:128],
                                  pb[b_s][0:64, 0:128]))
                t_up = vop(ve.tensor_add(s_run[64:128, gh1, 0:128],
                                         s_run[64:128, gh1, 0:128],
                                         pb[b_s][64:128, 128:256]))
                bank_used(b_s, cV, t_up)
                war["sadd"] = t_up
                t_cp1 = vop(ve.tensor_copy(numsb[:, 0:256], pb[b_o][:, 0:256]))
                vop(ve.scalar_tensor_tensor(densb[:, 0:2], pb[b_o][:, 320:449:128],
                                            ATTN_EPS, numsb[:, 64:193:128],
                                            ALU.add, ALU.add))
                vop(ve.reciprocal(densb[:, 2:4], densb[:, 0:2]))
                t_n = vop(ve.tensor_add(numsb[:, 0:256], numsb[:, 0:256],
                                        pb[b_o][:, 256:512]))
                bank_used(b_o, cV, t_n)
                S.wait(ae, cV, t_n)
                aop(ae.activation(attn_tm[:, c, 0:64], numsb[:, 0:64], AF.Copy,
                                  scale=densb[:, 2:3]))
                aop(ae.activation(attn_tm[:, c, 64:128], numsb[:, 128:192], AF.Copy,
                                  scale=densb[:, 3:4]))
            pend_mix = this_mix
        tok_hi_o, tok_lo_o = w_split(1, pend_wo)
        t_wo_last = 0
        for j in range(TC):
            b_tr = 6
            S.wait(pe, cV, cV.n)
            bank_wait(b_tr)
            t_tr = cP.tick(pe.transpose(pb[b_tr][:, 0:128], attn_tm[:, j, :], id_s[:, :]))
            S.wait(ae, cP, t_tr)
            t_ah = aop(ae.activation(ah_sb[:, :], pb[b_tr][:, 0:128], AF.Copy))
            S.wait(ve, cP, t_tr)
            S.wait(ve, cA, t_ah)
            t_al = vop(ve.tensor_sub(al_sb[:, :], pb[b_tr][:, 0:128], ah_sb[:, :]))
            bank_used(b_tr, cV, t_al)
            for fh in range(NFH):
                b = (0, 1)[fh % 2]
                S.wait(pe, cA, max(t_ah, tok_hi_o))
                S.wait(pe, cV, max(t_al, tok_lo_o))
                bank_wait(b)
                pe.matmul(pb[b][:, 0:W2W], ah_sb[:, :], wsh[1][:, fh * W2W:(fh + 1) * W2W],
                          start=True, stop=False)
                pe.matmul(pb[b][:, 0:W2W], ah_sb[:, :], wsl[1][:, fh * W2W:(fh + 1) * W2W],
                          start=False, stop=False)
                i3 = pe.matmul(pb[b][:, 0:W2W], al_sb[:, :], wsh[1][:, fh * W2W:(fh + 1) * W2W],
                               start=False, stop=True)
                t_wo = cP.tick(i3)
                t_wo_last = t_wo
                S.wait(ve, cP, t_wo)
                t_acc = vop(ve.scalar_tensor_tensor(xbuf[:, j, fh * W2W:(fh + 1) * W2W],
                                                    pb[b][:, 0:W2W], 1.0,
                                                    xbuf[:, j, fh * W2W:(fh + 1) * W2W],
                                                    ALU.mult, ALU.add))
                bank_used(b, cV, t_acc)
        war["split"][1] = cP.n
        if hp + 1 < HP:
            pend_q = w_dma(0, wq[hp + 1])
            pend_k = w_dma(1, wk[hp + 1])

    S.wait(gp, cV, cV.n)
    S.wait(gp, dO, dO.n)
    gp.dma_start(out=r2h, in_=xbuf[:, :, :]).then_inc(dO.h, 16)
    dO.n += 16
    tok_r2w = dO.n

    if cfg.get("stop") == "own":
        return probe_exit()

    barrier()
    for eng in ALL_ENG:
        S.wait(eng, dO, tok_r2w)
    attn_es.close()

    r2t = sbuf("r2t", [128, FTC, D])
    hst = sbuf("hst", [128, FT])
    hh = [sbuf(f"hh{i}", [128, FT], BF16) for i in range(2)]
    hl = [sbuf(f"hl{i}", [128, FT], BF16) for i in range(2)]
    w2stg = [sbuf(f"w2stg{i}", [128, D]) for i in range(2)]
    w2h = [sbuf(f"w2h{i}", [128, D], BF16) for i in range(2)]
    w2l = [sbuf(f"w2l{i}", [128, D], BF16) for i in range(2)]
    lg = sbuf("lg", [128, 8])
    lgm1 = sbuf("lgm1", [128, 8])
    lgk = sbuf("lgk", [128, 8])
    lgm2 = sbuf("lgm2", [128, 8])
    lgsc = sbuf("lgsc", [128, 8])
    combine = sbuf("combine", [128, TC, 8])
    comb_pad = sbuf("comb_pad", [128, 128])
    cth = sbuf("cth", [8, TC, 128], BF16)
    ctl = sbuf("ctl", [8, TC, 128], BF16)
    ct_f = sbuf("ct_f", [8, 128])
    slt_s = sbuf("slt_s", [128, 128], BF16)
    onesb_s = sbuf("onesb_s", [128, 128], BF16)
    offs_s = sbuf("offs_s", [128, 8])
    capd_s = sbuf("capd_s", [128, 1])
    iota0_s = sbuf("iota0_s", [128, TC])
    iop_s = sbuf("iop_s", [128, 1])
    maskp = sbuf("maskp", [128, 128])
    masks_bf = sbuf("masks_bf", [128, 128], BF16)
    base_sb = sbuf("base_sb", [128, 8])
    posx = sbuf("posx", [128, 8])
    post = sbuf("post", [128, 8])
    posf = sbuf("posf", [128, 2])
    posu = sbuf("posu", [128, 2], U32)
    xg = [sbuf(f"xg{i}", [128, CAPS, XCW]) for i in range(2)]
    xgh = sbuf("xgh", [128, KC, FT], BF16)
    xgl = sbuf("xgl", [128, KC, FT], BF16)
    ysb = sbuf("ysb", [128, FT // 128, D])
    idxu = [sbuf(f"idxu{i}", [128, CAPS], U32) for i in range(2)]
    dSc = Ctr(nc, "dSc")
    sdma(slt_s[:, :], slt_in)
    sdma(onesb_s[:, :], onesb_in)
    sdma(offs_s[:, :], offs_in)
    sdma(capd_s[:, :], capd_in)
    sdma(iota0_s[:, :], iota0_in)
    tok_iot = sdma(iop_s[:, :], iop_in)
    vop(ve.memset(comb_pad[:, :], 0.0))
    vop(ve.memset(maskp[:, :], 0.0))
    vop(ve.memset(base_sb[:, :], 0.0))
    vop(ve.memset(xg[0][:, :, :], 0.0))
    vop(ve.memset(xg[1][:, :, :], 0.0))

    war2 = {"w2stage": [[(None, 0)], [(None, 0)]], "w2split": [0, 0], "h": [0, 0]}

    def w2_dma(par, src_ap):
        for ctr, tok in war2["w2stage"][par]:
            if ctr is not None:
                S.wait(gp, ctr, tok)
        S.wait(gp, dG[par], dG[par].n)
        gp.dma_start(out=w2stg[par][:, :], in_=src_ap).then_inc(dG[par].h, 16)
        dG[par].n += 16
        return dG[par].n

    def w2_split(par, tok_dma):
        S.wait(ae, dG[par], tok_dma)
        S.wait(ae, cP, war2["w2split"][par])
        t_hi = aop(ae.activation(w2h[par][:, :], w2stg[par][:, :], AF.Copy))
        S.wait(ve, dG[par], tok_dma)
        S.wait(ve, cP, war2["w2split"][par])
        S.wait(ve, cA, t_hi)
        t_lo = vop(ve.tensor_sub(w2l[par][:, :], w2stg[par][:, :], w2h[par][:, :]))
        war2["w2stage"][par] = [(cA, t_hi), (cV, t_lo)]
        return t_hi, t_lo

    S.wait(ve, dX, tok_iot)
    for s0 in range(XCR // 128):
        S.wait(ve, dX, dX.n)
        t_id = vop(ve.tensor_scalar_add(xg[0][:, 0, D + 8:D + 9], iop_s[:, :],
                                        float(T + s0 * 128)))
        S.wait(sy, cV, t_id)
        sdma(xc[s0 * 128:(s0 + 1) * 128, :], xg[0][:, 0, :])
    S.wait(sy, cV, cV.n)
    for r0 in range(T, YAR, 128):
        n_r = min(128, YAR - r0)
        sdma(y_acc[r0:r0 + n_r, :], xg[1][0:n_r, 0, 0:D])
    tok_xc_init = dX.n

    NG = E * NCS * HC

    def g_src(g):
        e = g // (NCS * HC)
        m = g % HC
        return e, m

    tok_w1 = {}
    tok_w2 = {}
    for g0 in range(min(2, NG)):
        e0, m0 = g_src(g0)
        tok_w1[g0] = w_dma(g0 % 2, w1[e0, m0])
        tok_w2[g0] = w2_dma(g0 % 2, w2[e0, m0])

    last_out_dma = 0
    tok_r2t_consumed = [(None, 0)]
    for ft in range(NFT):
        for ctr, tok in tok_r2t_consumed:
            if ctr is not None:
                S.wait(sy, ctr, tok)
        S.wait(sy, dO, last_out_dma)
        tok_r2t = sdma(r2t[:, :, :], r2h[:, ft * FTC:(ft + 1) * FTC, :])
        S.wait(ae, dX, tok_r2t)
        S.wait(ve, dX, tok_r2t)
        for jj in range(FTC):
            jabs = ft * FTC + jj
            col0 = jabs * 128
            t_ap = emit_ln_chunk(r2t[:, jj, :], col0, (4, 5))
            first = True
            i = None
            for k in range(KC):
                for p in range(3):
                    lhsT = (lnh if p != 2 else lnl)[:, k, col0:col0 + 128]
                    rhs = (wr_h if p != 1 else wr_l)[:, k, :]
                    if first:
                        S.wait(pe, cV, cV.n)
                        S.wait(pe, cA, cA.n)
                        bank_wait(6)
                    i = pe.matmul(pb[6][:, 0:8], lhsT, rhs, start=first,
                                  stop=(k == KC - 1 and p == 2))
                    first = False
            t_lgmm = cP.tick(i)
            S.wait(ve, cP, t_lgmm)
            t_lg = vop(ve.tensor_copy(lg[:, :], pb[6][:, 0:8]))
            bank_used(6, cV, t_lg)
            vop(ve.reduce_max(lgsc[:, 0:1], lg[:, :], axis=AX.X))
            vop(ve.tensor_scalar(lgm1[:, :], lg[:, :], lgsc[:, 0:1], None, ALU.is_equal))
            vop(ve.scalar_tensor_tensor(lgk[:, :], lgm1[:, :], -1e30, lg[:, :],
                                        ALU.mult, ALU.add))
            vop(ve.reduce_max(lgsc[:, 1:2], lgk[:, :], axis=AX.X))
            vop(ve.tensor_scalar(lgm2[:, :], lgk[:, :], lgsc[:, 1:2], None, ALU.is_equal))
            t_d = vop(ve.tensor_sub(lgsc[:, 2:3], lgsc[:, 1:2], lgsc[:, 0:1]))
            S.wait(ae, cV, t_d)
            t_e = aop(ae.activation(lgsc[:, 3:4], lgsc[:, 2:3], AF.Exp))
            S.wait(ve, cA, t_e)
            vop(ve.tensor_scalar_add(lgsc[:, 4:5], lgsc[:, 3:4], 1.0))
            vop(ve.reciprocal(lgsc[:, 5:6], lgsc[:, 4:5]))
            vop(ve.tensor_mul(lgsc[:, 6:7], lgsc[:, 3:4], lgsc[:, 5:6]))
            vop(ve.tensor_scalar(combine[:, jabs, :], lgm1[:, :], lgsc[:, 5:6], None, ALU.mult))
            t_c2 = vop(ve.scalar_tensor_tensor(combine[:, jabs, :], lgm2[:, :], lgsc[:, 6:7],
                                               combine[:, jabs, :], ALU.mult, ALU.add))
            t_cp = vop(ve.tensor_copy(comb_pad[:, 0:8], combine[:, jabs, :]))
            S.wait(pe, cV, t_cp)
            bank_wait(7)
            t_ct = cP.tick(pe.transpose(pb[7][:, 0:128], comb_pad[:, :], id_s[:, :]))
            S.wait(ve, cP, t_ct)
            t_ctf = vop(ve.tensor_copy(ct_f[:, :], pb[7][0:8, 0:128]))
            bank_used(7, cV, t_ctf)
            t_cth = vop(ve.tensor_copy(cth[:, jabs, :], ct_f[:, :]))
            vop(ve.tensor_sub(ctl[:, jabs, :], ct_f[:, :], cth[:, jabs, :]))
            S.wait(ve, cP, war["mbf_pe"])
            vop(ve.tensor_add(maskp[:, 0:8], lgm1[:, :], lgm2[:, :]))
            t_mb = vop(ve.tensor_copy(masks_bf[:, :], maskp[:, :]))
            S.wait(pe, cV, t_mb)
            bank_wait(3)
            t_pf = cP.tick(pe.matmul(pb[3][:, 0:128], slt_s[:, :], masks_bf[:, :],
                                     start=True, stop=True))
            S.wait(ve, cP, t_pf)
            t_px = vop(ve.tensor_add(posx[:, :], base_sb[:, :], pb[3][:, 0:8]))
            bank_used(3, cV, t_px)
            vop(ve.tensor_scalar(posx[:, :], posx[:, :], capd_s[:, 0:1], None, ALU.min))
            vop(ve.tensor_add(posx[:, :], posx[:, :], offs_s[:, :]))
            vop(ve.tensor_mul(post[:, :], lgm1[:, :], posx[:, :]))
            vop(ve.reduce_sum(posf[:, 0:1], post[:, :], axis=AX.X))
            vop(ve.tensor_mul(post[:, :], lgm2[:, :], posx[:, :]))
            vop(ve.reduce_sum(posf[:, 1:2], post[:, :], axis=AX.X))
            S.wait(ve, dSc, dSc.n)
            t_pu = vop(ve.tensor_copy(posu[:, :], posf[:, :]))
            bank_wait(7)
            t_tt = cP.tick(pe.matmul(pb[7][:, 0:128], onesb_s[:, :], masks_bf[:, :],
                                     start=True, stop=True))
            war["mbf_pe"] = t_tt
            S.wait(ve, cP, t_tt)
            t_bu = vop(ve.tensor_add(base_sb[:, :], base_sb[:, :], pb[7][:, 0:8]))
            bank_used(7, cV, t_bu)
            t_g8 = vop(ve.tensor_copy(lnx[:, D:D + 8], combine[:, jabs, :]))
            t_gi = vop(ve.tensor_copy(lnx[:, D + 8:D + 9], iota0_s[:, jabs:jabs + 1]))
            S.wait(gp, cV, cV.n)
            S.wait(gp, dX, tok_xc_init)
            for sl in range(2):
                S.wait(gp, dSc, dSc.n)
                gp.indirect_dma_start(
                    xc, bass.IndirectOffsetOnAxis(ap=posu[:, sl:sl + 1], axis=0),
                    lnx[:, 0:XCW], None).then_inc(dSc.h, 16)
                dSc.n += 16
            war["lntm_dma"] = (dSc, dSc.n)
        S.wait(ve, dO, last_out_dma)
        for jj in range(FTC):
            jabs = ft * FTC + jj
            for fh in range(NFH):
                b = 6 + ((jj * NFH + fh) % 2)
                S.wait(pe, cV, cV.n)
                bank_wait(b)
                pe.matmul(pb[b][:, 0:W2W], cth[:, jabs, :], b2_h[:, fh * W2W:(fh + 1) * W2W],
                          start=True, stop=False)
                pe.matmul(pb[b][:, 0:W2W], cth[:, jabs, :], b2_l[:, fh * W2W:(fh + 1) * W2W],
                          start=False, stop=False)
                i3 = pe.matmul(pb[b][:, 0:W2W], ctl[:, jabs, :], b2_h[:, fh * W2W:(fh + 1) * W2W],
                               start=False, stop=True)
                t_b = cP.tick(i3)
                S.wait(ve, cP, t_b)
                t_ev = vop(ve.scalar_tensor_tensor(
                    r2t[:, jj, fh * W2W:(fh + 1) * W2W], pb[b][:, 0:W2W], 1.0,
                    r2t[:, jj, fh * W2W:(fh + 1) * W2W], ALU.mult, ALU.add))
                bank_used(b, cV, t_ev)
        S.wait(gp, cV, cV.n)
        S.wait(gp, dO, dO.n)
        gp.dma_start(out=y_acc[ft * FT:(ft + 1) * FT, :].rearrange("(c p) d -> p c d", p=128),
                     in_=r2t[:, :, :]).then_inc(dO.h, 16)
        dO.n += 16
        last_out_dma = dO.n
        tok_r2t_consumed = [(cA, cA.n), (cV, cV.n)]

    if cfg.get("ffn_mode") == "router":
        return probe_exit(src=r2t[:, :, :], dst=y_pc[:, 0:FTC, :], close_attn=False)

    tok_comp_done = dSc.n
    war3 = {"xg_pe": [0, 0], "xg_dve": [0, 0]}
    tok_xr = {}

    def load_region(e):
        pe_ = e % 2
        S.wait(sy, dSc, tok_comp_done)
        S.wait(sy, cP, war3["xg_pe"][pe_])
        S.wait(sy, cV, war3["xg_dve"][pe_])
        tok_xr[e] = sdma(xg[pe_][:, :, :],
                         xc[e * (CAP + 128):e * (CAP + 128) + CAP, :]
                         .rearrange("(q p) c -> p q c", p=128))

    load_region(0)
    g = 0
    for e in range(E):
        par_e = e % 2
        xge = xg[par_e]
        idxe = idxu[par_e]
        if e + 1 < E:
            load_region(e + 1)
        S.wait(ve, dX, tok_xr[e])
        S.wait(ve, dSc, dSc.n)
        t_idx = vop(ve.tensor_copy(idxe[:, :], xge[:, :, D + 8]))
        for csub in range(NCS):
            for sl in range(CSUB // 128):
                slot = csub * (CSUB // 128) + sl
                for k in range(KC):
                    b = 6 + (k % 2)
                    S.wait(pe, dX, tok_xr[e])
                    S.wait(pe, cV, cV.n)
                    bank_wait(b)
                    t_tr = cP.tick(pe.transpose(pb[b][:, 0:128],
                                                xge[:, slot, k * 128:(k + 1) * 128],
                                                id_s[:, :]))
                    S.wait(ae, cP, t_tr)
                    t_hi = aop(ae.activation(xgh[:, k, sl * 128:(sl + 1) * 128],
                                             pb[b][:, 0:128], AF.Copy))
                    S.wait(ve, cP, t_tr)
                    S.wait(ve, cA, t_hi)
                    t_lo = vop(ve.tensor_sub(xgl[:, k, sl * 128:(sl + 1) * 128],
                                             pb[b][:, 0:128],
                                             xgh[:, k, sl * 128:(sl + 1) * 128]))
                    bank_used(b, cV, t_lo)
                    bank_used(b, cA, t_hi)
            war3["xg_pe"][par_e] = cP.n
            t_split = cV.n
            y_stop = [0] * NBY
            pend_w2 = None
            for mstep in range(HC + 1):
                if mstep < HC:
                    m = mstep
                    par = g % 2
                    tok_hi, tok_lo = w_split(par, tok_w1[g])
                    bw1 = 4 + (m % 2)
                    S.wait(pe, cA, tok_hi)
                    S.wait(pe, cV, max(tok_lo, t_split))
                    bank_wait(bw1)
                    tok_mm = mm3(bw1, par, CSUB, 0, src_h=xgh, src_l=xgl)
                    war["split"][par] = tok_mm
                    tok2_hi, tok2_lo = w2_split(par, tok_w2[g])
                    if g + 2 < NG:
                        e2, m2 = g_src(g + 2)
                        tok_w1[g + 2] = w_dma(par, w1[e2, m2])
                        tok_w2[g + 2] = w2_dma(par, w2[e2, m2])
                    hpar = m % 2
                    S.wait(ae, cP, max(tok_mm, war2["h"][hpar]))
                    t_g = aop(ae.activation(hst[:, 0:CSUB], pb[bw1][:, 0:CSUB], AF.Gelu,
                                            bias=b1_s[:, e, m:m + 1]))
                    bank_used(bw1, cA, t_g)
                    S.wait(ve, cA, t_g)
                    S.wait(ve, cP, war2["h"][hpar])
                    t_hh = vop(ve.tensor_copy(hh[hpar][:, :], hst[:, 0:CSUB]))
                    t_hl = vop(ve.tensor_sub(hl[hpar][:, :], hst[:, 0:CSUB], hh[hpar][:, :]))
                    this_w2 = (m, par, hpar, t_hh, t_hl, tok2_hi, tok2_lo)
                    g += 1
                else:
                    this_w2 = None
                if pend_w2 is not None:
                    m, par, hpar, t_hh, t_hl, tok2_hi, tok2_lo = pend_w2
                    S.wait(pe, cA, tok2_hi)
                    S.wait(pe, cV, max(t_hh, t_hl, tok2_lo))
                    i = None
                    for j in range(CSUB // 128):
                        for fh in range(NFH):
                            by = j * NFH + fh
                            if m == 0:
                                bank_wait(by)
                            for p in range(3):
                                lhsT = (hh if p != 2 else hl)[hpar][:, j * 128:(j + 1) * 128]
                                rhs = (w2h if p != 1 else w2l)[par][:, fh * W2W:(fh + 1) * W2W]
                                i = pe.matmul(pb[by][:, 0:W2W], lhsT, rhs,
                                              start=(m == 0 and p == 0),
                                              stop=(m == HC - 1 and p == 2))
                            if m == HC - 1:
                                y_stop[by] = cP.tick(i)
                    if m != HC - 1:
                        t_end = cP.tick(i)
                        war2["w2split"][par] = t_end
                        war2["h"][hpar] = t_end
                    else:
                        war2["w2split"][par] = y_stop[-1]
                        war2["h"][hpar] = y_stop[-1]
                pend_w2 = this_w2
            for j in range(CSUB // 128):
                slot = csub * (CSUB // 128) + j
                for fh in range(NFH):
                    by = j * NFH + fh
                    S.wait(ve, cP, y_stop[by])
                    S.wait(ve, dSc, dSc.n)
                    t_ev = vop(ve.tensor_scalar(
                        ysb[:, j, fh * W2W:(fh + 1) * W2W], pb[by][:, 0:W2W],
                        xge[:, slot, D + e:D + e + 1], None, ALU.mult))
                    bank_used(by, cV, t_ev)
            for j in range(CSUB // 128):
                slot = csub * (CSUB // 128) + j
                S.wait(gp, cV, cV.n)
                S.wait(gp, dO, dO.n)
                S.wait(gp, dSc, dSc.n)
                gp.indirect_dma_start(
                    y_acc, bass.IndirectOffsetOnAxis(ap=idxe[:, slot:slot + 1], axis=0),
                    ysb[:, j, :], None,
                    compute_op=ALU.add).then_inc(dSc.h, 16)
                dSc.n += 16
        war3["xg_dve"][par_e] = cV.n

    for ft in range(NFT):
        S.wait(sy, dSc, dSc.n)
        S.wait(sy, dO, dO.n)
        S.wait(sy, cV, cV.n)
        tok_l = sdma(r2t[:, :, :],
                     y_acc[ft * FT:(ft + 1) * FT, :].rearrange("(c p) d -> p c d", p=128))
        S.wait(gp, dX, tok_l)
        S.wait(gp, dO, dO.n)
        gp.dma_start(out=y_pc[:, ft * FTC:(ft + 1) * FTC, :], in_=r2t[:, :, :]).then_inc(dO.h, 16)
        dO.n += 16

    S.wait(gp, dO, dO.n)
    S.wait(gp, dSc, dSc.n)
    es.close()
    return nc


def _stage_inputs(inputs, cfg, core):
    D, T = cfg["D"], cfg["T"]
    TC = T // 128
    x = np.ascontiguousarray(np.asarray(inputs["L0_attn_x"], dtype=np.float32))
    b, half = core // 2, core % 2
    own = x[b, half * T:(half + 1) * T]
    pre = x[b, 0:T] if half == 1 else own
    shuffle = lambda a: np.ascontiguousarray(a.reshape(TC, 128, D).transpose(1, 0, 2))
    return {
        "x_own": shuffle(own),
        "x_pre": shuffle(pre),
        "gate": np.full((128, 1), float(half), np.float32),
    }


def _stage_weights(inputs, cfg):
    D, HID, E = cfg["D"], cfg["HID"], cfg["E"]
    KC, HC = D // 128, HID // 128
    HP = (D // 64) // 2
    f32 = lambda a: np.asarray(a, dtype=np.float32)
    Wq, Wk, Wv, Wo = (f32(inputs[f"L0_attn_W{n}"]) for n in "qkvo")
    W1, W2 = f32(inputs["L0_ffn_W1"]), f32(inputs["L0_ffn_W2"])
    b1, b2, Wr = f32(inputs["L0_ffn_b1"]), f32(inputs["L0_ffn_b2"]), f32(inputs["L0_ffn_Wr"])

    def lhsT_lay(W):
        a = W.reshape(HP, 128, KC, 128).transpose(0, 3, 2, 1)
        return np.ascontiguousarray(a.reshape(HP, 128, KC * 128))

    import ml_dtypes
    T, CAP, E_ = cfg["T"], cfg["CAP"], cfg["E"]
    TC = T // 128
    bfl = np.dtype(ml_dtypes.bfloat16)
    return {
        "slt_in": np.tril(np.ones((128, 128), np.float32), -1).T.astype(bfl),
        "onesb_in": np.ones((128, 128), bfl),
        "offs_in": np.tile((np.arange(8) * (CAP + 128)).astype(np.float32), (128, 1)),
        "capd_in": (CAP + np.arange(128, dtype=np.float32))[:, None].copy(),
        "iota0_in": np.ascontiguousarray(
            (np.arange(TC)[None, :] * 128 + np.arange(128)[:, None]).astype(np.float32)),
        "iop_in": np.arange(128, dtype=np.float32)[:, None].copy(),
        "wq": lhsT_lay(Wq), "wk": lhsT_lay(Wk), "wv": lhsT_lay(Wv),
        "wo": np.ascontiguousarray(Wo.T.reshape(HP, 128, D)),
        "w1": np.ascontiguousarray(
            W1.reshape(E, KC, 128, HC, 128).transpose(0, 3, 2, 1, 4).reshape(E, HC, 128, KC * 128)),
        "w2": np.ascontiguousarray(W2.reshape(E, HC, 128, D)),
        "b1": np.ascontiguousarray(b1.reshape(E, HC, 128).transpose(2, 0, 1)),
        "b2": np.ascontiguousarray(b2),
        "wr": np.ascontiguousarray(Wr.reshape(KC, 128, 8).transpose(1, 0, 2)),
        "um": np.tile(np.triu(np.ones((128, 128), np.float32)), (1, 2)),
        "ident": np.eye(128, dtype=np.float32),
    }


def kernel(**inputs):
    cfg = FULL
    n_cores = 8
    weights = _stage_weights(inputs, cfg)
    in_maps = []
    for core in range(n_cores):
        m = dict(weights)
        m.update(_stage_inputs(inputs, cfg, core))
        in_maps.append(m)
    nc = build_program(cfg)
    res = run_bass_kernel_spmd(nc, in_maps, core_ids=list(range(n_cores)))
    T, D = cfg["T"], cfg["D"]
    out = np.empty((4, 4096, D), np.float32)
    for core in range(n_cores):
        b, half = core // 2, core % 2
        out[b, half * T:(half + 1) * T] = np.asarray(res.results[core]["y"])
    return out


# version 18
# speedup vs baseline: 1.1669x; 1.0319x over previous
import contextlib
import numpy as np
import concourse.bass as bass
import concourse.mybir as mybir
from concourse.bass_utils import run_bass_kernel_spmd

from concourse import library_config

F32 = mybir.dt.float32
BF16 = mybir.dt.bfloat16
I16 = mybir.dt.int16
U32 = mybir.dt.uint32
AF = mybir.ActivationFunctionType
ALU = mybir.AluOpType
AX = mybir.AxisListType

FULL = dict(D=1024, T=2048, HID=4096, E=8, FT=256, CAP=768)
MINI = dict(D=256, T=256, HID=512, E=8, FT=256, CAP=256)

LN_EPS = 1e-5
ATTN_EPS = 1e-8


class Ctr:
    def __init__(self, nc, name):
        self.h = nc.alloc_semaphore(name=name)
        self.n = 0

    def tick(self, instr, n=1):
        instr.then_inc(self.h, n)
        self.n += n
        return self.n


class Syncer:
    def __init__(self):
        self.wm = {}

    def wait(self, eng, ctr, val):
        if not val or val <= 0:
            return
        k = (id(eng), id(ctr))
        if self.wm.get(k, 0) >= val:
            return
        eng.wait_ge(ctr.h, val)
        self.wm[k] = val


def build_program(cfg):
    D, T, HID, E, FT = (cfg[k] for k in ("D", "T", "HID", "E", "FT"))
    CAP = cfg["CAP"]
    CSUB = FT
    NCS = CAP // CSUB
    CAPS = CAP // 128
    KC = D // 128
    NH = D // 64
    HP = NH // 2
    TC = T // 128
    HC = HID // 128
    FTC = FT // 128
    NFT = T // FT
    PT = min(512, T)
    W2W = min(512, D)
    NFH = D // W2W
    NBY = FTC * NFH
    assert NBY <= 4

    nc = bass.Bass("TRN2", target_bir_lowering=False, debug=False)
    S = Syncer()

    x_own = nc.dram_tensor("x_own", [128, TC, D], F32, kind="ExternalInput").ap()
    x_pre = nc.dram_tensor("x_pre", [128, TC, D], F32, kind="ExternalInput").ap()
    gate = nc.dram_tensor("gate", [128, 1], F32, kind="ExternalInput").ap()
    wq = nc.dram_tensor("wq", [HP, 128, KC * 128], F32, kind="ExternalInput").ap()
    wk = nc.dram_tensor("wk", [HP, 128, KC * 128], F32, kind="ExternalInput").ap()
    wv = nc.dram_tensor("wv", [HP, 128, KC * 128], F32, kind="ExternalInput").ap()
    wo = nc.dram_tensor("wo", [HP, 128, D], F32, kind="ExternalInput").ap()
    w1 = nc.dram_tensor("w1", [E, HC, 128, KC * 128], F32, kind="ExternalInput").ap()
    w2 = nc.dram_tensor("w2", [E, HC, 128, D], F32, kind="ExternalInput").ap()
    b1 = nc.dram_tensor("b1", [128, E, HC], F32, kind="ExternalInput").ap()
    b2 = nc.dram_tensor("b2", [8, D], F32, kind="ExternalInput").ap()
    wr = nc.dram_tensor("wr", [128, KC, 8], F32, kind="ExternalInput").ap()
    um = nc.dram_tensor("um", [128, 256], F32, kind="ExternalInput").ap()
    ident = nc.dram_tensor("ident", [128, 128], F32, kind="ExternalInput").ap()
    slt_in = nc.dram_tensor("slt_in", [128, 128], BF16, kind="ExternalInput").ap()
    onesb_in = nc.dram_tensor("onesb_in", [128, 128], BF16, kind="ExternalInput").ap()
    offs_in = nc.dram_tensor("offs_in", [128, 8], F32, kind="ExternalInput").ap()
    capd_in = nc.dram_tensor("capd_in", [128, 1], F32, kind="ExternalInput").ap()
    iota0_in = nc.dram_tensor("iota0_in", [128, TC], F32, kind="ExternalInput").ap()
    iop_in = nc.dram_tensor("iop_in", [128, 1], F32, kind="ExternalInput").ap()
    y = nc.dram_tensor("y", [T, D], F32, kind="ExternalOutput").ap()
    y_pc = y.rearrange("(c p) d -> p c d", p=128)
    r2h = nc.dram_tensor("r2h", [128, TC, D], F32, kind="Internal").ap()
    XCR = E * (CAP + 128)
    XCW = D + 64
    YAR = T + XCR + 256
    xc = nc.dram_tensor("xc", [XCR, XCW], F32, kind="Internal").ap()
    y_acc = nc.dram_tensor("y_acc", [YAR, D], F32, kind="Internal").ap()

    es = contextlib.ExitStack()
    sbuf = lambda name, shape, dt=F32: es.enter_context(nc.sbuf_tensor(name, shape, dt))

    cP = Ctr(nc, "cP")
    cV = Ctr(nc, "cV")
    cA = Ctr(nc, "cA")
    dX = Ctr(nc, "dX")
    dW = [Ctr(nc, "dW0"), Ctr(nc, "dW1")]
    dG = [Ctr(nc, "dG0"), Ctr(nc, "dG1")]
    dO = Ctr(nc, "dO")

    pe, ve, ae, sy, gp = nc.tensor, nc.vector, nc.scalar, nc.sync, nc.gpsimd
    ALL_ENG = (pe, ve, ae, sy, gp)

    def vop(instr):
        t = cV.tick(instr)
        S.wait(ve, cV, t)
        return t

    def aop(instr):
        t = cA.tick(instr)
        S.wait(ae, cA, t)
        return t

    def sdma(out, in_):
        S.wait(sy, dX, dX.n)
        sy.dma_start(out=out, in_=in_).then_inc(dX.h, 16)
        dX.n += 16
        return dX.n

    def barrier():
        for ctr in (cP, cV, cA):
            for eng in ALL_ENG:
                S.wait(eng, ctr, ctr.n)

    um_s = sbuf("um_s", [128, 256])
    id_s = sbuf("id_s", [128, 128])
    gate_s = sbuf("gate_s", [128, 1])
    wr_f = sbuf("wr_f", [128, KC, 8])
    wr_h = sbuf("wr_h", [128, KC, 8], BF16)
    wr_l = sbuf("wr_l", [128, KC, 8], BF16)
    b1_s = sbuf("b1_s", [128, E, HC])
    b2_h = sbuf("b2_h", [8, D], BF16)
    b2_l = sbuf("b2_l", [8, D], BF16)
    lnx = sbuf("lnx", [128, max(D + 64, max(D, 2 * min(512, T)) + min(512, T))])
    lntm = lnx[:, 0:D]
    lnsc = sbuf("lnsc", [128, 16])
    lnh = sbuf("lnh", [128, KC, T], BF16)
    lnl = sbuf("lnl", [128, KC, T], BF16)
    wstg = [sbuf(f"wstg{i}", [128, KC * 128]) for i in range(2)]
    wsh = [sbuf(f"wsh{i}", [128, KC * 128], BF16) for i in range(2)]
    wsl = [sbuf(f"wsl{i}", [128, KC * 128], BF16) for i in range(2)]

    pb = [es.enter_context(nc.psum_tensor(f"pb{i}", [128, 512], F32)) for i in range(8)]
    bank_free = {i: [] for i in range(8)}

    def bank_wait(b):
        for ctr, tok in bank_free[b]:
            S.wait(pe, ctr, tok)
        bank_free[b] = []

    def bank_used(b, ctr, tok):
        bank_free[b].append((ctr, tok))

    sdma(um_s[:, :], um)
    sdma(id_s[:, :], ident)
    sdma(gate_s[:, :], gate)
    sdma(wr_f[:, :, :], wr)
    sdma(b1_s[:, :, :], b1)
    tok_consts = sdma(lnx[0:8, 0:D], b2)
    S.wait(ae, dX, tok_consts)
    t_wrh = aop(ae.activation(wr_h[:, :, :], wr_f[:, :, :], AF.Copy))
    t_b2h = aop(ae.activation(b2_h[:, :], lnx[0:8, 0:D], AF.Copy))
    S.wait(ve, dX, tok_consts)
    S.wait(ve, cA, t_b2h)
    t_wrl = vop(ve.tensor_sub(wr_l[:, :, :], wr_f[:, :, :], wr_h[:, :, :]))
    t_b2l = vop(ve.tensor_sub(b2_l[:, :], lnx[0:8, 0:D], b2_h[:, :]))

    attn_es = contextlib.ExitStack()
    asb = lambda name, shape, dt=F32: attn_es.enter_context(nc.sbuf_tensor(name, shape, dt))
    xbuf = asb("xbuf", [128, TC, D])
    qb = asb("qb", [128, T])
    kb = asb("kb", [128, T])
    vb = asb("vb", [128, T])
    s_run = asb("s_run", [128, NH, 128])
    ktm = [asb(f"ktm{i}", [128, 128]) for i in range(2)]
    vaug = [asb(f"vaug{i}", [128, 256]) for i in range(2)]
    qpad = [[asb(f"qpad{i}_{j}", [128, 128]) for i in range(2)] for j in range(2)]
    kpad = [[asb(f"kpad{i}_{j}", [128, 128]) for i in range(2)] for j in range(2)]
    am2_sb = [asb(f"am2_{j}", [128, 256]) for j in range(2)]
    attn_tm = asb("attn_tm", [128, TC, 128])
    ah_sb = asb("ah_sb", [128, 128], BF16)
    al_sb = asb("al_sb", [128, 128], BF16)
    esc1 = lnx[:, 0:PT]
    esc2 = lnx[:, PT:2 * PT]
    esc3 = lnx[:, 2 * PT:3 * PT] if 3 * PT <= lnx.shape[1] else lnx[:, lnx.shape[1] - PT:]
    numsb = asb("numsb", [128, 256])
    densb = asb("densb", [128, 4])

    vop(ve.memset(s_run[:, :, :], 0.0))
    for h in range(2):
        for j in range(2):
            vop(ve.memset(qpad[h][j][:, :], 0.0))
            vop(ve.memset(kpad[h][j][:, :], 0.0))
    for i in range(2):
        vop(ve.memset(vaug[i][:, :], 0.0))
        vop(ve.memset(vaug[i][:, 64:65], 1.0))
        vop(ve.memset(vaug[i][:, 192:193], 1.0))

    def probe_exit(src=None, dst=None, close_attn=True):
        S.wait(gp, cV, cV.n)
        S.wait(gp, cA, cA.n)
        for ctr in (cP,):
            S.wait(gp, ctr, ctr.n)
        S.wait(gp, dO, dO.n)
        gp.dma_start(out=(y_pc if dst is None else dst),
                     in_=(xbuf[:, :, :] if src is None else src)).then_inc(dO.h, 16)
        dO.n += 16
        S.wait(gp, dO, dO.n)
        if close_attn:
            attn_es.close()
        es.close()
        return nc

    war = {
        "lntm_pe": 0,
        "lntm_dma": None,
        "combpad_dma": None,
        "mbf_pe": 0,
        "stage": [[(None, 0)], [(None, 0)]],
        "split": [0, 0],
        "vaug": [0, 0],
        "ktm": [0, 0],
        "pad": [0, 0],
        "am2": [0, 0],
        "sadd": 0,
    }

    def emit_ln_chunk(src_ap, col0, tr_banks):
        S.wait(ae, cP, war["lntm_pe"])
        if war["lntm_dma"]:
            S.wait(ae, *war["lntm_dma"])
        t1 = aop(ae.activation(lntm, src_ap, AF.Square, accum_out=lnsc[:, 0:1]))
        t2 = aop(ae.activation(lntm, src_ap, AF.Identity, accum_out=lnsc[:, 1:2]))
        S.wait(ve, cA, t2)
        vop(ve.tensor_scalar_mul(lnsc[:, 2:3], lnsc[:, 1:2], 1.0 / D))
        vop(ve.tensor_mul(lnsc[:, 3:4], lnsc[:, 2:3], lnsc[:, 2:3]))
        vop(ve.scalar_tensor_tensor(lnsc[:, 4:5], lnsc[:, 0:1], 1.0 / D, lnsc[:, 3:4],
                                    ALU.mult, ALU.subtract))
        tv = vop(ve.tensor_scalar_add(lnsc[:, 5:6], lnsc[:, 4:5], LN_EPS))
        S.wait(ae, cV, tv)
        ts = aop(ae.activation(lnsc[:, 6:7], lnsc[:, 5:6], AF.Sqrt))
        S.wait(ve, cA, ts)
        vop(ve.reciprocal(lnsc[:, 7:8], lnsc[:, 6:7]))
        ycol = 7
        for it in range(2):
            a, b_, c_, yn = 8, 9, 10, 11 + it
            vop(ve.tensor_mul(lnsc[:, a:a + 1], lnsc[:, ycol:ycol + 1], lnsc[:, ycol:ycol + 1]))
            vop(ve.tensor_mul(lnsc[:, b_:b_ + 1], lnsc[:, 5:6], lnsc[:, a:a + 1]))
            vop(ve.tensor_scalar(lnsc[:, c_:c_ + 1], lnsc[:, b_:b_ + 1], -0.5, 1.5,
                                 ALU.mult, ALU.add))
            vop(ve.tensor_mul(lnsc[:, yn:yn + 1], lnsc[:, ycol:ycol + 1], lnsc[:, c_:c_ + 1]))
            ycol = yn
        S.wait(ve, cA, t2)
        S.wait(ve, cP, war["lntm_pe"])
        if war["lntm_dma"]:
            S.wait(ve, *war["lntm_dma"])
        t_ap = vop(ve.tensor_scalar(lntm, src_ap, lnsc[:, 2:3], lnsc[:, ycol:ycol + 1],
                                    ALU.subtract, ALU.mult))
        for k in range(KC):
            b = tr_banks[k % 2]
            S.wait(pe, cV, t_ap)
            bank_wait(b)
            t_tr = cP.tick(pe.transpose(pb[b][:, 0:128], lntm[:, k * 128:(k + 1) * 128], id_s[:, :]))
            S.wait(ae, cP, t_tr)
            t_hi = aop(ae.activation(lnh[:, k, col0:col0 + 128], pb[b][:, 0:128], AF.Copy))
            S.wait(ve, cP, t_tr)
            S.wait(ve, cA, t_hi)
            t_lo = vop(ve.tensor_sub(lnl[:, k, col0:col0 + 128], pb[b][:, 0:128],
                                     lnh[:, k, col0:col0 + 128]))
            bank_used(b, cV, t_lo)
            bank_used(b, cA, t_hi)
        war["lntm_pe"] = cP.n
        return t_ap

    def w_dma(par, src_ap):
        for ctr, tok in war["stage"][par]:
            if ctr is not None:
                S.wait(sy, ctr, tok)
        S.wait(sy, dW[par], dW[par].n)
        sy.dma_start(out=wstg[par][:, :], in_=src_ap).then_inc(dW[par].h, 16)
        dW[par].n += 16
        return dW[par].n

    def w_split(par, tok_dma):
        S.wait(ae, dW[par], tok_dma)
        S.wait(ae, cP, war["split"][par])
        t_hi = aop(ae.activation(wsh[par][:, :], wstg[par][:, :], AF.Copy))
        S.wait(ve, dW[par], tok_dma)
        S.wait(ve, cP, war["split"][par])
        S.wait(ve, cA, t_hi)
        t_lo = vop(ve.tensor_sub(wsl[par][:, :], wstg[par][:, :], wsh[par][:, :]))
        war["stage"][par] = [(cA, t_hi), (cV, t_lo)]
        return t_hi, t_lo

    def mm3(bank, wpar, n, col, src_h=None, src_l=None):
        if src_h is None:
            src_h, src_l = lnh, lnl
        tok = None
        first = True
        for k in range(KC):
            for p in range(3):
                lhsT = (wsh if p != 2 else wsl)[wpar][:, k * 128:(k + 1) * 128]
                rhs = (src_h if p != 1 else src_l)[:, k, col:col + n]
                last = (k == KC - 1 and p == 2)
                i = pe.matmul(pb[bank][:, 0:n], lhsT, rhs, start=first, stop=last)
                first = False
                if last:
                    tok = cP.tick(i)
        return tok

    def emit_elu_evac(ps_ap, dst_ap, n):
        t_r = vop(ve.tensor_relu(esc1[:, 0:n], ps_ap))
        t_m = vop(ve.tensor_single_scalar(esc2[:, 0:n], ps_ap, 0.0, ALU.min))
        S.wait(ae, cV, t_m)
        t_e = aop(ae.activation(esc3[:, 0:n], esc2[:, 0:n], AF.Exp))
        S.wait(ve, cA, t_e)
        return vop(ve.tensor_add(dst_ap, esc1[:, 0:n], esc3[:, 0:n]))

    def emit_proj(dst, elu, wpar, tok_hi, tok_lo, banks):
        t_ev = 0
        for i0, t0 in enumerate(range(0, T, PT)):
            b = banks[i0 % len(banks)]
            S.wait(pe, cA, tok_hi)
            S.wait(pe, cV, tok_lo)
            bank_wait(b)
            tok_mm = mm3(b, wpar, PT, t0)
            S.wait(ve, cP, tok_mm)
            if elu:
                t_ev = emit_elu_evac(pb[b][:, 0:PT], dst[:, t0:t0 + PT], PT)
            else:
                t_ev = vop(ve.tensor_copy(dst[:, t0:t0 + PT], pb[b][:, 0:PT]))
            bank_used(b, cV, t_ev)
        war["split"][wpar] = cP.n
        return t_ev

    def mix_state_chunk(c, hp, par, tok_k, tok_v, update_state=True):
        bt, bs = 6 + (c % 2), 5
        S.wait(pe, cV, max(tok_k, tok_v))
        bank_wait(bt)
        t_trk = cP.tick(pe.transpose(pb[bt][:, 0:128], kb[:, c * 128:(c + 1) * 128], id_s[:, :]))
        t_trv = cP.tick(pe.transpose(pb[bt][:, 128:256], vb[:, c * 128:(c + 1) * 128], id_s[:, :]))
        S.wait(ae, cP, max(t_trv, war["ktm"][par], war["vaug"][par]))
        t_ktm = aop(ae.activation(ktm[par][:, :], pb[bt][:, 0:128], AF.Copy))
        aop(ae.activation(vaug[par][:, 0:64], pb[bt][:, 128:192], AF.Copy))
        t_v1 = aop(ae.activation(vaug[par][:, 128:192], pb[bt][:, 192:256], AF.Copy))
        bank_used(bt, cA, t_v1)
        if update_state:
            S.wait(pe, cA, t_v1)
            bank_wait(bs)
            cP.tick(pe.matmul(pb[bs][:, 0:128], ktm[par][:, :], vaug[par][:, 0:128],
                              start=True, stop=True))
            t_s = cP.tick(pe.matmul(pb[bs][:, 128:256], ktm[par][:, :], vaug[par][:, 128:256],
                                    start=True, stop=True))
            S.wait(ve, cP, t_s)
            gh0, gh1 = 2 * hp, 2 * hp + 1
            vop(ve.tensor_add(s_run[0:64, gh0, 0:128], s_run[0:64, gh0, 0:128],
                              pb[bs][0:64, 0:128]))
            t_up = vop(ve.tensor_add(s_run[64:128, gh1, 0:128], s_run[64:128, gh1, 0:128],
                                     pb[bs][64:128, 128:256]))
            bank_used(bs, cV, t_up)
        war["ktm"][par] = cP.n
        war["vaug"][par] = cP.n
        return t_ktm, t_v1

    if cfg.get("stop") == "ph0":
        return probe_exit()
    tok_xpre = sdma(xbuf[:, :, :], x_pre)
    S.wait(ae, dX, tok_xpre)
    S.wait(ve, dX, tok_xpre)
    S.wait(ae, cV, t_b2l)
    for j in range(TC):
        emit_ln_chunk(xbuf[:, j, :], j * 128, (0, 1))

    pend_k = w_dma(0, wk[0])
    pend_v = w_dma(1, wv[0])
    for hp in range(HP):
        tok_hi, tok_lo = w_split(0, pend_k)
        tok_k = emit_proj(kb, True, 0, tok_hi, tok_lo, (2, 3, 0))
        tok_hi, tok_lo = w_split(1, pend_v)
        tok_v = emit_proj(vb, False, 1, tok_hi, tok_lo, (2, 3, 0))
        if hp + 1 < HP:
            pend_k = w_dma(0, wk[hp + 1])
            pend_v = w_dma(1, wv[hp + 1])
        pend_st = None
        for cstep in range(TC + 1):
            if cstep < TC:
                _, t_v1p = mix_state_chunk(cstep, hp, cstep % 2, tok_k, tok_v,
                                           update_state=False)
                this_st = (cstep % 2, t_v1p)
            else:
                this_st = None
            if pend_st is not None:
                parp, t_v1p = pend_st
                bs = 5
                S.wait(pe, cA, t_v1p)
                bank_wait(bs)
                cP.tick(pe.matmul(pb[bs][:, 0:128], ktm[parp][:, :],
                                  vaug[parp][:, 0:128], start=True, stop=True))
                t_s = cP.tick(pe.matmul(pb[bs][:, 128:256], ktm[parp][:, :],
                                        vaug[parp][:, 128:256], start=True, stop=True))
                war["ktm"][parp] = max(war["ktm"][parp], t_s)
                war["vaug"][parp] = max(war["vaug"][parp], t_s)
                gh0, gh1 = 2 * hp, 2 * hp + 1
                S.wait(ve, cP, t_s)
                vop(ve.tensor_add(s_run[0:64, gh0, 0:128], s_run[0:64, gh0, 0:128],
                                  pb[bs][0:64, 0:128]))
                t_up = vop(ve.tensor_add(s_run[64:128, gh1, 0:128],
                                         s_run[64:128, gh1, 0:128],
                                         pb[bs][64:128, 128:256]))
                bank_used(bs, cV, t_up)
            pend_st = this_st

    vop(ve.tensor_scalar_mul(s_run[:, :, :], s_run[:, :, :], gate_s[:, 0:1]))
    if cfg.get("stop") == "prefix":
        return probe_exit()

    barrier()
    tok_xown = sdma(xbuf[:, :, :], x_own)
    S.wait(ae, dX, tok_xown)
    S.wait(ve, dX, tok_xown)
    for j in range(TC):
        emit_ln_chunk(xbuf[:, j, :], j * 128, (0, 1))

    pend_q = w_dma(0, wq[0])
    pend_k = w_dma(1, wk[0])
    t_outlast = 0
    for hp in range(HP):
        tok_hi, tok_lo = w_split(0, pend_q)
        tok_q = emit_proj(qb, True, 0, tok_hi, tok_lo, (2, 3, 0))
        tok_hi, tok_lo = w_split(1, pend_k)
        tok_k = emit_proj(kb, True, 1, tok_hi, tok_lo, (2, 3, 0))
        pend_v = w_dma(0, wv[hp])
        tok_hi, tok_lo = w_split(0, pend_v)
        tok_v = emit_proj(vb, False, 0, tok_hi, tok_lo, (2, 3, 0))
        pend_wo = w_dma(1, wo[hp])

        pend_mix = None
        for cstep in range(TC + 1):
            if cstep < TC:
                c = cstep
                cpar = c % 2
                par = c % 2
                t_ktm, t_v1 = mix_state_chunk(c, hp, par, tok_k, tok_v, update_state=False)
                S.wait(ae, cV, tok_q)
                S.wait(ae, cP, war["pad"][cpar])
                aop(ae.activation(qpad[0][cpar][0:64, :], qb[0:64, c * 128:(c + 1) * 128],
                                  AF.Copy))
                aop(ae.activation(kpad[0][cpar][0:64, :], kb[0:64, c * 128:(c + 1) * 128],
                                  AF.Copy))
                aop(ae.activation(qpad[1][cpar][64:128, :],
                                  qb[64:128, c * 128:(c + 1) * 128], AF.Copy))
                t_kp = aop(ae.activation(kpad[1][cpar][64:128, :],
                                         kb[64:128, c * 128:(c + 1) * 128], AF.Copy))
                this_mix = (c, cpar, par, t_kp, t_v1)
            else:
                this_mix = None
            if pend_mix is not None:
                c, cpar, par, t_kp, t_v1 = pend_mix
                b_at = 2 + cpar
                b_o = cpar
                b_s = 5
                gh0, gh1 = 2 * hp, 2 * hp + 1
                S.wait(pe, cA, t_kp)
                bank_wait(b_at)
                cP.tick(pe.matmul(pb[b_at][:, 0:128], kpad[0][cpar][:, :],
                                  qpad[0][cpar][:, :], start=True, stop=True))
                t_at = cP.tick(pe.matmul(pb[b_at][:, 128:256], kpad[1][cpar][:, :],
                                         qpad[1][cpar][:, :], start=True, stop=True))
                S.wait(ve, cP, max(t_at, war["am2"][cpar]))
                t_am = vop(ve.tensor_mul(am2_sb[cpar][:, :], pb[b_at][:, 0:256], um_s[:, :]))
                bank_used(b_at, cV, t_am)
                S.wait(pe, cV, t_am)
                S.wait(pe, cA, t_v1)
                bank_wait(b_o)
                cP.tick(pe.matmul(pb[b_o][:, 0:128], am2_sb[cpar][:, 0:128],
                                  vaug[par][:, 0:128], start=True, stop=True))
                t_o1 = cP.tick(pe.matmul(pb[b_o][:, 128:256], am2_sb[cpar][:, 128:256],
                                         vaug[par][:, 128:256], start=True, stop=True))
                S.wait(pe, cV, war["sadd"])
                cP.tick(pe.matmul(pb[b_o][:, 256:384], qpad[0][cpar][:, :],
                                  s_run[:, gh0, :], start=True, stop=True))
                t_o2 = cP.tick(pe.matmul(pb[b_o][:, 384:512], qpad[1][cpar][:, :],
                                         s_run[:, gh1, :], start=True, stop=True))
                war["pad"][cpar] = t_o2
                war["am2"][cpar] = t_o1
                war["vaug"][par] = max(war["vaug"][par], t_o1)
                bank_wait(b_s)
                cP.tick(pe.matmul(pb[b_s][:, 0:128], ktm[par][:, :], vaug[par][:, 0:128],
                                  start=True, stop=True))
                t_s = cP.tick(pe.matmul(pb[b_s][:, 128:256], ktm[par][:, :],
                                        vaug[par][:, 128:256], start=True, stop=True))
                war["ktm"][par] = max(war["ktm"][par], t_s)
                war["vaug"][par] = max(war["vaug"][par], t_s)
                S.wait(ve, cP, t_s)
                vop(ve.tensor_add(s_run[0:64, gh0, 0:128], s_run[0:64, gh0, 0:128],
                                  pb[b_s][0:64, 0:128]))
                t_up = vop(ve.tensor_add(s_run[64:128, gh1, 0:128],
                                         s_run[64:128, gh1, 0:128],
                                         pb[b_s][64:128, 128:256]))
                bank_used(b_s, cV, t_up)
                war["sadd"] = t_up
                S.wait(ve, cA, t_outlast)
                t_cp1 = vop(ve.tensor_copy(numsb[:, 0:256], pb[b_o][:, 0:256]))
                vop(ve.scalar_tensor_tensor(densb[:, 0:2], pb[b_o][:, 320:449:128],
                                            ATTN_EPS, numsb[:, 64:193:128],
                                            ALU.add, ALU.add))
                vop(ve.reciprocal(densb[:, 2:4], densb[:, 0:2]))
                t_n = vop(ve.tensor_add(numsb[:, 0:256], numsb[:, 0:256],
                                        pb[b_o][:, 256:512]))
                bank_used(b_o, cV, t_n)
                S.wait(ae, cV, t_n)
                aop(ae.activation(attn_tm[:, c, 0:64], numsb[:, 0:64], AF.Copy,
                                  scale=densb[:, 2:3]))
                t_outlast = aop(ae.activation(attn_tm[:, c, 64:128], numsb[:, 128:192], AF.Copy,
                                              scale=densb[:, 3:4]))
            pend_mix = this_mix
        tok_hi_o, tok_lo_o = w_split(1, pend_wo)
        t_wo_last = 0
        for j in range(TC):
            b_tr = 6
            S.wait(pe, cA, t_outlast)
            bank_wait(b_tr)
            t_tr = cP.tick(pe.transpose(pb[b_tr][:, 0:128], attn_tm[:, j, :], id_s[:, :]))
            S.wait(ae, cP, t_tr)
            t_ah = aop(ae.activation(ah_sb[:, :], pb[b_tr][:, 0:128], AF.Copy))
            S.wait(ve, cP, t_tr)
            S.wait(ve, cA, t_ah)
            t_al = vop(ve.tensor_sub(al_sb[:, :], pb[b_tr][:, 0:128], ah_sb[:, :]))
            bank_used(b_tr, cV, t_al)
            for fh in range(NFH):
                b = (0, 1)[fh % 2]
                S.wait(pe, cA, max(t_ah, tok_hi_o))
                S.wait(pe, cV, max(t_al, tok_lo_o))
                bank_wait(b)
                pe.matmul(pb[b][:, 0:W2W], ah_sb[:, :], wsh[1][:, fh * W2W:(fh + 1) * W2W],
                          start=True, stop=False)
                pe.matmul(pb[b][:, 0:W2W], ah_sb[:, :], wsl[1][:, fh * W2W:(fh + 1) * W2W],
                          start=False, stop=False)
                i3 = pe.matmul(pb[b][:, 0:W2W], al_sb[:, :], wsh[1][:, fh * W2W:(fh + 1) * W2W],
                               start=False, stop=True)
                t_wo = cP.tick(i3)
                t_wo_last = t_wo
                S.wait(ve, cP, t_wo)
                t_acc = vop(ve.scalar_tensor_tensor(xbuf[:, j, fh * W2W:(fh + 1) * W2W],
                                                    pb[b][:, 0:W2W], 1.0,
                                                    xbuf[:, j, fh * W2W:(fh + 1) * W2W],
                                                    ALU.mult, ALU.add))
                bank_used(b, cV, t_acc)
        war["split"][1] = cP.n
        if hp + 1 < HP:
            pend_q = w_dma(0, wq[hp + 1])
            pend_k = w_dma(1, wk[hp + 1])

    S.wait(gp, cV, cV.n)
    S.wait(gp, dO, dO.n)
    gp.dma_start(out=r2h, in_=xbuf[:, :, :]).then_inc(dO.h, 16)
    dO.n += 16
    tok_r2w = dO.n

    if cfg.get("stop") == "own":
        return probe_exit()

    barrier()
    for eng in ALL_ENG:
        S.wait(eng, dO, tok_r2w)
    attn_es.close()

    r2t = sbuf("r2t", [128, FTC, D])
    hst = sbuf("hst", [128, FT])
    hh = [sbuf(f"hh{i}", [128, FT], BF16) for i in range(2)]
    hl = [sbuf(f"hl{i}", [128, FT], BF16) for i in range(2)]
    w2stg = [sbuf(f"w2stg{i}", [128, D]) for i in range(2)]
    w2h = [sbuf(f"w2h{i}", [128, D], BF16) for i in range(2)]
    w2l = [sbuf(f"w2l{i}", [128, D], BF16) for i in range(2)]
    lg = sbuf("lg", [128, 8])
    lgm1 = sbuf("lgm1", [128, 8])
    lgk = sbuf("lgk", [128, 8])
    lgm2 = sbuf("lgm2", [128, 8])
    lgsc = sbuf("lgsc", [128, 8])
    combine = sbuf("combine", [128, TC, 8])
    comb_pad = sbuf("comb_pad", [128, 128])
    cth = sbuf("cth", [8, TC, 128], BF16)
    ctl = sbuf("ctl", [8, TC, 128], BF16)
    ct_f = sbuf("ct_f", [8, 128])
    slt_s = sbuf("slt_s", [128, 128], BF16)
    onesb_s = sbuf("onesb_s", [128, 128], BF16)
    offs_s = sbuf("offs_s", [128, 8])
    capd_s = sbuf("capd_s", [128, 1])
    iota0_s = sbuf("iota0_s", [128, TC])
    iop_s = sbuf("iop_s", [128, 1])
    maskp = sbuf("maskp", [128, 128])
    masks_bf = sbuf("masks_bf", [128, 128], BF16)
    base_sb = sbuf("base_sb", [128, 8])
    posx = sbuf("posx", [128, 8])
    post = sbuf("post", [128, 8])
    posf = sbuf("posf", [128, 2])
    posu = sbuf("posu", [128, 2], U32)
    xg = [sbuf(f"xg{i}", [128, CAPS, XCW]) for i in range(2)]
    xgh = sbuf("xgh", [128, KC, FT], BF16)
    xgl = sbuf("xgl", [128, KC, FT], BF16)
    ysb = sbuf("ysb", [128, FT // 128, D])
    idxu = [sbuf(f"idxu{i}", [128, CAPS], U32) for i in range(2)]
    dSc = Ctr(nc, "dSc")
    sdma(slt_s[:, :], slt_in)
    sdma(onesb_s[:, :], onesb_in)
    sdma(offs_s[:, :], offs_in)
    sdma(capd_s[:, :], capd_in)
    sdma(iota0_s[:, :], iota0_in)
    tok_iot = sdma(iop_s[:, :], iop_in)
    vop(ve.memset(comb_pad[:, :], 0.0))
    vop(ve.memset(maskp[:, :], 0.0))
    vop(ve.memset(base_sb[:, :], 0.0))
    vop(ve.memset(xg[0][:, :, :], 0.0))
    vop(ve.memset(xg[1][:, :, :], 0.0))

    war2 = {"w2stage": [[(None, 0)], [(None, 0)]], "w2split": [0, 0], "h": [0, 0]}

    def w2_dma(par, src_ap):
        for ctr, tok in war2["w2stage"][par]:
            if ctr is not None:
                S.wait(gp, ctr, tok)
        S.wait(gp, dG[par], dG[par].n)
        gp.dma_start(out=w2stg[par][:, :], in_=src_ap).then_inc(dG[par].h, 16)
        dG[par].n += 16
        return dG[par].n

    def w2_split(par, tok_dma):
        S.wait(ae, dG[par], tok_dma)
        S.wait(ae, cP, war2["w2split"][par])
        t_hi = aop(ae.activation(w2h[par][:, :], w2stg[par][:, :], AF.Copy))
        S.wait(ve, dG[par], tok_dma)
        S.wait(ve, cP, war2["w2split"][par])
        S.wait(ve, cA, t_hi)
        t_lo = vop(ve.tensor_sub(w2l[par][:, :], w2stg[par][:, :], w2h[par][:, :]))
        war2["w2stage"][par] = [(cA, t_hi), (cV, t_lo)]
        return t_hi, t_lo

    S.wait(ve, dX, tok_iot)
    for s0 in range(XCR // 128):
        S.wait(ve, dX, dX.n)
        t_id = vop(ve.tensor_scalar_add(xg[0][:, 0, D + 8:D + 9], iop_s[:, :],
                                        float(T + s0 * 128)))
        S.wait(sy, cV, t_id)
        sdma(xc[s0 * 128:(s0 + 1) * 128, :], xg[0][:, 0, :])
    S.wait(sy, cV, cV.n)
    for r0 in range(T, YAR, 128):
        n_r = min(128, YAR - r0)
        sdma(y_acc[r0:r0 + n_r, :], xg[1][0:n_r, 0, 0:D])
    tok_xc_init = dX.n

    NG = E * NCS * HC

    def g_src(g):
        e = g // (NCS * HC)
        m = g % HC
        return e, m

    tok_w1 = {}
    tok_w2 = {}
    for g0 in range(min(2, NG)):
        e0, m0 = g_src(g0)
        tok_w1[g0] = w_dma(g0 % 2, w1[e0, m0])
        tok_w2[g0] = w2_dma(g0 % 2, w2[e0, m0])

    last_out_dma = 0
    tok_r2t_consumed = [(None, 0)]
    for ft in range(NFT):
        for ctr, tok in tok_r2t_consumed:
            if ctr is not None:
                S.wait(sy, ctr, tok)
        S.wait(sy, dO, last_out_dma)
        tok_r2t = sdma(r2t[:, :, :], r2h[:, ft * FTC:(ft + 1) * FTC, :])
        S.wait(ae, dX, tok_r2t)
        S.wait(ve, dX, tok_r2t)
        for jj in range(FTC):
            jabs = ft * FTC + jj
            col0 = jabs * 128
            t_ap = emit_ln_chunk(r2t[:, jj, :], col0, (4, 5))
            first = True
            i = None
            for k in range(KC):
                for p in range(3):
                    lhsT = (lnh if p != 2 else lnl)[:, k, col0:col0 + 128]
                    rhs = (wr_h if p != 1 else wr_l)[:, k, :]
                    if first:
                        S.wait(pe, cV, cV.n)
                        S.wait(pe, cA, cA.n)
                        bank_wait(6)
                    i = pe.matmul(pb[6][:, 0:8], lhsT, rhs, start=first,
                                  stop=(k == KC - 1 and p == 2))
                    first = False
            t_lgmm = cP.tick(i)
            S.wait(ve, cP, t_lgmm)
            t_lg = vop(ve.tensor_copy(lg[:, :], pb[6][:, 0:8]))
            bank_used(6, cV, t_lg)
            vop(ve.reduce_max(lgsc[:, 0:1], lg[:, :], axis=AX.X))
            vop(ve.tensor_scalar(lgm1[:, :], lg[:, :], lgsc[:, 0:1], None, ALU.is_equal))
            vop(ve.scalar_tensor_tensor(lgk[:, :], lgm1[:, :], -1e30, lg[:, :],
                                        ALU.mult, ALU.add))
            vop(ve.reduce_max(lgsc[:, 1:2], lgk[:, :], axis=AX.X))
            vop(ve.tensor_scalar(lgm2[:, :], lgk[:, :], lgsc[:, 1:2], None, ALU.is_equal))
            t_d = vop(ve.tensor_sub(lgsc[:, 2:3], lgsc[:, 1:2], lgsc[:, 0:1]))
            S.wait(ae, cV, t_d)
            t_e = aop(ae.activation(lgsc[:, 3:4], lgsc[:, 2:3], AF.Exp))
            S.wait(ve, cA, t_e)
            vop(ve.tensor_scalar_add(lgsc[:, 4:5], lgsc[:, 3:4], 1.0))
            vop(ve.reciprocal(lgsc[:, 5:6], lgsc[:, 4:5]))
            vop(ve.tensor_mul(lgsc[:, 6:7], lgsc[:, 3:4], lgsc[:, 5:6]))
            vop(ve.tensor_scalar(combine[:, jabs, :], lgm1[:, :], lgsc[:, 5:6], None, ALU.mult))
            t_c2 = vop(ve.scalar_tensor_tensor(combine[:, jabs, :], lgm2[:, :], lgsc[:, 6:7],
                                               combine[:, jabs, :], ALU.mult, ALU.add))
            t_cp = vop(ve.tensor_copy(comb_pad[:, 0:8], combine[:, jabs, :]))
            S.wait(pe, cV, t_cp)
            bank_wait(7)
            t_ct = cP.tick(pe.transpose(pb[7][:, 0:128], comb_pad[:, :], id_s[:, :]))
            S.wait(ve, cP, t_ct)
            t_ctf = vop(ve.tensor_copy(ct_f[:, :], pb[7][0:8, 0:128]))
            bank_used(7, cV, t_ctf)
            t_cth = vop(ve.tensor_copy(cth[:, jabs, :], ct_f[:, :]))
            vop(ve.tensor_sub(ctl[:, jabs, :], ct_f[:, :], cth[:, jabs, :]))
            S.wait(ve, cP, war["mbf_pe"])
            vop(ve.tensor_add(maskp[:, 0:8], lgm1[:, :], lgm2[:, :]))
            t_mb = vop(ve.tensor_copy(masks_bf[:, :], maskp[:, :]))
            S.wait(pe, cV, t_mb)
            bank_wait(3)
            t_pf = cP.tick(pe.matmul(pb[3][:, 0:128], slt_s[:, :], masks_bf[:, :],
                                     start=True, stop=True))
            S.wait(ve, cP, t_pf)
            t_px = vop(ve.tensor_add(posx[:, :], base_sb[:, :], pb[3][:, 0:8]))
            bank_used(3, cV, t_px)
            vop(ve.tensor_scalar(posx[:, :], posx[:, :], capd_s[:, 0:1], None, ALU.min))
            vop(ve.tensor_add(posx[:, :], posx[:, :], offs_s[:, :]))
            vop(ve.tensor_mul(post[:, :], lgm1[:, :], posx[:, :]))
            vop(ve.reduce_sum(posf[:, 0:1], post[:, :], axis=AX.X))
            vop(ve.tensor_mul(post[:, :], lgm2[:, :], posx[:, :]))
            vop(ve.reduce_sum(posf[:, 1:2], post[:, :], axis=AX.X))
            S.wait(ve, dSc, dSc.n)
            t_pu = vop(ve.tensor_copy(posu[:, :], posf[:, :]))
            bank_wait(7)
            t_tt = cP.tick(pe.matmul(pb[7][:, 0:128], onesb_s[:, :], masks_bf[:, :],
                                     start=True, stop=True))
            war["mbf_pe"] = t_tt
            S.wait(ve, cP, t_tt)
            t_bu = vop(ve.tensor_add(base_sb[:, :], base_sb[:, :], pb[7][:, 0:8]))
            bank_used(7, cV, t_bu)
            t_g8 = vop(ve.tensor_copy(lnx[:, D:D + 8], combine[:, jabs, :]))
            t_gi = vop(ve.tensor_copy(lnx[:, D + 8:D + 9], iota0_s[:, jabs:jabs + 1]))
            S.wait(gp, cV, cV.n)
            S.wait(gp, dX, tok_xc_init)
            for sl in range(2):
                S.wait(gp, dSc, dSc.n)
                gp.indirect_dma_start(
                    xc, bass.IndirectOffsetOnAxis(ap=posu[:, sl:sl + 1], axis=0),
                    lnx[:, 0:XCW], None).then_inc(dSc.h, 16)
                dSc.n += 16
            war["lntm_dma"] = (dSc, dSc.n)
        S.wait(ve, dO, last_out_dma)
        for jj in range(FTC):
            jabs = ft * FTC + jj
            for fh in range(NFH):
                b = 6 + ((jj * NFH + fh) % 2)
                S.wait(pe, cV, cV.n)
                bank_wait(b)
                pe.matmul(pb[b][:, 0:W2W], cth[:, jabs, :], b2_h[:, fh * W2W:(fh + 1) * W2W],
                          start=True, stop=False)
                pe.matmul(pb[b][:, 0:W2W], cth[:, jabs, :], b2_l[:, fh * W2W:(fh + 1) * W2W],
                          start=False, stop=False)
                i3 = pe.matmul(pb[b][:, 0:W2W], ctl[:, jabs, :], b2_h[:, fh * W2W:(fh + 1) * W2W],
                               start=False, stop=True)
                t_b = cP.tick(i3)
                S.wait(ve, cP, t_b)
                t_ev = vop(ve.scalar_tensor_tensor(
                    r2t[:, jj, fh * W2W:(fh + 1) * W2W], pb[b][:, 0:W2W], 1.0,
                    r2t[:, jj, fh * W2W:(fh + 1) * W2W], ALU.mult, ALU.add))
                bank_used(b, cV, t_ev)
        S.wait(gp, cV, cV.n)
        S.wait(gp, dO, dO.n)
        gp.dma_start(out=y_acc[ft * FT:(ft + 1) * FT, :].rearrange("(c p) d -> p c d", p=128),
                     in_=r2t[:, :, :]).then_inc(dO.h, 16)
        dO.n += 16
        last_out_dma = dO.n
        tok_r2t_consumed = [(cA, cA.n), (cV, cV.n)]

    if cfg.get("ffn_mode") == "router":
        return probe_exit(src=r2t[:, :, :], dst=y_pc[:, 0:FTC, :], close_attn=False)

    tok_comp_done = dSc.n
    war3 = {"xg_pe": [0, 0], "xg_dve": [0, 0]}
    tok_xr = {}

    def load_region(e):
        pe_ = e % 2
        S.wait(sy, dSc, tok_comp_done)
        S.wait(sy, cP, war3["xg_pe"][pe_])
        S.wait(sy, cV, war3["xg_dve"][pe_])
        tok_xr[e] = sdma(xg[pe_][:, :, :],
                         xc[e * (CAP + 128):e * (CAP + 128) + CAP, :]
                         .rearrange("(q p) c -> p q c", p=128))

    load_region(0)
    g = 0
    for e in range(E):
        par_e = e % 2
        xge = xg[par_e]
        idxe = idxu[par_e]
        if e + 1 < E:
            load_region(e + 1)
        S.wait(ve, dX, tok_xr[e])
        S.wait(ve, dSc, dSc.n)
        t_idx = vop(ve.tensor_copy(idxe[:, :], xge[:, :, D + 8]))
        for csub in range(NCS):
            for sl in range(CSUB // 128):
                slot = csub * (CSUB // 128) + sl
                for k in range(KC):
                    b = 6 + (k % 2)
                    S.wait(pe, dX, tok_xr[e])
                    S.wait(pe, cV, cV.n)
                    bank_wait(b)
                    t_tr = cP.tick(pe.transpose(pb[b][:, 0:128],
                                                xge[:, slot, k * 128:(k + 1) * 128],
                                                id_s[:, :]))
                    S.wait(ae, cP, t_tr)
                    t_hi = aop(ae.activation(xgh[:, k, sl * 128:(sl + 1) * 128],
                                             pb[b][:, 0:128], AF.Copy))
                    S.wait(ve, cP, t_tr)
                    S.wait(ve, cA, t_hi)
                    t_lo = vop(ve.tensor_sub(xgl[:, k, sl * 128:(sl + 1) * 128],
                                             pb[b][:, 0:128],
                                             xgh[:, k, sl * 128:(sl + 1) * 128]))
                    bank_used(b, cV, t_lo)
                    bank_used(b, cA, t_hi)
            war3["xg_pe"][par_e] = cP.n
            t_split = cV.n
            y_stop = [0] * NBY
            pend_w2 = None
            for mstep in range(HC + 1):
                if mstep < HC:
                    m = mstep
                    par = g % 2
                    tok_hi, tok_lo = w_split(par, tok_w1[g])
                    bw1 = 4 + (m % 2)
                    S.wait(pe, cA, tok_hi)
                    S.wait(pe, cV, max(tok_lo, t_split))
                    bank_wait(bw1)
                    tok_mm = mm3(bw1, par, CSUB, 0, src_h=xgh, src_l=xgl)
                    war["split"][par] = tok_mm
                    tok2_hi, tok2_lo = w2_split(par, tok_w2[g])
                    if g + 2 < NG:
                        e2, m2 = g_src(g + 2)
                        tok_w1[g + 2] = w_dma(par, w1[e2, m2])
                        tok_w2[g + 2] = w2_dma(par, w2[e2, m2])
                    hpar = m % 2
                    S.wait(ae, cP, max(tok_mm, war2["h"][hpar]))
                    t_g = aop(ae.activation(hst[:, 0:CSUB], pb[bw1][:, 0:CSUB], AF.Gelu,
                                            bias=b1_s[:, e, m:m + 1]))
                    bank_used(bw1, cA, t_g)
                    S.wait(ve, cA, t_g)
                    S.wait(ve, cP, war2["h"][hpar])
                    t_hh = vop(ve.tensor_copy(hh[hpar][:, :], hst[:, 0:CSUB]))
                    t_hl = vop(ve.tensor_sub(hl[hpar][:, :], hst[:, 0:CSUB], hh[hpar][:, :]))
                    this_w2 = (m, par, hpar, t_hh, t_hl, tok2_hi, tok2_lo)
                    g += 1
                else:
                    this_w2 = None
                if pend_w2 is not None:
                    m, par, hpar, t_hh, t_hl, tok2_hi, tok2_lo = pend_w2
                    S.wait(pe, cA, tok2_hi)
                    S.wait(pe, cV, max(t_hh, t_hl, tok2_lo))
                    i = None
                    for j in range(CSUB // 128):
                        for fh in range(NFH):
                            by = j * NFH + fh
                            if m == 0:
                                bank_wait(by)
                            for p in range(3):
                                lhsT = (hh if p != 2 else hl)[hpar][:, j * 128:(j + 1) * 128]
                                rhs = (w2h if p != 1 else w2l)[par][:, fh * W2W:(fh + 1) * W2W]
                                i = pe.matmul(pb[by][:, 0:W2W], lhsT, rhs,
                                              start=(m == 0 and p == 0),
                                              stop=(m == HC - 1 and p == 2))
                            if m == HC - 1:
                                y_stop[by] = cP.tick(i)
                    if m != HC - 1:
                        t_end = cP.tick(i)
                        war2["w2split"][par] = t_end
                        war2["h"][hpar] = t_end
                    else:
                        war2["w2split"][par] = y_stop[-1]
                        war2["h"][hpar] = y_stop[-1]
                pend_w2 = this_w2
            for j in range(CSUB // 128):
                slot = csub * (CSUB // 128) + j
                for fh in range(NFH):
                    by = j * NFH + fh
                    S.wait(ve, cP, y_stop[by])
                    S.wait(ve, dSc, dSc.n)
                    t_ev = vop(ve.tensor_scalar(
                        ysb[:, j, fh * W2W:(fh + 1) * W2W], pb[by][:, 0:W2W],
                        xge[:, slot, D + e:D + e + 1], None, ALU.mult))
                    bank_used(by, cV, t_ev)
            for j in range(CSUB // 128):
                slot = csub * (CSUB // 128) + j
                S.wait(gp, cV, cV.n)
                S.wait(gp, dO, dO.n)
                S.wait(gp, dSc, dSc.n)
                gp.indirect_dma_start(
                    y_acc, bass.IndirectOffsetOnAxis(ap=idxe[:, slot:slot + 1], axis=0),
                    ysb[:, j, :], None,
                    compute_op=ALU.add).then_inc(dSc.h, 16)
                dSc.n += 16
        war3["xg_dve"][par_e] = cV.n

    for ft in range(NFT):
        S.wait(sy, dSc, dSc.n)
        S.wait(sy, dO, dO.n)
        S.wait(sy, cV, cV.n)
        tok_l = sdma(r2t[:, :, :],
                     y_acc[ft * FT:(ft + 1) * FT, :].rearrange("(c p) d -> p c d", p=128))
        S.wait(gp, dX, tok_l)
        S.wait(gp, dO, dO.n)
        gp.dma_start(out=y_pc[:, ft * FTC:(ft + 1) * FTC, :], in_=r2t[:, :, :]).then_inc(dO.h, 16)
        dO.n += 16

    S.wait(gp, dO, dO.n)
    S.wait(gp, dSc, dSc.n)
    es.close()
    return nc


def _stage_inputs(inputs, cfg, core):
    D, T = cfg["D"], cfg["T"]
    TC = T // 128
    x = np.ascontiguousarray(np.asarray(inputs["L0_attn_x"], dtype=np.float32))
    b, half = core // 2, core % 2
    own = x[b, half * T:(half + 1) * T]
    pre = x[b, 0:T] if half == 1 else own
    shuffle = lambda a: np.ascontiguousarray(a.reshape(TC, 128, D).transpose(1, 0, 2))
    return {
        "x_own": shuffle(own),
        "x_pre": shuffle(pre),
        "gate": np.full((128, 1), float(half), np.float32),
    }


def _stage_weights(inputs, cfg):
    D, HID, E = cfg["D"], cfg["HID"], cfg["E"]
    KC, HC = D // 128, HID // 128
    HP = (D // 64) // 2
    f32 = lambda a: np.asarray(a, dtype=np.float32)
    Wq, Wk, Wv, Wo = (f32(inputs[f"L0_attn_W{n}"]) for n in "qkvo")
    W1, W2 = f32(inputs["L0_ffn_W1"]), f32(inputs["L0_ffn_W2"])
    b1, b2, Wr = f32(inputs["L0_ffn_b1"]), f32(inputs["L0_ffn_b2"]), f32(inputs["L0_ffn_Wr"])

    def lhsT_lay(W):
        a = W.reshape(HP, 128, KC, 128).transpose(0, 3, 2, 1)
        return np.ascontiguousarray(a.reshape(HP, 128, KC * 128))

    import ml_dtypes
    T, CAP, E_ = cfg["T"], cfg["CAP"], cfg["E"]
    TC = T // 128
    bfl = np.dtype(ml_dtypes.bfloat16)
    return {
        "slt_in": np.tril(np.ones((128, 128), np.float32), -1).T.astype(bfl),
        "onesb_in": np.ones((128, 128), bfl),
        "offs_in": np.tile((np.arange(8) * (CAP + 128)).astype(np.float32), (128, 1)),
        "capd_in": (CAP + np.arange(128, dtype=np.float32))[:, None].copy(),
        "iota0_in": np.ascontiguousarray(
            (np.arange(TC)[None, :] * 128 + np.arange(128)[:, None]).astype(np.float32)),
        "iop_in": np.arange(128, dtype=np.float32)[:, None].copy(),
        "wq": lhsT_lay(Wq), "wk": lhsT_lay(Wk), "wv": lhsT_lay(Wv),
        "wo": np.ascontiguousarray(Wo.T.reshape(HP, 128, D)),
        "w1": np.ascontiguousarray(
            W1.reshape(E, KC, 128, HC, 128).transpose(0, 3, 2, 1, 4).reshape(E, HC, 128, KC * 128)),
        "w2": np.ascontiguousarray(W2.reshape(E, HC, 128, D)),
        "b1": np.ascontiguousarray(b1.reshape(E, HC, 128).transpose(2, 0, 1)),
        "b2": np.ascontiguousarray(b2),
        "wr": np.ascontiguousarray(Wr.reshape(KC, 128, 8).transpose(1, 0, 2)),
        "um": np.tile(np.triu(np.ones((128, 128), np.float32)), (1, 2)),
        "ident": np.eye(128, dtype=np.float32),
    }


def kernel(**inputs):
    cfg = FULL
    n_cores = 8
    weights = _stage_weights(inputs, cfg)
    in_maps = []
    for core in range(n_cores):
        m = dict(weights)
        m.update(_stage_inputs(inputs, cfg, core))
        in_maps.append(m)
    nc = build_program(cfg)
    res = run_bass_kernel_spmd(nc, in_maps, core_ids=list(range(n_cores)))
    T, D = cfg["T"], cfg["D"]
    out = np.empty((4, 4096, D), np.float32)
    for core in range(n_cores):
        b, half = core // 2, core % 2
        out[b, half * T:(half + 1) * T] = np.asarray(res.results[core]["y"])
    return out


# version 20
# speedup vs baseline: 1.1840x; 1.0146x over previous
import contextlib
import numpy as np
import concourse.bass as bass
import concourse.mybir as mybir
from concourse.bass_utils import run_bass_kernel_spmd

from concourse import library_config

F32 = mybir.dt.float32
BF16 = mybir.dt.bfloat16
I16 = mybir.dt.int16
U32 = mybir.dt.uint32
AF = mybir.ActivationFunctionType
ALU = mybir.AluOpType
AX = mybir.AxisListType

FULL = dict(D=1024, T=2048, HID=4096, E=8, FT=256, CAP=768)
MINI = dict(D=256, T=256, HID=512, E=8, FT=256, CAP=256)

LN_EPS = 1e-5
ATTN_EPS = 1e-8


class Ctr:
    def __init__(self, nc, name):
        self.h = nc.alloc_semaphore(name=name)
        self.n = 0

    def tick(self, instr, n=1):
        instr.then_inc(self.h, n)
        self.n += n
        return self.n


class Syncer:
    def __init__(self):
        self.wm = {}

    def wait(self, eng, ctr, val):
        if not val or val <= 0:
            return
        k = (id(eng), id(ctr))
        if self.wm.get(k, 0) >= val:
            return
        eng.wait_ge(ctr.h, val)
        self.wm[k] = val


def build_program(cfg):
    D, T, HID, E, FT = (cfg[k] for k in ("D", "T", "HID", "E", "FT"))
    CAP = cfg["CAP"]
    CSUB = FT
    NCS = CAP // CSUB
    CAPS = CAP // 128
    KC = D // 128
    NH = D // 64
    HP = NH // 2
    TC = T // 128
    HC = HID // 128
    FTC = FT // 128
    NFT = T // FT
    PT = min(512, T)
    W2W = min(512, D)
    NFH = D // W2W
    NBY = FTC * NFH
    assert NBY <= 4

    nc = bass.Bass("TRN2", target_bir_lowering=False, debug=False)
    S = Syncer()

    x_own = nc.dram_tensor("x_own", [128, TC, D], F32, kind="ExternalInput").ap()
    x_pre = nc.dram_tensor("x_pre", [128, TC, D], F32, kind="ExternalInput").ap()
    gate = nc.dram_tensor("gate", [128, 1], F32, kind="ExternalInput").ap()
    wq = nc.dram_tensor("wq", [HP, 128, KC * 128], F32, kind="ExternalInput").ap()
    wk = nc.dram_tensor("wk", [HP, 128, KC * 128], F32, kind="ExternalInput").ap()
    wv = nc.dram_tensor("wv", [HP, 128, KC * 128], F32, kind="ExternalInput").ap()
    wo = nc.dram_tensor("wo", [HP, 128, D], F32, kind="ExternalInput").ap()
    w1 = nc.dram_tensor("w1", [E, HC, 128, KC * 128], F32, kind="ExternalInput").ap()
    w2 = nc.dram_tensor("w2", [E, HC, 128, D], F32, kind="ExternalInput").ap()
    b1 = nc.dram_tensor("b1", [128, E, HC], F32, kind="ExternalInput").ap()
    b2 = nc.dram_tensor("b2", [8, D], F32, kind="ExternalInput").ap()
    wr = nc.dram_tensor("wr", [128, KC, 8], F32, kind="ExternalInput").ap()
    um = nc.dram_tensor("um", [128, 256], F32, kind="ExternalInput").ap()
    ident = nc.dram_tensor("ident", [128, 128], F32, kind="ExternalInput").ap()
    slt_in = nc.dram_tensor("slt_in", [128, 128], BF16, kind="ExternalInput").ap()
    onesb_in = nc.dram_tensor("onesb_in", [128, 128], BF16, kind="ExternalInput").ap()
    offs_in = nc.dram_tensor("offs_in", [128, 8], F32, kind="ExternalInput").ap()
    capd_in = nc.dram_tensor("capd_in", [128, 1], F32, kind="ExternalInput").ap()
    iota0_in = nc.dram_tensor("iota0_in", [128, TC], F32, kind="ExternalInput").ap()
    iop_in = nc.dram_tensor("iop_in", [128, 1], F32, kind="ExternalInput").ap()
    y = nc.dram_tensor("y", [T, D], F32, kind="ExternalOutput").ap()
    y_pc = y.rearrange("(c p) d -> p c d", p=128)
    r2h = nc.dram_tensor("r2h", [128, TC, D], F32, kind="Internal").ap()
    XCR = E * (CAP + 128)
    XCW = D + 64
    YAR = T + XCR + 256
    xc = nc.dram_tensor("xc", [XCR, XCW], F32, kind="Internal").ap()
    y_acc = nc.dram_tensor("y_acc", [YAR, D], F32, kind="Internal").ap()

    es = contextlib.ExitStack()
    sbuf = lambda name, shape, dt=F32: es.enter_context(nc.sbuf_tensor(name, shape, dt))

    cP = Ctr(nc, "cP")
    cV = Ctr(nc, "cV")
    cA = Ctr(nc, "cA")
    dX = Ctr(nc, "dX")
    dW = [Ctr(nc, "dW0"), Ctr(nc, "dW1")]
    dG = [Ctr(nc, "dG0"), Ctr(nc, "dG1")]
    dO = Ctr(nc, "dO")

    pe, ve, ae, sy, gp = nc.tensor, nc.vector, nc.scalar, nc.sync, nc.gpsimd
    ALL_ENG = (pe, ve, ae, sy, gp)

    def vop(instr):
        t = cV.tick(instr)
        S.wait(ve, cV, t)
        return t

    def aop(instr):
        t = cA.tick(instr)
        S.wait(ae, cA, t)
        return t

    def sdma(out, in_):
        S.wait(sy, dX, dX.n)
        sy.dma_start(out=out, in_=in_).then_inc(dX.h, 16)
        dX.n += 16
        return dX.n

    def barrier():
        for ctr in (cP, cV, cA):
            for eng in ALL_ENG:
                S.wait(eng, ctr, ctr.n)

    um_s = sbuf("um_s", [128, 256])
    id_s = sbuf("id_s", [128, 128])
    gate_s = sbuf("gate_s", [128, 1])
    wr_f = sbuf("wr_f", [128, KC, 8])
    wr_h = sbuf("wr_h", [128, KC, 8], BF16)
    wr_l = sbuf("wr_l", [128, KC, 8], BF16)
    b1_s = sbuf("b1_s", [128, E, HC])
    b2_h = sbuf("b2_h", [8, D], BF16)
    b2_l = sbuf("b2_l", [8, D], BF16)
    lnx = sbuf("lnx", [128, max(D + 64, max(D, 2 * min(512, T)) + min(512, T))])
    lntm = lnx[:, 0:D]
    lnsc = sbuf("lnsc", [128, 16])
    lnh = sbuf("lnh", [128, KC, T], BF16)
    lnl = sbuf("lnl", [128, KC, T], BF16)
    wstg = [sbuf(f"wstg{i}", [128, KC * 128]) for i in range(2)]
    wsh = [sbuf(f"wsh{i}", [128, KC * 128], BF16) for i in range(2)]
    wsl = [sbuf(f"wsl{i}", [128, KC * 128], BF16) for i in range(2)]

    pb = [es.enter_context(nc.psum_tensor(f"pb{i}", [128, 512], F32)) for i in range(8)]
    bank_free = {i: [] for i in range(8)}

    def bank_wait(b):
        for ctr, tok in bank_free[b]:
            S.wait(pe, ctr, tok)
        bank_free[b] = []

    def bank_used(b, ctr, tok):
        bank_free[b].append((ctr, tok))

    sdma(um_s[:, :], um)
    sdma(id_s[:, :], ident)
    sdma(gate_s[:, :], gate)
    sdma(wr_f[:, :, :], wr)
    sdma(b1_s[:, :, :], b1)
    tok_consts = sdma(lnx[0:8, 0:D], b2)
    S.wait(ae, dX, tok_consts)
    t_wrh = aop(ae.activation(wr_h[:, :, :], wr_f[:, :, :], AF.Copy))
    t_b2h = aop(ae.activation(b2_h[:, :], lnx[0:8, 0:D], AF.Copy))
    S.wait(ve, dX, tok_consts)
    S.wait(ve, cA, t_b2h)
    t_wrl = vop(ve.tensor_sub(wr_l[:, :, :], wr_f[:, :, :], wr_h[:, :, :]))
    t_b2l = vop(ve.tensor_sub(b2_l[:, :], lnx[0:8, 0:D], b2_h[:, :]))

    attn_es = contextlib.ExitStack()
    asb = lambda name, shape, dt=F32: attn_es.enter_context(nc.sbuf_tensor(name, shape, dt))
    xbuf = asb("xbuf", [128, TC, D])
    qb = asb("qb", [128, T])
    kb = asb("kb", [128, T])
    vb = asb("vb", [128, T])
    s_run = asb("s_run", [128, NH, 128])
    ktm = [asb(f"ktm{i}", [128, 128]) for i in range(2)]
    vaug = [asb(f"vaug{i}", [128, 256]) for i in range(2)]
    qpad = [[asb(f"qpad{i}_{j}", [128, 128]) for i in range(2)] for j in range(2)]
    kpad = [[asb(f"kpad{i}_{j}", [128, 128]) for i in range(2)] for j in range(2)]
    am2_sb = [asb(f"am2_{j}", [128, 256]) for j in range(2)]
    attn_tm = asb("attn_tm", [128, TC, 128])
    ah_sb = asb("ah_sb", [128, 128], BF16)
    al_sb = asb("al_sb", [128, 128], BF16)
    esc1 = lnx[:, 0:PT]
    esc2 = lnx[:, PT:2 * PT]
    esc3 = lnx[:, 2 * PT:3 * PT] if 3 * PT <= lnx.shape[1] else lnx[:, lnx.shape[1] - PT:]
    numsb = asb("numsb", [128, 256])
    densb = asb("densb", [128, 4])

    vop(ve.memset(s_run[:, :, :], 0.0))
    for h in range(2):
        for j in range(2):
            vop(ve.memset(qpad[h][j][:, :], 0.0))
            vop(ve.memset(kpad[h][j][:, :], 0.0))
    for i in range(2):
        vop(ve.memset(vaug[i][:, :], 0.0))
        vop(ve.memset(vaug[i][:, 64:65], 1.0))
        vop(ve.memset(vaug[i][:, 192:193], 1.0))

    def probe_exit(src=None, dst=None, close_attn=True):
        S.wait(gp, cV, cV.n)
        S.wait(gp, cA, cA.n)
        for ctr in (cP,):
            S.wait(gp, ctr, ctr.n)
        S.wait(gp, dO, dO.n)
        gp.dma_start(out=(y_pc if dst is None else dst),
                     in_=(xbuf[:, :, :] if src is None else src)).then_inc(dO.h, 16)
        dO.n += 16
        S.wait(gp, dO, dO.n)
        if close_attn:
            attn_es.close()
        es.close()
        return nc

    war = {
        "lntm_pe": 0,
        "lntm_dma": None,
        "combpad_dma": None,
        "mbf_pe": 0,
        "stage": [[(None, 0)], [(None, 0)]],
        "split": [0, 0],
        "vaug": [0, 0],
        "ktm": [0, 0],
        "pad": [0, 0],
        "am2": [0, 0],
        "sadd": 0,
    }

    def emit_ln_chunk(src_ap, col0, tr_banks):
        S.wait(ae, cP, war["lntm_pe"])
        if war["lntm_dma"]:
            S.wait(ae, *war["lntm_dma"])
        t1 = aop(ae.activation(lntm, src_ap, AF.Square, accum_out=lnsc[:, 0:1]))
        t2 = aop(ae.activation(lntm, src_ap, AF.Identity, accum_out=lnsc[:, 1:2]))
        S.wait(ve, cA, t2)
        vop(ve.tensor_scalar_mul(lnsc[:, 2:3], lnsc[:, 1:2], 1.0 / D))
        vop(ve.tensor_mul(lnsc[:, 3:4], lnsc[:, 2:3], lnsc[:, 2:3]))
        vop(ve.scalar_tensor_tensor(lnsc[:, 4:5], lnsc[:, 0:1], 1.0 / D, lnsc[:, 3:4],
                                    ALU.mult, ALU.subtract))
        tv = vop(ve.tensor_scalar_add(lnsc[:, 5:6], lnsc[:, 4:5], LN_EPS))
        S.wait(ae, cV, tv)
        ts = aop(ae.activation(lnsc[:, 6:7], lnsc[:, 5:6], AF.Sqrt))
        S.wait(ve, cA, ts)
        vop(ve.reciprocal(lnsc[:, 7:8], lnsc[:, 6:7]))
        ycol = 7
        for it in range(2):
            a, b_, c_, yn = 8, 9, 10, 11 + it
            vop(ve.tensor_mul(lnsc[:, a:a + 1], lnsc[:, ycol:ycol + 1], lnsc[:, ycol:ycol + 1]))
            vop(ve.tensor_mul(lnsc[:, b_:b_ + 1], lnsc[:, 5:6], lnsc[:, a:a + 1]))
            vop(ve.tensor_scalar(lnsc[:, c_:c_ + 1], lnsc[:, b_:b_ + 1], -0.5, 1.5,
                                 ALU.mult, ALU.add))
            vop(ve.tensor_mul(lnsc[:, yn:yn + 1], lnsc[:, ycol:ycol + 1], lnsc[:, c_:c_ + 1]))
            ycol = yn
        S.wait(ve, cA, t2)
        S.wait(ve, cP, war["lntm_pe"])
        if war["lntm_dma"]:
            S.wait(ve, *war["lntm_dma"])
        t_ap = vop(ve.tensor_scalar(lntm, src_ap, lnsc[:, 2:3], lnsc[:, ycol:ycol + 1],
                                    ALU.subtract, ALU.mult))
        for k in range(KC):
            b = tr_banks[k % 2]
            S.wait(pe, cV, t_ap)
            bank_wait(b)
            t_tr = cP.tick(pe.transpose(pb[b][:, 0:128], lntm[:, k * 128:(k + 1) * 128], id_s[:, :]))
            S.wait(ae, cP, t_tr)
            t_hi = aop(ae.activation(lnh[:, k, col0:col0 + 128], pb[b][:, 0:128], AF.Copy))
            S.wait(ve, cP, t_tr)
            S.wait(ve, cA, t_hi)
            t_lo = vop(ve.tensor_sub(lnl[:, k, col0:col0 + 128], pb[b][:, 0:128],
                                     lnh[:, k, col0:col0 + 128]))
            bank_used(b, cV, t_lo)
            bank_used(b, cA, t_hi)
        war["lntm_pe"] = cP.n
        return t_ap

    def w_dma(par, src_ap):
        eng = (sy, ae)[par]
        for ctr, tok in war["stage"][par]:
            if ctr is not None:
                S.wait(eng, ctr, tok)
        S.wait(eng, dW[par], dW[par].n)
        eng.dma_start(out=wstg[par][:, :], in_=src_ap).then_inc(dW[par].h, 16)
        dW[par].n += 16
        return dW[par].n

    def w_split(par, tok_dma):
        S.wait(ae, dW[par], tok_dma)
        S.wait(ae, cP, war["split"][par])
        t_hi = aop(ae.activation(wsh[par][:, :], wstg[par][:, :], AF.Copy))
        S.wait(ve, dW[par], tok_dma)
        S.wait(ve, cP, war["split"][par])
        S.wait(ve, cA, t_hi)
        t_lo = vop(ve.tensor_sub(wsl[par][:, :], wstg[par][:, :], wsh[par][:, :]))
        war["stage"][par] = [(cA, t_hi), (cV, t_lo)]
        return t_hi, t_lo

    def mm3(bank, wpar, n, col, src_h=None, src_l=None):
        if src_h is None:
            src_h, src_l = lnh, lnl
        tok = None
        first = True
        for k in range(KC):
            for p in range(3):
                lhsT = (wsh if p != 2 else wsl)[wpar][:, k * 128:(k + 1) * 128]
                rhs = (src_h if p != 1 else src_l)[:, k, col:col + n]
                last = (k == KC - 1 and p == 2)
                i = pe.matmul(pb[bank][:, 0:n], lhsT, rhs, start=first, stop=last)
                first = False
                if last:
                    tok = cP.tick(i)
        return tok

    def emit_elu_evac(ps_ap, dst_ap, n):
        t_r = vop(ve.tensor_relu(esc1[:, 0:n], ps_ap))
        t_m = vop(ve.tensor_single_scalar(esc2[:, 0:n], ps_ap, 0.0, ALU.min))
        S.wait(ae, cV, t_m)
        t_e = aop(ae.activation(esc3[:, 0:n], esc2[:, 0:n], AF.Exp))
        S.wait(ve, cA, t_e)
        return vop(ve.tensor_add(dst_ap, esc1[:, 0:n], esc3[:, 0:n]))

    def emit_proj(dst, elu, wpar, tok_hi, tok_lo, banks):
        t_ev = 0
        for i0, t0 in enumerate(range(0, T, PT)):
            b = banks[i0 % len(banks)]
            S.wait(pe, cA, tok_hi)
            S.wait(pe, cV, tok_lo)
            bank_wait(b)
            tok_mm = mm3(b, wpar, PT, t0)
            S.wait(ve, cP, tok_mm)
            if elu:
                t_ev = emit_elu_evac(pb[b][:, 0:PT], dst[:, t0:t0 + PT], PT)
            else:
                t_ev = vop(ve.tensor_copy(dst[:, t0:t0 + PT], pb[b][:, 0:PT]))
            bank_used(b, cV, t_ev)
        war["split"][wpar] = cP.n
        return t_ev

    def mix_state_chunk(c, hp, par, tok_k, tok_v, update_state=True):
        bt, bs = 6 + (c % 2), 5
        S.wait(pe, cV, max(tok_k, tok_v))
        bank_wait(bt)
        t_trk = cP.tick(pe.transpose(pb[bt][:, 0:128], kb[:, c * 128:(c + 1) * 128], id_s[:, :]))
        t_trv = cP.tick(pe.transpose(pb[bt][:, 128:256], vb[:, c * 128:(c + 1) * 128], id_s[:, :]))
        S.wait(ae, cP, max(t_trv, war["ktm"][par], war["vaug"][par]))
        t_ktm = aop(ae.activation(ktm[par][:, :], pb[bt][:, 0:128], AF.Copy))
        aop(ae.activation(vaug[par][:, 0:64], pb[bt][:, 128:192], AF.Copy))
        t_v1 = aop(ae.activation(vaug[par][:, 128:192], pb[bt][:, 192:256], AF.Copy))
        bank_used(bt, cA, t_v1)
        if update_state:
            S.wait(pe, cA, t_v1)
            bank_wait(bs)
            cP.tick(pe.matmul(pb[bs][:, 0:128], ktm[par][:, :], vaug[par][:, 0:128],
                              start=True, stop=True))
            t_s = cP.tick(pe.matmul(pb[bs][:, 128:256], ktm[par][:, :], vaug[par][:, 128:256],
                                    start=True, stop=True))
            S.wait(ve, cP, t_s)
            gh0, gh1 = 2 * hp, 2 * hp + 1
            vop(ve.tensor_add(s_run[0:64, gh0, 0:128], s_run[0:64, gh0, 0:128],
                              pb[bs][0:64, 0:128]))
            t_up = vop(ve.tensor_add(s_run[64:128, gh1, 0:128], s_run[64:128, gh1, 0:128],
                                     pb[bs][64:128, 128:256]))
            bank_used(bs, cV, t_up)
        war["ktm"][par] = cP.n
        war["vaug"][par] = cP.n
        return t_ktm, t_v1

    if cfg.get("stop") == "ph0":
        return probe_exit()
    tok_xpre = sdma(xbuf[:, :, :], x_pre)
    S.wait(ae, dX, tok_xpre)
    S.wait(ve, dX, tok_xpre)
    S.wait(ae, cV, t_b2l)
    for j in range(TC):
        emit_ln_chunk(xbuf[:, j, :], j * 128, (0, 1))

    pend_k = w_dma(0, wk[0])
    pend_v = w_dma(1, wv[0])
    for hp in range(HP):
        tok_hi, tok_lo = w_split(0, pend_k)
        tok_k = emit_proj(kb, True, 0, tok_hi, tok_lo, (2, 3, 0))
        tok_hi, tok_lo = w_split(1, pend_v)
        tok_v = emit_proj(vb, False, 1, tok_hi, tok_lo, (2, 3, 0))
        if hp + 1 < HP:
            pend_k = w_dma(0, wk[hp + 1])
            pend_v = w_dma(1, wv[hp + 1])
        pend_st = None
        for cstep in range(TC + 1):
            if cstep < TC:
                _, t_v1p = mix_state_chunk(cstep, hp, cstep % 2, tok_k, tok_v,
                                           update_state=False)
                this_st = (cstep % 2, t_v1p)
            else:
                this_st = None
            if pend_st is not None:
                parp, t_v1p = pend_st
                bs = 5
                S.wait(pe, cA, t_v1p)
                bank_wait(bs)
                cP.tick(pe.matmul(pb[bs][:, 0:128], ktm[parp][:, :],
                                  vaug[parp][:, 0:128], start=True, stop=True))
                t_s = cP.tick(pe.matmul(pb[bs][:, 128:256], ktm[parp][:, :],
                                        vaug[parp][:, 128:256], start=True, stop=True))
                war["ktm"][parp] = max(war["ktm"][parp], t_s)
                war["vaug"][parp] = max(war["vaug"][parp], t_s)
                gh0, gh1 = 2 * hp, 2 * hp + 1
                S.wait(ve, cP, t_s)
                vop(ve.tensor_add(s_run[0:64, gh0, 0:128], s_run[0:64, gh0, 0:128],
                                  pb[bs][0:64, 0:128]))
                t_up = vop(ve.tensor_add(s_run[64:128, gh1, 0:128],
                                         s_run[64:128, gh1, 0:128],
                                         pb[bs][64:128, 128:256]))
                bank_used(bs, cV, t_up)
            pend_st = this_st

    vop(ve.tensor_scalar_mul(s_run[:, :, :], s_run[:, :, :], gate_s[:, 0:1]))
    if cfg.get("stop") == "prefix":
        return probe_exit()

    barrier()
    tok_xown = sdma(xbuf[:, :, :], x_own)
    S.wait(ae, dX, tok_xown)
    S.wait(ve, dX, tok_xown)
    for j in range(TC):
        emit_ln_chunk(xbuf[:, j, :], j * 128, (0, 1))

    pend_q = w_dma(0, wq[0])
    pend_k = w_dma(1, wk[0])
    t_outlast = 0
    for hp in range(HP):
        tok_hi, tok_lo = w_split(0, pend_q)
        tok_q = emit_proj(qb, True, 0, tok_hi, tok_lo, (2, 3, 0))
        tok_hi, tok_lo = w_split(1, pend_k)
        tok_k = emit_proj(kb, True, 1, tok_hi, tok_lo, (2, 3, 0))
        pend_v = w_dma(0, wv[hp])
        tok_hi, tok_lo = w_split(0, pend_v)
        tok_v = emit_proj(vb, False, 0, tok_hi, tok_lo, (2, 3, 0))
        pend_wo = w_dma(1, wo[hp])

        pend_mix = None
        for cstep in range(TC + 1):
            if cstep < TC:
                c = cstep
                cpar = c % 2
                par = c % 2
                t_ktm, t_v1 = mix_state_chunk(c, hp, par, tok_k, tok_v, update_state=False)
                S.wait(ae, cV, tok_q)
                S.wait(ae, cP, war["pad"][cpar])
                aop(ae.activation(qpad[0][cpar][0:64, :], qb[0:64, c * 128:(c + 1) * 128],
                                  AF.Copy))
                aop(ae.activation(kpad[0][cpar][0:64, :], kb[0:64, c * 128:(c + 1) * 128],
                                  AF.Copy))
                aop(ae.activation(qpad[1][cpar][64:128, :],
                                  qb[64:128, c * 128:(c + 1) * 128], AF.Copy))
                t_kp = aop(ae.activation(kpad[1][cpar][64:128, :],
                                         kb[64:128, c * 128:(c + 1) * 128], AF.Copy))
                this_mix = (c, cpar, par, t_kp, t_v1)
            else:
                this_mix = None
            if pend_mix is not None:
                c, cpar, par, t_kp, t_v1 = pend_mix
                b_at = 2 + cpar
                b_o = cpar
                b_s = 5
                gh0, gh1 = 2 * hp, 2 * hp + 1
                S.wait(pe, cA, t_kp)
                bank_wait(b_at)
                cP.tick(pe.matmul(pb[b_at][:, 0:128], kpad[0][cpar][:, :],
                                  qpad[0][cpar][:, :], start=True, stop=True))
                t_at = cP.tick(pe.matmul(pb[b_at][:, 128:256], kpad[1][cpar][:, :],
                                         qpad[1][cpar][:, :], start=True, stop=True))
                S.wait(ve, cP, max(t_at, war["am2"][cpar]))
                t_am = vop(ve.tensor_mul(am2_sb[cpar][:, :], pb[b_at][:, 0:256], um_s[:, :]))
                bank_used(b_at, cV, t_am)
                S.wait(pe, cV, t_am)
                S.wait(pe, cA, t_v1)
                bank_wait(b_o)
                cP.tick(pe.matmul(pb[b_o][:, 0:128], am2_sb[cpar][:, 0:128],
                                  vaug[par][:, 0:128], start=True, stop=True))
                t_o1 = cP.tick(pe.matmul(pb[b_o][:, 128:256], am2_sb[cpar][:, 128:256],
                                         vaug[par][:, 128:256], start=True, stop=True))
                S.wait(pe, cV, war["sadd"])
                cP.tick(pe.matmul(pb[b_o][:, 256:384], qpad[0][cpar][:, :],
                                  s_run[:, gh0, :], start=True, stop=True))
                t_o2 = cP.tick(pe.matmul(pb[b_o][:, 384:512], qpad[1][cpar][:, :],
                                         s_run[:, gh1, :], start=True, stop=True))
                war["pad"][cpar] = t_o2
                war["am2"][cpar] = t_o1
                war["vaug"][par] = max(war["vaug"][par], t_o1)
                bank_wait(b_s)
                cP.tick(pe.matmul(pb[b_s][:, 0:128], ktm[par][:, :], vaug[par][:, 0:128],
                                  start=True, stop=True))
                t_s = cP.tick(pe.matmul(pb[b_s][:, 128:256], ktm[par][:, :],
                                        vaug[par][:, 128:256], start=True, stop=True))
                war["ktm"][par] = max(war["ktm"][par], t_s)
                war["vaug"][par] = max(war["vaug"][par], t_s)
                S.wait(ve, cP, t_s)
                vop(ve.tensor_add(s_run[0:64, gh0, 0:128], s_run[0:64, gh0, 0:128],
                                  pb[b_s][0:64, 0:128]))
                t_up = vop(ve.tensor_add(s_run[64:128, gh1, 0:128],
                                         s_run[64:128, gh1, 0:128],
                                         pb[b_s][64:128, 128:256]))
                bank_used(b_s, cV, t_up)
                war["sadd"] = t_up
                S.wait(ve, cA, t_outlast)
                t_cp1 = vop(ve.tensor_copy(numsb[:, 0:256], pb[b_o][:, 0:256]))
                vop(ve.scalar_tensor_tensor(densb[:, 0:2], pb[b_o][:, 320:449:128],
                                            ATTN_EPS, numsb[:, 64:193:128],
                                            ALU.add, ALU.add))
                vop(ve.reciprocal(densb[:, 2:4], densb[:, 0:2]))
                t_n = vop(ve.tensor_add(numsb[:, 0:256], numsb[:, 0:256],
                                        pb[b_o][:, 256:512]))
                bank_used(b_o, cV, t_n)
                S.wait(ae, cV, t_n)
                aop(ae.activation(attn_tm[:, c, 0:64], numsb[:, 0:64], AF.Copy,
                                  scale=densb[:, 2:3]))
                t_outlast = aop(ae.activation(attn_tm[:, c, 64:128], numsb[:, 128:192], AF.Copy,
                                              scale=densb[:, 3:4]))
            pend_mix = this_mix
        tok_hi_o, tok_lo_o = w_split(1, pend_wo)
        t_wo_last = 0
        for j in range(TC):
            b_tr = 6
            S.wait(pe, cA, t_outlast)
            bank_wait(b_tr)
            t_tr = cP.tick(pe.transpose(pb[b_tr][:, 0:128], attn_tm[:, j, :], id_s[:, :]))
            S.wait(ae, cP, t_tr)
            t_ah = aop(ae.activation(ah_sb[:, :], pb[b_tr][:, 0:128], AF.Copy))
            S.wait(ve, cP, t_tr)
            S.wait(ve, cA, t_ah)
            t_al = vop(ve.tensor_sub(al_sb[:, :], pb[b_tr][:, 0:128], ah_sb[:, :]))
            bank_used(b_tr, cV, t_al)
            for fh in range(NFH):
                b = (0, 1)[fh % 2]
                S.wait(pe, cA, max(t_ah, tok_hi_o))
                S.wait(pe, cV, max(t_al, tok_lo_o))
                bank_wait(b)
                pe.matmul(pb[b][:, 0:W2W], ah_sb[:, :], wsh[1][:, fh * W2W:(fh + 1) * W2W],
                          start=True, stop=False)
                pe.matmul(pb[b][:, 0:W2W], ah_sb[:, :], wsl[1][:, fh * W2W:(fh + 1) * W2W],
                          start=False, stop=False)
                i3 = pe.matmul(pb[b][:, 0:W2W], al_sb[:, :], wsh[1][:, fh * W2W:(fh + 1) * W2W],
                               start=False, stop=True)
                t_wo = cP.tick(i3)
                t_wo_last = t_wo
                S.wait(ve, cP, t_wo)
                t_acc = vop(ve.scalar_tensor_tensor(xbuf[:, j, fh * W2W:(fh + 1) * W2W],
                                                    pb[b][:, 0:W2W], 1.0,
                                                    xbuf[:, j, fh * W2W:(fh + 1) * W2W],
                                                    ALU.mult, ALU.add))
                bank_used(b, cV, t_acc)
        war["split"][1] = cP.n
        if hp + 1 < HP:
            pend_q = w_dma(0, wq[hp + 1])
            pend_k = w_dma(1, wk[hp + 1])

    S.wait(gp, cV, cV.n)
    S.wait(gp, dO, dO.n)
    gp.dma_start(out=r2h, in_=xbuf[:, :, :]).then_inc(dO.h, 16)
    dO.n += 16
    tok_r2w = dO.n

    if cfg.get("stop") == "own":
        return probe_exit()

    barrier()
    for eng in ALL_ENG:
        S.wait(eng, dO, tok_r2w)
    attn_es.close()

    r2t = sbuf("r2t", [128, FTC, D])
    hst = sbuf("hst", [128, FT])
    hh = [sbuf(f"hh{i}", [128, FT], BF16) for i in range(2)]
    hl = [sbuf(f"hl{i}", [128, FT], BF16) for i in range(2)]
    w2stg = [sbuf(f"w2stg{i}", [128, D]) for i in range(2)]
    w2h = [sbuf(f"w2h{i}", [128, D], BF16) for i in range(2)]
    w2l = [sbuf(f"w2l{i}", [128, D], BF16) for i in range(2)]
    lg = sbuf("lg", [128, 8])
    lgm1 = sbuf("lgm1", [128, 8])
    lgk = sbuf("lgk", [128, 8])
    lgm2 = sbuf("lgm2", [128, 8])
    lgsc = sbuf("lgsc", [128, 8])
    combine = sbuf("combine", [128, TC, 8])
    comb_pad = sbuf("comb_pad", [128, 128])
    cth = sbuf("cth", [8, TC, 128], BF16)
    ctl = sbuf("ctl", [8, TC, 128], BF16)
    ct_f = sbuf("ct_f", [8, 128])
    slt_s = sbuf("slt_s", [128, 128], BF16)
    onesb_s = sbuf("onesb_s", [128, 128], BF16)
    offs_s = sbuf("offs_s", [128, 8])
    capd_s = sbuf("capd_s", [128, 1])
    iota0_s = sbuf("iota0_s", [128, TC])
    iop_s = sbuf("iop_s", [128, 1])
    maskp = sbuf("maskp", [128, 128])
    masks_bf = sbuf("masks_bf", [128, 128], BF16)
    base_sb = sbuf("base_sb", [128, 8])
    posx = sbuf("posx", [128, 8])
    post = sbuf("post", [128, 8])
    posf = sbuf("posf", [128, 2])
    posu = sbuf("posu", [128, 2], U32)
    xg = [sbuf(f"xg{i}", [128, CAPS, XCW]) for i in range(2)]
    xgh = sbuf("xgh", [128, KC, FT], BF16)
    xgl = sbuf("xgl", [128, KC, FT], BF16)
    ysb = sbuf("ysb", [128, FT // 128, D])
    idxu = [sbuf(f"idxu{i}", [128, CAPS], U32) for i in range(2)]
    dSc = Ctr(nc, "dSc")
    sdma(slt_s[:, :], slt_in)
    sdma(onesb_s[:, :], onesb_in)
    sdma(offs_s[:, :], offs_in)
    sdma(capd_s[:, :], capd_in)
    sdma(iota0_s[:, :], iota0_in)
    tok_iot = sdma(iop_s[:, :], iop_in)
    vop(ve.memset(comb_pad[:, :], 0.0))
    vop(ve.memset(maskp[:, :], 0.0))
    vop(ve.memset(base_sb[:, :], 0.0))
    vop(ve.memset(xg[0][:, :, :], 0.0))
    vop(ve.memset(xg[1][:, :, :], 0.0))

    war2 = {"w2stage": [[(None, 0)], [(None, 0)]], "w2split": [0, 0], "h": [0, 0]}

    def w2_dma(par, src_ap):
        eng = (gp, ae)[par]
        for ctr, tok in war2["w2stage"][par]:
            if ctr is not None:
                S.wait(eng, ctr, tok)
        S.wait(eng, dG[par], dG[par].n)
        eng.dma_start(out=w2stg[par][:, :], in_=src_ap).then_inc(dG[par].h, 16)
        dG[par].n += 16
        return dG[par].n

    def w2_split(par, tok_dma):
        S.wait(ae, dG[par], tok_dma)
        S.wait(ae, cP, war2["w2split"][par])
        t_hi = aop(ae.activation(w2h[par][:, :], w2stg[par][:, :], AF.Copy))
        S.wait(ve, dG[par], tok_dma)
        S.wait(ve, cP, war2["w2split"][par])
        S.wait(ve, cA, t_hi)
        t_lo = vop(ve.tensor_sub(w2l[par][:, :], w2stg[par][:, :], w2h[par][:, :]))
        war2["w2stage"][par] = [(cA, t_hi), (cV, t_lo)]
        return t_hi, t_lo

    S.wait(ve, dX, tok_iot)
    for s0 in range(XCR // 128):
        S.wait(ve, dX, dX.n)
        t_id = vop(ve.tensor_scalar_add(xg[0][:, 0, D + 8:D + 9], iop_s[:, :],
                                        float(T + s0 * 128)))
        S.wait(sy, cV, t_id)
        sdma(xc[s0 * 128:(s0 + 1) * 128, :], xg[0][:, 0, :])
    S.wait(sy, cV, cV.n)
    for r0 in range(T, YAR, 128):
        n_r = min(128, YAR - r0)
        sdma(y_acc[r0:r0 + n_r, :], xg[1][0:n_r, 0, 0:D])
    tok_xc_init = dX.n

    NG = E * NCS * HC

    def g_src(g):
        e = g // (NCS * HC)
        m = g % HC
        return e, m

    tok_w1 = {}
    tok_w2 = {}
    for g0 in range(min(2, NG)):
        e0, m0 = g_src(g0)
        tok_w1[g0] = w_dma(g0 % 2, w1[e0, m0])
        tok_w2[g0] = w2_dma(g0 % 2, w2[e0, m0])

    last_out_dma = 0
    tok_r2t_consumed = [(None, 0)]
    for ft in range(NFT):
        for ctr, tok in tok_r2t_consumed:
            if ctr is not None:
                S.wait(sy, ctr, tok)
        S.wait(sy, dO, last_out_dma)
        tok_r2t = sdma(r2t[:, :, :], r2h[:, ft * FTC:(ft + 1) * FTC, :])
        S.wait(ae, dX, tok_r2t)
        S.wait(ve, dX, tok_r2t)
        for jj in range(FTC):
            jabs = ft * FTC + jj
            col0 = jabs * 128
            t_ap = emit_ln_chunk(r2t[:, jj, :], col0, (4, 5))
            first = True
            i = None
            for k in range(KC):
                for p in range(3):
                    lhsT = (lnh if p != 2 else lnl)[:, k, col0:col0 + 128]
                    rhs = (wr_h if p != 1 else wr_l)[:, k, :]
                    if first:
                        S.wait(pe, cV, cV.n)
                        S.wait(pe, cA, cA.n)
                        bank_wait(6)
                    i = pe.matmul(pb[6][:, 0:8], lhsT, rhs, start=first,
                                  stop=(k == KC - 1 and p == 2))
                    first = False
            t_lgmm = cP.tick(i)
            S.wait(ve, cP, t_lgmm)
            t_lg = vop(ve.tensor_copy(lg[:, :], pb[6][:, 0:8]))
            bank_used(6, cV, t_lg)
            vop(ve.reduce_max(lgsc[:, 0:1], lg[:, :], axis=AX.X))
            vop(ve.tensor_scalar(lgm1[:, :], lg[:, :], lgsc[:, 0:1], None, ALU.is_equal))
            vop(ve.scalar_tensor_tensor(lgk[:, :], lgm1[:, :], -1e30, lg[:, :],
                                        ALU.mult, ALU.add))
            vop(ve.reduce_max(lgsc[:, 1:2], lgk[:, :], axis=AX.X))
            vop(ve.tensor_scalar(lgm2[:, :], lgk[:, :], lgsc[:, 1:2], None, ALU.is_equal))
            t_d = vop(ve.tensor_sub(lgsc[:, 2:3], lgsc[:, 1:2], lgsc[:, 0:1]))
            S.wait(ae, cV, t_d)
            t_e = aop(ae.activation(lgsc[:, 3:4], lgsc[:, 2:3], AF.Exp))
            S.wait(ve, cA, t_e)
            vop(ve.tensor_scalar_add(lgsc[:, 4:5], lgsc[:, 3:4], 1.0))
            vop(ve.reciprocal(lgsc[:, 5:6], lgsc[:, 4:5]))
            vop(ve.tensor_mul(lgsc[:, 6:7], lgsc[:, 3:4], lgsc[:, 5:6]))
            vop(ve.tensor_scalar(combine[:, jabs, :], lgm1[:, :], lgsc[:, 5:6], None, ALU.mult))
            t_c2 = vop(ve.scalar_tensor_tensor(combine[:, jabs, :], lgm2[:, :], lgsc[:, 6:7],
                                               combine[:, jabs, :], ALU.mult, ALU.add))
            t_cp = vop(ve.tensor_copy(comb_pad[:, 0:8], combine[:, jabs, :]))
            S.wait(pe, cV, t_cp)
            bank_wait(7)
            t_ct = cP.tick(pe.transpose(pb[7][:, 0:128], comb_pad[:, :], id_s[:, :]))
            S.wait(ve, cP, t_ct)
            t_ctf = vop(ve.tensor_copy(ct_f[:, :], pb[7][0:8, 0:128]))
            bank_used(7, cV, t_ctf)
            t_cth = vop(ve.tensor_copy(cth[:, jabs, :], ct_f[:, :]))
            vop(ve.tensor_sub(ctl[:, jabs, :], ct_f[:, :], cth[:, jabs, :]))
            S.wait(ve, cP, war["mbf_pe"])
            vop(ve.tensor_add(maskp[:, 0:8], lgm1[:, :], lgm2[:, :]))
            t_mb = vop(ve.tensor_copy(masks_bf[:, :], maskp[:, :]))
            S.wait(pe, cV, t_mb)
            bank_wait(3)
            t_pf = cP.tick(pe.matmul(pb[3][:, 0:128], slt_s[:, :], masks_bf[:, :],
                                     start=True, stop=True))
            S.wait(ve, cP, t_pf)
            t_px = vop(ve.tensor_add(posx[:, :], base_sb[:, :], pb[3][:, 0:8]))
            bank_used(3, cV, t_px)
            vop(ve.tensor_scalar(posx[:, :], posx[:, :], capd_s[:, 0:1], None, ALU.min))
            vop(ve.tensor_add(posx[:, :], posx[:, :], offs_s[:, :]))
            vop(ve.tensor_mul(post[:, :], lgm1[:, :], posx[:, :]))
            vop(ve.reduce_sum(posf[:, 0:1], post[:, :], axis=AX.X))
            vop(ve.tensor_mul(post[:, :], lgm2[:, :], posx[:, :]))
            vop(ve.reduce_sum(posf[:, 1:2], post[:, :], axis=AX.X))
            S.wait(ve, dSc, dSc.n)
            t_pu = vop(ve.tensor_copy(posu[:, :], posf[:, :]))
            bank_wait(7)
            t_tt = cP.tick(pe.matmul(pb[7][:, 0:128], onesb_s[:, :], masks_bf[:, :],
                                     start=True, stop=True))
            war["mbf_pe"] = t_tt
            S.wait(ve, cP, t_tt)
            t_bu = vop(ve.tensor_add(base_sb[:, :], base_sb[:, :], pb[7][:, 0:8]))
            bank_used(7, cV, t_bu)
            t_g8 = vop(ve.tensor_copy(lnx[:, D:D + 8], combine[:, jabs, :]))
            t_gi = vop(ve.tensor_copy(lnx[:, D + 8:D + 9], iota0_s[:, jabs:jabs + 1]))
            S.wait(gp, cV, cV.n)
            S.wait(gp, dX, tok_xc_init)
            for sl in range(2):
                S.wait(gp, dSc, dSc.n)
                gp.indirect_dma_start(
                    xc, bass.IndirectOffsetOnAxis(ap=posu[:, sl:sl + 1], axis=0),
                    lnx[:, 0:XCW], None).then_inc(dSc.h, 16)
                dSc.n += 16
            war["lntm_dma"] = (dSc, dSc.n)
        S.wait(ve, dO, last_out_dma)
        for jj in range(FTC):
            jabs = ft * FTC + jj
            for fh in range(NFH):
                b = 6 + ((jj * NFH + fh) % 2)
                S.wait(pe, cV, cV.n)
                bank_wait(b)
                pe.matmul(pb[b][:, 0:W2W], cth[:, jabs, :], b2_h[:, fh * W2W:(fh + 1) * W2W],
                          start=True, stop=False)
                pe.matmul(pb[b][:, 0:W2W], cth[:, jabs, :], b2_l[:, fh * W2W:(fh + 1) * W2W],
                          start=False, stop=False)
                i3 = pe.matmul(pb[b][:, 0:W2W], ctl[:, jabs, :], b2_h[:, fh * W2W:(fh + 1) * W2W],
                               start=False, stop=True)
                t_b = cP.tick(i3)
                S.wait(ve, cP, t_b)
                t_ev = vop(ve.scalar_tensor_tensor(
                    r2t[:, jj, fh * W2W:(fh + 1) * W2W], pb[b][:, 0:W2W], 1.0,
                    r2t[:, jj, fh * W2W:(fh + 1) * W2W], ALU.mult, ALU.add))
                bank_used(b, cV, t_ev)
        S.wait(gp, cV, cV.n)
        S.wait(gp, dO, dO.n)
        gp.dma_start(out=y_acc[ft * FT:(ft + 1) * FT, :].rearrange("(c p) d -> p c d", p=128),
                     in_=r2t[:, :, :]).then_inc(dO.h, 16)
        dO.n += 16
        last_out_dma = dO.n
        tok_r2t_consumed = [(cA, cA.n), (cV, cV.n)]

    if cfg.get("ffn_mode") == "router":
        return probe_exit(src=r2t[:, :, :], dst=y_pc[:, 0:FTC, :], close_attn=False)

    tok_comp_done = dSc.n
    dS2 = [Ctr(nc, "dS2a"), Ctr(nc, "dS2b")]
    war3 = {"xg_pe": [0, 0], "xg_dve": [0, 0],
            "idx_ds2": {0: (0, 0), 1: (0, 0)},
            "ysb_tok": {}}
    tok_xr = {}

    def load_region(e):
        pe_ = e % 2
        S.wait(sy, dSc, tok_comp_done)
        S.wait(sy, cP, war3["xg_pe"][pe_])
        S.wait(sy, cV, war3["xg_dve"][pe_])
        tok_xr[e] = sdma(xg[pe_][:, :, :],
                         xc[e * (CAP + 128):e * (CAP + 128) + CAP, :]
                         .rearrange("(q p) c -> p q c", p=128))

    load_region(0)
    g = 0
    for e in range(E):
        par_e = e % 2
        xge = xg[par_e]
        idxe = idxu[par_e]
        if e + 1 < E:
            load_region(e + 1)
        S.wait(ve, dX, tok_xr[e])
        tk0, tk1 = war3["idx_ds2"][par_e]
        S.wait(ve, dS2[0], tk0)
        S.wait(ve, dS2[1], tk1)
        t_idx = vop(ve.tensor_copy(idxe[:, :], xge[:, :, D + 8]))
        ns = 0
        for csub in range(NCS):
            for sl in range(CSUB // 128):
                slot = csub * (CSUB // 128) + sl
                for k in range(KC):
                    b = 6 + (k % 2)
                    S.wait(pe, dX, tok_xr[e])
                    S.wait(pe, cV, cV.n)
                    bank_wait(b)
                    t_tr = cP.tick(pe.transpose(pb[b][:, 0:128],
                                                xge[:, slot, k * 128:(k + 1) * 128],
                                                id_s[:, :]))
                    S.wait(ae, cP, t_tr)
                    t_hi = aop(ae.activation(xgh[:, k, sl * 128:(sl + 1) * 128],
                                             pb[b][:, 0:128], AF.Copy))
                    S.wait(ve, cP, t_tr)
                    S.wait(ve, cA, t_hi)
                    t_lo = vop(ve.tensor_sub(xgl[:, k, sl * 128:(sl + 1) * 128],
                                             pb[b][:, 0:128],
                                             xgh[:, k, sl * 128:(sl + 1) * 128]))
                    bank_used(b, cV, t_lo)
                    bank_used(b, cA, t_hi)
            war3["xg_pe"][par_e] = cP.n
            t_split = cV.n
            y_stop = [0] * NBY
            pend_w2 = None
            for mstep in range(HC + 1):
                if mstep < HC:
                    m = mstep
                    par = g % 2
                    tok_hi, tok_lo = w_split(par, tok_w1[g])
                    bw1 = 4 + (m % 2)
                    S.wait(pe, cA, tok_hi)
                    S.wait(pe, cV, max(tok_lo, t_split))
                    bank_wait(bw1)
                    tok_mm = mm3(bw1, par, CSUB, 0, src_h=xgh, src_l=xgl)
                    war["split"][par] = tok_mm
                    tok2_hi, tok2_lo = w2_split(par, tok_w2[g])
                    if g + 2 < NG:
                        e2, m2 = g_src(g + 2)
                        tok_w1[g + 2] = w_dma(par, w1[e2, m2])
                        tok_w2[g + 2] = w2_dma(par, w2[e2, m2])
                    hpar = m % 2
                    S.wait(ae, cP, max(tok_mm, war2["h"][hpar]))
                    t_g = aop(ae.activation(hst[:, 0:CSUB], pb[bw1][:, 0:CSUB], AF.Gelu,
                                            bias=b1_s[:, e, m:m + 1]))
                    bank_used(bw1, cA, t_g)
                    S.wait(ve, cA, t_g)
                    S.wait(ve, cP, war2["h"][hpar])
                    t_hh = vop(ve.tensor_copy(hh[hpar][:, :], hst[:, 0:CSUB]))
                    t_hl = vop(ve.tensor_sub(hl[hpar][:, :], hst[:, 0:CSUB], hh[hpar][:, :]))
                    this_w2 = (m, par, hpar, t_hh, t_hl, tok2_hi, tok2_lo)
                    g += 1
                else:
                    this_w2 = None
                if pend_w2 is not None:
                    m, par, hpar, t_hh, t_hl, tok2_hi, tok2_lo = pend_w2
                    S.wait(pe, cA, tok2_hi)
                    S.wait(pe, cV, max(t_hh, t_hl, tok2_lo))
                    i = None
                    for j in range(CSUB // 128):
                        for fh in range(NFH):
                            by = j * NFH + fh
                            if m == 0:
                                bank_wait(by)
                            for p in range(3):
                                lhsT = (hh if p != 2 else hl)[hpar][:, j * 128:(j + 1) * 128]
                                rhs = (w2h if p != 1 else w2l)[par][:, fh * W2W:(fh + 1) * W2W]
                                i = pe.matmul(pb[by][:, 0:W2W], lhsT, rhs,
                                              start=(m == 0 and p == 0),
                                              stop=(m == HC - 1 and p == 2))
                            if m == HC - 1:
                                y_stop[by] = cP.tick(i)
                    if m != HC - 1:
                        t_end = cP.tick(i)
                        war2["w2split"][par] = t_end
                        war2["h"][hpar] = t_end
                    else:
                        war2["w2split"][par] = y_stop[-1]
                        war2["h"][hpar] = y_stop[-1]
                pend_w2 = this_w2
            for j in range(CSUB // 128):
                slot = csub * (CSUB // 128) + j
                for fh in range(NFH):
                    by = j * NFH + fh
                    S.wait(ve, cP, y_stop[by])
                    if j in war3["ysb_tok"]:
                        pty, ptk = war3["ysb_tok"][j]
                        S.wait(ve, dS2[pty], ptk)
                    t_ev = vop(ve.tensor_scalar(
                        ysb[:, j, fh * W2W:(fh + 1) * W2W], pb[by][:, 0:W2W],
                        xge[:, slot, D + e:D + e + 1], None, ALU.mult))
                    bank_used(by, cV, t_ev)
            for j in range(CSUB // 128):
                slot = csub * (CSUB // 128) + j
                p = ns % 2
                S.wait(gp, cV, cV.n)
                S.wait(gp, dO, dO.n)
                if ns < 2:
                    S.wait(gp, dS2[0], dS2[0].n)
                    S.wait(gp, dS2[1], dS2[1].n)
                else:
                    S.wait(gp, dS2[p], dS2[p].n)
                gp.indirect_dma_start(
                    y_acc, bass.IndirectOffsetOnAxis(ap=idxe[:, slot:slot + 1], axis=0),
                    ysb[:, j, :], None,
                    compute_op=ALU.add).then_inc(dS2[p].h, 16)
                dS2[p].n += 16
                war3["ysb_tok"][j] = (p, dS2[p].n)
                ns += 1
        war3["xg_dve"][par_e] = cV.n
        war3["idx_ds2"][par_e] = (dS2[0].n, dS2[1].n)

    for ft in range(NFT):
        S.wait(sy, dSc, dSc.n)
        S.wait(sy, dS2[0], dS2[0].n)
        S.wait(sy, dS2[1], dS2[1].n)
        S.wait(sy, dO, dO.n)
        S.wait(sy, cV, cV.n)
        tok_l = sdma(r2t[:, :, :],
                     y_acc[ft * FT:(ft + 1) * FT, :].rearrange("(c p) d -> p c d", p=128))
        S.wait(gp, dX, tok_l)
        S.wait(gp, dO, dO.n)
        gp.dma_start(out=y_pc[:, ft * FTC:(ft + 1) * FTC, :], in_=r2t[:, :, :]).then_inc(dO.h, 16)
        dO.n += 16

    S.wait(gp, dO, dO.n)
    S.wait(gp, dSc, dSc.n)
    S.wait(gp, dS2[0], dS2[0].n)
    S.wait(gp, dS2[1], dS2[1].n)
    es.close()
    return nc


def _stage_inputs(inputs, cfg, core):
    D, T = cfg["D"], cfg["T"]
    TC = T // 128
    x = np.ascontiguousarray(np.asarray(inputs["L0_attn_x"], dtype=np.float32))
    b, half = core // 2, core % 2
    own = x[b, half * T:(half + 1) * T]
    pre = x[b, 0:T] if half == 1 else own
    shuffle = lambda a: np.ascontiguousarray(a.reshape(TC, 128, D).transpose(1, 0, 2))
    return {
        "x_own": shuffle(own),
        "x_pre": shuffle(pre),
        "gate": np.full((128, 1), float(half), np.float32),
    }


def _stage_weights(inputs, cfg):
    D, HID, E = cfg["D"], cfg["HID"], cfg["E"]
    KC, HC = D // 128, HID // 128
    HP = (D // 64) // 2
    f32 = lambda a: np.asarray(a, dtype=np.float32)
    Wq, Wk, Wv, Wo = (f32(inputs[f"L0_attn_W{n}"]) for n in "qkvo")
    W1, W2 = f32(inputs["L0_ffn_W1"]), f32(inputs["L0_ffn_W2"])
    b1, b2, Wr = f32(inputs["L0_ffn_b1"]), f32(inputs["L0_ffn_b2"]), f32(inputs["L0_ffn_Wr"])

    def lhsT_lay(W):
        a = W.reshape(HP, 128, KC, 128).transpose(0, 3, 2, 1)
        return np.ascontiguousarray(a.reshape(HP, 128, KC * 128))

    import ml_dtypes
    T, CAP, E_ = cfg["T"], cfg["CAP"], cfg["E"]
    TC = T // 128
    bfl = np.dtype(ml_dtypes.bfloat16)
    return {
        "slt_in": np.tril(np.ones((128, 128), np.float32), -1).T.astype(bfl),
        "onesb_in": np.ones((128, 128), bfl),
        "offs_in": np.tile((np.arange(8) * (CAP + 128)).astype(np.float32), (128, 1)),
        "capd_in": (CAP + np.arange(128, dtype=np.float32))[:, None].copy(),
        "iota0_in": np.ascontiguousarray(
            (np.arange(TC)[None, :] * 128 + np.arange(128)[:, None]).astype(np.float32)),
        "iop_in": np.arange(128, dtype=np.float32)[:, None].copy(),
        "wq": lhsT_lay(Wq), "wk": lhsT_lay(Wk), "wv": lhsT_lay(Wv),
        "wo": np.ascontiguousarray(Wo.T.reshape(HP, 128, D)),
        "w1": np.ascontiguousarray(
            W1.reshape(E, KC, 128, HC, 128).transpose(0, 3, 2, 1, 4).reshape(E, HC, 128, KC * 128)),
        "w2": np.ascontiguousarray(W2.reshape(E, HC, 128, D)),
        "b1": np.ascontiguousarray(b1.reshape(E, HC, 128).transpose(2, 0, 1)),
        "b2": np.ascontiguousarray(b2),
        "wr": np.ascontiguousarray(Wr.reshape(KC, 128, 8).transpose(1, 0, 2)),
        "um": np.tile(np.triu(np.ones((128, 128), np.float32)), (1, 2)),
        "ident": np.eye(128, dtype=np.float32),
    }


def kernel(**inputs):
    cfg = FULL
    n_cores = 8
    weights = _stage_weights(inputs, cfg)
    in_maps = []
    for core in range(n_cores):
        m = dict(weights)
        m.update(_stage_inputs(inputs, cfg, core))
        in_maps.append(m)
    nc = build_program(cfg)
    res = run_bass_kernel_spmd(nc, in_maps, core_ids=list(range(n_cores)))
    T, D = cfg["T"], cfg["D"]
    out = np.empty((4, 4096, D), np.float32)
    for core in range(n_cores):
        b, half = core // 2, core % 2
        out[b, half * T:(half + 1) * T] = np.asarray(res.results[core]["y"])
    return out


# version 21
# speedup vs baseline: 1.2008x; 1.0142x over previous
import contextlib
import numpy as np
import concourse.bass as bass
import concourse.mybir as mybir
from concourse.bass_utils import run_bass_kernel_spmd

from concourse import library_config

F32 = mybir.dt.float32
BF16 = mybir.dt.bfloat16
I16 = mybir.dt.int16
U32 = mybir.dt.uint32
AF = mybir.ActivationFunctionType
ALU = mybir.AluOpType
AX = mybir.AxisListType

FULL = dict(D=1024, T=2048, HID=4096, E=8, FT=256, CAP=768)
MINI = dict(D=256, T=256, HID=512, E=8, FT=256, CAP=256)

LN_EPS = 1e-5
ATTN_EPS = 1e-8


class Ctr:
    def __init__(self, nc, name):
        self.h = nc.alloc_semaphore(name=name)
        self.n = 0

    def tick(self, instr, n=1):
        instr.then_inc(self.h, n)
        self.n += n
        return self.n


class Syncer:
    def __init__(self):
        self.wm = {}

    def wait(self, eng, ctr, val):
        if not val or val <= 0:
            return
        k = (id(eng), id(ctr))
        if self.wm.get(k, 0) >= val:
            return
        eng.wait_ge(ctr.h, val)
        self.wm[k] = val


def build_program(cfg):
    D, T, HID, E, FT = (cfg[k] for k in ("D", "T", "HID", "E", "FT"))
    CAP = cfg["CAP"]
    CSUB = FT
    NCS = CAP // CSUB
    CAPS = CAP // 128
    KC = D // 128
    NH = D // 64
    HP = NH // 2
    TC = T // 128
    HC = HID // 128
    FTC = FT // 128
    NFT = T // FT
    PT = min(512, T)
    W2W = min(512, D)
    NFH = D // W2W
    NBY = FTC * NFH
    assert NBY <= 4

    nc = bass.Bass("TRN2", target_bir_lowering=False, debug=False)
    S = Syncer()

    x_own = nc.dram_tensor("x_own", [128, TC, D], F32, kind="ExternalInput").ap()
    x_pre = nc.dram_tensor("x_pre", [128, TC, D], F32, kind="ExternalInput").ap()
    gate = nc.dram_tensor("gate", [128, 1], F32, kind="ExternalInput").ap()
    wq = nc.dram_tensor("wq", [HP, 128, KC * 128], F32, kind="ExternalInput").ap()
    wk = nc.dram_tensor("wk", [HP, 128, KC * 128], F32, kind="ExternalInput").ap()
    wv = nc.dram_tensor("wv", [HP, 128, KC * 128], F32, kind="ExternalInput").ap()
    wo = nc.dram_tensor("wo", [HP, 128, D], F32, kind="ExternalInput").ap()
    w1 = nc.dram_tensor("w1", [E, HC, 128, KC * 128], F32, kind="ExternalInput").ap()
    w2 = nc.dram_tensor("w2", [E, HC, 128, D], F32, kind="ExternalInput").ap()
    b1 = nc.dram_tensor("b1", [128, E, HC], F32, kind="ExternalInput").ap()
    b2 = nc.dram_tensor("b2", [8, D], F32, kind="ExternalInput").ap()
    wr = nc.dram_tensor("wr", [128, KC, 8], F32, kind="ExternalInput").ap()
    um = nc.dram_tensor("um", [128, 256], F32, kind="ExternalInput").ap()
    ident = nc.dram_tensor("ident", [128, 128], F32, kind="ExternalInput").ap()
    slt_in = nc.dram_tensor("slt_in", [128, 128], BF16, kind="ExternalInput").ap()
    onesb_in = nc.dram_tensor("onesb_in", [128, 128], BF16, kind="ExternalInput").ap()
    offs_in = nc.dram_tensor("offs_in", [128, 8], F32, kind="ExternalInput").ap()
    capd_in = nc.dram_tensor("capd_in", [128, 1], F32, kind="ExternalInput").ap()
    iota0_in = nc.dram_tensor("iota0_in", [128, TC], F32, kind="ExternalInput").ap()
    iop_in = nc.dram_tensor("iop_in", [128, 1], F32, kind="ExternalInput").ap()
    y = nc.dram_tensor("y", [T, D], F32, kind="ExternalOutput").ap()
    y_pc = y.rearrange("(c p) d -> p c d", p=128)
    r2h = nc.dram_tensor("r2h", [128, TC, D], F32, kind="Internal").ap()
    XCR = E * (CAP + 128)
    XCW = D + 64
    YAR = T + XCR + 256
    xc = nc.dram_tensor("xc", [XCR, XCW], F32, kind="Internal").ap()
    y_acc = nc.dram_tensor("y_acc", [YAR, D], F32, kind="Internal").ap()

    es = contextlib.ExitStack()
    sbuf = lambda name, shape, dt=F32: es.enter_context(nc.sbuf_tensor(name, shape, dt))

    cP = Ctr(nc, "cP")
    cV = Ctr(nc, "cV")
    cA = Ctr(nc, "cA")
    dX = Ctr(nc, "dX")
    dW = [Ctr(nc, "dW0"), Ctr(nc, "dW1")]
    dG = [Ctr(nc, "dG0"), Ctr(nc, "dG1")]
    dO = Ctr(nc, "dO")

    pe, ve, ae, sy, gp = nc.tensor, nc.vector, nc.scalar, nc.sync, nc.gpsimd
    ALL_ENG = (pe, ve, ae, sy, gp)

    def vop(instr):
        t = cV.tick(instr)
        S.wait(ve, cV, t)
        return t

    def aop(instr):
        t = cA.tick(instr)
        S.wait(ae, cA, t)
        return t

    def sdma(out, in_):
        S.wait(sy, dX, dX.n)
        sy.dma_start(out=out, in_=in_).then_inc(dX.h, 16)
        dX.n += 16
        return dX.n

    def barrier():
        for ctr in (cP, cV, cA):
            for eng in ALL_ENG:
                S.wait(eng, ctr, ctr.n)

    um_s = sbuf("um_s", [128, 256])
    id_s = sbuf("id_s", [128, 128])
    gate_s = sbuf("gate_s", [128, 1])
    wr_f = sbuf("wr_f", [128, KC, 8])
    wr_h = sbuf("wr_h", [128, KC, 8], BF16)
    wr_l = sbuf("wr_l", [128, KC, 8], BF16)
    b1_s = sbuf("b1_s", [128, E, HC])
    b2_h = sbuf("b2_h", [8, D], BF16)
    b2_l = sbuf("b2_l", [8, D], BF16)
    lnx = sbuf("lnx", [128, max(D + 64, max(D, 2 * min(512, T)) + min(512, T))])
    lntm = lnx[:, 0:D]
    lnsc = sbuf("lnsc", [128, 16])
    lnh = sbuf("lnh", [128, KC, T], BF16)
    lnl = sbuf("lnl", [128, KC, T], BF16)
    wstg = [sbuf(f"wstg{i}", [128, KC * 128]) for i in range(2)]
    wsh = [sbuf(f"wsh{i}", [128, KC * 128], BF16) for i in range(2)]
    wsl = [sbuf(f"wsl{i}", [128, KC * 128], BF16) for i in range(2)]

    pb = [es.enter_context(nc.psum_tensor(f"pb{i}", [128, 512], F32)) for i in range(8)]
    bank_free = {i: [] for i in range(8)}

    def bank_wait(b):
        for ctr, tok in bank_free[b]:
            S.wait(pe, ctr, tok)
        bank_free[b] = []

    def bank_used(b, ctr, tok):
        bank_free[b].append((ctr, tok))

    sdma(um_s[:, :], um)
    sdma(id_s[:, :], ident)
    sdma(gate_s[:, :], gate)
    sdma(wr_f[:, :, :], wr)
    sdma(b1_s[:, :, :], b1)
    tok_consts = sdma(lnx[0:8, 0:D], b2)
    S.wait(ae, dX, tok_consts)
    t_wrh = aop(ae.activation(wr_h[:, :, :], wr_f[:, :, :], AF.Copy))
    t_b2h = aop(ae.activation(b2_h[:, :], lnx[0:8, 0:D], AF.Copy))
    S.wait(ve, dX, tok_consts)
    S.wait(ve, cA, t_b2h)
    t_wrl = vop(ve.tensor_sub(wr_l[:, :, :], wr_f[:, :, :], wr_h[:, :, :]))
    t_b2l = vop(ve.tensor_sub(b2_l[:, :], lnx[0:8, 0:D], b2_h[:, :]))

    attn_es = contextlib.ExitStack()
    asb = lambda name, shape, dt=F32: attn_es.enter_context(nc.sbuf_tensor(name, shape, dt))
    xbuf = asb("xbuf", [128, TC, D])
    qb = asb("qb", [128, T])
    kb = asb("kb", [128, T])
    vb = asb("vb", [128, T])
    s_run = asb("s_run", [128, NH, 128])
    ktm = [asb(f"ktm{i}", [128, 128]) for i in range(2)]
    vaug = [asb(f"vaug{i}", [128, 256]) for i in range(2)]
    qpad = [[asb(f"qpad{i}_{j}", [128, 128]) for i in range(2)] for j in range(2)]
    kpad = [[asb(f"kpad{i}_{j}", [128, 128]) for i in range(2)] for j in range(2)]
    am2_sb = [asb(f"am2_{j}", [128, 256]) for j in range(2)]
    attn_tm = asb("attn_tm", [128, TC, 128])
    ah_sb = asb("ah_sb", [128, 128], BF16)
    al_sb = asb("al_sb", [128, 128], BF16)
    esc1 = lnx[:, 0:PT]
    esc2 = lnx[:, PT:2 * PT]
    esc3 = lnx[:, 2 * PT:3 * PT] if 3 * PT <= lnx.shape[1] else lnx[:, lnx.shape[1] - PT:]
    numsb = asb("numsb", [128, 256])
    densb = asb("densb", [128, 4])

    vop(ve.memset(s_run[:, :, :], 0.0))
    for h in range(2):
        for j in range(2):
            vop(ve.memset(qpad[h][j][:, :], 0.0))
            vop(ve.memset(kpad[h][j][:, :], 0.0))
    for i in range(2):
        vop(ve.memset(vaug[i][:, :], 0.0))
        vop(ve.memset(vaug[i][:, 64:65], 1.0))
        vop(ve.memset(vaug[i][:, 192:193], 1.0))

    def probe_exit(src=None, dst=None, close_attn=True):
        S.wait(gp, cV, cV.n)
        S.wait(gp, cA, cA.n)
        for ctr in (cP,):
            S.wait(gp, ctr, ctr.n)
        S.wait(gp, dO, dO.n)
        gp.dma_start(out=(y_pc if dst is None else dst),
                     in_=(xbuf[:, :, :] if src is None else src)).then_inc(dO.h, 16)
        dO.n += 16
        S.wait(gp, dO, dO.n)
        if close_attn:
            attn_es.close()
        es.close()
        return nc

    war = {
        "lntm_pe": 0,
        "lntm_dma": None,
        "combpad_dma": None,
        "mbf_pe": 0,
        "stage": [[(None, 0)], [(None, 0)]],
        "split": [0, 0],
        "vaug": [0, 0],
        "ktm": [0, 0],
        "pad": [0, 0],
        "am2": [0, 0],
        "sadd": 0,
    }

    def emit_ln_chunk(src_ap, col0, tr_banks):
        S.wait(ae, cP, war["lntm_pe"])
        if war["lntm_dma"]:
            S.wait(ae, *war["lntm_dma"])
        t1 = aop(ae.activation(lntm, src_ap, AF.Square, accum_out=lnsc[:, 0:1]))
        t2 = aop(ae.activation(lntm, src_ap, AF.Identity, accum_out=lnsc[:, 1:2]))
        S.wait(ve, cA, t2)
        vop(ve.tensor_scalar_mul(lnsc[:, 2:3], lnsc[:, 1:2], 1.0 / D))
        vop(ve.tensor_mul(lnsc[:, 3:4], lnsc[:, 2:3], lnsc[:, 2:3]))
        vop(ve.scalar_tensor_tensor(lnsc[:, 4:5], lnsc[:, 0:1], 1.0 / D, lnsc[:, 3:4],
                                    ALU.mult, ALU.subtract))
        tv = vop(ve.tensor_scalar_add(lnsc[:, 5:6], lnsc[:, 4:5], LN_EPS))
        S.wait(ae, cV, tv)
        ts = aop(ae.activation(lnsc[:, 6:7], lnsc[:, 5:6], AF.Sqrt))
        S.wait(ve, cA, ts)
        vop(ve.reciprocal(lnsc[:, 7:8], lnsc[:, 6:7]))
        ycol = 7
        for it in range(2):
            a, b_, c_, yn = 8, 9, 10, 11 + it
            vop(ve.tensor_mul(lnsc[:, a:a + 1], lnsc[:, ycol:ycol + 1], lnsc[:, ycol:ycol + 1]))
            vop(ve.tensor_mul(lnsc[:, b_:b_ + 1], lnsc[:, 5:6], lnsc[:, a:a + 1]))
            vop(ve.tensor_scalar(lnsc[:, c_:c_ + 1], lnsc[:, b_:b_ + 1], -0.5, 1.5,
                                 ALU.mult, ALU.add))
            vop(ve.tensor_mul(lnsc[:, yn:yn + 1], lnsc[:, ycol:ycol + 1], lnsc[:, c_:c_ + 1]))
            ycol = yn
        S.wait(ve, cA, t2)
        S.wait(ve, cP, war["lntm_pe"])
        if war["lntm_dma"]:
            S.wait(ve, *war["lntm_dma"])
        t_ap = vop(ve.tensor_scalar(lntm, src_ap, lnsc[:, 2:3], lnsc[:, ycol:ycol + 1],
                                    ALU.subtract, ALU.mult))
        for k in range(KC):
            b = tr_banks[k % 2]
            S.wait(pe, cV, t_ap)
            bank_wait(b)
            t_tr = cP.tick(pe.transpose(pb[b][:, 0:128], lntm[:, k * 128:(k + 1) * 128], id_s[:, :]))
            S.wait(ae, cP, t_tr)
            t_hi = aop(ae.activation(lnh[:, k, col0:col0 + 128], pb[b][:, 0:128], AF.Copy))
            S.wait(ve, cP, t_tr)
            S.wait(ve, cA, t_hi)
            t_lo = vop(ve.tensor_sub(lnl[:, k, col0:col0 + 128], pb[b][:, 0:128],
                                     lnh[:, k, col0:col0 + 128]))
            bank_used(b, cV, t_lo)
            bank_used(b, cA, t_hi)
        war["lntm_pe"] = cP.n
        return t_ap

    def w_dma(par, src_ap):
        eng = (sy, ae)[par]
        for ctr, tok in war["stage"][par]:
            if ctr is not None:
                S.wait(eng, ctr, tok)
        S.wait(eng, dW[par], dW[par].n)
        eng.dma_start(out=wstg[par][:, :], in_=src_ap).then_inc(dW[par].h, 16)
        dW[par].n += 16
        return dW[par].n

    def w_split(par, tok_dma):
        S.wait(ae, dW[par], tok_dma)
        S.wait(ae, cP, war["split"][par])
        t_hi = aop(ae.activation(wsh[par][:, :], wstg[par][:, :], AF.Copy))
        S.wait(ve, dW[par], tok_dma)
        S.wait(ve, cP, war["split"][par])
        S.wait(ve, cA, t_hi)
        t_lo = vop(ve.tensor_sub(wsl[par][:, :], wstg[par][:, :], wsh[par][:, :]))
        war["stage"][par] = [(cA, t_hi), (cV, t_lo)]
        return t_hi, t_lo

    def mm3(bank, wpar, n, col, src_h=None, src_l=None):
        if src_h is None:
            src_h, src_l = lnh, lnl
        tok = None
        first = True
        for k in range(KC):
            for p in range(3):
                lhsT = (wsh if p != 2 else wsl)[wpar][:, k * 128:(k + 1) * 128]
                rhs = (src_h if p != 1 else src_l)[:, k, col:col + n]
                last = (k == KC - 1 and p == 2)
                i = pe.matmul(pb[bank][:, 0:n], lhsT, rhs, start=first, stop=last)
                first = False
                if last:
                    tok = cP.tick(i)
        return tok

    def emit_elu_evac(ps_ap, dst_ap, n):
        t_r = vop(ve.tensor_relu(esc1[:, 0:n], ps_ap))
        t_m = vop(ve.tensor_single_scalar(esc2[:, 0:n], ps_ap, 0.0, ALU.min))
        S.wait(ae, cV, t_m)
        t_e = aop(ae.activation(esc3[:, 0:n], esc2[:, 0:n], AF.Exp))
        S.wait(ve, cA, t_e)
        return vop(ve.tensor_add(dst_ap, esc1[:, 0:n], esc3[:, 0:n]))

    def emit_proj(dst, elu, wpar, tok_hi, tok_lo, banks):
        t_ev = 0
        for i0, t0 in enumerate(range(0, T, PT)):
            b = banks[i0 % len(banks)]
            S.wait(pe, cA, tok_hi)
            S.wait(pe, cV, tok_lo)
            bank_wait(b)
            tok_mm = mm3(b, wpar, PT, t0)
            S.wait(ve, cP, tok_mm)
            if elu:
                t_ev = emit_elu_evac(pb[b][:, 0:PT], dst[:, t0:t0 + PT], PT)
            else:
                t_ev = vop(ve.tensor_copy(dst[:, t0:t0 + PT], pb[b][:, 0:PT]))
            bank_used(b, cV, t_ev)
        war["split"][wpar] = cP.n
        return t_ev

    def mix_state_chunk(c, hp, par, tok_k, tok_v, update_state=True):
        bt, bs = 6 + (c % 2), 5
        S.wait(pe, cV, max(tok_k, tok_v))
        bank_wait(bt)
        t_trk = cP.tick(pe.transpose(pb[bt][:, 0:128], kb[:, c * 128:(c + 1) * 128], id_s[:, :]))
        t_trv = cP.tick(pe.transpose(pb[bt][:, 128:256], vb[:, c * 128:(c + 1) * 128], id_s[:, :]))
        S.wait(ae, cP, max(t_trv, war["ktm"][par], war["vaug"][par]))
        t_ktm = aop(ae.activation(ktm[par][:, :], pb[bt][:, 0:128], AF.Copy))
        aop(ae.activation(vaug[par][:, 0:64], pb[bt][:, 128:192], AF.Copy))
        t_v1 = aop(ae.activation(vaug[par][:, 128:192], pb[bt][:, 192:256], AF.Copy))
        bank_used(bt, cA, t_v1)
        if update_state:
            S.wait(pe, cA, t_v1)
            bank_wait(bs)
            cP.tick(pe.matmul(pb[bs][:, 0:128], ktm[par][:, :], vaug[par][:, 0:128],
                              start=True, stop=True))
            t_s = cP.tick(pe.matmul(pb[bs][:, 128:256], ktm[par][:, :], vaug[par][:, 128:256],
                                    start=True, stop=True))
            S.wait(ve, cP, t_s)
            gh0, gh1 = 2 * hp, 2 * hp + 1
            vop(ve.tensor_add(s_run[0:64, gh0, 0:128], s_run[0:64, gh0, 0:128],
                              pb[bs][0:64, 0:128]))
            t_up = vop(ve.tensor_add(s_run[64:128, gh1, 0:128], s_run[64:128, gh1, 0:128],
                                     pb[bs][64:128, 128:256]))
            bank_used(bs, cV, t_up)
        war["ktm"][par] = cP.n
        war["vaug"][par] = cP.n
        return t_ktm, t_v1

    if cfg.get("stop") == "ph0":
        return probe_exit()
    tok_xpre = sdma(xbuf[:, :, :], x_pre)
    S.wait(ae, dX, tok_xpre)
    S.wait(ve, dX, tok_xpre)
    S.wait(ae, cV, t_b2l)
    for j in range(TC):
        emit_ln_chunk(xbuf[:, j, :], j * 128, (0, 1))

    pend_k = w_dma(0, wk[0])
    pend_v = w_dma(1, wv[0])
    for hp in range(HP):
        tok_hi, tok_lo = w_split(0, pend_k)
        tok_k = emit_proj(kb, True, 0, tok_hi, tok_lo, (2, 3, 0))
        tok_hi, tok_lo = w_split(1, pend_v)
        tok_v = emit_proj(vb, False, 1, tok_hi, tok_lo, (2, 3, 0))
        if hp + 1 < HP:
            pend_k = w_dma(0, wk[hp + 1])
            pend_v = w_dma(1, wv[hp + 1])
        pend_st = None
        for cstep in range(TC + 1):
            if cstep < TC:
                _, t_v1p = mix_state_chunk(cstep, hp, cstep % 2, tok_k, tok_v,
                                           update_state=False)
                this_st = (cstep % 2, t_v1p)
            else:
                this_st = None
            if pend_st is not None:
                parp, t_v1p = pend_st
                bs = 5
                S.wait(pe, cA, t_v1p)
                bank_wait(bs)
                cP.tick(pe.matmul(pb[bs][:, 0:128], ktm[parp][:, :],
                                  vaug[parp][:, 0:128], start=True, stop=True))
                t_s = cP.tick(pe.matmul(pb[bs][:, 128:256], ktm[parp][:, :],
                                        vaug[parp][:, 128:256], start=True, stop=True))
                war["ktm"][parp] = max(war["ktm"][parp], t_s)
                war["vaug"][parp] = max(war["vaug"][parp], t_s)
                gh0, gh1 = 2 * hp, 2 * hp + 1
                S.wait(ve, cP, t_s)
                vop(ve.tensor_add(s_run[0:64, gh0, 0:128], s_run[0:64, gh0, 0:128],
                                  pb[bs][0:64, 0:128]))
                t_up = vop(ve.tensor_add(s_run[64:128, gh1, 0:128],
                                         s_run[64:128, gh1, 0:128],
                                         pb[bs][64:128, 128:256]))
                bank_used(bs, cV, t_up)
            pend_st = this_st

    vop(ve.tensor_scalar_mul(s_run[:, :, :], s_run[:, :, :], gate_s[:, 0:1]))
    if cfg.get("stop") == "prefix":
        return probe_exit()

    barrier()
    tok_xown = sdma(xbuf[:, :, :], x_own)
    S.wait(ae, dX, tok_xown)
    S.wait(ve, dX, tok_xown)
    for j in range(TC):
        emit_ln_chunk(xbuf[:, j, :], j * 128, (0, 1))

    dR = [Ctr(nc, f"dR{i}") for i in range(3)]
    dump_tok = {}
    pend_q = w_dma(0, wq[0])
    pend_k = w_dma(1, wk[0])
    t_outlast = 0
    for hp in range(HP):
        tok_hi, tok_lo = w_split(0, pend_q)
        tok_q = emit_proj(qb, True, 0, tok_hi, tok_lo, (2, 3, 0))
        tok_hi, tok_lo = w_split(1, pend_k)
        tok_k = emit_proj(kb, True, 1, tok_hi, tok_lo, (2, 3, 0))
        pend_v = w_dma(0, wv[hp])
        tok_hi, tok_lo = w_split(0, pend_v)
        tok_v = emit_proj(vb, False, 0, tok_hi, tok_lo, (2, 3, 0))
        pend_wo = w_dma(1, wo[hp])

        pend_mix = None
        for cstep in range(TC + 1):
            if cstep < TC:
                c = cstep
                cpar = c % 2
                par = c % 2
                t_ktm, t_v1 = mix_state_chunk(c, hp, par, tok_k, tok_v, update_state=False)
                S.wait(ae, cV, tok_q)
                S.wait(ae, cP, war["pad"][cpar])
                aop(ae.activation(qpad[0][cpar][0:64, :], qb[0:64, c * 128:(c + 1) * 128],
                                  AF.Copy))
                aop(ae.activation(kpad[0][cpar][0:64, :], kb[0:64, c * 128:(c + 1) * 128],
                                  AF.Copy))
                aop(ae.activation(qpad[1][cpar][64:128, :],
                                  qb[64:128, c * 128:(c + 1) * 128], AF.Copy))
                t_kp = aop(ae.activation(kpad[1][cpar][64:128, :],
                                         kb[64:128, c * 128:(c + 1) * 128], AF.Copy))
                this_mix = (c, cpar, par, t_kp, t_v1)
            else:
                this_mix = None
            if pend_mix is not None:
                c, cpar, par, t_kp, t_v1 = pend_mix
                b_at = 2 + cpar
                b_o = cpar
                b_s = 5
                gh0, gh1 = 2 * hp, 2 * hp + 1
                S.wait(pe, cA, t_kp)
                bank_wait(b_at)
                cP.tick(pe.matmul(pb[b_at][:, 0:128], kpad[0][cpar][:, :],
                                  qpad[0][cpar][:, :], start=True, stop=True))
                t_at = cP.tick(pe.matmul(pb[b_at][:, 128:256], kpad[1][cpar][:, :],
                                         qpad[1][cpar][:, :], start=True, stop=True))
                S.wait(ve, cP, max(t_at, war["am2"][cpar]))
                t_am = vop(ve.tensor_mul(am2_sb[cpar][:, :], pb[b_at][:, 0:256], um_s[:, :]))
                bank_used(b_at, cV, t_am)
                S.wait(pe, cV, t_am)
                S.wait(pe, cA, t_v1)
                bank_wait(b_o)
                cP.tick(pe.matmul(pb[b_o][:, 0:128], am2_sb[cpar][:, 0:128],
                                  vaug[par][:, 0:128], start=True, stop=True))
                t_o1 = cP.tick(pe.matmul(pb[b_o][:, 128:256], am2_sb[cpar][:, 128:256],
                                         vaug[par][:, 128:256], start=True, stop=True))
                S.wait(pe, cV, war["sadd"])
                cP.tick(pe.matmul(pb[b_o][:, 256:384], qpad[0][cpar][:, :],
                                  s_run[:, gh0, :], start=True, stop=True))
                t_o2 = cP.tick(pe.matmul(pb[b_o][:, 384:512], qpad[1][cpar][:, :],
                                         s_run[:, gh1, :], start=True, stop=True))
                war["pad"][cpar] = t_o2
                war["am2"][cpar] = t_o1
                war["vaug"][par] = max(war["vaug"][par], t_o1)
                bank_wait(b_s)
                cP.tick(pe.matmul(pb[b_s][:, 0:128], ktm[par][:, :], vaug[par][:, 0:128],
                                  start=True, stop=True))
                t_s = cP.tick(pe.matmul(pb[b_s][:, 128:256], ktm[par][:, :],
                                        vaug[par][:, 128:256], start=True, stop=True))
                war["ktm"][par] = max(war["ktm"][par], t_s)
                war["vaug"][par] = max(war["vaug"][par], t_s)
                S.wait(ve, cP, t_s)
                vop(ve.tensor_add(s_run[0:64, gh0, 0:128], s_run[0:64, gh0, 0:128],
                                  pb[b_s][0:64, 0:128]))
                t_up = vop(ve.tensor_add(s_run[64:128, gh1, 0:128],
                                         s_run[64:128, gh1, 0:128],
                                         pb[b_s][64:128, 128:256]))
                bank_used(b_s, cV, t_up)
                war["sadd"] = t_up
                S.wait(ve, cA, t_outlast)
                t_cp1 = vop(ve.tensor_copy(numsb[:, 0:256], pb[b_o][:, 0:256]))
                vop(ve.scalar_tensor_tensor(densb[:, 0:2], pb[b_o][:, 320:449:128],
                                            ATTN_EPS, numsb[:, 64:193:128],
                                            ALU.add, ALU.add))
                vop(ve.reciprocal(densb[:, 2:4], densb[:, 0:2]))
                t_n = vop(ve.tensor_add(numsb[:, 0:256], numsb[:, 0:256],
                                        pb[b_o][:, 256:512]))
                bank_used(b_o, cV, t_n)
                S.wait(ae, cV, t_n)
                aop(ae.activation(attn_tm[:, c, 0:64], numsb[:, 0:64], AF.Copy,
                                  scale=densb[:, 2:3]))
                t_outlast = aop(ae.activation(attn_tm[:, c, 64:128], numsb[:, 128:192], AF.Copy,
                                              scale=densb[:, 3:4]))
            pend_mix = this_mix
        tok_hi_o, tok_lo_o = w_split(1, pend_wo)
        t_wo_last = 0
        for j in range(TC):
            b_tr = 6
            S.wait(pe, cA, t_outlast)
            bank_wait(b_tr)
            t_acc_last = 0
            t_tr = cP.tick(pe.transpose(pb[b_tr][:, 0:128], attn_tm[:, j, :], id_s[:, :]))
            S.wait(ae, cP, t_tr)
            t_ah = aop(ae.activation(ah_sb[:, :], pb[b_tr][:, 0:128], AF.Copy))
            S.wait(ve, cP, t_tr)
            S.wait(ve, cA, t_ah)
            t_al = vop(ve.tensor_sub(al_sb[:, :], pb[b_tr][:, 0:128], ah_sb[:, :]))
            bank_used(b_tr, cV, t_al)
            for fh in range(NFH):
                b = (0, 1)[fh % 2]
                S.wait(pe, cA, max(t_ah, tok_hi_o))
                S.wait(pe, cV, max(t_al, tok_lo_o))
                bank_wait(b)
                pe.matmul(pb[b][:, 0:W2W], ah_sb[:, :], wsh[1][:, fh * W2W:(fh + 1) * W2W],
                          start=True, stop=False)
                pe.matmul(pb[b][:, 0:W2W], ah_sb[:, :], wsl[1][:, fh * W2W:(fh + 1) * W2W],
                          start=False, stop=False)
                i3 = pe.matmul(pb[b][:, 0:W2W], al_sb[:, :], wsh[1][:, fh * W2W:(fh + 1) * W2W],
                               start=False, stop=True)
                t_wo = cP.tick(i3)
                t_wo_last = t_wo
                S.wait(ve, cP, t_wo)
                t_acc = vop(ve.scalar_tensor_tensor(xbuf[:, j, fh * W2W:(fh + 1) * W2W],
                                                    pb[b][:, 0:W2W], 1.0,
                                                    xbuf[:, j, fh * W2W:(fh + 1) * W2W],
                                                    ALU.mult, ALU.add))
                bank_used(b, cV, t_acc)
                t_acc_last = t_acc
            if hp == HP - 1:
                q = j % 3
                eng_q = (sy, ae, gp)[q]
                S.wait(eng_q, dR[q], dR[q].n)
                S.wait(eng_q, cV, t_acc_last)
                eng_q.dma_start(out=r2h[:, j, :], in_=xbuf[:, j, :]).then_inc(dR[q].h, 16)
                dR[q].n += 16
                dump_tok[j] = (q, dR[q].n)
        war["split"][1] = cP.n
        if hp + 1 < HP:
            pend_q = w_dma(0, wq[hp + 1])
            pend_k = w_dma(1, wk[hp + 1])


    if cfg.get("stop") == "own":
        return probe_exit()

    barrier()
    for eng in ALL_ENG:
        for q in range(3):
            S.wait(eng, dR[q], dR[q].n)
    attn_es.close()

    r2t = sbuf("r2t", [128, FTC, D])
    hst = sbuf("hst", [128, FT])
    hh = [sbuf(f"hh{i}", [128, FT], BF16) for i in range(2)]
    hl = [sbuf(f"hl{i}", [128, FT], BF16) for i in range(2)]
    w2stg = [sbuf(f"w2stg{i}", [128, D]) for i in range(2)]
    w2h = [sbuf(f"w2h{i}", [128, D], BF16) for i in range(2)]
    w2l = [sbuf(f"w2l{i}", [128, D], BF16) for i in range(2)]
    lg = sbuf("lg", [128, 8])
    lgm1 = sbuf("lgm1", [128, 8])
    lgk = sbuf("lgk", [128, 8])
    lgm2 = sbuf("lgm2", [128, 8])
    lgsc = sbuf("lgsc", [128, 8])
    combine = sbuf("combine", [128, TC, 8])
    comb_pad = sbuf("comb_pad", [128, 128])
    cth = sbuf("cth", [8, TC, 128], BF16)
    ctl = sbuf("ctl", [8, TC, 128], BF16)
    ct_f = sbuf("ct_f", [8, 128])
    slt_s = sbuf("slt_s", [128, 128], BF16)
    onesb_s = sbuf("onesb_s", [128, 128], BF16)
    offs_s = sbuf("offs_s", [128, 8])
    capd_s = sbuf("capd_s", [128, 1])
    iota0_s = sbuf("iota0_s", [128, TC])
    iop_s = sbuf("iop_s", [128, 1])
    maskp = sbuf("maskp", [128, 128])
    masks_bf = sbuf("masks_bf", [128, 128], BF16)
    base_sb = sbuf("base_sb", [128, 8])
    posx = sbuf("posx", [128, 8])
    post = sbuf("post", [128, 8])
    posf = sbuf("posf", [128, 2])
    posu = sbuf("posu", [128, 2], U32)
    xg = [sbuf(f"xg{i}", [128, CAPS, XCW]) for i in range(2)]
    xgh = sbuf("xgh", [128, KC, FT], BF16)
    xgl = sbuf("xgl", [128, KC, FT], BF16)
    ysb = sbuf("ysb", [128, FT // 128, D])
    idxu = [sbuf(f"idxu{i}", [128, CAPS], U32) for i in range(2)]
    dSc = Ctr(nc, "dSc")
    sdma(slt_s[:, :], slt_in)
    sdma(onesb_s[:, :], onesb_in)
    sdma(offs_s[:, :], offs_in)
    sdma(capd_s[:, :], capd_in)
    sdma(iota0_s[:, :], iota0_in)
    tok_iot = sdma(iop_s[:, :], iop_in)
    vop(ve.memset(comb_pad[:, :], 0.0))
    vop(ve.memset(maskp[:, :], 0.0))
    vop(ve.memset(base_sb[:, :], 0.0))
    vop(ve.memset(xg[0][:, :, :], 0.0))
    vop(ve.memset(xg[1][:, :, :], 0.0))

    war2 = {"w2stage": [[(None, 0)], [(None, 0)]], "w2split": [0, 0], "h": [0, 0]}

    def w2_dma(par, src_ap):
        eng = (gp, ae)[par]
        for ctr, tok in war2["w2stage"][par]:
            if ctr is not None:
                S.wait(eng, ctr, tok)
        S.wait(eng, dG[par], dG[par].n)
        eng.dma_start(out=w2stg[par][:, :], in_=src_ap).then_inc(dG[par].h, 16)
        dG[par].n += 16
        return dG[par].n

    def w2_split(par, tok_dma):
        S.wait(ae, dG[par], tok_dma)
        S.wait(ae, cP, war2["w2split"][par])
        t_hi = aop(ae.activation(w2h[par][:, :], w2stg[par][:, :], AF.Copy))
        S.wait(ve, dG[par], tok_dma)
        S.wait(ve, cP, war2["w2split"][par])
        S.wait(ve, cA, t_hi)
        t_lo = vop(ve.tensor_sub(w2l[par][:, :], w2stg[par][:, :], w2h[par][:, :]))
        war2["w2stage"][par] = [(cA, t_hi), (cV, t_lo)]
        return t_hi, t_lo

    S.wait(ve, dX, tok_iot)
    for s0 in range(XCR // 128):
        S.wait(ve, dX, dX.n)
        t_id = vop(ve.tensor_scalar_add(xg[0][:, 0, D + 8:D + 9], iop_s[:, :],
                                        float(T + s0 * 128)))
        S.wait(sy, cV, t_id)
        sdma(xc[s0 * 128:(s0 + 1) * 128, :], xg[0][:, 0, :])
    S.wait(sy, cV, cV.n)
    for r0 in range(T, YAR, 128):
        n_r = min(128, YAR - r0)
        sdma(y_acc[r0:r0 + n_r, :], xg[1][0:n_r, 0, 0:D])
    tok_xc_init = dX.n

    NG = E * NCS * HC

    def g_src(g):
        e = g // (NCS * HC)
        m = g % HC
        return e, m

    tok_w1 = {}
    tok_w2 = {}
    for g0 in range(min(2, NG)):
        e0, m0 = g_src(g0)
        tok_w1[g0] = w_dma(g0 % 2, w1[e0, m0])
        tok_w2[g0] = w2_dma(g0 % 2, w2[e0, m0])

    last_out_dma = 0
    tok_r2t_consumed = [(None, 0)]
    for ft in range(NFT):
        for ctr, tok in tok_r2t_consumed:
            if ctr is not None:
                S.wait(sy, ctr, tok)
        S.wait(sy, dO, last_out_dma)
        tok_r2t = sdma(r2t[:, :, :], r2h[:, ft * FTC:(ft + 1) * FTC, :])
        S.wait(ae, dX, tok_r2t)
        S.wait(ve, dX, tok_r2t)
        for jj in range(FTC):
            jabs = ft * FTC + jj
            col0 = jabs * 128
            t_ap = emit_ln_chunk(r2t[:, jj, :], col0, (4, 5))
            first = True
            i = None
            for k in range(KC):
                for p in range(3):
                    lhsT = (lnh if p != 2 else lnl)[:, k, col0:col0 + 128]
                    rhs = (wr_h if p != 1 else wr_l)[:, k, :]
                    if first:
                        S.wait(pe, cV, cV.n)
                        S.wait(pe, cA, cA.n)
                        bank_wait(6)
                    i = pe.matmul(pb[6][:, 0:8], lhsT, rhs, start=first,
                                  stop=(k == KC - 1 and p == 2))
                    first = False
            t_lgmm = cP.tick(i)
            S.wait(ve, cP, t_lgmm)
            t_lg = vop(ve.tensor_copy(lg[:, :], pb[6][:, 0:8]))
            bank_used(6, cV, t_lg)
            vop(ve.reduce_max(lgsc[:, 0:1], lg[:, :], axis=AX.X))
            vop(ve.tensor_scalar(lgm1[:, :], lg[:, :], lgsc[:, 0:1], None, ALU.is_equal))
            vop(ve.scalar_tensor_tensor(lgk[:, :], lgm1[:, :], -1e30, lg[:, :],
                                        ALU.mult, ALU.add))
            vop(ve.reduce_max(lgsc[:, 1:2], lgk[:, :], axis=AX.X))
            vop(ve.tensor_scalar(lgm2[:, :], lgk[:, :], lgsc[:, 1:2], None, ALU.is_equal))
            t_d = vop(ve.tensor_sub(lgsc[:, 2:3], lgsc[:, 1:2], lgsc[:, 0:1]))
            S.wait(ae, cV, t_d)
            t_e = aop(ae.activation(lgsc[:, 3:4], lgsc[:, 2:3], AF.Exp))
            S.wait(ve, cA, t_e)
            vop(ve.tensor_scalar_add(lgsc[:, 4:5], lgsc[:, 3:4], 1.0))
            vop(ve.reciprocal(lgsc[:, 5:6], lgsc[:, 4:5]))
            vop(ve.tensor_mul(lgsc[:, 6:7], lgsc[:, 3:4], lgsc[:, 5:6]))
            vop(ve.tensor_scalar(combine[:, jabs, :], lgm1[:, :], lgsc[:, 5:6], None, ALU.mult))
            t_c2 = vop(ve.scalar_tensor_tensor(combine[:, jabs, :], lgm2[:, :], lgsc[:, 6:7],
                                               combine[:, jabs, :], ALU.mult, ALU.add))
            t_cp = vop(ve.tensor_copy(comb_pad[:, 0:8], combine[:, jabs, :]))
            S.wait(pe, cV, t_cp)
            bank_wait(7)
            t_ct = cP.tick(pe.transpose(pb[7][:, 0:128], comb_pad[:, :], id_s[:, :]))
            S.wait(ve, cP, t_ct)
            t_ctf = vop(ve.tensor_copy(ct_f[:, :], pb[7][0:8, 0:128]))
            bank_used(7, cV, t_ctf)
            t_cth = vop(ve.tensor_copy(cth[:, jabs, :], ct_f[:, :]))
            vop(ve.tensor_sub(ctl[:, jabs, :], ct_f[:, :], cth[:, jabs, :]))
            S.wait(ve, cP, war["mbf_pe"])
            vop(ve.tensor_add(maskp[:, 0:8], lgm1[:, :], lgm2[:, :]))
            t_mb = vop(ve.tensor_copy(masks_bf[:, :], maskp[:, :]))
            S.wait(pe, cV, t_mb)
            bank_wait(3)
            t_pf = cP.tick(pe.matmul(pb[3][:, 0:128], slt_s[:, :], masks_bf[:, :],
                                     start=True, stop=True))
            S.wait(ve, cP, t_pf)
            t_px = vop(ve.tensor_add(posx[:, :], base_sb[:, :], pb[3][:, 0:8]))
            bank_used(3, cV, t_px)
            vop(ve.tensor_scalar(posx[:, :], posx[:, :], capd_s[:, 0:1], None, ALU.min))
            vop(ve.tensor_add(posx[:, :], posx[:, :], offs_s[:, :]))
            vop(ve.tensor_mul(post[:, :], lgm1[:, :], posx[:, :]))
            vop(ve.reduce_sum(posf[:, 0:1], post[:, :], axis=AX.X))
            vop(ve.tensor_mul(post[:, :], lgm2[:, :], posx[:, :]))
            vop(ve.reduce_sum(posf[:, 1:2], post[:, :], axis=AX.X))
            S.wait(ve, dSc, dSc.n)
            t_pu = vop(ve.tensor_copy(posu[:, :], posf[:, :]))
            bank_wait(7)
            t_tt = cP.tick(pe.matmul(pb[7][:, 0:128], onesb_s[:, :], masks_bf[:, :],
                                     start=True, stop=True))
            war["mbf_pe"] = t_tt
            S.wait(ve, cP, t_tt)
            t_bu = vop(ve.tensor_add(base_sb[:, :], base_sb[:, :], pb[7][:, 0:8]))
            bank_used(7, cV, t_bu)
            t_g8 = vop(ve.tensor_copy(lnx[:, D:D + 8], combine[:, jabs, :]))
            t_gi = vop(ve.tensor_copy(lnx[:, D + 8:D + 9], iota0_s[:, jabs:jabs + 1]))
            S.wait(gp, cV, cV.n)
            S.wait(gp, dX, tok_xc_init)
            for sl in range(2):
                S.wait(gp, dSc, dSc.n)
                gp.indirect_dma_start(
                    xc, bass.IndirectOffsetOnAxis(ap=posu[:, sl:sl + 1], axis=0),
                    lnx[:, 0:XCW], None).then_inc(dSc.h, 16)
                dSc.n += 16
            war["lntm_dma"] = (dSc, dSc.n)
        S.wait(ve, dO, last_out_dma)
        for jj in range(FTC):
            jabs = ft * FTC + jj
            for fh in range(NFH):
                b = 6 + ((jj * NFH + fh) % 2)
                S.wait(pe, cV, cV.n)
                bank_wait(b)
                pe.matmul(pb[b][:, 0:W2W], cth[:, jabs, :], b2_h[:, fh * W2W:(fh + 1) * W2W],
                          start=True, stop=False)
                pe.matmul(pb[b][:, 0:W2W], cth[:, jabs, :], b2_l[:, fh * W2W:(fh + 1) * W2W],
                          start=False, stop=False)
                i3 = pe.matmul(pb[b][:, 0:W2W], ctl[:, jabs, :], b2_h[:, fh * W2W:(fh + 1) * W2W],
                               start=False, stop=True)
                t_b = cP.tick(i3)
                S.wait(ve, cP, t_b)
                t_ev = vop(ve.scalar_tensor_tensor(
                    r2t[:, jj, fh * W2W:(fh + 1) * W2W], pb[b][:, 0:W2W], 1.0,
                    r2t[:, jj, fh * W2W:(fh + 1) * W2W], ALU.mult, ALU.add))
                bank_used(b, cV, t_ev)
        S.wait(gp, cV, cV.n)
        S.wait(gp, dO, dO.n)
        gp.dma_start(out=y_acc[ft * FT:(ft + 1) * FT, :].rearrange("(c p) d -> p c d", p=128),
                     in_=r2t[:, :, :]).then_inc(dO.h, 16)
        dO.n += 16
        last_out_dma = dO.n
        tok_r2t_consumed = [(cA, cA.n), (cV, cV.n)]

    if cfg.get("ffn_mode") == "router":
        return probe_exit(src=r2t[:, :, :], dst=y_pc[:, 0:FTC, :], close_attn=False)

    tok_comp_done = dSc.n
    dS2 = [Ctr(nc, "dS2a"), Ctr(nc, "dS2b")]
    war3 = {"xg_pe": [0, 0], "xg_dve": [0, 0],
            "idx_ds2": {0: (0, 0), 1: (0, 0)},
            "ysb_tok": {}}
    tok_xr = {}

    def load_region(e):
        pe_ = e % 2
        S.wait(sy, dSc, tok_comp_done)
        S.wait(sy, cP, war3["xg_pe"][pe_])
        S.wait(sy, cV, war3["xg_dve"][pe_])
        tok_xr[e] = sdma(xg[pe_][:, :, :],
                         xc[e * (CAP + 128):e * (CAP + 128) + CAP, :]
                         .rearrange("(q p) c -> p q c", p=128))

    load_region(0)
    g = 0
    for e in range(E):
        par_e = e % 2
        xge = xg[par_e]
        idxe = idxu[par_e]
        if e + 1 < E:
            load_region(e + 1)
        S.wait(ve, dX, tok_xr[e])
        tk0, tk1 = war3["idx_ds2"][par_e]
        S.wait(ve, dS2[0], tk0)
        S.wait(ve, dS2[1], tk1)
        t_idx = vop(ve.tensor_copy(idxe[:, :], xge[:, :, D + 8]))
        ns = 0
        for csub in range(NCS):
            for sl in range(CSUB // 128):
                slot = csub * (CSUB // 128) + sl
                for k in range(KC):
                    b = 6 + (k % 2)
                    S.wait(pe, dX, tok_xr[e])
                    S.wait(pe, cV, cV.n)
                    bank_wait(b)
                    t_tr = cP.tick(pe.transpose(pb[b][:, 0:128],
                                                xge[:, slot, k * 128:(k + 1) * 128],
                                                id_s[:, :]))
                    S.wait(ae, cP, t_tr)
                    t_hi = aop(ae.activation(xgh[:, k, sl * 128:(sl + 1) * 128],
                                             pb[b][:, 0:128], AF.Copy))
                    S.wait(ve, cP, t_tr)
                    S.wait(ve, cA, t_hi)
                    t_lo = vop(ve.tensor_sub(xgl[:, k, sl * 128:(sl + 1) * 128],
                                             pb[b][:, 0:128],
                                             xgh[:, k, sl * 128:(sl + 1) * 128]))
                    bank_used(b, cV, t_lo)
                    bank_used(b, cA, t_hi)
            war3["xg_pe"][par_e] = cP.n
            t_split = cV.n
            y_stop = [0] * NBY
            pend_w2 = None
            for mstep in range(HC + 1):
                if mstep < HC:
                    m = mstep
                    par = g % 2
                    tok_hi, tok_lo = w_split(par, tok_w1[g])
                    bw1 = 4 + (m % 2)
                    S.wait(pe, cA, tok_hi)
                    S.wait(pe, cV, max(tok_lo, t_split))
                    bank_wait(bw1)
                    tok_mm = mm3(bw1, par, CSUB, 0, src_h=xgh, src_l=xgl)
                    war["split"][par] = tok_mm
                    tok2_hi, tok2_lo = w2_split(par, tok_w2[g])
                    if g + 2 < NG:
                        e2, m2 = g_src(g + 2)
                        tok_w1[g + 2] = w_dma(par, w1[e2, m2])
                        tok_w2[g + 2] = w2_dma(par, w2[e2, m2])
                    hpar = m % 2
                    S.wait(ae, cP, max(tok_mm, war2["h"][hpar]))
                    t_g = aop(ae.activation(hst[:, 0:CSUB], pb[bw1][:, 0:CSUB], AF.Gelu,
                                            bias=b1_s[:, e, m:m + 1]))
                    bank_used(bw1, cA, t_g)
                    S.wait(ve, cA, t_g)
                    S.wait(ve, cP, war2["h"][hpar])
                    t_hh = vop(ve.tensor_copy(hh[hpar][:, :], hst[:, 0:CSUB]))
                    t_hl = vop(ve.tensor_sub(hl[hpar][:, :], hst[:, 0:CSUB], hh[hpar][:, :]))
                    this_w2 = (m, par, hpar, t_hh, t_hl, tok2_hi, tok2_lo)
                    g += 1
                else:
                    this_w2 = None
                if pend_w2 is not None:
                    m, par, hpar, t_hh, t_hl, tok2_hi, tok2_lo = pend_w2
                    S.wait(pe, cA, tok2_hi)
                    S.wait(pe, cV, max(t_hh, t_hl, tok2_lo))
                    i = None
                    for j in range(CSUB // 128):
                        for fh in range(NFH):
                            by = j * NFH + fh
                            if m == 0:
                                bank_wait(by)
                            for p in range(3):
                                lhsT = (hh if p != 2 else hl)[hpar][:, j * 128:(j + 1) * 128]
                                rhs = (w2h if p != 1 else w2l)[par][:, fh * W2W:(fh + 1) * W2W]
                                i = pe.matmul(pb[by][:, 0:W2W], lhsT, rhs,
                                              start=(m == 0 and p == 0),
                                              stop=(m == HC - 1 and p == 2))
                            if m == HC - 1:
                                y_stop[by] = cP.tick(i)
                    if m != HC - 1:
                        t_end = cP.tick(i)
                        war2["w2split"][par] = t_end
                        war2["h"][hpar] = t_end
                    else:
                        war2["w2split"][par] = y_stop[-1]
                        war2["h"][hpar] = y_stop[-1]
                pend_w2 = this_w2
            for j in range(CSUB // 128):
                slot = csub * (CSUB // 128) + j
                for fh in range(NFH):
                    by = j * NFH + fh
                    S.wait(ve, cP, y_stop[by])
                    if j in war3["ysb_tok"]:
                        pty, ptk = war3["ysb_tok"][j]
                        S.wait(ve, dS2[pty], ptk)
                    t_ev = vop(ve.tensor_scalar(
                        ysb[:, j, fh * W2W:(fh + 1) * W2W], pb[by][:, 0:W2W],
                        xge[:, slot, D + e:D + e + 1], None, ALU.mult))
                    bank_used(by, cV, t_ev)
            for j in range(CSUB // 128):
                slot = csub * (CSUB // 128) + j
                p = ns % 2
                S.wait(gp, cV, cV.n)
                S.wait(gp, dO, dO.n)
                if ns < 2:
                    S.wait(gp, dS2[0], dS2[0].n)
                    S.wait(gp, dS2[1], dS2[1].n)
                else:
                    S.wait(gp, dS2[p], dS2[p].n)
                gp.indirect_dma_start(
                    y_acc, bass.IndirectOffsetOnAxis(ap=idxe[:, slot:slot + 1], axis=0),
                    ysb[:, j, :], None,
                    compute_op=ALU.add).then_inc(dS2[p].h, 16)
                dS2[p].n += 16
                war3["ysb_tok"][j] = (p, dS2[p].n)
                ns += 1
        war3["xg_dve"][par_e] = cV.n
        war3["idx_ds2"][par_e] = (dS2[0].n, dS2[1].n)

    for ft in range(NFT):
        S.wait(sy, dSc, dSc.n)
        S.wait(sy, dS2[0], dS2[0].n)
        S.wait(sy, dS2[1], dS2[1].n)
        S.wait(sy, dO, dO.n)
        S.wait(sy, cV, cV.n)
        tok_l = sdma(r2t[:, :, :],
                     y_acc[ft * FT:(ft + 1) * FT, :].rearrange("(c p) d -> p c d", p=128))
        S.wait(gp, dX, tok_l)
        S.wait(gp, dO, dO.n)
        gp.dma_start(out=y_pc[:, ft * FTC:(ft + 1) * FTC, :], in_=r2t[:, :, :]).then_inc(dO.h, 16)
        dO.n += 16

    S.wait(gp, dO, dO.n)
    S.wait(gp, dSc, dSc.n)
    S.wait(gp, dS2[0], dS2[0].n)
    S.wait(gp, dS2[1], dS2[1].n)
    es.close()
    return nc


def _stage_inputs(inputs, cfg, core):
    D, T = cfg["D"], cfg["T"]
    TC = T // 128
    x = np.ascontiguousarray(np.asarray(inputs["L0_attn_x"], dtype=np.float32))
    b, half = core // 2, core % 2
    own = x[b, half * T:(half + 1) * T]
    pre = x[b, 0:T] if half == 1 else own
    shuffle = lambda a: np.ascontiguousarray(a.reshape(TC, 128, D).transpose(1, 0, 2))
    return {
        "x_own": shuffle(own),
        "x_pre": shuffle(pre),
        "gate": np.full((128, 1), float(half), np.float32),
    }


def _stage_weights(inputs, cfg):
    D, HID, E = cfg["D"], cfg["HID"], cfg["E"]
    KC, HC = D // 128, HID // 128
    HP = (D // 64) // 2
    f32 = lambda a: np.asarray(a, dtype=np.float32)
    Wq, Wk, Wv, Wo = (f32(inputs[f"L0_attn_W{n}"]) for n in "qkvo")
    W1, W2 = f32(inputs["L0_ffn_W1"]), f32(inputs["L0_ffn_W2"])
    b1, b2, Wr = f32(inputs["L0_ffn_b1"]), f32(inputs["L0_ffn_b2"]), f32(inputs["L0_ffn_Wr"])

    def lhsT_lay(W):
        a = W.reshape(HP, 128, KC, 128).transpose(0, 3, 2, 1)
        return np.ascontiguousarray(a.reshape(HP, 128, KC * 128))

    import ml_dtypes
    T, CAP, E_ = cfg["T"], cfg["CAP"], cfg["E"]
    TC = T // 128
    bfl = np.dtype(ml_dtypes.bfloat16)
    return {
        "slt_in": np.tril(np.ones((128, 128), np.float32), -1).T.astype(bfl),
        "onesb_in": np.ones((128, 128), bfl),
        "offs_in": np.tile((np.arange(8) * (CAP + 128)).astype(np.float32), (128, 1)),
        "capd_in": (CAP + np.arange(128, dtype=np.float32))[:, None].copy(),
        "iota0_in": np.ascontiguousarray(
            (np.arange(TC)[None, :] * 128 + np.arange(128)[:, None]).astype(np.float32)),
        "iop_in": np.arange(128, dtype=np.float32)[:, None].copy(),
        "wq": lhsT_lay(Wq), "wk": lhsT_lay(Wk), "wv": lhsT_lay(Wv),
        "wo": np.ascontiguousarray(Wo.T.reshape(HP, 128, D)),
        "w1": np.ascontiguousarray(
            W1.reshape(E, KC, 128, HC, 128).transpose(0, 3, 2, 1, 4).reshape(E, HC, 128, KC * 128)),
        "w2": np.ascontiguousarray(W2.reshape(E, HC, 128, D)),
        "b1": np.ascontiguousarray(b1.reshape(E, HC, 128).transpose(2, 0, 1)),
        "b2": np.ascontiguousarray(b2),
        "wr": np.ascontiguousarray(Wr.reshape(KC, 128, 8).transpose(1, 0, 2)),
        "um": np.tile(np.triu(np.ones((128, 128), np.float32)), (1, 2)),
        "ident": np.eye(128, dtype=np.float32),
    }


def kernel(**inputs):
    cfg = FULL
    n_cores = 8
    weights = _stage_weights(inputs, cfg)
    in_maps = []
    for core in range(n_cores):
        m = dict(weights)
        m.update(_stage_inputs(inputs, cfg, core))
        in_maps.append(m)
    nc = build_program(cfg)
    res = run_bass_kernel_spmd(nc, in_maps, core_ids=list(range(n_cores)))
    T, D = cfg["T"], cfg["D"]
    out = np.empty((4, 4096, D), np.float32)
    for core in range(n_cores):
        b, half = core // 2, core % 2
        out[b, half * T:(half + 1) * T] = np.asarray(res.results[core]["y"])
    return out


# version 23
# speedup vs baseline: 1.2311x; 1.0252x over previous
import contextlib
import numpy as np
import concourse.bass as bass
import concourse.mybir as mybir
from concourse.bass_utils import run_bass_kernel_spmd

from concourse import library_config

F32 = mybir.dt.float32
BF16 = mybir.dt.bfloat16
I16 = mybir.dt.int16
U32 = mybir.dt.uint32
AF = mybir.ActivationFunctionType
ALU = mybir.AluOpType
AX = mybir.AxisListType

FULL = dict(D=1024, T=2048, HID=4096, E=8, FT=256, CAP=768)
MINI = dict(D=256, T=256, HID=512, E=8, FT=256, CAP=256)

LN_EPS = 1e-5
ATTN_EPS = 1e-8


class Ctr:
    def __init__(self, nc, name):
        self.h = nc.alloc_semaphore(name=name)
        self.n = 0

    def tick(self, instr, n=1):
        instr.then_inc(self.h, n)
        self.n += n
        return self.n


class Syncer:
    def __init__(self):
        self.wm = {}

    def wait(self, eng, ctr, val):
        if not val or val <= 0:
            return
        k = (id(eng), id(ctr))
        if self.wm.get(k, 0) >= val:
            return
        eng.wait_ge(ctr.h, val)
        self.wm[k] = val


def build_program(cfg):
    D, T, HID, E, FT = (cfg[k] for k in ("D", "T", "HID", "E", "FT"))
    CAP = cfg["CAP"]
    CSUB = FT
    NCS = CAP // CSUB
    CAPS = CAP // 128
    KC = D // 128
    NH = D // 64
    HP = NH // 2
    TC = T // 128
    HC = HID // 128
    FTC = FT // 128
    NFT = T // FT
    PT = min(512, T)
    W2W = min(512, D)
    NFH = D // W2W
    NBY = FTC * NFH
    assert NBY <= 4

    nc = bass.Bass("TRN2", target_bir_lowering=False, debug=False)
    S = Syncer()

    x_own = nc.dram_tensor("x_own", [128, TC, D], F32, kind="ExternalInput").ap()
    x_pre = nc.dram_tensor("x_pre", [128, TC, D], F32, kind="ExternalInput").ap()
    gate = nc.dram_tensor("gate", [128, 1], F32, kind="ExternalInput").ap()
    wq = nc.dram_tensor("wq", [HP, 128, KC * 128], F32, kind="ExternalInput").ap()
    wk = nc.dram_tensor("wk", [HP, 128, KC * 128], F32, kind="ExternalInput").ap()
    wv = nc.dram_tensor("wv", [HP, 128, KC * 128], F32, kind="ExternalInput").ap()
    wo = nc.dram_tensor("wo", [HP, 128, D], F32, kind="ExternalInput").ap()
    w1 = nc.dram_tensor("w1", [E, HC, 128, KC * 128], F32, kind="ExternalInput").ap()
    w2 = nc.dram_tensor("w2", [E, HC, 128, D], F32, kind="ExternalInput").ap()
    b1 = nc.dram_tensor("b1", [128, E, HC], F32, kind="ExternalInput").ap()
    b2 = nc.dram_tensor("b2", [8, D], F32, kind="ExternalInput").ap()
    wr = nc.dram_tensor("wr", [128, KC, 8], F32, kind="ExternalInput").ap()
    um = nc.dram_tensor("um", [128, 256], F32, kind="ExternalInput").ap()
    ident = nc.dram_tensor("ident", [128, 128], F32, kind="ExternalInput").ap()
    slt_in = nc.dram_tensor("slt_in", [128, 128], BF16, kind="ExternalInput").ap()
    onesb_in = nc.dram_tensor("onesb_in", [128, 128], BF16, kind="ExternalInput").ap()
    offs_in = nc.dram_tensor("offs_in", [128, 8], F32, kind="ExternalInput").ap()
    capd_in = nc.dram_tensor("capd_in", [128, 1], F32, kind="ExternalInput").ap()
    iota0_in = nc.dram_tensor("iota0_in", [128, TC], F32, kind="ExternalInput").ap()
    iop_in = nc.dram_tensor("iop_in", [128, 1], F32, kind="ExternalInput").ap()
    xcid_in = nc.dram_tensor("xcid_in", [128, E * (CAP + 128) // 128, 9], F32, kind="ExternalInput").ap()
    zx_in = nc.dram_tensor("zx_in", [128, D], F32, kind="ExternalInput").ap()
    y = nc.dram_tensor("y", [T, D], F32, kind="ExternalOutput").ap()
    y_pc = y.rearrange("(c p) d -> p c d", p=128)
    r2h = nc.dram_tensor("r2h", [128, TC, D], F32, kind="Internal").ap()
    XCR = E * (CAP + 128)
    XCW = D + 64
    YAR = T + XCR + 256
    xc = nc.dram_tensor("xc", [XCR, XCW], F32, kind="Internal").ap()
    y_acc = nc.dram_tensor("y_acc", [YAR, D], F32, kind="Internal").ap()

    es = contextlib.ExitStack()
    sbuf = lambda name, shape, dt=F32: es.enter_context(nc.sbuf_tensor(name, shape, dt))

    cP = Ctr(nc, "cP")
    cV = Ctr(nc, "cV")
    cA = Ctr(nc, "cA")
    dX = Ctr(nc, "dX")
    dW = [Ctr(nc, "dW0"), Ctr(nc, "dW1")]
    dG = [Ctr(nc, "dG0"), Ctr(nc, "dG1")]
    dO = Ctr(nc, "dO")

    pe, ve, ae, sy, gp = nc.tensor, nc.vector, nc.scalar, nc.sync, nc.gpsimd
    ALL_ENG = (pe, ve, ae, sy, gp)

    def vop(instr):
        t = cV.tick(instr)
        S.wait(ve, cV, t)
        return t

    def aop(instr):
        t = cA.tick(instr)
        S.wait(ae, cA, t)
        return t

    def sdma(out, in_):
        S.wait(sy, dX, dX.n)
        sy.dma_start(out=out, in_=in_).then_inc(dX.h, 16)
        dX.n += 16
        return dX.n

    def barrier():
        for ctr in (cP, cV, cA):
            for eng in ALL_ENG:
                S.wait(eng, ctr, ctr.n)

    um_s = sbuf("um_s", [128, 256])
    id_s = sbuf("id_s", [128, 128])
    gate_s = sbuf("gate_s", [128, 1])
    wr_f = sbuf("wr_f", [128, KC, 8])
    wr_h = sbuf("wr_h", [128, KC, 8], BF16)
    wr_l = sbuf("wr_l", [128, KC, 8], BF16)
    b1_s = sbuf("b1_s", [128, E, HC])
    b2_h = sbuf("b2_h", [8, D], BF16)
    b2_l = sbuf("b2_l", [8, D], BF16)
    lnx = sbuf("lnx", [128, max(D + 64, max(D, 2 * min(512, T)) + min(512, T))])
    lntm = lnx[:, 0:D]
    lnsc = sbuf("lnsc", [128, 16])
    lnh = sbuf("lnh", [128, KC, T], BF16)
    lnl = sbuf("lnl", [128, KC, T], BF16)
    wstg = [sbuf(f"wstg{i}", [128, KC * 128]) for i in range(2)]
    wsh = [sbuf(f"wsh{i}", [128, KC * 128], BF16) for i in range(2)]
    wsl = [sbuf(f"wsl{i}", [128, KC * 128], BF16) for i in range(2)]

    pb = [es.enter_context(nc.psum_tensor(f"pb{i}", [128, 512], F32)) for i in range(8)]
    bank_free = {i: [] for i in range(8)}

    def bank_wait(b):
        for ctr, tok in bank_free[b]:
            S.wait(pe, ctr, tok)
        bank_free[b] = []

    def bank_used(b, ctr, tok):
        bank_free[b].append((ctr, tok))

    sdma(um_s[:, :], um)
    sdma(id_s[:, :], ident)
    sdma(gate_s[:, :], gate)
    sdma(wr_f[:, :, :], wr)
    sdma(b1_s[:, :, :], b1)
    tok_consts = sdma(lnx[0:8, 0:D], b2)
    S.wait(ae, dX, tok_consts)
    t_wrh = aop(ae.activation(wr_h[:, :, :], wr_f[:, :, :], AF.Copy))
    t_b2h = aop(ae.activation(b2_h[:, :], lnx[0:8, 0:D], AF.Copy))
    S.wait(ve, dX, tok_consts)
    S.wait(ve, cA, t_b2h)
    t_wrl = vop(ve.tensor_sub(wr_l[:, :, :], wr_f[:, :, :], wr_h[:, :, :]))
    t_b2l = vop(ve.tensor_sub(b2_l[:, :], lnx[0:8, 0:D], b2_h[:, :]))

    attn_es = contextlib.ExitStack()
    asb = lambda name, shape, dt=F32: attn_es.enter_context(nc.sbuf_tensor(name, shape, dt))
    xbuf = asb("xbuf", [128, TC, D])
    qb = asb("qb", [128, T])
    kb = asb("kb", [128, T])
    vb = asb("vb", [128, T])
    s_run = asb("s_run", [128, NH, 128])
    ktm = [asb(f"ktm{i}", [128, 128]) for i in range(2)]
    vaug = [asb(f"vaug{i}", [128, 256]) for i in range(2)]
    qpad = [[asb(f"qpad{i}_{j}", [128, 128]) for i in range(2)] for j in range(2)]
    kpad = [[asb(f"kpad{i}_{j}", [128, 128]) for i in range(2)] for j in range(2)]
    am2_sb = [asb(f"am2_{j}", [128, 256]) for j in range(2)]
    attn_tm = asb("attn_tm", [128, TC, 128])
    ah_sb = asb("ah_sb", [128, 128], BF16)
    al_sb = asb("al_sb", [128, 128], BF16)
    esc1 = lnx[:, 0:PT]
    esc2 = lnx[:, PT:2 * PT]
    esc3 = lnx[:, 2 * PT:3 * PT] if 3 * PT <= lnx.shape[1] else lnx[:, lnx.shape[1] - PT:]
    numsb = asb("numsb", [128, 256])
    densb = asb("densb", [128, 4])

    vop(ve.memset(s_run[:, :, :], 0.0))
    for h in range(2):
        for j in range(2):
            vop(ve.memset(qpad[h][j][:, :], 0.0))
            vop(ve.memset(kpad[h][j][:, :], 0.0))
    for i in range(2):
        vop(ve.memset(vaug[i][:, :], 0.0))
        vop(ve.memset(vaug[i][:, 64:65], 1.0))
        vop(ve.memset(vaug[i][:, 192:193], 1.0))

    def probe_exit(src=None, dst=None, close_attn=True):
        S.wait(gp, cV, cV.n)
        S.wait(gp, cA, cA.n)
        for ctr in (cP,):
            S.wait(gp, ctr, ctr.n)
        S.wait(gp, dO, dO.n)
        gp.dma_start(out=(y_pc if dst is None else dst),
                     in_=(xbuf[:, :, :] if src is None else src)).then_inc(dO.h, 16)
        dO.n += 16
        S.wait(gp, dO, dO.n)
        if close_attn:
            attn_es.close()
        es.close()
        return nc

    war = {
        "lntm_pe": 0,
        "lntm_dma": None,
        "combpad_dma": None,
        "mbf_pe": 0,
        "stage": [[(None, 0)], [(None, 0)]],
        "split": [0, 0],
        "vaug": [0, 0],
        "ktm": [0, 0],
        "pad": [0, 0],
        "am2": [0, 0],
        "sadd": 0,
    }

    def emit_ln_chunk(src_ap, col0, tr_banks):
        S.wait(ae, cP, war["lntm_pe"])
        if war["lntm_dma"]:
            S.wait(ae, *war["lntm_dma"])
        t1 = aop(ae.activation(lntm, src_ap, AF.Square, accum_out=lnsc[:, 0:1]))
        t2 = aop(ae.activation(lntm, src_ap, AF.Identity, accum_out=lnsc[:, 1:2]))
        S.wait(ve, cA, t2)
        vop(ve.tensor_scalar_mul(lnsc[:, 2:3], lnsc[:, 1:2], 1.0 / D))
        vop(ve.tensor_mul(lnsc[:, 3:4], lnsc[:, 2:3], lnsc[:, 2:3]))
        vop(ve.scalar_tensor_tensor(lnsc[:, 4:5], lnsc[:, 0:1], 1.0 / D, lnsc[:, 3:4],
                                    ALU.mult, ALU.subtract))
        tv = vop(ve.tensor_scalar_add(lnsc[:, 5:6], lnsc[:, 4:5], LN_EPS))
        S.wait(ae, cV, tv)
        ts = aop(ae.activation(lnsc[:, 6:7], lnsc[:, 5:6], AF.Sqrt))
        S.wait(ve, cA, ts)
        vop(ve.reciprocal(lnsc[:, 7:8], lnsc[:, 6:7]))
        ycol = 7
        for it in range(2):
            a, b_, c_, yn = 8, 9, 10, 11 + it
            vop(ve.tensor_mul(lnsc[:, a:a + 1], lnsc[:, ycol:ycol + 1], lnsc[:, ycol:ycol + 1]))
            vop(ve.tensor_mul(lnsc[:, b_:b_ + 1], lnsc[:, 5:6], lnsc[:, a:a + 1]))
            vop(ve.tensor_scalar(lnsc[:, c_:c_ + 1], lnsc[:, b_:b_ + 1], -0.5, 1.5,
                                 ALU.mult, ALU.add))
            vop(ve.tensor_mul(lnsc[:, yn:yn + 1], lnsc[:, ycol:ycol + 1], lnsc[:, c_:c_ + 1]))
            ycol = yn
        S.wait(ve, cA, t2)
        S.wait(ve, cP, war["lntm_pe"])
        if war["lntm_dma"]:
            S.wait(ve, *war["lntm_dma"])
        t_ap = vop(ve.tensor_scalar(lntm, src_ap, lnsc[:, 2:3], lnsc[:, ycol:ycol + 1],
                                    ALU.subtract, ALU.mult))
        for k in range(KC):
            b = tr_banks[k % 2]
            S.wait(pe, cV, t_ap)
            bank_wait(b)
            t_tr = cP.tick(pe.transpose(pb[b][:, 0:128], lntm[:, k * 128:(k + 1) * 128], id_s[:, :]))
            S.wait(ae, cP, t_tr)
            t_hi = aop(ae.activation(lnh[:, k, col0:col0 + 128], pb[b][:, 0:128], AF.Copy))
            S.wait(ve, cP, t_tr)
            S.wait(ve, cA, t_hi)
            t_lo = vop(ve.tensor_sub(lnl[:, k, col0:col0 + 128], pb[b][:, 0:128],
                                     lnh[:, k, col0:col0 + 128]))
            bank_used(b, cV, t_lo)
            bank_used(b, cA, t_hi)
        war["lntm_pe"] = cP.n
        return t_ap

    def w_dma(par, src_ap):
        eng = (sy, ae)[par]
        for ctr, tok in war["stage"][par]:
            if ctr is not None:
                S.wait(eng, ctr, tok)
        S.wait(eng, dW[par], dW[par].n)
        eng.dma_start(out=wstg[par][:, :], in_=src_ap).then_inc(dW[par].h, 16)
        dW[par].n += 16
        return dW[par].n

    def w_split(par, tok_dma):
        S.wait(ae, dW[par], tok_dma)
        S.wait(ae, cP, war["split"][par])
        t_hi = aop(ae.activation(wsh[par][:, :], wstg[par][:, :], AF.Copy))
        S.wait(ve, dW[par], tok_dma)
        S.wait(ve, cP, war["split"][par])
        S.wait(ve, cA, t_hi)
        t_lo = vop(ve.tensor_sub(wsl[par][:, :], wstg[par][:, :], wsh[par][:, :]))
        war["stage"][par] = [(cA, t_hi), (cV, t_lo)]
        return t_hi, t_lo

    def mm3(bank, wpar, n, col, src_h=None, src_l=None):
        if src_h is None:
            src_h, src_l = lnh, lnl
        tok = None
        first = True
        for k in range(KC):
            for p in range(3):
                lhsT = (wsh if p != 2 else wsl)[wpar][:, k * 128:(k + 1) * 128]
                rhs = (src_h if p != 1 else src_l)[:, k, col:col + n]
                last = (k == KC - 1 and p == 2)
                i = pe.matmul(pb[bank][:, 0:n], lhsT, rhs, start=first, stop=last)
                first = False
                if last:
                    tok = cP.tick(i)
        return tok

    def emit_elu_evac(ps_ap, dst_ap, n):
        t_r = vop(ve.tensor_relu(esc1[:, 0:n], ps_ap))
        t_m = vop(ve.tensor_single_scalar(esc2[:, 0:n], ps_ap, 0.0, ALU.min))
        S.wait(ae, cV, t_m)
        t_e = aop(ae.activation(esc3[:, 0:n], esc2[:, 0:n], AF.Exp))
        S.wait(ve, cA, t_e)
        return vop(ve.tensor_add(dst_ap, esc1[:, 0:n], esc3[:, 0:n]))

    def emit_proj(dst, elu, wpar, tok_hi, tok_lo, banks):
        t_ev = 0
        for i0, t0 in enumerate(range(0, T, PT)):
            b = banks[i0 % len(banks)]
            S.wait(pe, cA, tok_hi)
            S.wait(pe, cV, tok_lo)
            bank_wait(b)
            tok_mm = mm3(b, wpar, PT, t0)
            S.wait(ve, cP, tok_mm)
            if elu:
                t_ev = emit_elu_evac(pb[b][:, 0:PT], dst[:, t0:t0 + PT], PT)
            else:
                t_ev = vop(ve.tensor_copy(dst[:, t0:t0 + PT], pb[b][:, 0:PT]))
            bank_used(b, cV, t_ev)
        war["split"][wpar] = cP.n
        return t_ev

    def mix_state_chunk(c, hp, par, tok_k, tok_v, update_state=True):
        bt, bs = 6 + (c % 2), 5
        S.wait(pe, cV, max(tok_k, tok_v))
        bank_wait(bt)
        t_trk = cP.tick(pe.transpose(pb[bt][:, 0:128], kb[:, c * 128:(c + 1) * 128], id_s[:, :]))
        t_trv = cP.tick(pe.transpose(pb[bt][:, 128:256], vb[:, c * 128:(c + 1) * 128], id_s[:, :]))
        S.wait(ae, cP, max(t_trv, war["ktm"][par], war["vaug"][par]))
        t_ktm = aop(ae.activation(ktm[par][:, :], pb[bt][:, 0:128], AF.Copy))
        aop(ae.activation(vaug[par][:, 0:64], pb[bt][:, 128:192], AF.Copy))
        t_v1 = aop(ae.activation(vaug[par][:, 128:192], pb[bt][:, 192:256], AF.Copy))
        bank_used(bt, cA, t_v1)
        if update_state:
            S.wait(pe, cA, t_v1)
            bank_wait(bs)
            cP.tick(pe.matmul(pb[bs][:, 0:128], ktm[par][:, :], vaug[par][:, 0:128],
                              start=True, stop=True))
            t_s = cP.tick(pe.matmul(pb[bs][:, 128:256], ktm[par][:, :], vaug[par][:, 128:256],
                                    start=True, stop=True))
            S.wait(ve, cP, t_s)
            gh0, gh1 = 2 * hp, 2 * hp + 1
            vop(ve.tensor_add(s_run[0:64, gh0, 0:128], s_run[0:64, gh0, 0:128],
                              pb[bs][0:64, 0:128]))
            t_up = vop(ve.tensor_add(s_run[64:128, gh1, 0:128], s_run[64:128, gh1, 0:128],
                                     pb[bs][64:128, 128:256]))
            bank_used(bs, cV, t_up)
        war["ktm"][par] = cP.n
        war["vaug"][par] = cP.n
        return t_ktm, t_v1

    if cfg.get("stop") == "ph0":
        return probe_exit()
    tok_xpre = sdma(xbuf[:, :, :], x_pre)
    S.wait(ae, dX, tok_xpre)
    S.wait(ve, dX, tok_xpre)
    S.wait(ae, cV, t_b2l)
    for j in range(TC):
        emit_ln_chunk(xbuf[:, j, :], j * 128, (0, 1))

    dI = Ctr(nc, "dI")
    for s0 in range(XCR // 128):
        gp.dma_start(out=xc[s0 * 128:(s0 + 1) * 128, 0:D],
                     in_=zx_in).then_inc(dI.h, 16)
        gp.dma_start(out=xc[s0 * 128:(s0 + 1) * 128, D:D + 9],
                     in_=xcid_in[:, s0, 0:9]).then_inc(dI.h, 16)
        gp.dma_start(out=xc[s0 * 128:(s0 + 1) * 128, D + 9:XCW],
                     in_=zx_in[:, 0:XCW - D - 9]).then_inc(dI.h, 16)
        dI.n += 48
    for r0 in range(T, YAR, 128):
        n_r = min(128, YAR - r0)
        gp.dma_start(out=y_acc[r0:r0 + n_r, :], in_=zx_in[0:n_r, :]).then_inc(dI.h, 16)
        dI.n += 16

    pend_k = w_dma(0, wk[0])
    pend_v = w_dma(1, wv[0])
    for hp in range(HP):
        tok_hi, tok_lo = w_split(0, pend_k)
        tok_k = emit_proj(kb, True, 0, tok_hi, tok_lo, (2, 3, 0))
        tok_hi, tok_lo = w_split(1, pend_v)
        tok_v = emit_proj(vb, False, 1, tok_hi, tok_lo, (2, 3, 0))
        if hp + 1 < HP:
            pend_k = w_dma(0, wk[hp + 1])
            pend_v = w_dma(1, wv[hp + 1])
        pend_st = None
        for cstep in range(TC + 1):
            if cstep < TC:
                _, t_v1p = mix_state_chunk(cstep, hp, cstep % 2, tok_k, tok_v,
                                           update_state=False)
                this_st = (cstep % 2, t_v1p)
            else:
                this_st = None
            if pend_st is not None:
                parp, t_v1p = pend_st
                bs = 5
                S.wait(pe, cA, t_v1p)
                bank_wait(bs)
                cP.tick(pe.matmul(pb[bs][:, 0:128], ktm[parp][:, :],
                                  vaug[parp][:, 0:128], start=True, stop=True))
                t_s = cP.tick(pe.matmul(pb[bs][:, 128:256], ktm[parp][:, :],
                                        vaug[parp][:, 128:256], start=True, stop=True))
                war["ktm"][parp] = max(war["ktm"][parp], t_s)
                war["vaug"][parp] = max(war["vaug"][parp], t_s)
                gh0, gh1 = 2 * hp, 2 * hp + 1
                S.wait(ve, cP, t_s)
                vop(ve.tensor_add(s_run[0:64, gh0, 0:128], s_run[0:64, gh0, 0:128],
                                  pb[bs][0:64, 0:128]))
                t_up = vop(ve.tensor_add(s_run[64:128, gh1, 0:128],
                                         s_run[64:128, gh1, 0:128],
                                         pb[bs][64:128, 128:256]))
                bank_used(bs, cV, t_up)
            pend_st = this_st

    vop(ve.tensor_scalar_mul(s_run[:, :, :], s_run[:, :, :], gate_s[:, 0:1]))
    if cfg.get("stop") == "prefix":
        return probe_exit()

    barrier()
    tok_xown = sdma(xbuf[:, :, :], x_own)
    S.wait(ae, dX, tok_xown)
    S.wait(ve, dX, tok_xown)
    for j in range(TC):
        emit_ln_chunk(xbuf[:, j, :], j * 128, (0, 1))

    dR = [Ctr(nc, f"dR{i}") for i in range(3)]
    dump_tok = {}
    pend_q = w_dma(0, wq[0])
    pend_k = w_dma(1, wk[0])
    t_outlast = 0
    for hp in range(HP):
        tok_hi, tok_lo = w_split(0, pend_q)
        tok_q = emit_proj(qb, True, 0, tok_hi, tok_lo, (2, 3, 0))
        tok_hi, tok_lo = w_split(1, pend_k)
        tok_k = emit_proj(kb, True, 1, tok_hi, tok_lo, (2, 3, 0))
        pend_v = w_dma(0, wv[hp])
        tok_hi, tok_lo = w_split(0, pend_v)
        tok_v = emit_proj(vb, False, 0, tok_hi, tok_lo, (2, 3, 0))
        pend_wo = w_dma(1, wo[hp])

        pend_mix = None
        for cstep in range(TC + 1):
            if cstep < TC:
                c = cstep
                cpar = c % 2
                par = c % 2
                t_ktm, t_v1 = mix_state_chunk(c, hp, par, tok_k, tok_v, update_state=False)
                S.wait(ae, cV, tok_q)
                S.wait(ae, cP, war["pad"][cpar])
                aop(ae.activation(qpad[0][cpar][0:64, :], qb[0:64, c * 128:(c + 1) * 128],
                                  AF.Copy))
                aop(ae.activation(kpad[0][cpar][0:64, :], kb[0:64, c * 128:(c + 1) * 128],
                                  AF.Copy))
                aop(ae.activation(qpad[1][cpar][64:128, :],
                                  qb[64:128, c * 128:(c + 1) * 128], AF.Copy))
                t_kp = aop(ae.activation(kpad[1][cpar][64:128, :],
                                         kb[64:128, c * 128:(c + 1) * 128], AF.Copy))
                this_mix = (c, cpar, par, t_kp, t_v1)
            else:
                this_mix = None
            if pend_mix is not None:
                c, cpar, par, t_kp, t_v1 = pend_mix
                b_at = 2 + cpar
                b_o = cpar
                b_s = 5
                gh0, gh1 = 2 * hp, 2 * hp + 1
                S.wait(pe, cA, t_kp)
                bank_wait(b_at)
                cP.tick(pe.matmul(pb[b_at][:, 0:128], kpad[0][cpar][:, :],
                                  qpad[0][cpar][:, :], start=True, stop=True))
                t_at = cP.tick(pe.matmul(pb[b_at][:, 128:256], kpad[1][cpar][:, :],
                                         qpad[1][cpar][:, :], start=True, stop=True))
                S.wait(ve, cP, max(t_at, war["am2"][cpar]))
                t_am = vop(ve.tensor_mul(am2_sb[cpar][:, :], pb[b_at][:, 0:256], um_s[:, :]))
                bank_used(b_at, cV, t_am)
                S.wait(pe, cV, t_am)
                S.wait(pe, cA, t_v1)
                bank_wait(b_o)
                cP.tick(pe.matmul(pb[b_o][:, 0:128], am2_sb[cpar][:, 0:128],
                                  vaug[par][:, 0:128], start=True, stop=True))
                t_o1 = cP.tick(pe.matmul(pb[b_o][:, 128:256], am2_sb[cpar][:, 128:256],
                                         vaug[par][:, 128:256], start=True, stop=True))
                S.wait(pe, cV, war["sadd"])
                cP.tick(pe.matmul(pb[b_o][:, 256:384], qpad[0][cpar][:, :],
                                  s_run[:, gh0, :], start=True, stop=True))
                t_o2 = cP.tick(pe.matmul(pb[b_o][:, 384:512], qpad[1][cpar][:, :],
                                         s_run[:, gh1, :], start=True, stop=True))
                war["pad"][cpar] = t_o2
                war["am2"][cpar] = t_o1
                war["vaug"][par] = max(war["vaug"][par], t_o1)
                bank_wait(b_s)
                cP.tick(pe.matmul(pb[b_s][:, 0:128], ktm[par][:, :], vaug[par][:, 0:128],
                                  start=True, stop=True))
                t_s = cP.tick(pe.matmul(pb[b_s][:, 128:256], ktm[par][:, :],
                                        vaug[par][:, 128:256], start=True, stop=True))
                war["ktm"][par] = max(war["ktm"][par], t_s)
                war["vaug"][par] = max(war["vaug"][par], t_s)
                S.wait(ve, cP, t_s)
                vop(ve.tensor_add(s_run[0:64, gh0, 0:128], s_run[0:64, gh0, 0:128],
                                  pb[b_s][0:64, 0:128]))
                t_up = vop(ve.tensor_add(s_run[64:128, gh1, 0:128],
                                         s_run[64:128, gh1, 0:128],
                                         pb[b_s][64:128, 128:256]))
                bank_used(b_s, cV, t_up)
                war["sadd"] = t_up
                S.wait(ve, cA, t_outlast)
                t_cp1 = vop(ve.tensor_copy(numsb[:, 0:256], pb[b_o][:, 0:256]))
                vop(ve.scalar_tensor_tensor(densb[:, 0:2], pb[b_o][:, 320:449:128],
                                            ATTN_EPS, numsb[:, 64:193:128],
                                            ALU.add, ALU.add))
                vop(ve.reciprocal(densb[:, 2:4], densb[:, 0:2]))
                t_n = vop(ve.tensor_add(numsb[:, 0:256], numsb[:, 0:256],
                                        pb[b_o][:, 256:512]))
                bank_used(b_o, cV, t_n)
                S.wait(ae, cV, t_n)
                aop(ae.activation(attn_tm[:, c, 0:64], numsb[:, 0:64], AF.Copy,
                                  scale=densb[:, 2:3]))
                t_outlast = aop(ae.activation(attn_tm[:, c, 64:128], numsb[:, 128:192], AF.Copy,
                                              scale=densb[:, 3:4]))
            pend_mix = this_mix
        tok_hi_o, tok_lo_o = w_split(1, pend_wo)
        t_wo_last = 0
        for j in range(TC):
            b_tr = 6
            S.wait(pe, cA, t_outlast)
            bank_wait(b_tr)
            t_acc_last = 0
            t_tr = cP.tick(pe.transpose(pb[b_tr][:, 0:128], attn_tm[:, j, :], id_s[:, :]))
            S.wait(ae, cP, t_tr)
            t_ah = aop(ae.activation(ah_sb[:, :], pb[b_tr][:, 0:128], AF.Copy))
            S.wait(ve, cP, t_tr)
            S.wait(ve, cA, t_ah)
            t_al = vop(ve.tensor_sub(al_sb[:, :], pb[b_tr][:, 0:128], ah_sb[:, :]))
            bank_used(b_tr, cV, t_al)
            for fh in range(NFH):
                b = (0, 1)[fh % 2]
                S.wait(pe, cA, max(t_ah, tok_hi_o))
                S.wait(pe, cV, max(t_al, tok_lo_o))
                bank_wait(b)
                pe.matmul(pb[b][:, 0:W2W], ah_sb[:, :], wsh[1][:, fh * W2W:(fh + 1) * W2W],
                          start=True, stop=False)
                pe.matmul(pb[b][:, 0:W2W], ah_sb[:, :], wsl[1][:, fh * W2W:(fh + 1) * W2W],
                          start=False, stop=False)
                i3 = pe.matmul(pb[b][:, 0:W2W], al_sb[:, :], wsh[1][:, fh * W2W:(fh + 1) * W2W],
                               start=False, stop=True)
                t_wo = cP.tick(i3)
                t_wo_last = t_wo
                S.wait(ve, cP, t_wo)
                t_acc = vop(ve.scalar_tensor_tensor(xbuf[:, j, fh * W2W:(fh + 1) * W2W],
                                                    pb[b][:, 0:W2W], 1.0,
                                                    xbuf[:, j, fh * W2W:(fh + 1) * W2W],
                                                    ALU.mult, ALU.add))
                bank_used(b, cV, t_acc)
                t_acc_last = t_acc
            if hp == HP - 1:
                q = j % 3
                eng_q = (sy, ae, gp)[q]
                S.wait(eng_q, dR[q], dR[q].n)
                S.wait(eng_q, cV, t_acc_last)
                eng_q.dma_start(out=r2h[:, j, :], in_=xbuf[:, j, :]).then_inc(dR[q].h, 16)
                dR[q].n += 16
                dump_tok[j] = (q, dR[q].n)
        war["split"][1] = cP.n
        if hp + 1 < HP:
            pend_q = w_dma(0, wq[hp + 1])
            pend_k = w_dma(1, wk[hp + 1])


    if cfg.get("stop") == "own":
        return probe_exit()

    barrier()
    for eng in ALL_ENG:
        for q in range(3):
            S.wait(eng, dR[q], dR[q].n)
        S.wait(eng, dI, dI.n)
    attn_es.close()

    r2t = sbuf("r2t", [128, FTC, D])
    hst = sbuf("hst", [128, FT])
    hh = [sbuf(f"hh{i}", [128, FT], BF16) for i in range(2)]
    hl = [sbuf(f"hl{i}", [128, FT], BF16) for i in range(2)]
    w2stg = [sbuf(f"w2stg{i}", [128, D]) for i in range(2)]
    w2h = [sbuf(f"w2h{i}", [128, D], BF16) for i in range(2)]
    w2l = [sbuf(f"w2l{i}", [128, D], BF16) for i in range(2)]
    lg = sbuf("lg", [128, 8])
    lgm1 = sbuf("lgm1", [128, 8])
    lgk = sbuf("lgk", [128, 8])
    lgm2 = sbuf("lgm2", [128, 8])
    lgsc = sbuf("lgsc", [128, 8])
    combine = sbuf("combine", [128, TC, 8])
    comb_pad = sbuf("comb_pad", [128, 128])
    cth = sbuf("cth", [8, TC, 128], BF16)
    ctl = sbuf("ctl", [8, TC, 128], BF16)
    ct_f = sbuf("ct_f", [8, 128])
    slt_s = sbuf("slt_s", [128, 128], BF16)
    onesb_s = sbuf("onesb_s", [128, 128], BF16)
    offs_s = sbuf("offs_s", [128, 8])
    capd_s = sbuf("capd_s", [128, 1])
    iota0_s = sbuf("iota0_s", [128, TC])
    iop_s = sbuf("iop_s", [128, 1])
    maskp = sbuf("maskp", [128, 128])
    masks_bf = sbuf("masks_bf", [128, 128], BF16)
    base_sb = sbuf("base_sb", [128, 8])
    posx = sbuf("posx", [128, 8])
    post = sbuf("post", [128, 8])
    posf = sbuf("posf", [128, 2])
    posu = sbuf("posu", [128, 2], U32)
    xg = [sbuf(f"xg{i}", [128, CAPS, XCW]) for i in range(2)]
    xgh = sbuf("xgh", [128, KC, FT], BF16)
    xgl = sbuf("xgl", [128, KC, FT], BF16)
    ysb = sbuf("ysb", [128, FT // 128, D])
    idxu = [sbuf(f"idxu{i}", [128, CAPS], U32) for i in range(2)]
    dSc = Ctr(nc, "dSc")
    sdma(slt_s[:, :], slt_in)
    sdma(onesb_s[:, :], onesb_in)
    sdma(offs_s[:, :], offs_in)
    sdma(capd_s[:, :], capd_in)
    sdma(iota0_s[:, :], iota0_in)
    tok_iot = sdma(iop_s[:, :], iop_in)
    vop(ve.memset(comb_pad[:, :], 0.0))
    vop(ve.memset(maskp[:, :], 0.0))
    vop(ve.memset(base_sb[:, :], 0.0))
    vop(ve.memset(xg[0][:, :, :], 0.0))
    vop(ve.memset(xg[1][:, :, :], 0.0))

    war2 = {"w2stage": [[(None, 0)], [(None, 0)]], "w2split": [0, 0], "h": [0, 0]}

    def w2_dma(par, src_ap):
        eng = (gp, ae)[par]
        for ctr, tok in war2["w2stage"][par]:
            if ctr is not None:
                S.wait(eng, ctr, tok)
        S.wait(eng, dG[par], dG[par].n)
        eng.dma_start(out=w2stg[par][:, :], in_=src_ap).then_inc(dG[par].h, 16)
        dG[par].n += 16
        return dG[par].n

    def w2_split(par, tok_dma):
        S.wait(ae, dG[par], tok_dma)
        S.wait(ae, cP, war2["w2split"][par])
        t_hi = aop(ae.activation(w2h[par][:, :], w2stg[par][:, :], AF.Copy))
        S.wait(ve, dG[par], tok_dma)
        S.wait(ve, cP, war2["w2split"][par])
        S.wait(ve, cA, t_hi)
        t_lo = vop(ve.tensor_sub(w2l[par][:, :], w2stg[par][:, :], w2h[par][:, :]))
        war2["w2stage"][par] = [(cA, t_hi), (cV, t_lo)]
        return t_hi, t_lo

    tok_xc_init = dI.n

    NG = E * NCS * HC

    def g_src(g):
        e = g // (NCS * HC)
        m = g % HC
        return e, m

    tok_w1 = {}
    tok_w2 = {}
    for g0 in range(min(2, NG)):
        e0, m0 = g_src(g0)
        tok_w1[g0] = w_dma(g0 % 2, w1[e0, m0])
        tok_w2[g0] = w2_dma(g0 % 2, w2[e0, m0])

    last_out_dma = 0
    tok_r2t_consumed = [(None, 0)]
    for ft in range(NFT):
        for ctr, tok in tok_r2t_consumed:
            if ctr is not None:
                S.wait(sy, ctr, tok)
        S.wait(sy, dO, last_out_dma)
        tok_r2t = sdma(r2t[:, :, :], r2h[:, ft * FTC:(ft + 1) * FTC, :])
        S.wait(ae, dX, tok_r2t)
        S.wait(ve, dX, tok_r2t)
        for jj in range(FTC):
            jabs = ft * FTC + jj
            col0 = jabs * 128
            t_ap = emit_ln_chunk(r2t[:, jj, :], col0, (4, 5))
            first = True
            i = None
            for k in range(KC):
                for p in range(3):
                    lhsT = (lnh if p != 2 else lnl)[:, k, col0:col0 + 128]
                    rhs = (wr_h if p != 1 else wr_l)[:, k, :]
                    if first:
                        S.wait(pe, cV, cV.n)
                        S.wait(pe, cA, cA.n)
                        bank_wait(6)
                    i = pe.matmul(pb[6][:, 0:8], lhsT, rhs, start=first,
                                  stop=(k == KC - 1 and p == 2))
                    first = False
            t_lgmm = cP.tick(i)
            S.wait(ve, cP, t_lgmm)
            t_lg = vop(ve.tensor_copy(lg[:, :], pb[6][:, 0:8]))
            bank_used(6, cV, t_lg)
            vop(ve.reduce_max(lgsc[:, 0:1], lg[:, :], axis=AX.X))
            vop(ve.tensor_scalar(lgm1[:, :], lg[:, :], lgsc[:, 0:1], None, ALU.is_equal))
            vop(ve.scalar_tensor_tensor(lgk[:, :], lgm1[:, :], -1e30, lg[:, :],
                                        ALU.mult, ALU.add))
            vop(ve.reduce_max(lgsc[:, 1:2], lgk[:, :], axis=AX.X))
            vop(ve.tensor_scalar(lgm2[:, :], lgk[:, :], lgsc[:, 1:2], None, ALU.is_equal))
            t_d = vop(ve.tensor_sub(lgsc[:, 2:3], lgsc[:, 1:2], lgsc[:, 0:1]))
            S.wait(ae, cV, t_d)
            t_e = aop(ae.activation(lgsc[:, 3:4], lgsc[:, 2:3], AF.Exp))
            S.wait(ve, cA, t_e)
            vop(ve.tensor_scalar_add(lgsc[:, 4:5], lgsc[:, 3:4], 1.0))
            vop(ve.reciprocal(lgsc[:, 5:6], lgsc[:, 4:5]))
            vop(ve.tensor_mul(lgsc[:, 6:7], lgsc[:, 3:4], lgsc[:, 5:6]))
            vop(ve.tensor_scalar(combine[:, jabs, :], lgm1[:, :], lgsc[:, 5:6], None, ALU.mult))
            t_c2 = vop(ve.scalar_tensor_tensor(combine[:, jabs, :], lgm2[:, :], lgsc[:, 6:7],
                                               combine[:, jabs, :], ALU.mult, ALU.add))
            t_cp = vop(ve.tensor_copy(comb_pad[:, 0:8], combine[:, jabs, :]))
            S.wait(pe, cV, t_cp)
            bank_wait(7)
            t_ct = cP.tick(pe.transpose(pb[7][:, 0:128], comb_pad[:, :], id_s[:, :]))
            S.wait(ve, cP, t_ct)
            t_ctf = vop(ve.tensor_copy(ct_f[:, :], pb[7][0:8, 0:128]))
            bank_used(7, cV, t_ctf)
            t_cth = vop(ve.tensor_copy(cth[:, jabs, :], ct_f[:, :]))
            vop(ve.tensor_sub(ctl[:, jabs, :], ct_f[:, :], cth[:, jabs, :]))
            S.wait(ve, cP, war["mbf_pe"])
            vop(ve.tensor_add(maskp[:, 0:8], lgm1[:, :], lgm2[:, :]))
            t_mb = vop(ve.tensor_copy(masks_bf[:, :], maskp[:, :]))
            S.wait(pe, cV, t_mb)
            bank_wait(3)
            t_pf = cP.tick(pe.matmul(pb[3][:, 0:128], slt_s[:, :], masks_bf[:, :],
                                     start=True, stop=True))
            S.wait(ve, cP, t_pf)
            t_px = vop(ve.tensor_add(posx[:, :], base_sb[:, :], pb[3][:, 0:8]))
            bank_used(3, cV, t_px)
            vop(ve.tensor_scalar(posx[:, :], posx[:, :], capd_s[:, 0:1], None, ALU.min))
            vop(ve.tensor_add(posx[:, :], posx[:, :], offs_s[:, :]))
            vop(ve.tensor_mul(post[:, :], lgm1[:, :], posx[:, :]))
            vop(ve.reduce_sum(posf[:, 0:1], post[:, :], axis=AX.X))
            vop(ve.tensor_mul(post[:, :], lgm2[:, :], posx[:, :]))
            vop(ve.reduce_sum(posf[:, 1:2], post[:, :], axis=AX.X))
            S.wait(ve, dSc, dSc.n)
            t_pu = vop(ve.tensor_copy(posu[:, :], posf[:, :]))
            bank_wait(7)
            t_tt = cP.tick(pe.matmul(pb[7][:, 0:128], onesb_s[:, :], masks_bf[:, :],
                                     start=True, stop=True))
            war["mbf_pe"] = t_tt
            S.wait(ve, cP, t_tt)
            t_bu = vop(ve.tensor_add(base_sb[:, :], base_sb[:, :], pb[7][:, 0:8]))
            bank_used(7, cV, t_bu)
            t_g8 = vop(ve.tensor_copy(lnx[:, D:D + 8], combine[:, jabs, :]))
            t_gi = vop(ve.tensor_copy(lnx[:, D + 8:D + 9], iota0_s[:, jabs:jabs + 1]))
            S.wait(gp, cV, cV.n)
            S.wait(gp, dI, tok_xc_init)
            for sl in range(2):
                S.wait(gp, dSc, dSc.n)
                gp.indirect_dma_start(
                    xc, bass.IndirectOffsetOnAxis(ap=posu[:, sl:sl + 1], axis=0),
                    lnx[:, 0:XCW], None).then_inc(dSc.h, 16)
                dSc.n += 16
            war["lntm_dma"] = (dSc, dSc.n)
        S.wait(ve, dO, last_out_dma)
        for jj in range(FTC):
            jabs = ft * FTC + jj
            for fh in range(NFH):
                b = 6 + ((jj * NFH + fh) % 2)
                S.wait(pe, cV, cV.n)
                bank_wait(b)
                pe.matmul(pb[b][:, 0:W2W], cth[:, jabs, :], b2_h[:, fh * W2W:(fh + 1) * W2W],
                          start=True, stop=False)
                pe.matmul(pb[b][:, 0:W2W], cth[:, jabs, :], b2_l[:, fh * W2W:(fh + 1) * W2W],
                          start=False, stop=False)
                i3 = pe.matmul(pb[b][:, 0:W2W], ctl[:, jabs, :], b2_h[:, fh * W2W:(fh + 1) * W2W],
                               start=False, stop=True)
                t_b = cP.tick(i3)
                S.wait(ve, cP, t_b)
                t_ev = vop(ve.scalar_tensor_tensor(
                    r2t[:, jj, fh * W2W:(fh + 1) * W2W], pb[b][:, 0:W2W], 1.0,
                    r2t[:, jj, fh * W2W:(fh + 1) * W2W], ALU.mult, ALU.add))
                bank_used(b, cV, t_ev)
        S.wait(gp, cV, cV.n)
        S.wait(gp, dO, dO.n)
        gp.dma_start(out=y_acc[ft * FT:(ft + 1) * FT, :].rearrange("(c p) d -> p c d", p=128),
                     in_=r2t[:, :, :]).then_inc(dO.h, 16)
        dO.n += 16
        last_out_dma = dO.n
        tok_r2t_consumed = [(cA, cA.n), (cV, cV.n)]

    if cfg.get("ffn_mode") == "router":
        return probe_exit(src=r2t[:, :, :], dst=y_pc[:, 0:FTC, :], close_attn=False)

    tok_comp_done = dSc.n
    dS2 = [Ctr(nc, "dS2a"), Ctr(nc, "dS2b")]
    war3 = {"xg_pe": [0, 0], "xg_dve": [0, 0],
            "idx_ds2": {0: (0, 0), 1: (0, 0)},
            "ysb_tok": {}}
    tok_xr = {}

    def load_region(e):
        pe_ = e % 2
        S.wait(sy, dI, tok_xc_init)
        S.wait(sy, dSc, tok_comp_done)
        S.wait(sy, cP, war3["xg_pe"][pe_])
        S.wait(sy, cV, war3["xg_dve"][pe_])
        tok_xr[e] = sdma(xg[pe_][:, :, :],
                         xc[e * (CAP + 128):e * (CAP + 128) + CAP, :]
                         .rearrange("(q p) c -> p q c", p=128))

    load_region(0)
    g = 0
    for e in range(E):
        par_e = e % 2
        xge = xg[par_e]
        idxe = idxu[par_e]
        if e + 1 < E:
            load_region(e + 1)
        S.wait(ve, dX, tok_xr[e])
        tk0, tk1 = war3["idx_ds2"][par_e]
        S.wait(ve, dS2[0], tk0)
        S.wait(ve, dS2[1], tk1)
        t_idx = vop(ve.tensor_copy(idxe[:, :], xge[:, :, D + 8]))
        ns = 0
        for csub in range(NCS):
            for sl in range(CSUB // 128):
                slot = csub * (CSUB // 128) + sl
                for k in range(KC):
                    b = 6 + (k % 2)
                    S.wait(pe, dX, tok_xr[e])
                    S.wait(pe, cV, cV.n)
                    bank_wait(b)
                    t_tr = cP.tick(pe.transpose(pb[b][:, 0:128],
                                                xge[:, slot, k * 128:(k + 1) * 128],
                                                id_s[:, :]))
                    S.wait(ae, cP, t_tr)
                    t_hi = aop(ae.activation(xgh[:, k, sl * 128:(sl + 1) * 128],
                                             pb[b][:, 0:128], AF.Copy))
                    S.wait(ve, cP, t_tr)
                    S.wait(ve, cA, t_hi)
                    t_lo = vop(ve.tensor_sub(xgl[:, k, sl * 128:(sl + 1) * 128],
                                             pb[b][:, 0:128],
                                             xgh[:, k, sl * 128:(sl + 1) * 128]))
                    bank_used(b, cV, t_lo)
                    bank_used(b, cA, t_hi)
            war3["xg_pe"][par_e] = cP.n
            t_split = cV.n
            y_stop = [0] * NBY
            pend_w2 = None
            for mstep in range(HC + 1):
                if mstep < HC:
                    m = mstep
                    par = g % 2
                    tok_hi, tok_lo = w_split(par, tok_w1[g])
                    bw1 = 4 + (m % 2)
                    S.wait(pe, cA, tok_hi)
                    S.wait(pe, cV, max(tok_lo, t_split))
                    bank_wait(bw1)
                    tok_mm = mm3(bw1, par, CSUB, 0, src_h=xgh, src_l=xgl)
                    war["split"][par] = tok_mm
                    tok2_hi, tok2_lo = w2_split(par, tok_w2[g])
                    if g + 2 < NG:
                        e2, m2 = g_src(g + 2)
                        tok_w1[g + 2] = w_dma(par, w1[e2, m2])
                        tok_w2[g + 2] = w2_dma(par, w2[e2, m2])
                    hpar = m % 2
                    S.wait(ae, cP, max(tok_mm, war2["h"][hpar]))
                    t_g = aop(ae.activation(hst[:, 0:CSUB], pb[bw1][:, 0:CSUB], AF.Gelu,
                                            bias=b1_s[:, e, m:m + 1]))
                    bank_used(bw1, cA, t_g)
                    S.wait(ve, cA, t_g)
                    S.wait(ve, cP, war2["h"][hpar])
                    t_hh = vop(ve.tensor_copy(hh[hpar][:, :], hst[:, 0:CSUB]))
                    t_hl = vop(ve.tensor_sub(hl[hpar][:, :], hst[:, 0:CSUB], hh[hpar][:, :]))
                    this_w2 = (m, par, hpar, t_hh, t_hl, tok2_hi, tok2_lo)
                    g += 1
                else:
                    this_w2 = None
                if pend_w2 is not None:
                    m, par, hpar, t_hh, t_hl, tok2_hi, tok2_lo = pend_w2
                    S.wait(pe, cA, tok2_hi)
                    S.wait(pe, cV, max(t_hh, t_hl, tok2_lo))
                    i = None
                    for j in range(CSUB // 128):
                        for fh in range(NFH):
                            by = j * NFH + fh
                            if m == 0:
                                bank_wait(by)
                            for p in range(3):
                                lhsT = (hh if p != 2 else hl)[hpar][:, j * 128:(j + 1) * 128]
                                rhs = (w2h if p != 1 else w2l)[par][:, fh * W2W:(fh + 1) * W2W]
                                i = pe.matmul(pb[by][:, 0:W2W], lhsT, rhs,
                                              start=(m == 0 and p == 0),
                                              stop=(m == HC - 1 and p == 2))
                            if m == HC - 1:
                                y_stop[by] = cP.tick(i)
                    if m != HC - 1:
                        t_end = cP.tick(i)
                        war2["w2split"][par] = t_end
                        war2["h"][hpar] = t_end
                    else:
                        war2["w2split"][par] = y_stop[-1]
                        war2["h"][hpar] = y_stop[-1]
                pend_w2 = this_w2
            for j in range(CSUB // 128):
                slot = csub * (CSUB // 128) + j
                for fh in range(NFH):
                    by = j * NFH + fh
                    S.wait(ve, cP, y_stop[by])
                    if j in war3["ysb_tok"]:
                        pty, ptk = war3["ysb_tok"][j]
                        S.wait(ve, dS2[pty], ptk)
                    t_ev = vop(ve.tensor_scalar(
                        ysb[:, j, fh * W2W:(fh + 1) * W2W], pb[by][:, 0:W2W],
                        xge[:, slot, D + e:D + e + 1], None, ALU.mult))
                    bank_used(by, cV, t_ev)
            for j in range(CSUB // 128):
                slot = csub * (CSUB // 128) + j
                p = ns % 2
                S.wait(gp, cV, cV.n)
                S.wait(gp, dO, dO.n)
                if ns < 2:
                    S.wait(gp, dS2[0], dS2[0].n)
                    S.wait(gp, dS2[1], dS2[1].n)
                else:
                    S.wait(gp, dS2[p], dS2[p].n)
                gp.indirect_dma_start(
                    y_acc, bass.IndirectOffsetOnAxis(ap=idxe[:, slot:slot + 1], axis=0),
                    ysb[:, j, :], None,
                    compute_op=ALU.add).then_inc(dS2[p].h, 16)
                dS2[p].n += 16
                war3["ysb_tok"][j] = (p, dS2[p].n)
                ns += 1
        war3["xg_dve"][par_e] = cV.n
        war3["idx_ds2"][par_e] = (dS2[0].n, dS2[1].n)

    for ft in range(NFT):
        S.wait(sy, dSc, dSc.n)
        S.wait(sy, dS2[0], dS2[0].n)
        S.wait(sy, dS2[1], dS2[1].n)
        S.wait(sy, dO, dO.n)
        S.wait(sy, cV, cV.n)
        tok_l = sdma(r2t[:, :, :],
                     y_acc[ft * FT:(ft + 1) * FT, :].rearrange("(c p) d -> p c d", p=128))
        S.wait(gp, dX, tok_l)
        S.wait(gp, dO, dO.n)
        gp.dma_start(out=y_pc[:, ft * FTC:(ft + 1) * FTC, :], in_=r2t[:, :, :]).then_inc(dO.h, 16)
        dO.n += 16

    S.wait(gp, dO, dO.n)
    S.wait(gp, dSc, dSc.n)
    S.wait(gp, dS2[0], dS2[0].n)
    S.wait(gp, dS2[1], dS2[1].n)
    es.close()
    return nc


def _stage_inputs(inputs, cfg, core):
    D, T = cfg["D"], cfg["T"]
    TC = T // 128
    x = np.ascontiguousarray(np.asarray(inputs["L0_attn_x"], dtype=np.float32))
    b, half = core // 2, core % 2
    own = x[b, half * T:(half + 1) * T]
    pre = x[b, 0:T] if half == 1 else own
    shuffle = lambda a: np.ascontiguousarray(a.reshape(TC, 128, D).transpose(1, 0, 2))
    return {
        "x_own": shuffle(own),
        "x_pre": shuffle(pre),
        "gate": np.full((128, 1), float(half), np.float32),
    }


def _stage_weights(inputs, cfg):
    D, HID, E = cfg["D"], cfg["HID"], cfg["E"]
    KC, HC = D // 128, HID // 128
    HP = (D // 64) // 2
    f32 = lambda a: np.asarray(a, dtype=np.float32)
    Wq, Wk, Wv, Wo = (f32(inputs[f"L0_attn_W{n}"]) for n in "qkvo")
    W1, W2 = f32(inputs["L0_ffn_W1"]), f32(inputs["L0_ffn_W2"])
    b1, b2, Wr = f32(inputs["L0_ffn_b1"]), f32(inputs["L0_ffn_b2"]), f32(inputs["L0_ffn_Wr"])

    def lhsT_lay(W):
        a = W.reshape(HP, 128, KC, 128).transpose(0, 3, 2, 1)
        return np.ascontiguousarray(a.reshape(HP, 128, KC * 128))

    import ml_dtypes
    T, CAP, E_ = cfg["T"], cfg["CAP"], cfg["E"]
    TC = T // 128
    bfl = np.dtype(ml_dtypes.bfloat16)
    return {
        "slt_in": np.tril(np.ones((128, 128), np.float32), -1).T.astype(bfl),
        "onesb_in": np.ones((128, 128), bfl),
        "offs_in": np.tile((np.arange(8) * (CAP + 128)).astype(np.float32), (128, 1)),
        "capd_in": (CAP + np.arange(128, dtype=np.float32))[:, None].copy(),
        "iota0_in": np.ascontiguousarray(
            (np.arange(TC)[None, :] * 128 + np.arange(128)[:, None]).astype(np.float32)),
        "iop_in": np.arange(128, dtype=np.float32)[:, None].copy(),
        "xcid_in": (lambda ns: (lambda a: (a.__setitem__((slice(None), slice(None), 8),
                    (np.float32(T) + 128 * np.arange(ns, dtype=np.float32)[None, :]
                     + np.arange(128, dtype=np.float32)[:, None])), a)[1])(
                    np.zeros((128, ns, 9), np.float32)))(E * (CAP + 128) // 128),
        "zx_in": np.zeros((128, D), np.float32),
        "wq": lhsT_lay(Wq), "wk": lhsT_lay(Wk), "wv": lhsT_lay(Wv),
        "wo": np.ascontiguousarray(Wo.T.reshape(HP, 128, D)),
        "w1": np.ascontiguousarray(
            W1.reshape(E, KC, 128, HC, 128).transpose(0, 3, 2, 1, 4).reshape(E, HC, 128, KC * 128)),
        "w2": np.ascontiguousarray(W2.reshape(E, HC, 128, D)),
        "b1": np.ascontiguousarray(b1.reshape(E, HC, 128).transpose(2, 0, 1)),
        "b2": np.ascontiguousarray(b2),
        "wr": np.ascontiguousarray(Wr.reshape(KC, 128, 8).transpose(1, 0, 2)),
        "um": np.tile(np.triu(np.ones((128, 128), np.float32)), (1, 2)),
        "ident": np.eye(128, dtype=np.float32),
    }


def kernel(**inputs):
    cfg = FULL
    n_cores = 8
    weights = _stage_weights(inputs, cfg)
    in_maps = []
    for core in range(n_cores):
        m = dict(weights)
        m.update(_stage_inputs(inputs, cfg, core))
        in_maps.append(m)
    nc = build_program(cfg)
    res = run_bass_kernel_spmd(nc, in_maps, core_ids=list(range(n_cores)))
    T, D = cfg["T"], cfg["D"]
    out = np.empty((4, 4096, D), np.float32)
    for core in range(n_cores):
        b, half = core // 2, core % 2
        out[b, half * T:(half + 1) * T] = np.asarray(res.results[core]["y"])
    return out


# version 24
# speedup vs baseline: 1.2349x; 1.0031x over previous
import contextlib
import numpy as np
import concourse.bass as bass
import concourse.mybir as mybir
from concourse.bass_utils import run_bass_kernel_spmd

from concourse import library_config

F32 = mybir.dt.float32
BF16 = mybir.dt.bfloat16
I16 = mybir.dt.int16
U32 = mybir.dt.uint32
AF = mybir.ActivationFunctionType
ALU = mybir.AluOpType
AX = mybir.AxisListType

FULL = dict(D=1024, T=2048, HID=4096, E=8, FT=256, CAP=768)
MINI = dict(D=256, T=256, HID=512, E=8, FT=256, CAP=256)

LN_EPS = 1e-5
ATTN_EPS = 1e-8


class Ctr:
    def __init__(self, nc, name):
        self.h = nc.alloc_semaphore(name=name)
        self.n = 0

    def tick(self, instr, n=1):
        instr.then_inc(self.h, n)
        self.n += n
        return self.n


class Syncer:
    def __init__(self):
        self.wm = {}

    def wait(self, eng, ctr, val):
        if not val or val <= 0:
            return
        k = (id(eng), id(ctr))
        if self.wm.get(k, 0) >= val:
            return
        eng.wait_ge(ctr.h, val)
        self.wm[k] = val


def build_program(cfg):
    D, T, HID, E, FT = (cfg[k] for k in ("D", "T", "HID", "E", "FT"))
    CAP = cfg["CAP"]
    CSUB = FT
    NCS = CAP // CSUB
    CAPS = CAP // 128
    KC = D // 128
    NH = D // 64
    HP = NH // 2
    TC = T // 128
    HC = HID // 128
    FTC = FT // 128
    NFT = T // FT
    PT = min(512, T)
    W2W = min(512, D)
    NFH = D // W2W
    NBY = FTC * NFH
    assert NBY <= 4

    nc = bass.Bass("TRN2", target_bir_lowering=False, debug=False)
    S = Syncer()

    x_own = nc.dram_tensor("x_own", [128, TC, D], F32, kind="ExternalInput").ap()
    x_pre = nc.dram_tensor("x_pre", [128, TC, D], F32, kind="ExternalInput").ap()
    gate = nc.dram_tensor("gate", [128, 1], F32, kind="ExternalInput").ap()
    wq = nc.dram_tensor("wq", [HP, 128, KC * 128], F32, kind="ExternalInput").ap()
    wk = nc.dram_tensor("wk", [HP, 128, KC * 128], F32, kind="ExternalInput").ap()
    wv = nc.dram_tensor("wv", [HP, 128, KC * 128], F32, kind="ExternalInput").ap()
    wo = nc.dram_tensor("wo", [HP, 128, D], F32, kind="ExternalInput").ap()
    w1 = nc.dram_tensor("w1", [E, HC, 128, KC * 128], F32, kind="ExternalInput").ap()
    w2 = nc.dram_tensor("w2", [E, HC, 128, D], F32, kind="ExternalInput").ap()
    b1 = nc.dram_tensor("b1", [128, E, HC], F32, kind="ExternalInput").ap()
    b2 = nc.dram_tensor("b2", [8, D], F32, kind="ExternalInput").ap()
    wr = nc.dram_tensor("wr", [128, KC, 8], F32, kind="ExternalInput").ap()
    um = nc.dram_tensor("um", [128, 256], F32, kind="ExternalInput").ap()
    ident = nc.dram_tensor("ident", [128, 128], F32, kind="ExternalInput").ap()
    slt_in = nc.dram_tensor("slt_in", [128, 128], BF16, kind="ExternalInput").ap()
    onesb_in = nc.dram_tensor("onesb_in", [128, 128], BF16, kind="ExternalInput").ap()
    offs_in = nc.dram_tensor("offs_in", [128, 8], F32, kind="ExternalInput").ap()
    capd_in = nc.dram_tensor("capd_in", [128, 1], F32, kind="ExternalInput").ap()
    iota0_in = nc.dram_tensor("iota0_in", [128, TC], F32, kind="ExternalInput").ap()
    iop_in = nc.dram_tensor("iop_in", [128, 1], F32, kind="ExternalInput").ap()
    xcid_in = nc.dram_tensor("xcid_in", [128, E * (CAP + 128) // 128, 9], F32, kind="ExternalInput").ap()
    zx_in = nc.dram_tensor("zx_in", [128, D], F32, kind="ExternalInput").ap()
    y = nc.dram_tensor("y", [T, D], F32, kind="ExternalOutput").ap()
    y_pc = y.rearrange("(c p) d -> p c d", p=128)
    r2h = nc.dram_tensor("r2h", [128, TC, D], F32, kind="Internal").ap()
    XCR = E * (CAP + 128)
    XCW = D + 64
    YAR = T + XCR + 256
    xc = nc.dram_tensor("xc", [XCR, XCW], F32, kind="Internal").ap()
    y_acc = nc.dram_tensor("y_acc", [YAR, D], F32, kind="Internal").ap()

    es = contextlib.ExitStack()
    sbuf = lambda name, shape, dt=F32: es.enter_context(nc.sbuf_tensor(name, shape, dt))

    cP = Ctr(nc, "cP")
    cV = Ctr(nc, "cV")
    cA = Ctr(nc, "cA")
    dX = Ctr(nc, "dX")
    dW = [Ctr(nc, "dW0"), Ctr(nc, "dW1")]
    dG = [Ctr(nc, "dG0"), Ctr(nc, "dG1")]
    dO = Ctr(nc, "dO")

    pe, ve, ae, sy, gp = nc.tensor, nc.vector, nc.scalar, nc.sync, nc.gpsimd
    ALL_ENG = (pe, ve, ae, sy, gp)

    def vop(instr):
        t = cV.tick(instr)
        S.wait(ve, cV, t)
        return t

    def aop(instr):
        t = cA.tick(instr)
        S.wait(ae, cA, t)
        return t

    def sdma(out, in_):
        S.wait(sy, dX, dX.n)
        sy.dma_start(out=out, in_=in_).then_inc(dX.h, 16)
        dX.n += 16
        return dX.n

    def barrier():
        for ctr in (cP, cV, cA):
            for eng in ALL_ENG:
                S.wait(eng, ctr, ctr.n)

    um_s = sbuf("um_s", [128, 256])
    id_s = sbuf("id_s", [128, 128])
    gate_s = sbuf("gate_s", [128, 1])
    wr_f = sbuf("wr_f", [128, KC, 8])
    wr_h = sbuf("wr_h", [128, KC, 8], BF16)
    wr_l = sbuf("wr_l", [128, KC, 8], BF16)
    b1_s = sbuf("b1_s", [128, E, HC])
    b2_h = sbuf("b2_h", [8, D], BF16)
    b2_l = sbuf("b2_l", [8, D], BF16)
    lnx = sbuf("lnx", [128, max(D + 64, max(D, 2 * min(512, T)) + min(512, T))])
    lntm = lnx[:, 0:D]
    lnsc = sbuf("lnsc", [128, 16])
    lnh = sbuf("lnh", [128, KC, T], BF16)
    lnl = sbuf("lnl", [128, KC, T], BF16)
    wstg = [sbuf(f"wstg{i}", [128, KC * 128]) for i in range(2)]
    wsh = [sbuf(f"wsh{i}", [128, KC * 128], BF16) for i in range(2)]
    wsl = [sbuf(f"wsl{i}", [128, KC * 128], BF16) for i in range(2)]

    pb = [es.enter_context(nc.psum_tensor(f"pb{i}", [128, 512], F32)) for i in range(8)]
    bank_free = {i: [] for i in range(8)}

    def bank_wait(b):
        for ctr, tok in bank_free[b]:
            S.wait(pe, ctr, tok)
        bank_free[b] = []

    def bank_used(b, ctr, tok):
        bank_free[b].append((ctr, tok))

    sdma(um_s[:, :], um)
    sdma(id_s[:, :], ident)
    sdma(gate_s[:, :], gate)
    sdma(wr_f[:, :, :], wr)
    sdma(b1_s[:, :, :], b1)
    tok_consts = sdma(lnx[0:8, 0:D], b2)
    S.wait(ae, dX, tok_consts)
    t_wrh = aop(ae.activation(wr_h[:, :, :], wr_f[:, :, :], AF.Copy))
    t_b2h = aop(ae.activation(b2_h[:, :], lnx[0:8, 0:D], AF.Copy))
    S.wait(ve, dX, tok_consts)
    S.wait(ve, cA, t_b2h)
    t_wrl = vop(ve.tensor_sub(wr_l[:, :, :], wr_f[:, :, :], wr_h[:, :, :]))
    t_b2l = vop(ve.tensor_sub(b2_l[:, :], lnx[0:8, 0:D], b2_h[:, :]))

    attn_es = contextlib.ExitStack()
    asb = lambda name, shape, dt=F32: attn_es.enter_context(nc.sbuf_tensor(name, shape, dt))
    xbuf = asb("xbuf", [128, TC, D])
    qb = asb("qb", [128, T])
    kb = asb("kb", [128, T])
    vb = asb("vb", [128, T])
    s_run = asb("s_run", [128, NH, 128])
    ktm = [asb(f"ktm{i}", [128, 128]) for i in range(2)]
    vaug = [asb(f"vaug{i}", [128, 256]) for i in range(2)]
    qpad = [[asb(f"qpad{i}_{j}", [128, 128]) for i in range(2)] for j in range(2)]
    kpad = [[asb(f"kpad{i}_{j}", [128, 128]) for i in range(2)] for j in range(2)]
    am2_sb = [asb(f"am2_{j}", [128, 256]) for j in range(2)]
    attn_tm = asb("attn_tm", [128, TC, 128])
    ah_sb = asb("ah_sb", [128, 128], BF16)
    al_sb = asb("al_sb", [128, 128], BF16)
    esc1 = lnx[:, 0:PT]
    esc2 = lnx[:, PT:2 * PT]
    esc3 = lnx[:, 2 * PT:3 * PT] if 3 * PT <= lnx.shape[1] else lnx[:, lnx.shape[1] - PT:]
    numsb = asb("numsb", [128, 256])
    densb = asb("densb", [128, 4])

    vop(ve.memset(s_run[:, :, :], 0.0))
    for h in range(2):
        for j in range(2):
            vop(ve.memset(qpad[h][j][:, :], 0.0))
            vop(ve.memset(kpad[h][j][:, :], 0.0))
    for i in range(2):
        vop(ve.memset(vaug[i][:, :], 0.0))
        vop(ve.memset(vaug[i][:, 64:65], 1.0))
        vop(ve.memset(vaug[i][:, 192:193], 1.0))

    def probe_exit(src=None, dst=None, close_attn=True):
        S.wait(gp, cV, cV.n)
        S.wait(gp, cA, cA.n)
        for ctr in (cP,):
            S.wait(gp, ctr, ctr.n)
        S.wait(gp, dO, dO.n)
        gp.dma_start(out=(y_pc if dst is None else dst),
                     in_=(xbuf[:, :, :] if src is None else src)).then_inc(dO.h, 16)
        dO.n += 16
        S.wait(gp, dO, dO.n)
        if close_attn:
            attn_es.close()
        es.close()
        return nc

    war = {
        "lntm_pe": 0,
        "lntm_dma": None,
        "combpad_dma": None,
        "mbf_pe": 0,
        "stage": [[(None, 0)], [(None, 0)]],
        "split": [0, 0],
        "vaug": [0, 0],
        "ktm": [0, 0],
        "pad": [0, 0],
        "am2": [0, 0],
        "sadd": 0,
    }

    def emit_ln_chunk(src_ap, col0, tr_banks):
        S.wait(ae, cP, war["lntm_pe"])
        if war["lntm_dma"]:
            S.wait(ae, *war["lntm_dma"])
        t1 = aop(ae.activation(lntm, src_ap, AF.Square, accum_out=lnsc[:, 0:1]))
        t2 = aop(ae.activation(lntm, src_ap, AF.Identity, accum_out=lnsc[:, 1:2]))
        S.wait(ve, cA, t2)
        vop(ve.tensor_scalar_mul(lnsc[:, 2:3], lnsc[:, 1:2], 1.0 / D))
        vop(ve.tensor_mul(lnsc[:, 3:4], lnsc[:, 2:3], lnsc[:, 2:3]))
        vop(ve.scalar_tensor_tensor(lnsc[:, 4:5], lnsc[:, 0:1], 1.0 / D, lnsc[:, 3:4],
                                    ALU.mult, ALU.subtract))
        tv = vop(ve.tensor_scalar_add(lnsc[:, 5:6], lnsc[:, 4:5], LN_EPS))
        S.wait(ae, cV, tv)
        ts = aop(ae.activation(lnsc[:, 6:7], lnsc[:, 5:6], AF.Sqrt))
        S.wait(ve, cA, ts)
        vop(ve.reciprocal(lnsc[:, 7:8], lnsc[:, 6:7]))
        ycol = 7
        for it in range(2):
            a, b_, c_, yn = 8, 9, 10, 11 + it
            vop(ve.tensor_mul(lnsc[:, a:a + 1], lnsc[:, ycol:ycol + 1], lnsc[:, ycol:ycol + 1]))
            vop(ve.tensor_mul(lnsc[:, b_:b_ + 1], lnsc[:, 5:6], lnsc[:, a:a + 1]))
            vop(ve.tensor_scalar(lnsc[:, c_:c_ + 1], lnsc[:, b_:b_ + 1], -0.5, 1.5,
                                 ALU.mult, ALU.add))
            vop(ve.tensor_mul(lnsc[:, yn:yn + 1], lnsc[:, ycol:ycol + 1], lnsc[:, c_:c_ + 1]))
            ycol = yn
        S.wait(ve, cA, t2)
        S.wait(ve, cP, war["lntm_pe"])
        if war["lntm_dma"]:
            S.wait(ve, *war["lntm_dma"])
        t_ap = vop(ve.tensor_scalar(lntm, src_ap, lnsc[:, 2:3], lnsc[:, ycol:ycol + 1],
                                    ALU.subtract, ALU.mult))
        for k in range(KC):
            b = tr_banks[k % 2]
            S.wait(pe, cV, t_ap)
            bank_wait(b)
            t_tr = cP.tick(pe.transpose(pb[b][:, 0:128], lntm[:, k * 128:(k + 1) * 128], id_s[:, :]))
            S.wait(ae, cP, t_tr)
            t_hi = aop(ae.activation(lnh[:, k, col0:col0 + 128], pb[b][:, 0:128], AF.Copy))
            S.wait(ve, cP, t_tr)
            S.wait(ve, cA, t_hi)
            t_lo = vop(ve.tensor_sub(lnl[:, k, col0:col0 + 128], pb[b][:, 0:128],
                                     lnh[:, k, col0:col0 + 128]))
            bank_used(b, cV, t_lo)
            bank_used(b, cA, t_hi)
        war["lntm_pe"] = cP.n
        return t_ap

    def w_dma(par, src_ap):
        eng = (sy, ae)[par]
        for ctr, tok in war["stage"][par]:
            if ctr is not None:
                S.wait(eng, ctr, tok)
        S.wait(eng, dW[par], dW[par].n)
        eng.dma_start(out=wstg[par][:, :], in_=src_ap).then_inc(dW[par].h, 16)
        dW[par].n += 16
        return dW[par].n

    def w_split(par, tok_dma):
        S.wait(ae, dW[par], tok_dma)
        S.wait(ae, cP, war["split"][par])
        t_hi = aop(ae.activation(wsh[par][:, :], wstg[par][:, :], AF.Copy))
        S.wait(ve, dW[par], tok_dma)
        S.wait(ve, cP, war["split"][par])
        S.wait(ve, cA, t_hi)
        t_lo = vop(ve.tensor_sub(wsl[par][:, :], wstg[par][:, :], wsh[par][:, :]))
        war["stage"][par] = [(cA, t_hi), (cV, t_lo)]
        return t_hi, t_lo

    def mm3(bank, wpar, n, col, src_h=None, src_l=None):
        if src_h is None:
            src_h, src_l = lnh, lnl
        tok = None
        first = True
        for k in range(KC):
            for p in range(3):
                lhsT = (wsh if p != 2 else wsl)[wpar][:, k * 128:(k + 1) * 128]
                rhs = (src_h if p != 1 else src_l)[:, k, col:col + n]
                last = (k == KC - 1 and p == 2)
                i = pe.matmul(pb[bank][:, 0:n], lhsT, rhs, start=first, stop=last)
                first = False
                if last:
                    tok = cP.tick(i)
        return tok

    def emit_elu_evac(ps_ap, dst_ap, n):
        t_r = vop(ve.tensor_relu(esc1[:, 0:n], ps_ap))
        t_m = vop(ve.tensor_single_scalar(esc2[:, 0:n], ps_ap, 0.0, ALU.min))
        S.wait(ae, cV, t_m)
        t_e = aop(ae.activation(esc3[:, 0:n], esc2[:, 0:n], AF.Exp))
        S.wait(ve, cA, t_e)
        return vop(ve.tensor_add(dst_ap, esc1[:, 0:n], esc3[:, 0:n]))

    def emit_proj(dst, elu, wpar, tok_hi, tok_lo, banks):
        t_ev = 0
        for i0, t0 in enumerate(range(0, T, PT)):
            b = banks[i0 % len(banks)]
            S.wait(pe, cA, tok_hi)
            S.wait(pe, cV, tok_lo)
            bank_wait(b)
            tok_mm = mm3(b, wpar, PT, t0)
            S.wait(ve, cP, tok_mm)
            if elu:
                t_ev = emit_elu_evac(pb[b][:, 0:PT], dst[:, t0:t0 + PT], PT)
            else:
                t_ev = vop(ve.tensor_copy(dst[:, t0:t0 + PT], pb[b][:, 0:PT]))
            bank_used(b, cV, t_ev)
        war["split"][wpar] = cP.n
        return t_ev

    def mix_state_chunk(c, hp, par, tok_k, tok_v, update_state=True):
        bt, bs = 6 + (c % 2), 5
        S.wait(pe, cV, max(tok_k, tok_v))
        bank_wait(bt)
        t_trk = cP.tick(pe.transpose(pb[bt][:, 0:128], kb[:, c * 128:(c + 1) * 128], id_s[:, :]))
        t_trv = cP.tick(pe.transpose(pb[bt][:, 128:256], vb[:, c * 128:(c + 1) * 128], id_s[:, :]))
        S.wait(ae, cP, max(t_trv, war["ktm"][par], war["vaug"][par]))
        t_ktm = aop(ae.activation(ktm[par][:, :], pb[bt][:, 0:128], AF.Copy))
        aop(ae.activation(vaug[par][:, 0:64], pb[bt][:, 128:192], AF.Copy))
        t_v1 = aop(ae.activation(vaug[par][:, 128:192], pb[bt][:, 192:256], AF.Copy))
        bank_used(bt, cA, t_v1)
        if update_state:
            S.wait(pe, cA, t_v1)
            bank_wait(bs)
            cP.tick(pe.matmul(pb[bs][:, 0:128], ktm[par][:, :], vaug[par][:, 0:128],
                              start=True, stop=True))
            t_s = cP.tick(pe.matmul(pb[bs][:, 128:256], ktm[par][:, :], vaug[par][:, 128:256],
                                    start=True, stop=True))
            S.wait(ve, cP, t_s)
            gh0, gh1 = 2 * hp, 2 * hp + 1
            vop(ve.tensor_add(s_run[0:64, gh0, 0:128], s_run[0:64, gh0, 0:128],
                              pb[bs][0:64, 0:128]))
            t_up = vop(ve.tensor_add(s_run[64:128, gh1, 0:128], s_run[64:128, gh1, 0:128],
                                     pb[bs][64:128, 128:256]))
            bank_used(bs, cV, t_up)
        war["ktm"][par] = cP.n
        war["vaug"][par] = cP.n
        return t_ktm, t_v1

    if cfg.get("stop") == "ph0":
        return probe_exit()
    tok_xpre = sdma(xbuf[:, :, :], x_pre)
    S.wait(ae, dX, tok_xpre)
    S.wait(ve, dX, tok_xpre)
    S.wait(ae, cV, t_b2l)
    for j in range(TC):
        emit_ln_chunk(xbuf[:, j, :], j * 128, (0, 1))

    dI = Ctr(nc, "dI")
    for s0 in range(XCR // 128):
        gp.dma_start(out=xc[s0 * 128:(s0 + 1) * 128, 0:D],
                     in_=zx_in).then_inc(dI.h, 16)
        gp.dma_start(out=xc[s0 * 128:(s0 + 1) * 128, D:D + 9],
                     in_=xcid_in[:, s0, 0:9]).then_inc(dI.h, 16)
        gp.dma_start(out=xc[s0 * 128:(s0 + 1) * 128, D + 9:XCW],
                     in_=zx_in[:, 0:XCW - D - 9]).then_inc(dI.h, 16)
        dI.n += 48
    for r0 in range(T, YAR, 128):
        n_r = min(128, YAR - r0)
        gp.dma_start(out=y_acc[r0:r0 + n_r, :], in_=zx_in[0:n_r, :]).then_inc(dI.h, 16)
        dI.n += 16

    pend_k = w_dma(0, wk[0])
    pend_v = w_dma(1, wv[0])
    for hp in range(HP):
        tok_hi, tok_lo = w_split(0, pend_k)
        tok_k = emit_proj(kb, True, 0, tok_hi, tok_lo, (2, 3, 0))
        tok_hi, tok_lo = w_split(1, pend_v)
        tok_v = emit_proj(vb, False, 1, tok_hi, tok_lo, (2, 3, 0))
        if hp + 1 < HP:
            pend_k = w_dma(0, wk[hp + 1])
            pend_v = w_dma(1, wv[hp + 1])
        pend_st = None
        for cstep in range(TC + 1):
            if cstep < TC:
                _, t_v1p = mix_state_chunk(cstep, hp, cstep % 2, tok_k, tok_v,
                                           update_state=False)
                this_st = (cstep % 2, t_v1p)
            else:
                this_st = None
            if pend_st is not None:
                parp, t_v1p = pend_st
                bs = 5
                S.wait(pe, cA, t_v1p)
                bank_wait(bs)
                cP.tick(pe.matmul(pb[bs][:, 0:128], ktm[parp][:, :],
                                  vaug[parp][:, 0:128], start=True, stop=True))
                t_s = cP.tick(pe.matmul(pb[bs][:, 128:256], ktm[parp][:, :],
                                        vaug[parp][:, 128:256], start=True, stop=True))
                war["ktm"][parp] = max(war["ktm"][parp], t_s)
                war["vaug"][parp] = max(war["vaug"][parp], t_s)
                gh0, gh1 = 2 * hp, 2 * hp + 1
                S.wait(ve, cP, t_s)
                vop(ve.tensor_add(s_run[0:64, gh0, 0:128], s_run[0:64, gh0, 0:128],
                                  pb[bs][0:64, 0:128]))
                t_up = vop(ve.tensor_add(s_run[64:128, gh1, 0:128],
                                         s_run[64:128, gh1, 0:128],
                                         pb[bs][64:128, 128:256]))
                bank_used(bs, cV, t_up)
            pend_st = this_st

    vop(ve.tensor_scalar_mul(s_run[:, :, :], s_run[:, :, :], gate_s[:, 0:1]))
    if cfg.get("stop") == "prefix":
        return probe_exit()

    barrier()
    tok_xown = sdma(xbuf[:, :, :], x_own)
    S.wait(ae, dX, tok_xown)
    S.wait(ve, dX, tok_xown)
    for j in range(TC):
        emit_ln_chunk(xbuf[:, j, :], j * 128, (0, 1))

    dR = [Ctr(nc, f"dR{i}") for i in range(3)]
    dump_tok = {}
    pend_q = w_dma(0, wq[0])
    pend_k = w_dma(1, wk[0])
    t_outlast = 0
    for hp in range(HP):
        tok_hi, tok_lo = w_split(0, pend_q)
        tok_q = emit_proj(qb, True, 0, tok_hi, tok_lo, (2, 3, 0))
        tok_hi, tok_lo = w_split(1, pend_k)
        tok_k = emit_proj(kb, True, 1, tok_hi, tok_lo, (2, 3, 0))
        pend_v = w_dma(0, wv[hp])
        tok_hi, tok_lo = w_split(0, pend_v)
        tok_v = emit_proj(vb, False, 0, tok_hi, tok_lo, (2, 3, 0))
        pend_wo = w_dma(1, wo[hp])

        pend_mix = None
        for cstep in range(TC + 1):
            if cstep < TC:
                c = cstep
                cpar = c % 2
                par = c % 2
                t_ktm, t_v1 = mix_state_chunk(c, hp, par, tok_k, tok_v, update_state=False)
                S.wait(ae, cV, tok_q)
                S.wait(ae, cP, war["pad"][cpar])
                aop(ae.activation(qpad[0][cpar][0:64, :], qb[0:64, c * 128:(c + 1) * 128],
                                  AF.Copy))
                aop(ae.activation(kpad[0][cpar][0:64, :], kb[0:64, c * 128:(c + 1) * 128],
                                  AF.Copy))
                aop(ae.activation(qpad[1][cpar][64:128, :],
                                  qb[64:128, c * 128:(c + 1) * 128], AF.Copy))
                t_kp = aop(ae.activation(kpad[1][cpar][64:128, :],
                                         kb[64:128, c * 128:(c + 1) * 128], AF.Copy))
                this_mix = (c, cpar, par, t_kp, t_v1)
            else:
                this_mix = None
            if pend_mix is not None:
                c, cpar, par, t_kp, t_v1 = pend_mix
                b_at = 2 + cpar
                b_o = cpar
                b_s = 5
                gh0, gh1 = 2 * hp, 2 * hp + 1
                S.wait(pe, cA, t_kp)
                bank_wait(b_at)
                cP.tick(pe.matmul(pb[b_at][:, 0:128], kpad[0][cpar][:, :],
                                  qpad[0][cpar][:, :], start=True, stop=True))
                t_at = cP.tick(pe.matmul(pb[b_at][:, 128:256], kpad[1][cpar][:, :],
                                         qpad[1][cpar][:, :], start=True, stop=True))
                S.wait(ve, cP, max(t_at, war["am2"][cpar]))
                t_am = vop(ve.tensor_mul(am2_sb[cpar][:, :], pb[b_at][:, 0:256], um_s[:, :]))
                bank_used(b_at, cV, t_am)
                S.wait(pe, cV, t_am)
                S.wait(pe, cA, t_v1)
                bank_wait(b_o)
                cP.tick(pe.matmul(pb[b_o][:, 0:128], am2_sb[cpar][:, 0:128],
                                  vaug[par][:, 0:128], start=True, stop=True))
                t_o1 = cP.tick(pe.matmul(pb[b_o][:, 128:256], am2_sb[cpar][:, 128:256],
                                         vaug[par][:, 128:256], start=True, stop=True))
                S.wait(pe, cV, war["sadd"])
                cP.tick(pe.matmul(pb[b_o][:, 256:384], qpad[0][cpar][:, :],
                                  s_run[:, gh0, :], start=True, stop=True))
                t_o2 = cP.tick(pe.matmul(pb[b_o][:, 384:512], qpad[1][cpar][:, :],
                                         s_run[:, gh1, :], start=True, stop=True))
                war["pad"][cpar] = t_o2
                war["am2"][cpar] = t_o1
                war["vaug"][par] = max(war["vaug"][par], t_o1)
                bank_wait(b_s)
                cP.tick(pe.matmul(pb[b_s][:, 0:128], ktm[par][:, :], vaug[par][:, 0:128],
                                  start=True, stop=True))
                t_s = cP.tick(pe.matmul(pb[b_s][:, 128:256], ktm[par][:, :],
                                        vaug[par][:, 128:256], start=True, stop=True))
                war["ktm"][par] = max(war["ktm"][par], t_s)
                war["vaug"][par] = max(war["vaug"][par], t_s)
                S.wait(ve, cP, t_s)
                vop(ve.tensor_add(s_run[0:64, gh0, 0:128], s_run[0:64, gh0, 0:128],
                                  pb[b_s][0:64, 0:128]))
                t_up = vop(ve.tensor_add(s_run[64:128, gh1, 0:128],
                                         s_run[64:128, gh1, 0:128],
                                         pb[b_s][64:128, 128:256]))
                bank_used(b_s, cV, t_up)
                war["sadd"] = t_up
                S.wait(ve, cA, t_outlast)
                t_cp1 = vop(ve.tensor_copy(numsb[:, 0:256], pb[b_o][:, 0:256]))
                vop(ve.scalar_tensor_tensor(densb[:, 0:2], pb[b_o][:, 320:449:128],
                                            ATTN_EPS, numsb[:, 64:193:128],
                                            ALU.add, ALU.add))
                vop(ve.reciprocal(densb[:, 2:4], densb[:, 0:2]))
                t_n = vop(ve.tensor_add(numsb[:, 0:256], numsb[:, 0:256],
                                        pb[b_o][:, 256:512]))
                bank_used(b_o, cV, t_n)
                S.wait(ae, cV, t_n)
                aop(ae.activation(attn_tm[:, c, 0:64], numsb[:, 0:64], AF.Copy,
                                  scale=densb[:, 2:3]))
                t_outlast = aop(ae.activation(attn_tm[:, c, 64:128], numsb[:, 128:192], AF.Copy,
                                              scale=densb[:, 3:4]))
            pend_mix = this_mix
        tok_hi_o, tok_lo_o = w_split(1, pend_wo)
        t_wo_last = 0
        for j in range(TC):
            b_tr = 6
            S.wait(pe, cA, t_outlast)
            bank_wait(b_tr)
            t_acc_last = 0
            t_tr = cP.tick(pe.transpose(pb[b_tr][:, 0:128], attn_tm[:, j, :], id_s[:, :]))
            S.wait(ae, cP, t_tr)
            t_ah = aop(ae.activation(ah_sb[:, :], pb[b_tr][:, 0:128], AF.Copy))
            S.wait(ve, cP, t_tr)
            S.wait(ve, cA, t_ah)
            t_al = vop(ve.tensor_sub(al_sb[:, :], pb[b_tr][:, 0:128], ah_sb[:, :]))
            bank_used(b_tr, cV, t_al)
            for fh in range(NFH):
                b = (0, 1)[fh % 2]
                S.wait(pe, cA, max(t_ah, tok_hi_o))
                S.wait(pe, cV, max(t_al, tok_lo_o))
                bank_wait(b)
                pe.matmul(pb[b][:, 0:W2W], ah_sb[:, :], wsh[1][:, fh * W2W:(fh + 1) * W2W],
                          start=True, stop=False)
                pe.matmul(pb[b][:, 0:W2W], ah_sb[:, :], wsl[1][:, fh * W2W:(fh + 1) * W2W],
                          start=False, stop=False)
                i3 = pe.matmul(pb[b][:, 0:W2W], al_sb[:, :], wsh[1][:, fh * W2W:(fh + 1) * W2W],
                               start=False, stop=True)
                t_wo = cP.tick(i3)
                t_wo_last = t_wo
                S.wait(ve, cP, t_wo)
                t_acc = vop(ve.scalar_tensor_tensor(xbuf[:, j, fh * W2W:(fh + 1) * W2W],
                                                    pb[b][:, 0:W2W], 1.0,
                                                    xbuf[:, j, fh * W2W:(fh + 1) * W2W],
                                                    ALU.mult, ALU.add))
                bank_used(b, cV, t_acc)
                t_acc_last = t_acc
            if hp == HP - 1:
                q = j % 3
                eng_q = (sy, ae, gp)[q]
                S.wait(eng_q, cV, t_acc_last)
                eng_q.dma_start(out=r2h[:, j, :], in_=xbuf[:, j, :]).then_inc(dR[q].h, 16)
                dR[q].n += 16
                dump_tok[j] = (q, dR[q].n)
        war["split"][1] = cP.n
        if hp + 1 < HP:
            pend_q = w_dma(0, wq[hp + 1])
            pend_k = w_dma(1, wk[hp + 1])


    if cfg.get("stop") == "own":
        return probe_exit()

    barrier()
    for eng in ALL_ENG:
        for q in range(3):
            S.wait(eng, dR[q], dR[q].n)
        S.wait(eng, dI, dI.n)
    attn_es.close()

    r2t = sbuf("r2t", [128, FTC, D])
    hst = sbuf("hst", [128, FT])
    hh = [sbuf(f"hh{i}", [128, FT], BF16) for i in range(2)]
    hl = [sbuf(f"hl{i}", [128, FT], BF16) for i in range(2)]
    w2stg = [sbuf(f"w2stg{i}", [128, D]) for i in range(2)]
    w2h = [sbuf(f"w2h{i}", [128, D], BF16) for i in range(2)]
    w2l = [sbuf(f"w2l{i}", [128, D], BF16) for i in range(2)]
    lg = sbuf("lg", [128, 8])
    lgm1 = sbuf("lgm1", [128, 8])
    lgk = sbuf("lgk", [128, 8])
    lgm2 = sbuf("lgm2", [128, 8])
    lgsc = sbuf("lgsc", [128, 8])
    combine = sbuf("combine", [128, TC, 8])
    comb_pad = sbuf("comb_pad", [128, 128])
    cth = sbuf("cth", [8, TC, 128], BF16)
    ctl = sbuf("ctl", [8, TC, 128], BF16)
    ct_f = sbuf("ct_f", [8, 128])
    slt_s = sbuf("slt_s", [128, 128], BF16)
    onesb_s = sbuf("onesb_s", [128, 128], BF16)
    offs_s = sbuf("offs_s", [128, 8])
    capd_s = sbuf("capd_s", [128, 1])
    iota0_s = sbuf("iota0_s", [128, TC])
    iop_s = sbuf("iop_s", [128, 1])
    maskp = sbuf("maskp", [128, 128])
    masks_bf = sbuf("masks_bf", [128, 128], BF16)
    base_sb = sbuf("base_sb", [128, 8])
    posx = sbuf("posx", [128, 8])
    post = sbuf("post", [128, 8])
    posf = sbuf("posf", [128, 2])
    posu = sbuf("posu", [128, 2], U32)
    xg = [sbuf(f"xg{i}", [128, CAPS, XCW]) for i in range(2)]
    xgh = sbuf("xgh", [128, KC, FT], BF16)
    xgl = sbuf("xgl", [128, KC, FT], BF16)
    ysb = sbuf("ysb", [128, FT // 128, D])
    idxu = [sbuf(f"idxu{i}", [128, CAPS], U32) for i in range(2)]
    dSc = Ctr(nc, "dSc")
    sdma(slt_s[:, :], slt_in)
    sdma(onesb_s[:, :], onesb_in)
    sdma(offs_s[:, :], offs_in)
    sdma(capd_s[:, :], capd_in)
    sdma(iota0_s[:, :], iota0_in)
    tok_iot = sdma(iop_s[:, :], iop_in)
    vop(ve.memset(comb_pad[:, :], 0.0))
    vop(ve.memset(maskp[:, :], 0.0))
    vop(ve.memset(base_sb[:, :], 0.0))
    vop(ve.memset(xg[0][:, :, :], 0.0))
    vop(ve.memset(xg[1][:, :, :], 0.0))

    war2 = {"w2stage": [[(None, 0)], [(None, 0)]], "w2split": [0, 0], "h": [0, 0]}

    def w2_dma(par, src_ap):
        eng = (gp, ae)[par]
        for ctr, tok in war2["w2stage"][par]:
            if ctr is not None:
                S.wait(eng, ctr, tok)
        S.wait(eng, dG[par], dG[par].n)
        eng.dma_start(out=w2stg[par][:, :], in_=src_ap).then_inc(dG[par].h, 16)
        dG[par].n += 16
        return dG[par].n

    def w2_split(par, tok_dma):
        S.wait(ae, dG[par], tok_dma)
        S.wait(ae, cP, war2["w2split"][par])
        t_hi = aop(ae.activation(w2h[par][:, :], w2stg[par][:, :], AF.Copy))
        S.wait(ve, dG[par], tok_dma)
        S.wait(ve, cP, war2["w2split"][par])
        S.wait(ve, cA, t_hi)
        t_lo = vop(ve.tensor_sub(w2l[par][:, :], w2stg[par][:, :], w2h[par][:, :]))
        war2["w2stage"][par] = [(cA, t_hi), (cV, t_lo)]
        return t_hi, t_lo

    tok_xc_init = dI.n

    NG = E * NCS * HC

    def g_src(g):
        e = g // (NCS * HC)
        m = g % HC
        return e, m

    tok_w1 = {}
    tok_w2 = {}
    for g0 in range(min(2, NG)):
        e0, m0 = g_src(g0)
        tok_w1[g0] = w_dma(g0 % 2, w1[e0, m0])
        tok_w2[g0] = w2_dma(g0 % 2, w2[e0, m0])

    last_out_dma = 0
    tok_r2t_consumed = [(None, 0)]
    for ft in range(NFT):
        for ctr, tok in tok_r2t_consumed:
            if ctr is not None:
                S.wait(sy, ctr, tok)
        S.wait(sy, dO, last_out_dma)
        tok_r2t = sdma(r2t[:, :, :], r2h[:, ft * FTC:(ft + 1) * FTC, :])
        S.wait(ae, dX, tok_r2t)
        S.wait(ve, dX, tok_r2t)
        for jj in range(FTC):
            jabs = ft * FTC + jj
            col0 = jabs * 128
            t_ap = emit_ln_chunk(r2t[:, jj, :], col0, (4, 5))
            first = True
            i = None
            for k in range(KC):
                for p in range(3):
                    lhsT = (lnh if p != 2 else lnl)[:, k, col0:col0 + 128]
                    rhs = (wr_h if p != 1 else wr_l)[:, k, :]
                    if first:
                        S.wait(pe, cV, cV.n)
                        S.wait(pe, cA, cA.n)
                        bank_wait(6)
                    i = pe.matmul(pb[6][:, 0:8], lhsT, rhs, start=first,
                                  stop=(k == KC - 1 and p == 2))
                    first = False
            t_lgmm = cP.tick(i)
            S.wait(ve, cP, t_lgmm)
            t_lg = vop(ve.tensor_copy(lg[:, :], pb[6][:, 0:8]))
            bank_used(6, cV, t_lg)
            vop(ve.reduce_max(lgsc[:, 0:1], lg[:, :], axis=AX.X))
            vop(ve.tensor_scalar(lgm1[:, :], lg[:, :], lgsc[:, 0:1], None, ALU.is_equal))
            vop(ve.scalar_tensor_tensor(lgk[:, :], lgm1[:, :], -1e30, lg[:, :],
                                        ALU.mult, ALU.add))
            vop(ve.reduce_max(lgsc[:, 1:2], lgk[:, :], axis=AX.X))
            vop(ve.tensor_scalar(lgm2[:, :], lgk[:, :], lgsc[:, 1:2], None, ALU.is_equal))
            t_d = vop(ve.tensor_sub(lgsc[:, 2:3], lgsc[:, 1:2], lgsc[:, 0:1]))
            S.wait(ae, cV, t_d)
            t_e = aop(ae.activation(lgsc[:, 3:4], lgsc[:, 2:3], AF.Exp))
            S.wait(ve, cA, t_e)
            vop(ve.tensor_scalar_add(lgsc[:, 4:5], lgsc[:, 3:4], 1.0))
            vop(ve.reciprocal(lgsc[:, 5:6], lgsc[:, 4:5]))
            vop(ve.tensor_mul(lgsc[:, 6:7], lgsc[:, 3:4], lgsc[:, 5:6]))
            vop(ve.tensor_scalar(combine[:, jabs, :], lgm1[:, :], lgsc[:, 5:6], None, ALU.mult))
            t_c2 = vop(ve.scalar_tensor_tensor(combine[:, jabs, :], lgm2[:, :], lgsc[:, 6:7],
                                               combine[:, jabs, :], ALU.mult, ALU.add))
            t_cp = vop(ve.tensor_copy(comb_pad[:, 0:8], combine[:, jabs, :]))
            S.wait(pe, cV, t_cp)
            bank_wait(7)
            t_ct = cP.tick(pe.transpose(pb[7][:, 0:128], comb_pad[:, :], id_s[:, :]))
            S.wait(ve, cP, t_ct)
            t_ctf = vop(ve.tensor_copy(ct_f[:, :], pb[7][0:8, 0:128]))
            bank_used(7, cV, t_ctf)
            t_cth = vop(ve.tensor_copy(cth[:, jabs, :], ct_f[:, :]))
            vop(ve.tensor_sub(ctl[:, jabs, :], ct_f[:, :], cth[:, jabs, :]))
            S.wait(ve, cP, war["mbf_pe"])
            vop(ve.tensor_add(maskp[:, 0:8], lgm1[:, :], lgm2[:, :]))
            t_mb = vop(ve.tensor_copy(masks_bf[:, :], maskp[:, :]))
            S.wait(pe, cV, t_mb)
            bank_wait(3)
            t_pf = cP.tick(pe.matmul(pb[3][:, 0:128], slt_s[:, :], masks_bf[:, :],
                                     start=True, stop=True))
            S.wait(ve, cP, t_pf)
            t_px = vop(ve.tensor_add(posx[:, :], base_sb[:, :], pb[3][:, 0:8]))
            bank_used(3, cV, t_px)
            vop(ve.tensor_scalar(posx[:, :], posx[:, :], capd_s[:, 0:1], None, ALU.min))
            vop(ve.tensor_add(posx[:, :], posx[:, :], offs_s[:, :]))
            vop(ve.tensor_mul(post[:, :], lgm1[:, :], posx[:, :]))
            vop(ve.reduce_sum(posf[:, 0:1], post[:, :], axis=AX.X))
            vop(ve.tensor_mul(post[:, :], lgm2[:, :], posx[:, :]))
            vop(ve.reduce_sum(posf[:, 1:2], post[:, :], axis=AX.X))
            S.wait(ve, dSc, dSc.n)
            t_pu = vop(ve.tensor_copy(posu[:, :], posf[:, :]))
            bank_wait(7)
            t_tt = cP.tick(pe.matmul(pb[7][:, 0:128], onesb_s[:, :], masks_bf[:, :],
                                     start=True, stop=True))
            war["mbf_pe"] = t_tt
            S.wait(ve, cP, t_tt)
            t_bu = vop(ve.tensor_add(base_sb[:, :], base_sb[:, :], pb[7][:, 0:8]))
            bank_used(7, cV, t_bu)
            t_g8 = vop(ve.tensor_copy(lnx[:, D:D + 8], combine[:, jabs, :]))
            t_gi = vop(ve.tensor_copy(lnx[:, D + 8:D + 9], iota0_s[:, jabs:jabs + 1]))
            S.wait(gp, cV, cV.n)
            S.wait(gp, dI, tok_xc_init)
            for sl in range(2):
                S.wait(gp, dSc, dSc.n)
                gp.indirect_dma_start(
                    xc, bass.IndirectOffsetOnAxis(ap=posu[:, sl:sl + 1], axis=0),
                    lnx[:, 0:XCW], None).then_inc(dSc.h, 16)
                dSc.n += 16
            war["lntm_dma"] = (dSc, dSc.n)
        S.wait(ve, dO, last_out_dma)
        for jj in range(FTC):
            jabs = ft * FTC + jj
            for fh in range(NFH):
                b = 6 + ((jj * NFH + fh) % 2)
                S.wait(pe, cV, cV.n)
                bank_wait(b)
                pe.matmul(pb[b][:, 0:W2W], cth[:, jabs, :], b2_h[:, fh * W2W:(fh + 1) * W2W],
                          start=True, stop=False)
                pe.matmul(pb[b][:, 0:W2W], cth[:, jabs, :], b2_l[:, fh * W2W:(fh + 1) * W2W],
                          start=False, stop=False)
                i3 = pe.matmul(pb[b][:, 0:W2W], ctl[:, jabs, :], b2_h[:, fh * W2W:(fh + 1) * W2W],
                               start=False, stop=True)
                t_b = cP.tick(i3)
                S.wait(ve, cP, t_b)
                t_ev = vop(ve.scalar_tensor_tensor(
                    r2t[:, jj, fh * W2W:(fh + 1) * W2W], pb[b][:, 0:W2W], 1.0,
                    r2t[:, jj, fh * W2W:(fh + 1) * W2W], ALU.mult, ALU.add))
                bank_used(b, cV, t_ev)
        S.wait(gp, cV, cV.n)
        S.wait(gp, dO, dO.n)
        gp.dma_start(out=y_acc[ft * FT:(ft + 1) * FT, :].rearrange("(c p) d -> p c d", p=128),
                     in_=r2t[:, :, :]).then_inc(dO.h, 16)
        dO.n += 16
        last_out_dma = dO.n
        tok_r2t_consumed = [(cA, cA.n), (cV, cV.n)]

    if cfg.get("ffn_mode") == "router":
        return probe_exit(src=r2t[:, :, :], dst=y_pc[:, 0:FTC, :], close_attn=False)

    tok_comp_done = dSc.n
    dS2 = [Ctr(nc, "dS2a"), Ctr(nc, "dS2b")]
    war3 = {"xg_pe": [0, 0], "xg_dve": [0, 0],
            "idx_ds2": {0: (0, 0), 1: (0, 0)},
            "ysb_tok": {}}
    tok_xr = {}

    def load_region(e):
        pe_ = e % 2
        S.wait(sy, dI, tok_xc_init)
        S.wait(sy, dSc, tok_comp_done)
        S.wait(sy, cP, war3["xg_pe"][pe_])
        S.wait(sy, cV, war3["xg_dve"][pe_])
        tok_xr[e] = sdma(xg[pe_][:, :, :],
                         xc[e * (CAP + 128):e * (CAP + 128) + CAP, :]
                         .rearrange("(q p) c -> p q c", p=128))

    load_region(0)
    g = 0
    for e in range(E):
        par_e = e % 2
        xge = xg[par_e]
        idxe = idxu[par_e]
        if e + 1 < E:
            load_region(e + 1)
        S.wait(ve, dX, tok_xr[e])
        tk0, tk1 = war3["idx_ds2"][par_e]
        S.wait(ve, dS2[0], tk0)
        S.wait(ve, dS2[1], tk1)
        t_idx = vop(ve.tensor_copy(idxe[:, :], xge[:, :, D + 8]))
        ns = 0
        for csub in range(NCS):
            for sl in range(CSUB // 128):
                slot = csub * (CSUB // 128) + sl
                for k in range(KC):
                    b = 6 + (k % 2)
                    S.wait(pe, dX, tok_xr[e])
                    S.wait(pe, cV, cV.n)
                    bank_wait(b)
                    t_tr = cP.tick(pe.transpose(pb[b][:, 0:128],
                                                xge[:, slot, k * 128:(k + 1) * 128],
                                                id_s[:, :]))
                    S.wait(ae, cP, t_tr)
                    t_hi = aop(ae.activation(xgh[:, k, sl * 128:(sl + 1) * 128],
                                             pb[b][:, 0:128], AF.Copy))
                    S.wait(ve, cP, t_tr)
                    S.wait(ve, cA, t_hi)
                    t_lo = vop(ve.tensor_sub(xgl[:, k, sl * 128:(sl + 1) * 128],
                                             pb[b][:, 0:128],
                                             xgh[:, k, sl * 128:(sl + 1) * 128]))
                    bank_used(b, cV, t_lo)
                    bank_used(b, cA, t_hi)
            war3["xg_pe"][par_e] = cP.n
            t_split = cV.n
            y_stop = [0] * NBY
            pend_w2 = None
            for mstep in range(HC + 1):
                if mstep < HC:
                    m = mstep
                    par = g % 2
                    tok_hi, tok_lo = w_split(par, tok_w1[g])
                    bw1 = 4 + (m % 2)
                    S.wait(pe, cA, tok_hi)
                    S.wait(pe, cV, max(tok_lo, t_split))
                    bank_wait(bw1)
                    tok_mm = mm3(bw1, par, CSUB, 0, src_h=xgh, src_l=xgl)
                    war["split"][par] = tok_mm
                    tok2_hi, tok2_lo = w2_split(par, tok_w2[g])
                    if g + 2 < NG:
                        e2, m2 = g_src(g + 2)
                        tok_w1[g + 2] = w_dma(par, w1[e2, m2])
                        tok_w2[g + 2] = w2_dma(par, w2[e2, m2])
                    hpar = m % 2
                    S.wait(ae, cP, max(tok_mm, war2["h"][hpar]))
                    t_g = aop(ae.activation(hst[:, 0:CSUB], pb[bw1][:, 0:CSUB], AF.Gelu,
                                            bias=b1_s[:, e, m:m + 1]))
                    bank_used(bw1, cA, t_g)
                    S.wait(ve, cA, t_g)
                    S.wait(ve, cP, war2["h"][hpar])
                    t_hh = vop(ve.tensor_copy(hh[hpar][:, :], hst[:, 0:CSUB]))
                    t_hl = vop(ve.tensor_sub(hl[hpar][:, :], hst[:, 0:CSUB], hh[hpar][:, :]))
                    this_w2 = (m, par, hpar, t_hh, t_hl, tok2_hi, tok2_lo)
                    g += 1
                else:
                    this_w2 = None
                if pend_w2 is not None:
                    m, par, hpar, t_hh, t_hl, tok2_hi, tok2_lo = pend_w2
                    S.wait(pe, cA, tok2_hi)
                    S.wait(pe, cV, max(t_hh, t_hl, tok2_lo))
                    i = None
                    for j in range(CSUB // 128):
                        for fh in range(NFH):
                            by = j * NFH + fh
                            if m == 0:
                                bank_wait(by)
                            for p in range(3):
                                lhsT = (hh if p != 2 else hl)[hpar][:, j * 128:(j + 1) * 128]
                                rhs = (w2h if p != 1 else w2l)[par][:, fh * W2W:(fh + 1) * W2W]
                                i = pe.matmul(pb[by][:, 0:W2W], lhsT, rhs,
                                              start=(m == 0 and p == 0),
                                              stop=(m == HC - 1 and p == 2))
                            if m == HC - 1:
                                y_stop[by] = cP.tick(i)
                    if m != HC - 1:
                        t_end = cP.tick(i)
                        war2["w2split"][par] = t_end
                        war2["h"][hpar] = t_end
                    else:
                        war2["w2split"][par] = y_stop[-1]
                        war2["h"][hpar] = y_stop[-1]
                pend_w2 = this_w2
            for j in range(CSUB // 128):
                slot = csub * (CSUB // 128) + j
                for fh in range(NFH):
                    by = j * NFH + fh
                    S.wait(ve, cP, y_stop[by])
                    if j in war3["ysb_tok"]:
                        pty, ptk = war3["ysb_tok"][j]
                        S.wait(ve, dS2[pty], ptk)
                    t_ev = vop(ve.tensor_scalar(
                        ysb[:, j, fh * W2W:(fh + 1) * W2W], pb[by][:, 0:W2W],
                        xge[:, slot, D + e:D + e + 1], None, ALU.mult))
                    bank_used(by, cV, t_ev)
            for j in range(CSUB // 128):
                slot = csub * (CSUB // 128) + j
                p = ns % 2
                S.wait(gp, cV, cV.n)
                S.wait(gp, dO, dO.n)
                if ns < 2:
                    S.wait(gp, dS2[0], dS2[0].n)
                    S.wait(gp, dS2[1], dS2[1].n)
                else:
                    S.wait(gp, dS2[p], dS2[p].n)
                gp.indirect_dma_start(
                    y_acc, bass.IndirectOffsetOnAxis(ap=idxe[:, slot:slot + 1], axis=0),
                    ysb[:, j, :], None,
                    compute_op=ALU.add).then_inc(dS2[p].h, 16)
                dS2[p].n += 16
                war3["ysb_tok"][j] = (p, dS2[p].n)
                ns += 1
        war3["xg_dve"][par_e] = cV.n
        war3["idx_ds2"][par_e] = (dS2[0].n, dS2[1].n)

    for ft in range(NFT):
        S.wait(sy, dSc, dSc.n)
        S.wait(sy, dS2[0], dS2[0].n)
        S.wait(sy, dS2[1], dS2[1].n)
        S.wait(sy, dO, dO.n)
        S.wait(sy, cV, cV.n)
        tok_l = sdma(r2t[:, :, :],
                     y_acc[ft * FT:(ft + 1) * FT, :].rearrange("(c p) d -> p c d", p=128))
        S.wait(gp, dX, tok_l)
        S.wait(gp, dO, dO.n)
        gp.dma_start(out=y_pc[:, ft * FTC:(ft + 1) * FTC, :], in_=r2t[:, :, :]).then_inc(dO.h, 16)
        dO.n += 16

    S.wait(gp, dO, dO.n)
    S.wait(gp, dSc, dSc.n)
    S.wait(gp, dS2[0], dS2[0].n)
    S.wait(gp, dS2[1], dS2[1].n)
    es.close()
    return nc


def _stage_inputs(inputs, cfg, core):
    D, T = cfg["D"], cfg["T"]
    TC = T // 128
    x = np.ascontiguousarray(np.asarray(inputs["L0_attn_x"], dtype=np.float32))
    b, half = core // 2, core % 2
    own = x[b, half * T:(half + 1) * T]
    pre = x[b, 0:T] if half == 1 else own
    shuffle = lambda a: np.ascontiguousarray(a.reshape(TC, 128, D).transpose(1, 0, 2))
    return {
        "x_own": shuffle(own),
        "x_pre": shuffle(pre),
        "gate": np.full((128, 1), float(half), np.float32),
    }


def _stage_weights(inputs, cfg):
    D, HID, E = cfg["D"], cfg["HID"], cfg["E"]
    KC, HC = D // 128, HID // 128
    HP = (D // 64) // 2
    f32 = lambda a: np.asarray(a, dtype=np.float32)
    Wq, Wk, Wv, Wo = (f32(inputs[f"L0_attn_W{n}"]) for n in "qkvo")
    W1, W2 = f32(inputs["L0_ffn_W1"]), f32(inputs["L0_ffn_W2"])
    b1, b2, Wr = f32(inputs["L0_ffn_b1"]), f32(inputs["L0_ffn_b2"]), f32(inputs["L0_ffn_Wr"])

    def lhsT_lay(W):
        a = W.reshape(HP, 128, KC, 128).transpose(0, 3, 2, 1)
        return np.ascontiguousarray(a.reshape(HP, 128, KC * 128))

    import ml_dtypes
    T, CAP, E_ = cfg["T"], cfg["CAP"], cfg["E"]
    TC = T // 128
    bfl = np.dtype(ml_dtypes.bfloat16)
    return {
        "slt_in": np.tril(np.ones((128, 128), np.float32), -1).T.astype(bfl),
        "onesb_in": np.ones((128, 128), bfl),
        "offs_in": np.tile((np.arange(8) * (CAP + 128)).astype(np.float32), (128, 1)),
        "capd_in": (CAP + np.arange(128, dtype=np.float32))[:, None].copy(),
        "iota0_in": np.ascontiguousarray(
            (np.arange(TC)[None, :] * 128 + np.arange(128)[:, None]).astype(np.float32)),
        "iop_in": np.arange(128, dtype=np.float32)[:, None].copy(),
        "xcid_in": (lambda ns: (lambda a: (a.__setitem__((slice(None), slice(None), 8),
                    (np.float32(T) + 128 * np.arange(ns, dtype=np.float32)[None, :]
                     + np.arange(128, dtype=np.float32)[:, None])), a)[1])(
                    np.zeros((128, ns, 9), np.float32)))(E * (CAP + 128) // 128),
        "zx_in": np.zeros((128, D), np.float32),
        "wq": lhsT_lay(Wq), "wk": lhsT_lay(Wk), "wv": lhsT_lay(Wv),
        "wo": np.ascontiguousarray(Wo.T.reshape(HP, 128, D)),
        "w1": np.ascontiguousarray(
            W1.reshape(E, KC, 128, HC, 128).transpose(0, 3, 2, 1, 4).reshape(E, HC, 128, KC * 128)),
        "w2": np.ascontiguousarray(W2.reshape(E, HC, 128, D)),
        "b1": np.ascontiguousarray(b1.reshape(E, HC, 128).transpose(2, 0, 1)),
        "b2": np.ascontiguousarray(b2),
        "wr": np.ascontiguousarray(Wr.reshape(KC, 128, 8).transpose(1, 0, 2)),
        "um": np.tile(np.triu(np.ones((128, 128), np.float32)), (1, 2)),
        "ident": np.eye(128, dtype=np.float32),
    }


def kernel(**inputs):
    cfg = FULL
    n_cores = 8
    weights = _stage_weights(inputs, cfg)
    in_maps = []
    for core in range(n_cores):
        m = dict(weights)
        m.update(_stage_inputs(inputs, cfg, core))
        in_maps.append(m)
    nc = build_program(cfg)
    res = run_bass_kernel_spmd(nc, in_maps, core_ids=list(range(n_cores)))
    T, D = cfg["T"], cfg["D"]
    out = np.empty((4, 4096, D), np.float32)
    for core in range(n_cores):
        b, half = core // 2, core % 2
        out[b, half * T:(half + 1) * T] = np.asarray(res.results[core]["y"])
    return out


# version 25 (final)
# speedup vs baseline: 1.2796x; 1.0362x over previous
import contextlib
import numpy as np
import concourse.bass as bass
import concourse.mybir as mybir
from concourse.bass_utils import run_bass_kernel_spmd

from concourse import library_config

F32 = mybir.dt.float32
BF16 = mybir.dt.bfloat16
I16 = mybir.dt.int16
U32 = mybir.dt.uint32
AF = mybir.ActivationFunctionType
ALU = mybir.AluOpType
AX = mybir.AxisListType

FULL = dict(D=1024, T=2048, HID=4096, E=8, FT=256, CAP=768)
MINI = dict(D=256, T=256, HID=512, E=8, FT=256, CAP=256)

LN_EPS = 1e-5
ATTN_EPS = 1e-8


class Ctr:
    def __init__(self, nc, name):
        self.h = nc.alloc_semaphore(name=name)
        self.n = 0

    def tick(self, instr, n=1):
        instr.then_inc(self.h, n)
        self.n += n
        return self.n


class Syncer:
    def __init__(self):
        self.wm = {}

    def wait(self, eng, ctr, val):
        if not val or val <= 0:
            return
        k = (id(eng), id(ctr))
        if self.wm.get(k, 0) >= val:
            return
        eng.wait_ge(ctr.h, val)
        self.wm[k] = val


def build_program(cfg):
    D, T, HID, E, FT = (cfg[k] for k in ("D", "T", "HID", "E", "FT"))
    CAP = cfg["CAP"]
    CSUB = FT
    NCS = CAP // CSUB
    CAPS = CAP // 128
    KC = D // 128
    NH = D // 64
    HP = NH // 2
    TC = T // 128
    HC = HID // 128
    FTC = FT // 128
    NFT = T // FT
    PT = min(512, T)
    W2W = min(512, D)
    NFH = D // W2W
    NBY = FTC * NFH
    assert NBY <= 4

    nc = bass.Bass("TRN2", target_bir_lowering=False, debug=False)
    S = Syncer()

    x_own = nc.dram_tensor("x_own", [128, TC, D], F32, kind="ExternalInput").ap()
    x_pre = nc.dram_tensor("x_pre", [128, TC, D], F32, kind="ExternalInput").ap()
    gate = nc.dram_tensor("gate", [128, 1], F32, kind="ExternalInput").ap()
    wq = nc.dram_tensor("wq", [HP, 128, KC * 128], F32, kind="ExternalInput").ap()
    wk = nc.dram_tensor("wk", [HP, 128, KC * 128], F32, kind="ExternalInput").ap()
    wv = nc.dram_tensor("wv", [HP, 128, KC * 128], F32, kind="ExternalInput").ap()
    wo = nc.dram_tensor("wo", [HP, 128, D], F32, kind="ExternalInput").ap()
    w1 = nc.dram_tensor("w1", [E, HC, 128, KC * 128], F32, kind="ExternalInput").ap()
    w2 = nc.dram_tensor("w2", [E, HC, 128, D], F32, kind="ExternalInput").ap()
    b1 = nc.dram_tensor("b1", [128, E, HC], F32, kind="ExternalInput").ap()
    b2 = nc.dram_tensor("b2", [8, D], F32, kind="ExternalInput").ap()
    wr = nc.dram_tensor("wr", [128, KC, 8], F32, kind="ExternalInput").ap()
    um = nc.dram_tensor("um", [128, 256], F32, kind="ExternalInput").ap()
    ident = nc.dram_tensor("ident", [128, 128], F32, kind="ExternalInput").ap()
    slt_in = nc.dram_tensor("slt_in", [128, 128], BF16, kind="ExternalInput").ap()
    onesb_in = nc.dram_tensor("onesb_in", [128, 128], BF16, kind="ExternalInput").ap()
    offs_in = nc.dram_tensor("offs_in", [128, 8], F32, kind="ExternalInput").ap()
    capd_in = nc.dram_tensor("capd_in", [128, 1], F32, kind="ExternalInput").ap()
    iota0_in = nc.dram_tensor("iota0_in", [128, TC], F32, kind="ExternalInput").ap()
    iop_in = nc.dram_tensor("iop_in", [128, 1], F32, kind="ExternalInput").ap()
    xcid_in = nc.dram_tensor("xcid_in", [128, E * (CAP + 128) // 128, 9], F32, kind="ExternalInput").ap()
    zx_in = nc.dram_tensor("zx_in", [128, D], F32, kind="ExternalInput").ap()
    y = nc.dram_tensor("y", [T, D], F32, kind="ExternalOutput").ap()
    y_pc = y.rearrange("(c p) d -> p c d", p=128)
    r2h = nc.dram_tensor("r2h", [128, TC, D], F32, kind="Internal").ap()
    XCR = E * (CAP + 128)
    XCW = D + 64
    YAR = T + XCR + 256
    xc = nc.dram_tensor("xc", [XCR, XCW], F32, kind="Internal").ap()
    y_acc = nc.dram_tensor("y_acc", [YAR, D], F32, kind="Internal").ap()

    es = contextlib.ExitStack()
    sbuf = lambda name, shape, dt=F32: es.enter_context(nc.sbuf_tensor(name, shape, dt))

    cP = Ctr(nc, "cP")
    cV = Ctr(nc, "cV")
    cA = Ctr(nc, "cA")
    dX = Ctr(nc, "dX")
    dW = [Ctr(nc, "dW0"), Ctr(nc, "dW1")]
    dG = [Ctr(nc, "dG0"), Ctr(nc, "dG1")]
    dO = Ctr(nc, "dO")

    pe, ve, ae, sy, gp = nc.tensor, nc.vector, nc.scalar, nc.sync, nc.gpsimd
    ALL_ENG = (pe, ve, ae, sy, gp)

    def vop(instr):
        t = cV.tick(instr)
        S.wait(ve, cV, t)
        return t

    def aop(instr):
        t = cA.tick(instr)
        S.wait(ae, cA, t)
        return t

    def sdma(out, in_):
        S.wait(sy, dX, dX.n)
        sy.dma_start(out=out, in_=in_).then_inc(dX.h, 16)
        dX.n += 16
        return dX.n

    def barrier():
        for ctr in (cP, cV, cA):
            for eng in ALL_ENG:
                S.wait(eng, ctr, ctr.n)

    um_s = sbuf("um_s", [128, 256])
    id_s = sbuf("id_s", [128, 128])
    gate_s = sbuf("gate_s", [128, 1])
    wr_f = sbuf("wr_f", [128, KC, 8])
    wr_h = sbuf("wr_h", [128, KC, 8], BF16)
    wr_l = sbuf("wr_l", [128, KC, 8], BF16)
    b1_s = sbuf("b1_s", [128, E, HC])
    b2_h = sbuf("b2_h", [8, D], BF16)
    b2_l = sbuf("b2_l", [8, D], BF16)
    lnx = sbuf("lnx", [128, max(D + 64, max(D, 2 * min(512, T)) + min(512, T))])
    lntm = lnx[:, 0:D]
    lnsc = sbuf("lnsc", [128, 16])
    lnh = sbuf("lnh", [128, KC, T], BF16)
    lnl = sbuf("lnl", [128, KC, T], BF16)
    wstg = [sbuf(f"wstg{i}", [128, KC * 128]) for i in range(2)]
    wsh = [sbuf(f"wsh{i}", [128, KC * 128], BF16) for i in range(2)]
    wsl = [sbuf(f"wsl{i}", [128, KC * 128], BF16) for i in range(2)]

    pb = [es.enter_context(nc.psum_tensor(f"pb{i}", [128, 512], F32)) for i in range(8)]
    bank_free = {i: [] for i in range(8)}

    def bank_wait(b):
        for ctr, tok in bank_free[b]:
            S.wait(pe, ctr, tok)
        bank_free[b] = []

    def bank_used(b, ctr, tok):
        bank_free[b].append((ctr, tok))

    sdma(um_s[:, :], um)
    sdma(id_s[:, :], ident)
    sdma(gate_s[:, :], gate)
    sdma(wr_f[:, :, :], wr)
    sdma(b1_s[:, :, :], b1)
    tok_consts = sdma(lnx[0:8, 0:D], b2)
    S.wait(ae, dX, tok_consts)
    t_wrh = aop(ae.activation(wr_h[:, :, :], wr_f[:, :, :], AF.Copy))
    t_b2h = aop(ae.activation(b2_h[:, :], lnx[0:8, 0:D], AF.Copy))
    S.wait(ve, dX, tok_consts)
    S.wait(ve, cA, t_b2h)
    t_wrl = vop(ve.tensor_sub(wr_l[:, :, :], wr_f[:, :, :], wr_h[:, :, :]))
    t_b2l = vop(ve.tensor_sub(b2_l[:, :], lnx[0:8, 0:D], b2_h[:, :]))

    attn_es = contextlib.ExitStack()
    asb = lambda name, shape, dt=F32: attn_es.enter_context(nc.sbuf_tensor(name, shape, dt))
    xbuf = asb("xbuf", [128, TC, D])
    qb = asb("qb", [128, T])
    kb = asb("kb", [128, T])
    vb = asb("vb", [128, T])
    s_run = asb("s_run", [128, NH, 128])
    ktm = [asb(f"ktm{i}", [128, 128]) for i in range(2)]
    vaug = [asb(f"vaug{i}", [128, 256]) for i in range(2)]
    qpad = [[asb(f"qpad{i}_{j}", [128, 128]) for i in range(2)] for j in range(2)]
    kpad = [[asb(f"kpad{i}_{j}", [128, 128]) for i in range(2)] for j in range(2)]
    am2_sb = [asb(f"am2_{j}", [128, 256]) for j in range(2)]
    attn_tm = asb("attn_tm", [128, TC, 128])
    ah_sb = asb("ah_sb", [128, 128], BF16)
    al_sb = asb("al_sb", [128, 128], BF16)
    esc1 = lnx[:, 0:PT]
    esc2 = lnx[:, PT:2 * PT]
    esc3 = lnx[:, 2 * PT:3 * PT] if 3 * PT <= lnx.shape[1] else lnx[:, lnx.shape[1] - PT:]
    numsb = asb("numsb", [128, 256])
    densb = asb("densb", [128, 4])

    vop(ve.memset(s_run[:, :, :], 0.0))
    for h in range(2):
        for j in range(2):
            vop(ve.memset(qpad[h][j][:, :], 0.0))
            vop(ve.memset(kpad[h][j][:, :], 0.0))
    for i in range(2):
        vop(ve.memset(vaug[i][:, :], 0.0))
        vop(ve.memset(vaug[i][:, 64:65], 1.0))
        vop(ve.memset(vaug[i][:, 192:193], 1.0))

    def probe_exit(src=None, dst=None, close_attn=True):
        S.wait(gp, cV, cV.n)
        S.wait(gp, cA, cA.n)
        for ctr in (cP,):
            S.wait(gp, ctr, ctr.n)
        S.wait(gp, dO, dO.n)
        gp.dma_start(out=(y_pc if dst is None else dst),
                     in_=(xbuf[:, :, :] if src is None else src)).then_inc(dO.h, 16)
        dO.n += 16
        S.wait(gp, dO, dO.n)
        if close_attn:
            attn_es.close()
        es.close()
        return nc

    war = {
        "lntm_pe": 0,
        "lntm_dma": None,
        "combpad_dma": None,
        "mbf_pe": 0,
        "stage": [[(None, 0)], [(None, 0)]],
        "split": [0, 0],
        "vaug": [0, 0],
        "ktm": [0, 0],
        "pad": [0, 0],
        "am2": [0, 0],
        "sadd": 0,
    }

    def emit_ln_chunk(src_ap, col0, tr_banks):
        S.wait(ae, cP, war["lntm_pe"])
        if war["lntm_dma"]:
            S.wait(ae, *war["lntm_dma"])
        t1 = aop(ae.activation(lntm, src_ap, AF.Square, accum_out=lnsc[:, 0:1]))
        t2 = aop(ae.activation(lntm, src_ap, AF.Identity, accum_out=lnsc[:, 1:2]))
        S.wait(ve, cA, t2)
        vop(ve.tensor_scalar_mul(lnsc[:, 2:3], lnsc[:, 1:2], 1.0 / D))
        vop(ve.tensor_mul(lnsc[:, 3:4], lnsc[:, 2:3], lnsc[:, 2:3]))
        vop(ve.scalar_tensor_tensor(lnsc[:, 4:5], lnsc[:, 0:1], 1.0 / D, lnsc[:, 3:4],
                                    ALU.mult, ALU.subtract))
        tv = vop(ve.tensor_scalar_add(lnsc[:, 5:6], lnsc[:, 4:5], LN_EPS))
        S.wait(ae, cV, tv)
        ts = aop(ae.activation(lnsc[:, 6:7], lnsc[:, 5:6], AF.Sqrt))
        S.wait(ve, cA, ts)
        vop(ve.reciprocal(lnsc[:, 7:8], lnsc[:, 6:7]))
        ycol = 7
        for it in range(2):
            a, b_, c_, yn = 8, 9, 10, 11 + it
            vop(ve.tensor_mul(lnsc[:, a:a + 1], lnsc[:, ycol:ycol + 1], lnsc[:, ycol:ycol + 1]))
            vop(ve.tensor_mul(lnsc[:, b_:b_ + 1], lnsc[:, 5:6], lnsc[:, a:a + 1]))
            vop(ve.tensor_scalar(lnsc[:, c_:c_ + 1], lnsc[:, b_:b_ + 1], -0.5, 1.5,
                                 ALU.mult, ALU.add))
            vop(ve.tensor_mul(lnsc[:, yn:yn + 1], lnsc[:, ycol:ycol + 1], lnsc[:, c_:c_ + 1]))
            ycol = yn
        S.wait(ve, cA, t2)
        S.wait(ve, cP, war["lntm_pe"])
        if war["lntm_dma"]:
            S.wait(ve, *war["lntm_dma"])
        t_ap = vop(ve.tensor_scalar(lntm, src_ap, lnsc[:, 2:3], lnsc[:, ycol:ycol + 1],
                                    ALU.subtract, ALU.mult))
        for k in range(KC):
            b = tr_banks[k % 2]
            S.wait(pe, cV, t_ap)
            bank_wait(b)
            t_tr = cP.tick(pe.transpose(pb[b][:, 0:128], lntm[:, k * 128:(k + 1) * 128], id_s[:, :]))
            S.wait(ae, cP, t_tr)
            t_hi = aop(ae.activation(lnh[:, k, col0:col0 + 128], pb[b][:, 0:128], AF.Copy))
            S.wait(ve, cP, t_tr)
            S.wait(ve, cA, t_hi)
            t_lo = vop(ve.tensor_sub(lnl[:, k, col0:col0 + 128], pb[b][:, 0:128],
                                     lnh[:, k, col0:col0 + 128]))
            bank_used(b, cV, t_lo)
            bank_used(b, cA, t_hi)
        war["lntm_pe"] = cP.n
        return t_ap

    def w_dma(par, src_ap):
        eng = (sy, ae)[par]
        for ctr, tok in war["stage"][par]:
            if ctr is not None:
                S.wait(eng, ctr, tok)
        S.wait(eng, dW[par], dW[par].n)
        eng.dma_start(out=wstg[par][:, :], in_=src_ap).then_inc(dW[par].h, 16)
        dW[par].n += 16
        return dW[par].n

    def w_split(par, tok_dma):
        S.wait(ae, dW[par], tok_dma)
        S.wait(ae, cP, war["split"][par])
        t_hi = aop(ae.activation(wsh[par][:, :], wstg[par][:, :], AF.Copy))
        S.wait(ve, dW[par], tok_dma)
        S.wait(ve, cP, war["split"][par])
        S.wait(ve, cA, t_hi)
        t_lo = vop(ve.tensor_sub(wsl[par][:, :], wstg[par][:, :], wsh[par][:, :]))
        war["stage"][par] = [(cA, t_hi), (cV, t_lo)]
        return t_hi, t_lo

    def mm3(bank, wpar, n, col, src_h=None, src_l=None):
        if src_h is None:
            src_h, src_l = lnh, lnl
        tok = None
        first = True
        for k in range(KC):
            for p in range(3):
                lhsT = (wsh if p != 2 else wsl)[wpar][:, k * 128:(k + 1) * 128]
                rhs = (src_h if p != 1 else src_l)[:, k, col:col + n]
                last = (k == KC - 1 and p == 2)
                i = pe.matmul(pb[bank][:, 0:n], lhsT, rhs, start=first, stop=last)
                first = False
                if last:
                    tok = cP.tick(i)
        return tok

    def emit_elu_evac(ps_ap, dst_ap, n):
        t_r = vop(ve.tensor_relu(esc1[:, 0:n], ps_ap))
        t_m = vop(ve.tensor_single_scalar(esc2[:, 0:n], ps_ap, 0.0, ALU.min))
        S.wait(ae, cV, t_m)
        t_e = aop(ae.activation(esc3[:, 0:n], esc2[:, 0:n], AF.Exp))
        S.wait(ve, cA, t_e)
        return vop(ve.tensor_add(dst_ap, esc1[:, 0:n], esc3[:, 0:n]))

    def emit_proj(dst, elu, wpar, tok_hi, tok_lo, banks):
        t_ev = 0
        for i0, t0 in enumerate(range(0, T, PT)):
            b = banks[i0 % len(banks)]
            S.wait(pe, cA, tok_hi)
            S.wait(pe, cV, tok_lo)
            bank_wait(b)
            tok_mm = mm3(b, wpar, PT, t0)
            S.wait(ve, cP, tok_mm)
            if elu:
                t_ev = emit_elu_evac(pb[b][:, 0:PT], dst[:, t0:t0 + PT], PT)
            else:
                t_ev = vop(ve.tensor_copy(dst[:, t0:t0 + PT], pb[b][:, 0:PT]))
            bank_used(b, cV, t_ev)
        war["split"][wpar] = cP.n
        return t_ev

    def mix_state_chunk(c, hp, par, tok_k, tok_v, update_state=True):
        bt, bs = 6 + (c % 2), 5
        S.wait(pe, cV, max(tok_k, tok_v))
        bank_wait(bt)
        t_trk = cP.tick(pe.transpose(pb[bt][:, 0:128], kb[:, c * 128:(c + 1) * 128], id_s[:, :]))
        t_trv = cP.tick(pe.transpose(pb[bt][:, 128:256], vb[:, c * 128:(c + 1) * 128], id_s[:, :]))
        S.wait(ae, cP, max(t_trv, war["ktm"][par], war["vaug"][par]))
        t_ktm = aop(ae.activation(ktm[par][:, :], pb[bt][:, 0:128], AF.Copy))
        aop(ae.activation(vaug[par][:, 0:64], pb[bt][:, 128:192], AF.Copy))
        t_v1 = aop(ae.activation(vaug[par][:, 128:192], pb[bt][:, 192:256], AF.Copy))
        bank_used(bt, cA, t_v1)
        if update_state:
            S.wait(pe, cA, t_v1)
            bank_wait(bs)
            cP.tick(pe.matmul(pb[bs][:, 0:128], ktm[par][:, :], vaug[par][:, 0:128],
                              start=True, stop=True))
            t_s = cP.tick(pe.matmul(pb[bs][:, 128:256], ktm[par][:, :], vaug[par][:, 128:256],
                                    start=True, stop=True))
            S.wait(ve, cP, t_s)
            gh0, gh1 = 2 * hp, 2 * hp + 1
            vop(ve.tensor_add(s_run[0:64, gh0, 0:128], s_run[0:64, gh0, 0:128],
                              pb[bs][0:64, 0:128]))
            t_up = vop(ve.tensor_add(s_run[64:128, gh1, 0:128], s_run[64:128, gh1, 0:128],
                                     pb[bs][64:128, 128:256]))
            bank_used(bs, cV, t_up)
        war["ktm"][par] = cP.n
        war["vaug"][par] = cP.n
        return t_ktm, t_v1

    if cfg.get("stop") == "ph0":
        return probe_exit()
    tok_xpre = sdma(xbuf[:, :, :], x_pre)
    S.wait(ae, dX, tok_xpre)
    S.wait(ve, dX, tok_xpre)
    S.wait(ae, cV, t_b2l)
    for j in range(TC):
        emit_ln_chunk(xbuf[:, j, :], j * 128, (0, 1))

    dI = Ctr(nc, "dI")
    for s0 in range(XCR // 128):
        gp.dma_start(out=xc[s0 * 128:(s0 + 1) * 128, 0:D],
                     in_=zx_in).then_inc(dI.h, 16)
        gp.dma_start(out=xc[s0 * 128:(s0 + 1) * 128, D:D + 9],
                     in_=xcid_in[:, s0, 0:9]).then_inc(dI.h, 16)
        gp.dma_start(out=xc[s0 * 128:(s0 + 1) * 128, D + 9:XCW],
                     in_=zx_in[:, 0:XCW - D - 9]).then_inc(dI.h, 16)
        dI.n += 48
    for r0 in range(T, YAR, 128):
        n_r = min(128, YAR - r0)
        gp.dma_start(out=y_acc[r0:r0 + n_r, :], in_=zx_in[0:n_r, :]).then_inc(dI.h, 16)
        dI.n += 16

    pend_k = w_dma(0, wk[0])
    pend_v = w_dma(1, wv[0])
    for hp in range(HP):
        tok_hi, tok_lo = w_split(0, pend_k)
        tok_k = emit_proj(kb, True, 0, tok_hi, tok_lo, (2, 3, 0))
        tok_hi, tok_lo = w_split(1, pend_v)
        tok_v = emit_proj(vb, False, 1, tok_hi, tok_lo, (2, 3, 0))
        if hp + 1 < HP:
            pend_k = w_dma(0, wk[hp + 1])
            pend_v = w_dma(1, wv[hp + 1])
        pend_st = None
        for cstep in range(TC + 1):
            if cstep < TC:
                _, t_v1p = mix_state_chunk(cstep, hp, cstep % 2, tok_k, tok_v,
                                           update_state=False)
                this_st = (cstep % 2, t_v1p)
            else:
                this_st = None
            if pend_st is not None:
                parp, t_v1p = pend_st
                bs = 5
                S.wait(pe, cA, t_v1p)
                bank_wait(bs)
                cP.tick(pe.matmul(pb[bs][:, 0:128], ktm[parp][:, :],
                                  vaug[parp][:, 0:128], start=True, stop=True))
                t_s = cP.tick(pe.matmul(pb[bs][:, 128:256], ktm[parp][:, :],
                                        vaug[parp][:, 128:256], start=True, stop=True))
                war["ktm"][parp] = max(war["ktm"][parp], t_s)
                war["vaug"][parp] = max(war["vaug"][parp], t_s)
                gh0, gh1 = 2 * hp, 2 * hp + 1
                S.wait(ve, cP, t_s)
                vop(ve.tensor_add(s_run[0:64, gh0, 0:128], s_run[0:64, gh0, 0:128],
                                  pb[bs][0:64, 0:128]))
                t_up = vop(ve.tensor_add(s_run[64:128, gh1, 0:128],
                                         s_run[64:128, gh1, 0:128],
                                         pb[bs][64:128, 128:256]))
                bank_used(bs, cV, t_up)
            pend_st = this_st

    vop(ve.tensor_scalar_mul(s_run[:, :, :], s_run[:, :, :], gate_s[:, 0:1]))
    if cfg.get("stop") == "prefix":
        return probe_exit()

    barrier()
    tok_xown = sdma(xbuf[:, :, :], x_own)
    S.wait(ae, dX, tok_xown)
    S.wait(ve, dX, tok_xown)
    for j in range(TC):
        emit_ln_chunk(xbuf[:, j, :], j * 128, (0, 1))

    dR = [Ctr(nc, f"dR{i}") for i in range(3)]
    dump_tok = {}
    pend_q = w_dma(0, wq[0])
    pend_k = w_dma(1, wk[0])
    t_outlast = 0
    for hp in range(HP):
        tok_hi, tok_lo = w_split(0, pend_q)
        tok_q = emit_proj(qb, True, 0, tok_hi, tok_lo, (2, 3, 0))
        tok_hi, tok_lo = w_split(1, pend_k)
        tok_k = emit_proj(kb, True, 1, tok_hi, tok_lo, (2, 3, 0))
        pend_v = w_dma(0, wv[hp])
        tok_hi, tok_lo = w_split(0, pend_v)
        tok_v = emit_proj(vb, False, 0, tok_hi, tok_lo, (2, 3, 0))
        pend_wo = w_dma(1, wo[hp])

        pend_mix = None
        for cstep in range(TC + 1):
            if cstep < TC:
                c = cstep
                cpar = c % 2
                par = c % 2
                t_ktm, t_v1 = mix_state_chunk(c, hp, par, tok_k, tok_v, update_state=False)
                S.wait(ae, cV, tok_q)
                S.wait(ae, cP, war["pad"][cpar])
                aop(ae.activation(qpad[0][cpar][0:64, :], qb[0:64, c * 128:(c + 1) * 128],
                                  AF.Copy))
                aop(ae.activation(kpad[0][cpar][0:64, :], kb[0:64, c * 128:(c + 1) * 128],
                                  AF.Copy))
                aop(ae.activation(qpad[1][cpar][64:128, :],
                                  qb[64:128, c * 128:(c + 1) * 128], AF.Copy))
                t_kp = aop(ae.activation(kpad[1][cpar][64:128, :],
                                         kb[64:128, c * 128:(c + 1) * 128], AF.Copy))
                this_mix = (c, cpar, par, t_kp, t_v1)
            else:
                this_mix = None
            if pend_mix is not None:
                c, cpar, par, t_kp, t_v1 = pend_mix
                b_at = 2 + cpar
                b_o = cpar
                b_s = 5
                gh0, gh1 = 2 * hp, 2 * hp + 1
                S.wait(pe, cA, t_kp)
                bank_wait(b_at)
                cP.tick(pe.matmul(pb[b_at][:, 0:128], kpad[0][cpar][:, :],
                                  qpad[0][cpar][:, :], start=True, stop=True))
                t_at = cP.tick(pe.matmul(pb[b_at][:, 128:256], kpad[1][cpar][:, :],
                                         qpad[1][cpar][:, :], start=True, stop=True))
                S.wait(ve, cP, max(t_at, war["am2"][cpar]))
                t_am = vop(ve.tensor_mul(am2_sb[cpar][:, :], pb[b_at][:, 0:256], um_s[:, :]))
                bank_used(b_at, cV, t_am)
                S.wait(pe, cV, t_am)
                S.wait(pe, cA, t_v1)
                bank_wait(b_o)
                cP.tick(pe.matmul(pb[b_o][:, 0:128], am2_sb[cpar][:, 0:128],
                                  vaug[par][:, 0:128], start=True, stop=True))
                t_o1 = cP.tick(pe.matmul(pb[b_o][:, 128:256], am2_sb[cpar][:, 128:256],
                                         vaug[par][:, 128:256], start=True, stop=True))
                S.wait(pe, cV, war["sadd"])
                cP.tick(pe.matmul(pb[b_o][:, 256:384], qpad[0][cpar][:, :],
                                  s_run[:, gh0, :], start=True, stop=True))
                t_o2 = cP.tick(pe.matmul(pb[b_o][:, 384:512], qpad[1][cpar][:, :],
                                         s_run[:, gh1, :], start=True, stop=True))
                war["pad"][cpar] = t_o2
                war["am2"][cpar] = t_o1
                war["vaug"][par] = max(war["vaug"][par], t_o1)
                bank_wait(b_s)
                cP.tick(pe.matmul(pb[b_s][:, 0:128], ktm[par][:, :], vaug[par][:, 0:128],
                                  start=True, stop=True))
                t_s = cP.tick(pe.matmul(pb[b_s][:, 128:256], ktm[par][:, :],
                                        vaug[par][:, 128:256], start=True, stop=True))
                war["ktm"][par] = max(war["ktm"][par], t_s)
                war["vaug"][par] = max(war["vaug"][par], t_s)
                S.wait(ve, cP, t_s)
                vop(ve.tensor_add(s_run[0:64, gh0, 0:128], s_run[0:64, gh0, 0:128],
                                  pb[b_s][0:64, 0:128]))
                t_up = vop(ve.tensor_add(s_run[64:128, gh1, 0:128],
                                         s_run[64:128, gh1, 0:128],
                                         pb[b_s][64:128, 128:256]))
                bank_used(b_s, cV, t_up)
                war["sadd"] = t_up
                S.wait(ve, cA, t_outlast)
                t_cp1 = vop(ve.tensor_copy(numsb[:, 0:256], pb[b_o][:, 0:256]))
                vop(ve.scalar_tensor_tensor(densb[:, 0:2], pb[b_o][:, 320:449:128],
                                            ATTN_EPS, numsb[:, 64:193:128],
                                            ALU.add, ALU.add))
                vop(ve.reciprocal(densb[:, 2:4], densb[:, 0:2]))
                t_n = vop(ve.tensor_add(numsb[:, 0:256], numsb[:, 0:256],
                                        pb[b_o][:, 256:512]))
                bank_used(b_o, cV, t_n)
                S.wait(ae, cV, t_n)
                aop(ae.activation(attn_tm[:, c, 0:64], numsb[:, 0:64], AF.Copy,
                                  scale=densb[:, 2:3]))
                t_outlast = aop(ae.activation(attn_tm[:, c, 64:128], numsb[:, 128:192], AF.Copy,
                                              scale=densb[:, 3:4]))
            pend_mix = this_mix
        tok_hi_o, tok_lo_o = w_split(1, pend_wo)
        t_wo_last = 0
        for j in range(TC):
            b_tr = 6
            S.wait(pe, cA, t_outlast)
            bank_wait(b_tr)
            t_acc_last = 0
            t_tr = cP.tick(pe.transpose(pb[b_tr][:, 0:128], attn_tm[:, j, :], id_s[:, :]))
            S.wait(ae, cP, t_tr)
            t_ah = aop(ae.activation(ah_sb[:, :], pb[b_tr][:, 0:128], AF.Copy))
            S.wait(ve, cP, t_tr)
            S.wait(ve, cA, t_ah)
            t_al = vop(ve.tensor_sub(al_sb[:, :], pb[b_tr][:, 0:128], ah_sb[:, :]))
            bank_used(b_tr, cV, t_al)
            for fh in range(NFH):
                b = (0, 1)[fh % 2]
                S.wait(pe, cA, max(t_ah, tok_hi_o))
                S.wait(pe, cV, max(t_al, tok_lo_o))
                bank_wait(b)
                pe.matmul(pb[b][:, 0:W2W], ah_sb[:, :], wsh[1][:, fh * W2W:(fh + 1) * W2W],
                          start=True, stop=False)
                pe.matmul(pb[b][:, 0:W2W], ah_sb[:, :], wsl[1][:, fh * W2W:(fh + 1) * W2W],
                          start=False, stop=False)
                i3 = pe.matmul(pb[b][:, 0:W2W], al_sb[:, :], wsh[1][:, fh * W2W:(fh + 1) * W2W],
                               start=False, stop=True)
                t_wo = cP.tick(i3)
                t_wo_last = t_wo
                S.wait(ve, cP, t_wo)
                t_acc = vop(ve.scalar_tensor_tensor(xbuf[:, j, fh * W2W:(fh + 1) * W2W],
                                                    pb[b][:, 0:W2W], 1.0,
                                                    xbuf[:, j, fh * W2W:(fh + 1) * W2W],
                                                    ALU.mult, ALU.add))
                bank_used(b, cV, t_acc)
                t_acc_last = t_acc
            if hp == HP - 1:
                q = j % 3
                eng_q = (sy, ae, gp)[q]
                S.wait(eng_q, cV, t_acc_last)
                eng_q.dma_start(out=r2h[:, j, :], in_=xbuf[:, j, :]).then_inc(dR[q].h, 16)
                dR[q].n += 16
                dump_tok[j] = (q, dR[q].n)
        war["split"][1] = cP.n
        if hp + 1 < HP:
            pend_q = w_dma(0, wq[hp + 1])
            pend_k = w_dma(1, wk[hp + 1])


    if cfg.get("stop") == "own":
        return probe_exit()

    barrier()
    for eng in ALL_ENG:
        for q in range(3):
            S.wait(eng, dR[q], dR[q].n)
        S.wait(eng, dI, dI.n)
    attn_es.close()

    r2t = sbuf("r2t", [128, FTC, D])
    hst = sbuf("hst", [128, FT])
    hh = [sbuf(f"hh{i}", [128, FT], BF16) for i in range(2)]
    hl = [sbuf(f"hl{i}", [128, FT], BF16) for i in range(2)]
    w2stg = [sbuf(f"w2stg{i}", [128, D]) for i in range(2)]
    w2h = [sbuf(f"w2h{i}", [128, D], BF16) for i in range(2)]
    w2l = [sbuf(f"w2l{i}", [128, D], BF16) for i in range(2)]
    lg = sbuf("lg", [128, 8])
    lgm1 = sbuf("lgm1", [128, 8])
    lgk = sbuf("lgk", [128, 8])
    lgm2 = sbuf("lgm2", [128, 8])
    lgsc = sbuf("lgsc", [128, 8])
    combine = sbuf("combine", [128, TC, 8])
    comb_pad = sbuf("comb_pad", [128, 128])
    cth = sbuf("cth", [8, TC, 128], BF16)
    ctl = sbuf("ctl", [8, TC, 128], BF16)
    ct_f = sbuf("ct_f", [8, 128])
    slt_s = sbuf("slt_s", [128, 128], BF16)
    onesb_s = sbuf("onesb_s", [128, 128], BF16)
    offs_s = sbuf("offs_s", [128, 8])
    capd_s = sbuf("capd_s", [128, 1])
    iota0_s = sbuf("iota0_s", [128, TC])
    iop_s = sbuf("iop_s", [128, 1])
    maskp = sbuf("maskp", [128, 128])
    masks_bf = sbuf("masks_bf", [128, 128], BF16)
    base_sb = sbuf("base_sb", [128, 8])
    posx = sbuf("posx", [128, 8])
    post = sbuf("post", [128, 8])
    posf = sbuf("posf", [128, 2])
    posu = sbuf("posu", [128, 2], U32)
    xg = [sbuf(f"xg{i}", [128, CAPS, XCW]) for i in range(2)]
    xgh = sbuf("xgh", [128, KC, FT], BF16)
    xgl = sbuf("xgl", [128, KC, FT], BF16)
    ysb = sbuf("ysb", [128, FT // 128, D])
    idxu = [sbuf(f"idxu{i}", [128, CAPS], U32) for i in range(2)]
    dSc = Ctr(nc, "dSc")
    sdma(slt_s[:, :], slt_in)
    sdma(onesb_s[:, :], onesb_in)
    sdma(offs_s[:, :], offs_in)
    sdma(capd_s[:, :], capd_in)
    sdma(iota0_s[:, :], iota0_in)
    tok_iot = sdma(iop_s[:, :], iop_in)
    vop(ve.memset(comb_pad[:, :], 0.0))
    vop(ve.memset(maskp[:, :], 0.0))
    vop(ve.memset(base_sb[:, :], 0.0))
    vop(ve.memset(xg[0][:, :, :], 0.0))
    vop(ve.memset(xg[1][:, :, :], 0.0))

    war2 = {"w2stage": [[(None, 0)], [(None, 0)]], "w2split": [0, 0], "h": [0, 0]}

    def w2_dma(par, src_ap):
        eng = (gp, ae)[par]
        for ctr, tok in war2["w2stage"][par]:
            if ctr is not None:
                S.wait(eng, ctr, tok)
        S.wait(eng, dG[par], dG[par].n)
        eng.dma_start(out=w2stg[par][:, :], in_=src_ap).then_inc(dG[par].h, 16)
        dG[par].n += 16
        return dG[par].n

    def w2_split(par, tok_dma):
        S.wait(ae, dG[par], tok_dma)
        S.wait(ae, cP, war2["w2split"][par])
        t_hi = aop(ae.activation(w2h[par][:, :], w2stg[par][:, :], AF.Copy))
        S.wait(ve, dG[par], tok_dma)
        S.wait(ve, cP, war2["w2split"][par])
        S.wait(ve, cA, t_hi)
        t_lo = vop(ve.tensor_sub(w2l[par][:, :], w2stg[par][:, :], w2h[par][:, :]))
        war2["w2stage"][par] = [(cA, t_hi), (cV, t_lo)]
        return t_hi, t_lo

    tok_xc_init = dI.n

    NG = E * NCS * HC

    def g_src(g):
        e = g // (NCS * HC)
        m = g % HC
        return e, m

    tok_w1 = {}
    tok_w2 = {}
    for g0 in range(min(2, NG)):
        e0, m0 = g_src(g0)
        tok_w1[g0] = w_dma(g0 % 2, w1[e0, m0])
        tok_w2[g0] = w2_dma(g0 % 2, w2[e0, m0])

    last_out_dma = 0
    tok_r2t_consumed = [(None, 0)]
    for ft in range(NFT):
        for ctr, tok in tok_r2t_consumed:
            if ctr is not None:
                S.wait(sy, ctr, tok)
        S.wait(sy, dO, last_out_dma)
        tok_r2t = sdma(r2t[:, :, :], r2h[:, ft * FTC:(ft + 1) * FTC, :])
        S.wait(ae, dX, tok_r2t)
        S.wait(ve, dX, tok_r2t)
        for jj in range(FTC):
            jabs = ft * FTC + jj
            col0 = jabs * 128
            t_ap = emit_ln_chunk(r2t[:, jj, :], col0, (4, 5))
            first = True
            i = None
            for k in range(KC):
                for p in range(3):
                    lhsT = (lnh if p != 2 else lnl)[:, k, col0:col0 + 128]
                    rhs = (wr_h if p != 1 else wr_l)[:, k, :]
                    if first:
                        S.wait(pe, cV, cV.n)
                        S.wait(pe, cA, cA.n)
                        bank_wait(6)
                    i = pe.matmul(pb[6][:, 0:8], lhsT, rhs, start=first,
                                  stop=(k == KC - 1 and p == 2))
                    first = False
            t_lgmm = cP.tick(i)
            S.wait(ve, cP, t_lgmm)
            t_lg = vop(ve.tensor_copy(lg[:, :], pb[6][:, 0:8]))
            bank_used(6, cV, t_lg)
            vop(ve.reduce_max(lgsc[:, 0:1], lg[:, :], axis=AX.X))
            vop(ve.tensor_scalar(lgm1[:, :], lg[:, :], lgsc[:, 0:1], None, ALU.is_equal))
            vop(ve.scalar_tensor_tensor(lgk[:, :], lgm1[:, :], -1e30, lg[:, :],
                                        ALU.mult, ALU.add))
            vop(ve.reduce_max(lgsc[:, 1:2], lgk[:, :], axis=AX.X))
            vop(ve.tensor_scalar(lgm2[:, :], lgk[:, :], lgsc[:, 1:2], None, ALU.is_equal))
            t_d = vop(ve.tensor_sub(lgsc[:, 2:3], lgsc[:, 1:2], lgsc[:, 0:1]))
            S.wait(ae, cV, t_d)
            t_e = aop(ae.activation(lgsc[:, 3:4], lgsc[:, 2:3], AF.Exp))
            S.wait(ve, cA, t_e)
            vop(ve.tensor_scalar_add(lgsc[:, 4:5], lgsc[:, 3:4], 1.0))
            vop(ve.reciprocal(lgsc[:, 5:6], lgsc[:, 4:5]))
            vop(ve.tensor_mul(lgsc[:, 6:7], lgsc[:, 3:4], lgsc[:, 5:6]))
            vop(ve.tensor_scalar(combine[:, jabs, :], lgm1[:, :], lgsc[:, 5:6], None, ALU.mult))
            t_c2 = vop(ve.scalar_tensor_tensor(combine[:, jabs, :], lgm2[:, :], lgsc[:, 6:7],
                                               combine[:, jabs, :], ALU.mult, ALU.add))
            t_cp = vop(ve.tensor_copy(comb_pad[:, 0:8], combine[:, jabs, :]))
            S.wait(pe, cV, t_cp)
            bank_wait(7)
            t_ct = cP.tick(pe.transpose(pb[7][:, 0:128], comb_pad[:, :], id_s[:, :]))
            S.wait(ve, cP, t_ct)
            t_ctf = vop(ve.tensor_copy(ct_f[:, :], pb[7][0:8, 0:128]))
            bank_used(7, cV, t_ctf)
            t_cth = vop(ve.tensor_copy(cth[:, jabs, :], ct_f[:, :]))
            vop(ve.tensor_sub(ctl[:, jabs, :], ct_f[:, :], cth[:, jabs, :]))
            S.wait(ve, cP, war["mbf_pe"])
            vop(ve.tensor_add(maskp[:, 0:8], lgm1[:, :], lgm2[:, :]))
            t_mb = vop(ve.tensor_copy(masks_bf[:, :], maskp[:, :]))
            S.wait(pe, cV, t_mb)
            bank_wait(3)
            t_pf = cP.tick(pe.matmul(pb[3][:, 0:128], slt_s[:, :], masks_bf[:, :],
                                     start=True, stop=True))
            S.wait(ve, cP, t_pf)
            t_px = vop(ve.tensor_add(posx[:, :], base_sb[:, :], pb[3][:, 0:8]))
            bank_used(3, cV, t_px)
            vop(ve.tensor_scalar(posx[:, :], posx[:, :], capd_s[:, 0:1], None, ALU.min))
            vop(ve.tensor_add(posx[:, :], posx[:, :], offs_s[:, :]))
            vop(ve.tensor_mul(post[:, :], lgm1[:, :], posx[:, :]))
            vop(ve.reduce_sum(posf[:, 0:1], post[:, :], axis=AX.X))
            vop(ve.tensor_mul(post[:, :], lgm2[:, :], posx[:, :]))
            vop(ve.reduce_sum(posf[:, 1:2], post[:, :], axis=AX.X))
            S.wait(ve, dSc, dSc.n)
            t_pu = vop(ve.tensor_copy(posu[:, :], posf[:, :]))
            bank_wait(7)
            t_tt = cP.tick(pe.matmul(pb[7][:, 0:128], onesb_s[:, :], masks_bf[:, :],
                                     start=True, stop=True))
            war["mbf_pe"] = t_tt
            S.wait(ve, cP, t_tt)
            t_bu = vop(ve.tensor_add(base_sb[:, :], base_sb[:, :], pb[7][:, 0:8]))
            bank_used(7, cV, t_bu)
            t_g8 = vop(ve.tensor_copy(lnx[:, D:D + 8], combine[:, jabs, :]))
            t_gi = vop(ve.tensor_copy(lnx[:, D + 8:D + 9], iota0_s[:, jabs:jabs + 1]))
            S.wait(gp, cV, cV.n)
            S.wait(gp, dI, tok_xc_init)
            for sl in range(2):
                S.wait(gp, dSc, dSc.n)
                gp.indirect_dma_start(
                    xc, bass.IndirectOffsetOnAxis(ap=posu[:, sl:sl + 1], axis=0),
                    lnx[:, 0:XCW], None).then_inc(dSc.h, 16)
                dSc.n += 16
            war["lntm_dma"] = (dSc, dSc.n)
        S.wait(ve, dO, last_out_dma)
        for jj in range(FTC):
            jabs = ft * FTC + jj
            for fh in range(NFH):
                b = 6 + ((jj * NFH + fh) % 2)
                S.wait(pe, cV, cV.n)
                bank_wait(b)
                pe.matmul(pb[b][:, 0:W2W], cth[:, jabs, :], b2_h[:, fh * W2W:(fh + 1) * W2W],
                          start=True, stop=False)
                pe.matmul(pb[b][:, 0:W2W], cth[:, jabs, :], b2_l[:, fh * W2W:(fh + 1) * W2W],
                          start=False, stop=False)
                i3 = pe.matmul(pb[b][:, 0:W2W], ctl[:, jabs, :], b2_h[:, fh * W2W:(fh + 1) * W2W],
                               start=False, stop=True)
                t_b = cP.tick(i3)
                S.wait(ve, cP, t_b)
                t_ev = vop(ve.scalar_tensor_tensor(
                    r2t[:, jj, fh * W2W:(fh + 1) * W2W], pb[b][:, 0:W2W], 1.0,
                    r2t[:, jj, fh * W2W:(fh + 1) * W2W], ALU.mult, ALU.add))
                bank_used(b, cV, t_ev)
        S.wait(gp, cV, cV.n)
        S.wait(gp, dO, dO.n)
        gp.dma_start(out=y_acc[ft * FT:(ft + 1) * FT, :].rearrange("(c p) d -> p c d", p=128),
                     in_=r2t[:, :, :]).then_inc(dO.h, 16)
        dO.n += 16
        last_out_dma = dO.n
        tok_r2t_consumed = [(cA, cA.n), (cV, cV.n)]

    if cfg.get("ffn_mode") == "router":
        return probe_exit(src=r2t[:, :, :], dst=y_pc[:, 0:FTC, :], close_attn=False)

    tok_comp_done = dSc.n
    dS2 = [Ctr(nc, "dS2a"), Ctr(nc, "dS2b")]
    war3 = {"xg_pe": [0, 0], "xg_dve": [0, 0],
            "idx_ds2": {0: (0, 0), 1: (0, 0)},
            "ysb_tok": {}}
    tok_xr = {}

    def load_region(e):
        pe_ = e % 2
        S.wait(sy, dI, tok_xc_init)
        S.wait(sy, dSc, tok_comp_done)
        S.wait(sy, cP, war3["xg_pe"][pe_])
        S.wait(sy, cV, war3["xg_dve"][pe_])
        tok_xr[e] = sdma(xg[pe_][:, :, :],
                         xc[e * (CAP + 128):e * (CAP + 128) + CAP, :]
                         .rearrange("(q p) c -> p q c", p=128))

    load_region(0)
    g = 0
    for e in range(E):
        par_e = e % 2
        xge = xg[par_e]
        idxe = idxu[par_e]
        if e + 1 < E:
            load_region(e + 1)
        S.wait(ve, dX, tok_xr[e])
        tk0, tk1 = war3["idx_ds2"][par_e]
        S.wait(ve, dS2[0], tk0)
        S.wait(ve, dS2[1], tk1)
        t_idx = vop(ve.tensor_copy(idxe[:, :], xge[:, :, D + 8]))
        ns = 0
        for csub in range(NCS):
            for sl in range(CSUB // 128):
                slot = csub * (CSUB // 128) + sl
                for k in range(KC):
                    b = 6 + (k % 2)
                    S.wait(pe, dX, tok_xr[e])
                    bank_wait(b)
                    t_tr = cP.tick(pe.transpose(pb[b][:, 0:128],
                                                xge[:, slot, k * 128:(k + 1) * 128],
                                                id_s[:, :]))
                    S.wait(ae, cP, t_tr)
                    t_hi = aop(ae.activation(xgh[:, k, sl * 128:(sl + 1) * 128],
                                             pb[b][:, 0:128], AF.Copy))
                    S.wait(ve, cP, t_tr)
                    S.wait(ve, cA, t_hi)
                    t_lo = vop(ve.tensor_sub(xgl[:, k, sl * 128:(sl + 1) * 128],
                                             pb[b][:, 0:128],
                                             xgh[:, k, sl * 128:(sl + 1) * 128]))
                    bank_used(b, cV, t_lo)
                    bank_used(b, cA, t_hi)
            war3["xg_pe"][par_e] = cP.n
            t_split = cV.n
            y_stop = [0] * NBY
            pend_w2 = None
            for mstep in range(HC + 1):
                if mstep < HC:
                    m = mstep
                    par = g % 2
                    tok_hi, tok_lo = w_split(par, tok_w1[g])
                    bw1 = 4 + (m % 2)
                    S.wait(pe, cA, tok_hi)
                    S.wait(pe, cV, max(tok_lo, t_split))
                    bank_wait(bw1)
                    tok_mm = mm3(bw1, par, CSUB, 0, src_h=xgh, src_l=xgl)
                    war["split"][par] = tok_mm
                    tok2_hi, tok2_lo = w2_split(par, tok_w2[g])
                    if g + 2 < NG:
                        e2, m2 = g_src(g + 2)
                        tok_w1[g + 2] = w_dma(par, w1[e2, m2])
                        tok_w2[g + 2] = w2_dma(par, w2[e2, m2])
                    hpar = m % 2
                    S.wait(ae, cP, max(tok_mm, war2["h"][hpar]))
                    t_g = aop(ae.activation(hst[:, 0:CSUB], pb[bw1][:, 0:CSUB], AF.Gelu,
                                            bias=b1_s[:, e, m:m + 1]))
                    bank_used(bw1, cA, t_g)
                    S.wait(ve, cA, t_g)
                    S.wait(ve, cP, war2["h"][hpar])
                    t_hh = vop(ve.tensor_copy(hh[hpar][:, :], hst[:, 0:CSUB]))
                    t_hl = vop(ve.tensor_sub(hl[hpar][:, :], hst[:, 0:CSUB], hh[hpar][:, :]))
                    this_w2 = (m, par, hpar, t_hh, t_hl, tok2_hi, tok2_lo)
                    g += 1
                else:
                    this_w2 = None
                if pend_w2 is not None:
                    m, par, hpar, t_hh, t_hl, tok2_hi, tok2_lo = pend_w2
                    S.wait(pe, cA, tok2_hi)
                    S.wait(pe, cV, max(t_hh, t_hl, tok2_lo))
                    i = None
                    for j in range(CSUB // 128):
                        for fh in range(NFH):
                            by = j * NFH + fh
                            if m == 0:
                                bank_wait(by)
                            for p in range(3):
                                lhsT = (hh if p != 2 else hl)[hpar][:, j * 128:(j + 1) * 128]
                                rhs = (w2h if p != 1 else w2l)[par][:, fh * W2W:(fh + 1) * W2W]
                                i = pe.matmul(pb[by][:, 0:W2W], lhsT, rhs,
                                              start=(m == 0 and p == 0),
                                              stop=(m == HC - 1 and p == 2))
                            if m == HC - 1:
                                y_stop[by] = cP.tick(i)
                    if m != HC - 1:
                        t_end = cP.tick(i)
                        war2["w2split"][par] = t_end
                        war2["h"][hpar] = t_end
                    else:
                        war2["w2split"][par] = y_stop[-1]
                        war2["h"][hpar] = y_stop[-1]
                pend_w2 = this_w2
            for j in range(CSUB // 128):
                slot = csub * (CSUB // 128) + j
                for fh in range(NFH):
                    by = j * NFH + fh
                    S.wait(ve, cP, y_stop[by])
                    if j in war3["ysb_tok"]:
                        pty, ptk = war3["ysb_tok"][j]
                        S.wait(ve, dS2[pty], ptk)
                    t_ev = vop(ve.tensor_scalar(
                        ysb[:, j, fh * W2W:(fh + 1) * W2W], pb[by][:, 0:W2W],
                        xge[:, slot, D + e:D + e + 1], None, ALU.mult))
                    bank_used(by, cV, t_ev)
            for j in range(CSUB // 128):
                slot = csub * (CSUB // 128) + j
                p = ns % 2
                S.wait(gp, cV, cV.n)
                S.wait(gp, dO, dO.n)
                if ns < 2:
                    S.wait(gp, dS2[0], dS2[0].n)
                    S.wait(gp, dS2[1], dS2[1].n)
                else:
                    S.wait(gp, dS2[p], dS2[p].n)
                gp.indirect_dma_start(
                    y_acc, bass.IndirectOffsetOnAxis(ap=idxe[:, slot:slot + 1], axis=0),
                    ysb[:, j, :], None,
                    compute_op=ALU.add).then_inc(dS2[p].h, 16)
                dS2[p].n += 16
                war3["ysb_tok"][j] = (p, dS2[p].n)
                ns += 1
        war3["xg_dve"][par_e] = cV.n
        war3["idx_ds2"][par_e] = (dS2[0].n, dS2[1].n)

    for ft in range(NFT):
        S.wait(sy, dSc, dSc.n)
        S.wait(sy, dS2[0], dS2[0].n)
        S.wait(sy, dS2[1], dS2[1].n)
        S.wait(sy, dO, dO.n)
        S.wait(sy, cV, cV.n)
        tok_l = sdma(r2t[:, :, :],
                     y_acc[ft * FT:(ft + 1) * FT, :].rearrange("(c p) d -> p c d", p=128))
        S.wait(gp, dX, tok_l)
        S.wait(gp, dO, dO.n)
        gp.dma_start(out=y_pc[:, ft * FTC:(ft + 1) * FTC, :], in_=r2t[:, :, :]).then_inc(dO.h, 16)
        dO.n += 16

    S.wait(gp, dO, dO.n)
    S.wait(gp, dSc, dSc.n)
    S.wait(gp, dS2[0], dS2[0].n)
    S.wait(gp, dS2[1], dS2[1].n)
    es.close()
    return nc


def _stage_inputs(inputs, cfg, core):
    D, T = cfg["D"], cfg["T"]
    TC = T // 128
    x = np.ascontiguousarray(np.asarray(inputs["L0_attn_x"], dtype=np.float32))
    b, half = core // 2, core % 2
    own = x[b, half * T:(half + 1) * T]
    pre = x[b, 0:T] if half == 1 else own
    shuffle = lambda a: np.ascontiguousarray(a.reshape(TC, 128, D).transpose(1, 0, 2))
    return {
        "x_own": shuffle(own),
        "x_pre": shuffle(pre),
        "gate": np.full((128, 1), float(half), np.float32),
    }


def _stage_weights(inputs, cfg):
    D, HID, E = cfg["D"], cfg["HID"], cfg["E"]
    KC, HC = D // 128, HID // 128
    HP = (D // 64) // 2
    f32 = lambda a: np.asarray(a, dtype=np.float32)
    Wq, Wk, Wv, Wo = (f32(inputs[f"L0_attn_W{n}"]) for n in "qkvo")
    W1, W2 = f32(inputs["L0_ffn_W1"]), f32(inputs["L0_ffn_W2"])
    b1, b2, Wr = f32(inputs["L0_ffn_b1"]), f32(inputs["L0_ffn_b2"]), f32(inputs["L0_ffn_Wr"])

    def lhsT_lay(W):
        a = W.reshape(HP, 128, KC, 128).transpose(0, 3, 2, 1)
        return np.ascontiguousarray(a.reshape(HP, 128, KC * 128))

    import ml_dtypes
    T, CAP, E_ = cfg["T"], cfg["CAP"], cfg["E"]
    TC = T // 128
    bfl = np.dtype(ml_dtypes.bfloat16)
    return {
        "slt_in": np.tril(np.ones((128, 128), np.float32), -1).T.astype(bfl),
        "onesb_in": np.ones((128, 128), bfl),
        "offs_in": np.tile((np.arange(8) * (CAP + 128)).astype(np.float32), (128, 1)),
        "capd_in": (CAP + np.arange(128, dtype=np.float32))[:, None].copy(),
        "iota0_in": np.ascontiguousarray(
            (np.arange(TC)[None, :] * 128 + np.arange(128)[:, None]).astype(np.float32)),
        "iop_in": np.arange(128, dtype=np.float32)[:, None].copy(),
        "xcid_in": (lambda ns: (lambda a: (a.__setitem__((slice(None), slice(None), 8),
                    (np.float32(T) + 128 * np.arange(ns, dtype=np.float32)[None, :]
                     + np.arange(128, dtype=np.float32)[:, None])), a)[1])(
                    np.zeros((128, ns, 9), np.float32)))(E * (CAP + 128) // 128),
        "zx_in": np.zeros((128, D), np.float32),
        "wq": lhsT_lay(Wq), "wk": lhsT_lay(Wk), "wv": lhsT_lay(Wv),
        "wo": np.ascontiguousarray(Wo.T.reshape(HP, 128, D)),
        "w1": np.ascontiguousarray(
            W1.reshape(E, KC, 128, HC, 128).transpose(0, 3, 2, 1, 4).reshape(E, HC, 128, KC * 128)),
        "w2": np.ascontiguousarray(W2.reshape(E, HC, 128, D)),
        "b1": np.ascontiguousarray(b1.reshape(E, HC, 128).transpose(2, 0, 1)),
        "b2": np.ascontiguousarray(b2),
        "wr": np.ascontiguousarray(Wr.reshape(KC, 128, 8).transpose(1, 0, 2)),
        "um": np.tile(np.triu(np.ones((128, 128), np.float32)), (1, 2)),
        "ident": np.eye(128, dtype=np.float32),
    }


def kernel(**inputs):
    cfg = FULL
    n_cores = 8
    weights = _stage_weights(inputs, cfg)
    in_maps = []
    for core in range(n_cores):
        m = dict(weights)
        m.update(_stage_inputs(inputs, cfg, core))
        in_maps.append(m)
    nc = build_program(cfg)
    res = run_bass_kernel_spmd(nc, in_maps, core_ids=list(range(n_cores)))
    T, D = cfg["T"], cfg["D"]
    out = np.empty((4, 4096, D), np.float32)
    for core in range(n_cores):
        b, half = core // 2, core % 2
        out[b, half * T:(half + 1) * T] = np.asarray(res.results[core]["y"])
    return out
